# version 05
# baseline (speedup 1.0000x reference)
import math
import numpy as np
import ml_dtypes
import concourse.bass as bass
import concourse.mybir as mybir
from concourse.bass_utils import run_bass_kernel_spmd
from contextlib import ExitStack

F32 = mybir.dt.float32
BF16 = mybir.dt.bfloat16
U32 = mybir.dt.uint32
I32 = mybir.dt.int32
AF = mybir.ActivationFunctionType
ALU = mybir.AluOpType
AX = mybir.AxisListType

ENGS = ("sync", "vector", "scalar", "tensor", "gpsimd")


class Cfg:
    def __init__(self, T):
        self.D, self.H, self.DH, self.M, self.E, self.F, self.NB = 1024, 16, 64, 256, 8, 4096, 4
        self.CORES = 8
        self.T = T
        self.TPC = T // self.CORES
        self.N = T // self.NB
        self.CAP = T // self.E
        self.CELL = self.CAP // self.NB + 128
        self.SLOTS = self.CELL * self.NB
        self.TILES = self.SLOTS // 128
        self.CELLT = self.CELL // 128
        self.NT = self.TPC // 128
        self.GA = self.TPC // 512
        assert self.TPC % 512 == 0 and self.CELL % 128 == 0
        self.BGT = 3 if self.CELLT % 3 == 0 else (2 if self.CELLT % 2 == 0 else 1)
        self.BG = self.TILES // self.BGT
        self.EGT = 4 if self.TILES % 4 == 0 else self.BGT
        self.EG = self.TILES // self.EGT
        self.FH = self.F // 2
        self.EPS = 1e-4
        self.dn = float(self.DH) ** (-0.25)
        self.ratio = float(self.M) ** (-0.5)
        self.LNEPS = math.log(self.EPS)
        self.BIG = 1 << 28


class Sem:
    __slots__ = ("h", "name", "n")

    def __init__(self, h, name):
        self.h, self.name, self.n = h, name, 0


class Builder:
    def __init__(self, cfg):
        self.c = cfg
        self.nc = bass.Bass("TRN2", target_bir_lowering=False, debug=False)
        self.sched = {e: [] for e in ENGS}
        self.hw = {}
        self.scopes = []
        self.waited = set()
        self.sem_registry = []
        self.gp_regs = {}
        import os as _os
        self.opcut = int(_os.environ.get("MK_OPCUT", "1000000000"))
        self.opcount = 0
        self.cur_phase = "init"

    def sem(self, name):
        sm = Sem(self.st.enter_context(self.nc.semaphore(name)), name)
        self.sem_registry.append(sm)
        return sm

    def push_scope(self):
        self.scopes.append(ExitStack())
        self.scopes[-1].__enter__()

    def pop_scope(self):
        self.scopes.pop().__exit__(None, None, None)

    def fence(self):
        sems = {"tensor": self.sPE, "vector": self.sDVE, "scalar": self.sACT,
                "gpsimd": self.sGP}
        engset = set(sems.values())
        extra = [(sm, sm.n) for sm in self.sem_registry
                 if sm not in engset and sm.n > 0]
        tot = {}
        for eng, sm in sems.items():
            tot[eng] = self.op(eng, lambda e: e.nop(), inc=sm)
        for eng in ("tensor", "vector", "scalar", "gpsimd", "sync"):
            ws = [(sems[nm], tot[nm]) for nm in sems if nm != eng] + extra
            if eng == "sync":
                ws = [(sems[nm], tot[nm]) for nm in sems] + extra
                self.op("sync", lambda e: e.nop(), waits=ws)
            else:
                self.op(eng, lambda e: e.nop(), waits=ws, inc=sems[eng])

    def op(self, eng, fn, waits=(), inc=None, incn=1):
        ws = []
        for s, v in waits:
            if v <= 0:
                continue
            self.waited.add((s.name, v))
            key = (eng, s.name)
            if self.hw.get(key, 0) >= v:
                continue
            self.hw[key] = v
            ws.append((s, v))
        out = None
        if inc is not None:
            inc.n += incn
            out = inc.n
        self.opcount += 1
        if self.opcount <= self.opcut:
            self.sched[eng].append((ws, fn, inc, incn, out, self.cur_phase))
        return out

    def breg(self, e, val):
        r = self.gp_regs.get(val)
        if r is None:
            r = e.to_reg(val)
            self.gp_regs[val] = r
        return r

    def copy(self, eng, dst, src, waits=()):
        if eng == "scalar":
            return self.op(eng, lambda e: e.activation(out=dst, in_=src, func=AF.Copy,
                                                       bias=0.0, scale=1.0),
                           waits=waits, inc=self.sACT)
        return self.op(eng, lambda e: e.tensor_copy(out=dst, in_=src), waits=waits,
                       inc=self.sDVE)

    def sb(self, name, shape, dt=F32):
        st = self.scopes[-1] if self.scopes else self.st
        return st.enter_context(self.nc.sbuf_tensor(name, shape, dt))

    def sb_outer(self, name, shape, dt=F32):
        return self.st.enter_context(self.nc.sbuf_tensor(name, shape, dt))

    def ps(self, name, shape, dt=F32):
        return self.st.enter_context(self.nc.psum_tensor(name, shape, dt))

    def din(self, n, s, d=F32):
        return self.nc.dram_tensor(n, s, d, kind="ExternalInput").ap()

    def dout(self, n, s, d=F32):
        return self.nc.dram_tensor(n, s, d, kind="ExternalOutput").ap()

    def dint(self, n, s, d=F32):
        return self.nc.dram_tensor(n, s, d).ap()

    def build(self):
        c = self.c
        D, H, DH, M, E = c.D, c.H, c.DH, c.M, c.E

        self.xmy = self.din("xmy", [c.TPC, D])
        self.xfull = self.din("xfull", [c.T, D])
        self.wq = self.din("wq", [D, D]); self.wk = self.din("wk", [D, D])
        self.wv = self.din("wv", [D, D]); self.wo = self.din("wo", [D, D])
        self.boi = self.din("boi", [1, D])
        self.ppi = self.din("ppi", [M, DH])
        self.wri = self.din("wri", [D, E])
        self.w1i = self.din("w1i", [D, c.F])
        self.w2i = self.din("w2i", [c.F, D])
        self.meta = self.din("meta", [128, 8])
        self.cid = self.din("cident", [128, 128])
        self.cidb = self.din("cidentb", [128, 128], BF16)
        self.cust = self.din("custrict", [128, 128])
        self.cd16n = self.din("conesd16n", [128, 128], BF16)
        self.cd16p = self.din("conesd16p", [128, 128], BF16)
        self.coc8 = self.din("conescol8", [128, 64], BF16)
        self.c1x128f = self.din("cones1x128f", [1, 128])
        self.c1x128b = self.din("cones1x128b", [1, 128], BF16)
        self.c1x64b = self.din("cones1x64b", [1, 64], BF16)
        self.c128x1f = self.din("cones128x1f", [128, 1])
        self.c128x1b = self.din("cones128x1b", [128, 1], BF16)
        self.ciota8 = self.din("ciota8", [128, 8])
        self.ciotap = self.din("ciotap", [128, 1])
        self.cconstf = self.din("cconstf", [128, 4])

        self.o_rows = self.dout("o_rows", [c.SLOTS, D])
        self.o_idx = self.dout("o_idx", [128, c.TILES], I32)

        self.rec_in = self.dint("rec_in", [E * c.SLOTS, 2])
        self.rec_out = self.dint("rec_out", [8 * E * c.SLOTS, 2])
        self.cc1_in = self.dint("cc1_in", [1, 8])
        self.cc1_out = self.dint("cc1_out", [8, 8])
        self.PKW = 2048 + 32 + 16 + 1 + 8
        self.pk_in = self.dint("pk_in", [128, self.PKW])
        self.pk_out = self.dint("pk_out", [8 * 128, self.PKW])
        self.xye_scr = self.dint("xye_scr", [c.SLOTS, D])
        self.xefm_scr = self.dint("xefm_scr", [c.EG * 128, 8 * c.EGT * 128], BF16)
        self.wrow_dram = self.dint("wrow_dram", [c.BG * H * c.BGT, 128], BF16)
        self.srow_dram = self.dint("srow_dram", [1, 1024])

        with ExitStack() as st:
            self.st = st
            s = self
            s.sXL = [s.sem("sXL0"), s.sem("sXL1")]
            s.sWA = s.sem("sWA"); s.sSC = s.sem("sSC")
            s.sWE = [s.sem("sWE0"), s.sem("sWE1")]
            s.sC1 = s.sem("sC1"); s.sC2a = s.sem("sC2a"); s.sC2b = s.sem("sC2b")
            s.sCD = s.sem("sCD"); s.sMG = s.sem("sMG")
            s.sEG = [s.sem("sEG0"), s.sem("sEG1")]
            s.sXF = s.sem("sXF"); s.sYW = s.sem("sYW"); s.sYA = s.sem("sYA")
            s.sBR = [s.sem("sBR0"), s.sem("sBR1")]
            s.sWR = [s.sem("sWR0"), s.sem("sWR1")]
            s.sOUT = s.sem("sOUT"); s.sSR = s.sem("sSR")
            s.sOT = [s.sem("sOT0"), s.sem("sOT1")]
            s.sXE = [s.sem("sXE0"), s.sem("sXE1")]
            s.sYB = s.sem("sYB")
            s.sYWb = s.sem("sYWb")
            s.sXF2 = [s.sem("sXF0"), s.sem("sXF1")]
            s.sPE = s.sem("sPE"); s.sDVE = s.sem("sDVE")
            s.sACT = s.sem("sACT"); s.sGP = s.sem("sGP")
            s.sONE = s.sem("sONE")
            s.op("vector", lambda e: e.nop(), inc=s.sONE)

            s.PF = [s.ps(f"PF{i}", [128, 512]) for i in range(6)]
            s.PB = [s.ps(f"PBb{i}", [128, 256], BF16) for i in range(2)]

            FQ = c.F // 4
            s.FQ = FQ
            s.W1Q = [s.sb("W1Q0", [128, 8 * FQ], BF16)]
            s.W2Q = [s.sb("W2Q0", [128, (FQ // 128) * 1024], BF16)]
            s.v_wq = [0, 0, 0, 0]

            s.RSCL = s.sb_outer("RSCL", [128, c.TILES])
            s.GIDX = s.sb_outer("GIDX", [128, c.TILES], U32)
            steps = [
                self.phase0_consts,
                lambda: self.load_w_quarter(0),
                self.push_scope,
                self.load_wkv,
                self.phase_a1,
                self.phase_coll1,
                self.phase_scatter,
                self.phase_a2,
                self.phase_coll2,
                self.phase_merge,
                self.pop_scope,
                self.fence,
                self._alloc_post_a,
                self.load_wq_wo,
                lambda: self.load_w_quarter(1),
                self.push_scope,
                self.phase_expert,
                self.pop_scope,
                self.fence,
                self.push_scope,
                self.phase_ctx,
                self.pop_scope,
                self.fence,
                self.push_scope,
                self.phase_b,
                self.pop_scope,
            ]
            import os as _os
            nsteps = int(_os.environ.get("MK_STEPS", "99"))
            for st_ in steps[:nsteps]:
                self.cur_phase = getattr(st_, "__name__", "misc")
                st_()
            self.cur_phase = "tail"
            while self.scopes:
                self.pop_scope()

            with self.nc.Block() as block:
                def mk(name):
                    def body(e):
                        prev = None
                        cur_scope = None
                        scope_cm = None
                        for ws, fn, inc, incn, val, ph in self.sched[name]:
                            if ph != cur_scope:
                                if scope_cm is not None:
                                    scope_cm.__exit__(None, None, None)
                                scope_cm = self.nc.named_scope(f"{ph}")
                                scope_cm.__enter__()
                                cur_scope = ph
                            if prev is not None and not ws and name != "sync":
                                e.wait_ge(prev[0].h, prev[1])
                            for sm, v in ws:
                                e.wait_ge(sm.h, v)
                            ins = fn(e)
                            if inc is not None:
                                assert ins is not None, f"then_inc on None ({name})"
                                ins.then_inc(inc.h, incn)
                            prev = ((inc, val)
                                    if inc is not None and
                                    (inc.name, val) in self.waited else None)
                        if scope_cm is not None:
                            scope_cm.__exit__(None, None, None)
                    return body
                block.sync(mk("sync"))
                block.vector(mk("vector"))
                block.scalar(mk("scalar"))
                block.tensor(mk("tensor"))
                block.gpsimd(mk("gpsimd"))
        return self.nc

    def _alloc_post_a(self):
        s, c = self, self.c
        FQ = s.FQ
        s.WQB = s.sb("WQB", [128, 8192], BF16)
        s.WOB = s.sb("WOB", [128, 8192], BF16)
        s.CTXA = s.sb("CTXA", [128, c.NB * c.H * 2 * 65], BF16)
        s.B0T = s.sb("B0T", [1, c.NB * c.H * 65], BF16)
        s.W1Q.append(s.sb("W1Q1", [128, 8 * FQ], BF16))
        s.W2Q.append(s.sb("W2Q1", [128, (FQ // 128) * 1024], BF16))

    def phase0_consts(self):
        s, c = self, self.c
        s.IDE = s.sb("IDE", [128, 128])
        s.IDEB = s.sb("IDEB", [128, 128], BF16)
        s.USTR = s.sb("USTR", [128, 128])
        s.OD16N = s.sb("OD16N", [128, 128], BF16)
        s.OD16P = s.sb("OD16P", [128, 128], BF16)
        s.OC8 = s.sb("OC8", [128, 64], BF16)
        s.O1x128f = s.sb("O1x128f", [1, 128])
        s.OB1x128 = s.sb("OB1x128", [1, 128], BF16)
        s.O1x64b = s.sb("O1x64b", [1, 64], BF16)
        s.O128x1f = s.sb("O128x1f", [128, 1])
        s.O128x1b = s.sb("O128x1b", [128, 1], BF16)
        s.IOT8 = s.sb("IOT8", [128, 8])
        s.IOTP = s.sb("IOTP", [128, 1])
        s.META = s.sb("META", [128, 8])
        s.CONSTF = s.sb("CONSTF", [128, 4])
        for dst, src in [(s.IDE, s.cid), (s.IDEB, s.cidb), (s.USTR, s.cust),
                         (s.OD16N, s.cd16n), (s.OD16P, s.cd16p), (s.OC8, s.coc8),
                         (s.O1x128f, s.c1x128f), (s.OB1x128, s.c1x128b),
                         (s.O1x64b, s.c1x64b), (s.O128x1f, s.c128x1f),
                         (s.O128x1b, s.c128x1b), (s.IOT8, s.ciota8),
                         (s.IOTP, s.ciotap), (s.META, s.meta),
                         (s.CONSTF, s.cconstf)]:
            s.op("sync", lambda e, d=dst, sr=src: e.dma_start(out=d[:], in_=sr[:]),
                 inc=s.sXL[0], incn=16)
        s.v_consts = s.sXL[0].n
        s.nc.const_aps.register(F32, 0.0, s.CONSTF[:, 0:1])
        s.nc.const_aps.register(F32, 1e-5, s.CONSTF[:, 1:2])
        s.nc.const_aps.register(F32, c.LNEPS, s.CONSTF[:, 2:3])
        s.nc.const_aps.register(F32, math.log(c.ratio), s.CONSTF[:, 3:4])

        s.PTB = s.sb("PTB", [64, 256], BF16)
        s.PBD = s.sb("PBD", [128, 512], BF16)
        s.PBDT = s.sb("PBDT", [128, 512], BF16)
        s.BO2 = s.sb("BO2", [1, 1024], BF16)
        self.push_scope()
        PFt = s.sb("PFt", [128, 128])
        BOF = s.sb("BOF", [1, 1024])
        s.op("sync", lambda e: e.dma_start(out=PFt[:, 0:64], in_=s.ppi[0:128, :]),
             inc=s.sXL[0], incn=16)
        s.op("sync", lambda e: e.dma_start(out=PFt[:, 64:128], in_=s.ppi[128:256, :]),
             inc=s.sXL[0], incn=16)
        s.op("sync", lambda e: e.dma_start(out=BOF[:], in_=s.boi[:]), inc=s.sXL[0], incn=16)
        v_ph0 = s.sXL[0].n
        s.v_consts = v_ph0
        pP = s.PF[0][0:64, 0:256]
        s.op("tensor", lambda e: e.transpose(out=pP[:, 0:128], in_=PFt[:, 0:64],
                                             identity=s.IDE[:]),
             waits=[(s.sXL[0], v_ph0)])
        s.op("tensor", lambda e: e.transpose(out=pP[:, 128:256], in_=PFt[:, 64:128],
                                             identity=s.IDE[:]), inc=s.sPE)
        v_ptr = s.sPE.n
        s.op("scalar", lambda e: e.activation(out=s.PTB[:], in_=pP, func=AF.Copy,
                                              scale=c.dn, bias=0.0),
             waits=[(s.sPE, v_ptr)], inc=s.sACT)
        v_ptb = s.sACT.n
        s.op("vector", lambda e: e.memset(s.PBD[:], 0.0), inc=s.sDVE)
        s.op("vector", lambda e: e.memset(s.PBDT[:], 0.0), inc=s.sDVE)
        vz = s.sDVE.n
        s.copy("vector", s.PBD[0:64, 0:256], s.PTB[:], waits=[(s.sACT, v_ptb), (s.sDVE, vz)])
        s.copy("vector", s.PBD[64:128, 256:512], s.PTB[:])
        for hh in range(2):
            for mc in range(2):
                j = hh * 2 + mc
                s.copy("vector", s.PBDT[hh * 64:hh * 64 + 64, j * 128:(j + 1) * 128],
                       s.PTB[:, mc * 128:(mc + 1) * 128])
        s.op("vector", lambda e: e.nop(), inc=s.sDVE)
        s.v_pbd = s.sDVE.n

        s.copy("vector", s.BO2[:], BOF[:], waits=[(s.sXL[0], v_ph0)])
        s.op("vector", lambda e: e.nop(), inc=s.sDVE)
        s.v_bo2 = s.sDVE.n

        nzc = 2 * c.E * c.SLOTS // 128
        ZREC = s.sb("ZREC", [128, nzc])
        s.op("vector", lambda e: e.memset(ZREC[:], 0.0), inc=s.sDVE)
        vzr = s.sDVE.n
        s.op("sync", lambda e: e.dma_start(
            out=s.rec_in.rearrange("(p a) w -> p (a w)", p=128), in_=ZREC[:]),
            waits=[(s.sDVE, vzr)], inc=s.sXL[0], incn=16)
        s.v_zrec = s.sXL[0].n
        s.ph0_gate = [(s.sDVE, s.sDVE.n), (s.sPE, s.sPE.n), (s.sACT, s.sACT.n),
                      (s.sXL[0], s.sXL[0].n)]
        self.pop_scope()

    def load_wkv(self):
        s = self
        s.WKB = s.sb("WKB", [128, 8192], BF16)
        s.WVB = s.sb("WVB", [128, 8192], BF16)
        s.WRB = s.sb("WRB", [128, 64], BF16)
        for kc in range(8):
            s.op("gpsimd", lambda e, kc=kc: e.dma_start(
                out=s.WKB[:, kc * 1024:(kc + 1) * 1024], in_=s.wk[kc * 128:(kc + 1) * 128, :]),
                waits=s.ph0_gate if kc == 0 else (), inc=s.sWA, incn=16)
            s.op("gpsimd", lambda e, kc=kc: e.dma_start(
                out=s.WVB[:, kc * 1024:(kc + 1) * 1024], in_=s.wv[kc * 128:(kc + 1) * 128, :]),
                inc=s.sWA, incn=16)
            s.op("gpsimd", lambda e, kc=kc: e.dma_start(
                out=s.WRB[:, kc * 8:(kc + 1) * 8], in_=s.wri[kc * 128:(kc + 1) * 128, :]),
                inc=s.sWA, incn=16)
        s.v_wa = s.sWA.n

    def load_w_quarter(self, q, wait_pe=0):
        s, c = self, self.c
        FQ = s.FQ
        buf = q % 2
        KQ = FQ // 128
        sm = s.sWE[q % 2]
        for kc in range(8):
            s.op("gpsimd", lambda e, q=q, kc=kc, buf=buf, FQ=FQ: e.dma_start(
                out=s.W1Q[buf][:, kc * FQ:(kc + 1) * FQ],
                in_=s.w1i[kc * 128:(kc + 1) * 128, q * FQ:(q + 1) * FQ]),
                waits=[(s.sPE, wait_pe)] if kc == 0 else (), inc=sm, incn=16)
        for kq in range(KQ):
            s.op("gpsimd", lambda e, q=q, kq=kq, buf=buf, FQ=FQ: e.dma_start(
                out=s.W2Q[buf][:, kq * 1024:(kq + 1) * 1024],
                in_=s.w2i[q * FQ + kq * 128: q * FQ + (kq + 1) * 128, :]),
                inc=sm, incn=16)
        s.v_wq[q] = sm.n

    def load_wq_wo(self):
        s = self
        for kc in range(8):
            s.op("gpsimd", lambda e, kc=kc: e.dma_start(
                out=s.WQB[:, kc * 1024:(kc + 1) * 1024], in_=s.wq[kc * 128:(kc + 1) * 128, :]),
                inc=s.sWA, incn=16)
            s.op("gpsimd", lambda e, kc=kc: e.dma_start(
                out=s.WOB[:, kc * 1024:(kc + 1) * 1024], in_=s.wo[kc * 128:(kc + 1) * 128, :]),
                inc=s.sWA, incn=16)
        s.v_wqo = s.sWA.n

    def phase_a1(self):
        s, c = self, self.c
        D = c.D
        s.XNFM = s.sb("XNFM", [128, c.TPC * 8], BF16)
        XL = [s.sb("XL0", [128, 1024]), s.sb("XL1", [128, 1024])]
        XNF = s.sb("XNF", [128, 1024])
        XNH = [s.sb("XNH0", [128, 1024], BF16), s.sb("XNH1", [128, 1024], BF16)]
        LNS2 = [s.sb("LNS0", [128, 8]), s.sb("LNS1", [128, 8])]
        TTRS = XNF
        pTR = [s.PB[0], s.PB[1]]
        pRT = s.PF[0][0:8, :]
        RTF = s.sb("RTF", [8, 512])
        pRTT = s.PF[1][:, 0:32]
        LGT = s.sb("LGT", [128, 32])
        TOP8 = s.sb("TOP8", [128, 8])
        TI8 = s.sb("TI8", [128, 8], U32)
        EXPS = s.sb("EXPS", [128, 8])
        pLR = s.PF[2][:, 0:8]
        pCN = s.PF[2][0:1, 8:16]
        TMP8 = s.sb("TMP8", [128, 8])
        s.SEID = s.sb("SEID", [128, c.NT])
        s.SP = s.sb("SP", [128, c.NT])
        s.SLR = s.sb("SLR", [128, c.NT])
        s.SOH = s.sb("SOH", [128, 8 * c.NT])
        s.BASEC = [s.sb("BASEC0", [1, 8]), s.sb("BASEC1", [1, 8])]

        s.op("vector", lambda e: e.memset(s.BASEC[0][:], 0.0), inc=s.sDVE)
        base_ms = s.sDVE.n
        act_chain = 0
        xnf_free = 0
        lns1_free = [0, 0]

        xl_free = [0, 0]
        xnh_free = [0, 0]
        tr_free = [(s.sDVE, 0), (s.sDVE, 0)]
        rt_free = 0
        s.fm_done = {}
        for ts in range(c.NT):
            par = ts % 2
            g, sub = ts // 4, ts % 4
            s.op("sync", lambda e, ts=ts, par=par: e.dma_start(
                out=XL[par][:], in_=s.xmy[ts * 128:(ts + 1) * 128, :]),
                waits=[(s.sDVE, xl_free[par])] + (s.ph0_gate if ts < 2 else []),
                inc=s.sXL[par], incn=16)
            vload = s.sXL[par].n
            LNS = LNS2[par]
            s.op("vector", lambda e, par=par, LNS=LNS: e.tensor_reduce(
                out=LNS[:, 0:1], in_=XL[par][:], axis=AX.X, op=ALU.add),
                waits=[(s.sXL[par], vload), (s.sDVE, s.sDVE.n)], inc=s.sDVE)
            vst = s.sDVE.n
            s.op("scalar", lambda e, par=par, LNS=LNS: e.activation(
                out=TTRS[:], in_=XL[par][:], func=AF.Square, bias=0.0, scale=1.0,
                accum_out=LNS[:, 1:2]),
                waits=[(s.sXL[par], vload), (s.sDVE, xnf_free),
                       (s.sDVE, lns1_free[par]), (s.sACT, act_chain)], inc=s.sACT)
            vsq = s.sACT.n
            s.op("vector", lambda e, LNS=LNS: e.tensor_scalar(
                out=LNS[:, 2:3], in0=LNS[:, 0:1], scalar1=1.0 / D, scalar2=None,
                op0=ALU.mult), waits=[(s.sDVE, vst)], inc=s.sDVE)
            vmean = s.sDVE.n
            s.op("vector", lambda e, LNS=LNS: e.tensor_tensor(
                out=LNS[:, 3:4], in0=LNS[:, 2:3], in1=LNS[:, 2:3], op=ALU.mult),
                waits=[(s.sDVE, vmean)], inc=s.sDVE)
            vm2 = s.sDVE.n
            s.op("vector", lambda e, LNS=LNS: e.scalar_tensor_tensor(
                out=LNS[:, 4:5], in0=LNS[:, 1:2], scalar=1.0 / D, in1=LNS[:, 3:4],
                op0=ALU.mult, op1=ALU.subtract),
                waits=[(s.sDVE, vm2), (s.sACT, vsq)], inc=s.sDVE)
            vvar = s.sDVE.n
            lns1_free[par] = vvar
            s.op("scalar", lambda e, LNS=LNS: e.activation(
                out=LNS[:, 5:6], in_=LNS[:, 4:5], func=AF.Ln, bias=1e-5, scale=1.0),
                waits=[(s.sDVE, vvar), (s.sACT, act_chain)], inc=s.sACT)
            va1 = s.sACT.n
            s.op("scalar", lambda e, LNS=LNS: e.activation(
                out=LNS[:, 6:7], in_=LNS[:, 5:6], func=AF.Exp, bias=0.0, scale=-0.5),
                waits=[(s.sACT, va1)], inc=s.sACT)
            vrstd = s.sACT.n
            act_chain = vrstd
            s.op("vector", lambda e, par=par, LNS=LNS: e.tensor_scalar(
                out=XNF[:], in0=XL[par][:], scalar1=LNS[:, 2:3], scalar2=LNS[:, 6:7],
                op0=ALU.subtract, op1=ALU.mult),
                waits=[(s.sACT, vrstd), (s.sDVE, s.sDVE.n)], inc=s.sDVE)
            vxn = s.sDVE.n
            xl_free[par] = vxn
            s.op("vector", lambda e, par=par: e.tensor_copy(out=XNH[par][:], in_=XNF[:]),
                 waits=[(s.sPE, xnh_free[par]), (s.sDVE, vxn)], inc=s.sDVE)
            vhi = s.sDVE.n
            xnf_free = vhi
            for cp in range(4):
                tp = cp % 2
                trs_, trv_ = tr_free[tp]
                s.op("tensor", lambda e, par=par, cp=cp, tp=tp: e.transpose(
                    out=pTR[tp][:, 0:128], in_=XNH[par][:, (2 * cp) * 128:(2 * cp + 1) * 128],
                    identity=s.IDEB[:]),
                    waits=[(s.sDVE, vhi), (trs_, trv_), (s.sXL[0], s.v_consts)])
                s.op("tensor", lambda e, par=par, cp=cp, tp=tp: e.transpose(
                    out=pTR[tp][:, 128:256],
                    in_=XNH[par][:, (2 * cp + 1) * 128:(2 * cp + 2) * 128],
                    identity=s.IDEB[:]), inc=s.sPE)
                vtp = s.sPE.n
                eng = "vector" if cp % 2 == 0 else "scalar"
                b0 = g * 4096 + (2 * cp) * 512 + sub * 128
                b1 = g * 4096 + (2 * cp + 1) * 512 + sub * 128
                s.copy(eng, s.XNFM[:, b0:b0 + 128], pTR[tp][:, 0:128], waits=[(s.sPE, vtp)])
                lastc = s.copy(eng, s.XNFM[:, b1:b1 + 128], pTR[tp][:, 128:256])
                tr_free[tp] = (s.sDVE, lastc) if eng == "vector" else (s.sACT, lastc)
            xnh_free[par] = s.sPE.n
            vfa, vfd = s.sACT.n, s.sDVE.n
            s.op("vector", lambda e: e.nop(), waits=[(s.sACT, vfa), (s.sDVE, vfd)],
                 inc=s.sDVE)
            s.fm_done[ts] = s.sDVE.n

            if sub == 3:
                for ch in range(8):
                    s.op("tensor", lambda e, g=g, ch=ch: e.matmul(
                        out=pRT, lhsT=s.WRB[:, ch * 8:(ch + 1) * 8],
                        rhs=s.XNFM[:, g * 4096 + ch * 512: g * 4096 + (ch + 1) * 512],
                        start=(ch == 0), stop=(ch == 7)),
                        waits=[(s.sDVE, s.fm_done[ts]), (s.sWA, s.v_wa),
                               (s.sDVE, rt_free)] if ch == 0 else (),
                        inc=s.sPE if ch == 7 else None)
                vrt = s.sPE.n
                s.copy("vector", RTF[:], pRT, waits=[(s.sPE, vrt)])
                s.op("vector", lambda e: e.nop(), inc=s.sDVE)
                vrtf = s.sDVE.n
                for s2 in range(4):
                    s.op("tensor", lambda e, s2=s2: e.transpose(
                        out=pRTT[:, s2 * 8:(s2 + 1) * 8], in_=RTF[:, s2 * 128:(s2 + 1) * 128],
                        identity=s.IDE[0:8, 0:8]),
                        waits=[(s.sDVE, vrtf)] if s2 == 0 else (),
                        inc=s.sPE if s2 == 3 else None)
                vrtt = s.sPE.n
                s.copy("vector", LGT[:], pRTT, waits=[(s.sPE, vrtt)])
                s.op("vector", lambda e: e.nop(), inc=s.sDVE)
                vlgt = s.sDVE.n
                rt_free = vlgt
                for s2 in range(4):
                    ts2 = g * 4 + s2
                    LNR = LNS2[s2 % 2]
                    L = LGT[:, s2 * 8:(s2 + 1) * 8]
                    s.op("vector", lambda e, L=L, LNR=LNR: e.tensor_reduce(
                        out=LNR[:, 6:7], in_=L, axis=AX.X, op=ALU.max),
                        waits=[(s.sDVE, vlgt)], inc=s.sDVE)
                    vmx = s.sDVE.n
                    s.op("vector", lambda e, LNR=LNR: e.tensor_scalar(
                        out=LNR[:, 7:8], in0=LNR[:, 6:7], scalar1=-1.0, scalar2=None,
                        op0=ALU.mult), waits=[(s.sDVE, vmx)], inc=s.sDVE)
                    vnm = s.sDVE.n
                    s.op("scalar", lambda e, L=L, LNR=LNR: e.activation(
                        out=EXPS[:], in_=L, func=AF.Exp, bias=LNR[:, 7:8], scale=1.0,
                        accum_out=LNR[:, 0:1]),
                        waits=[(s.sDVE, vnm), (s.sACT, act_chain)], inc=s.sACT)
                    vse = s.sACT.n
                    act_chain = vse
                    s.op("vector", lambda e, ts2=ts2, LNR=LNR: e.reciprocal(
                        out=s.SP[:, ts2:ts2 + 1], in_=LNR[:, 0:1]),
                        waits=[(s.sACT, vse)], inc=s.sDVE)
                    s.op("vector", lambda e, L=L: e.max(out=TOP8[:], in_=L), inc=s.sDVE)
                    vt8 = s.sDVE.n
                    s.op("vector", lambda e, L=L: e.max_index(
                        out=TI8[:], in_max=TOP8[:], in_values=L),
                        waits=[(s.sDVE, vt8)], inc=s.sDVE)
                    vti = s.sDVE.n
                    s.op("vector", lambda e, ts2=ts2: e.tensor_copy(
                        out=s.SEID[:, ts2:ts2 + 1], in_=TI8[:, 0:1]),
                        waits=[(s.sDVE, vti)], inc=s.sDVE)
                    veid = s.sDVE.n
                    s.op("vector", lambda e, ts2=ts2: e.tensor_scalar(
                        out=s.SOH[:, ts2 * 8:(ts2 + 1) * 8], in0=s.IOT8[:],
                        scalar1=s.SEID[:, ts2:ts2 + 1], scalar2=None, op0=ALU.is_equal),
                        waits=[(s.sDVE, veid)], inc=s.sDVE)
                    voh = s.sDVE.n
                    bsrc = s.BASEC[ts2 % 2]
                    bdst = s.BASEC[(ts2 + 1) % 2]
                    s.op("tensor", lambda e, ts2=ts2: e.matmul(
                        out=pLR, lhsT=s.USTR[:], rhs=s.SOH[:, ts2 * 8:(ts2 + 1) * 8],
                        start=True, stop=False),
                        waits=[(s.sDVE, voh), (s.sDVE, base_ms)])
                    s.op("tensor", lambda e, bsrc=bsrc: e.matmul(
                        out=pLR, lhsT=s.O1x128f[:], rhs=bsrc[:], start=False, stop=True))
                    s.op("tensor", lambda e, ts2=ts2: e.matmul(
                        out=pCN, lhsT=s.O128x1f[:], rhs=s.SOH[:, ts2 * 8:(ts2 + 1) * 8],
                        start=True, stop=True), inc=s.sPE)
                    vlr = s.sPE.n
                    s.op("vector", lambda e, ts2=ts2: e.tensor_tensor(
                        out=TMP8[:], in0=s.SOH[:, ts2 * 8:(ts2 + 1) * 8], in1=pLR,
                        op=ALU.mult), waits=[(s.sPE, vlr)], inc=s.sDVE)
                    vt8b = s.sDVE.n
                    s.op("vector", lambda e, ts2=ts2: e.tensor_reduce(
                        out=s.SLR[:, ts2:ts2 + 1], in_=TMP8[:], axis=AX.X, op=ALU.add),
                        waits=[(s.sDVE, vt8b)], inc=s.sDVE)
                    s.op("vector", lambda e, bsrc=bsrc, bdst=bdst: e.tensor_tensor(
                        out=bdst[:], in0=bsrc[:], in1=pCN, op=ALU.add), inc=s.sDVE)
                    base_ms = s.sDVE.n
        s.a1_dve = s.sDVE.n
        s.fincnt = s.BASEC[c.NT % 2]

    def phase_coll1(self):
        s, c = self, self.c
        s.CNT8 = s.sb("CNT8", [8, 8])
        s.op("gpsimd", lambda e: e.dma_start(out=s.cc1_in, in_=s.fincnt[:]),
             waits=[(s.sDVE, s.a1_dve)], inc=s.sCD, incn=16)
        v1 = s.sCD.n
        s.op("gpsimd", lambda e: e.collective_compute(
            "AllGather", ALU.bypass, replica_groups=[list(range(8))],
            ins=[s.cc1_in], outs=[s.cc1_out]), waits=[(s.sCD, v1)], inc=s.sC1)
        vc1 = s.sC1.n
        s.op("gpsimd", lambda e: e.dma_start(out=s.CNT8[:], in_=s.cc1_out),
             waits=[(s.sC1, vc1)], inc=s.sCD, incn=16)
        s.v_cnt8 = s.sCD.n

        WCOL = s.sb("WCOL", [8, 2])
        pOFF = s.PF[3][0:1, 0:16]
        s.OFFG = s.sb("OFFG", [1, 8]); s.OFFEB = s.sb("OFFEB", [1, 8])
        s.op("gpsimd", lambda e: e.tensor_tensor(
            out=WCOL[:, 0:1], in0=s.IOTP[0:8, :], in1=s.META[0:8, 0:1],
            op=ALU.subtract), waits=[(s.sCD, s.v_cnt8), (s.sXL[0], s.v_consts)],
            inc=s.sGP)
        vws = s.sGP.n
        s.op("gpsimd", lambda e: e.tensor_scalar(
            out=WCOL[:, 0:1], in0=WCOL[:, 0:1], scalar1=0.0, scalar2=None,
            op0=ALU.is_lt), waits=[(s.sGP, vws)], inc=s.sGP)
        s.op("gpsimd", lambda e: e.tensor_tensor(
            out=WCOL[:, 1:2], in0=s.IOTP[0:8, :], in1=s.META[0:8, 3:4],
            op=ALU.subtract), inc=s.sGP)
        vws2 = s.sGP.n
        s.op("gpsimd", lambda e: e.tensor_scalar(
            out=WCOL[:, 1:2], in0=WCOL[:, 1:2], scalar1=0.0, scalar2=None,
            op0=ALU.is_equal), waits=[(s.sGP, vws2)], inc=s.sGP)
        vw = s.sGP.n
        s.op("tensor", lambda e: e.matmul(out=pOFF[:, 0:8], lhsT=WCOL[:, 0:1], rhs=s.CNT8[:],
                                          start=True, stop=True), waits=[(s.sGP, vw)])
        s.op("tensor", lambda e: e.matmul(out=pOFF[:, 8:16], lhsT=WCOL[:, 1:2], rhs=s.CNT8[:],
                                          start=True, stop=True), inc=s.sPE)
        voff = s.sPE.n
        s.copy("vector", s.OFFG[:], pOFF[:, 0:8], waits=[(s.sPE, voff)])
        s.copy("vector", s.OFFEB[:], pOFF[:, 8:16])
        s.op("vector", lambda e: e.nop(), inc=s.sDVE)
        voc = s.sDVE.n
        s.OFFGB = s.sb("OFFGB", [128, 8]); s.OFFEBB = s.sb("OFFEBB", [128, 8])
        pOFB = s.PF[4][:, 0:16]
        s.op("tensor", lambda e: e.matmul(out=pOFB[:, 0:8], lhsT=s.O1x128f[:], rhs=s.OFFG[:],
                                          start=True, stop=True), waits=[(s.sDVE, voc)])
        s.op("tensor", lambda e: e.matmul(out=pOFB[:, 8:16], lhsT=s.O1x128f[:], rhs=s.OFFEB[:],
                                          start=True, stop=True), inc=s.sPE)
        vofb = s.sPE.n
        s.copy("vector", s.OFFGB[:], pOFB[:, 0:8], waits=[(s.sPE, vofb)])
        s.copy("vector", s.OFFEBB[:], pOFB[:, 8:16])
        s.op("vector", lambda e: e.nop(), inc=s.sDVE)
        s.v_offb = s.sDVE.n

    def phase_scatter(self):
        s, c = self, self.c
        TMP8 = s.sb("STMP8", [128, 8])
        DC = s.sb("DC", [128, 16])
        SIDX = s.sb("SIDX", [128, c.NT], U32)
        RECT = s.sb("RECT", [128, 2 * c.NT])

        def gop(fn, waits=(), inc=None):
            return s.op("gpsimd", fn, waits=waits, inc=inc if inc is not None else s.sGP)

        for t in range(c.NT):
            oh = s.SOH[:, t * 8:(t + 1) * 8]
            def tree8(dst, src_ms):
                va = gop(lambda e: e.tensor_tensor(out=TMP8[:, 0:4], in0=TMP8[:, 0:4],
                                                   in1=TMP8[:, 4:8], op=ALU.add),
                         waits=[(s.sGP, src_ms)])
                vb = gop(lambda e: e.tensor_tensor(out=TMP8[:, 0:2], in0=TMP8[:, 0:2],
                                                   in1=TMP8[:, 2:4], op=ALU.add),
                         waits=[(s.sGP, va)])
                return gop(lambda e, dst=dst: e.tensor_tensor(out=dst, in0=TMP8[:, 0:1],
                                                              in1=TMP8[:, 1:2], op=ALU.add),
                           waits=[(s.sGP, vb)])
            v0 = gop(lambda e, oh=oh: e.tensor_tensor(out=TMP8[:], in0=oh, in1=s.OFFGB[:],
                                                      op=ALU.mult),
                     waits=[(s.sDVE, s.v_offb), (s.sGP, s.sGP.n)])
            v1 = tree8(DC[:, 0:1], v0)
            v2 = gop(lambda e, oh=oh: e.tensor_tensor(out=TMP8[:], in0=oh, in1=s.OFFEBB[:],
                                                      op=ALU.mult), waits=[(s.sGP, v1)])
            v3 = tree8(DC[:, 1:2], v2)
            v4 = gop(lambda e, t=t: e.tensor_tensor(out=DC[:, 2:3], in0=s.SLR[:, t:t + 1],
                                                    in1=DC[:, 1:2], op=ALU.add),
                     waits=[(s.sGP, v3)])
            v5 = gop(lambda e, t=t: e.tensor_tensor(out=DC[:, 3:4], in0=s.SLR[:, t:t + 1],
                                                    in1=DC[:, 0:1], op=ALU.add))
            v6 = gop(lambda e: e.tensor_scalar(out=DC[:, 4:5], in0=DC[:, 2:3],
                                               scalar1=float(c.CELL), scalar2=None,
                                               op0=ALU.is_lt), waits=[(s.sGP, v5)])
            v7 = gop(lambda e: e.tensor_scalar(out=DC[:, 5:6], in0=DC[:, 3:4],
                                               scalar1=float(c.CAP), scalar2=None,
                                               op0=ALU.is_lt))
            v8 = gop(lambda e, t=t: e.tensor_tensor(out=DC[:, 6:7], in0=s.SP[:, t:t + 1],
                                                    in1=DC[:, 5:6], op=ALU.mult),
                     waits=[(s.sGP, v7)])
            v9a = gop(lambda e, t=t: e.tensor_scalar(out=DC[:, 7:8], in0=s.IOTP[:],
                                                     scalar1=float(t * 128 + 1),
                                                     scalar2=None, op0=ALU.add))
            v9 = gop(lambda e: e.tensor_tensor(out=DC[:, 7:8], in0=DC[:, 7:8],
                                               in1=s.META[:, 1:2], op=ALU.add),
                     waits=[(s.sGP, v9a)])
            v10 = gop(lambda e: e.tensor_tensor(out=DC[:, 8:9], in0=DC[:, 2:3],
                                                in1=s.META[:, 4:5], op=ALU.add),
                      waits=[(s.sGP, v6)])
            M23 = float(1 << 23)
            n1 = gop(lambda e, M23=M23: e.tensor_scalar(
                out=DC[:, 9:10], in0=DC[:, 8:9], scalar1=1.0 / 128.0, scalar2=M23,
                op0=ALU.mult, op1=ALU.add), waits=[(s.sGP, v10)])
            n2 = gop(lambda e, M23=M23: e.tensor_scalar(
                out=DC[:, 9:10], in0=DC[:, 9:10], scalar1=-M23, scalar2=None,
                op0=ALU.add), waits=[(s.sGP, n1)])
            n3 = gop(lambda e: e.tensor_scalar(
                out=DC[:, 10:11], in0=DC[:, 9:10], scalar1=128.0, scalar2=None,
                op0=ALU.mult), waits=[(s.sGP, n2)])
            n4 = gop(lambda e: e.tensor_tensor(
                out=DC[:, 10:11], in0=DC[:, 8:9], in1=DC[:, 10:11], op=ALU.subtract),
                waits=[(s.sGP, n3)])
            n5 = gop(lambda e: e.tensor_scalar(
                out=DC[:, 11:12], in0=DC[:, 10:11], scalar1=0.0, scalar2=None,
                op0=ALU.is_lt), waits=[(s.sGP, n4)])
            n6 = gop(lambda e: e.tensor_tensor(
                out=DC[:, 9:10], in0=DC[:, 9:10], in1=DC[:, 11:12], op=ALU.subtract),
                waits=[(s.sGP, n5)])
            n7 = gop(lambda e: e.tensor_scalar(
                out=DC[:, 11:12], in0=DC[:, 11:12], scalar1=128.0, scalar2=None,
                op0=ALU.mult), waits=[(s.sGP, n6)])
            n8 = gop(lambda e: e.tensor_tensor(
                out=DC[:, 10:11], in0=DC[:, 10:11], in1=DC[:, 11:12], op=ALU.add),
                waits=[(s.sGP, n7)])
            n9 = gop(lambda e: e.tensor_scalar(
                out=DC[:, 10:11], in0=DC[:, 10:11], scalar1=float(c.TILES),
                scalar2=None, op0=ALU.mult), waits=[(s.sGP, n8)])
            n10 = gop(lambda e: e.tensor_tensor(
                out=DC[:, 9:10], in0=DC[:, 9:10], in1=DC[:, 10:11], op=ALU.add),
                waits=[(s.sGP, n9)])
            n11 = gop(lambda e, t=t: e.tensor_scalar(
                out=DC[:, 11:12], in0=s.SEID[:, t:t + 1], scalar1=float(c.SLOTS),
                scalar2=None, op0=ALU.mult), waits=[(s.sGP, n10)])
            n12 = gop(lambda e: e.tensor_tensor(
                out=DC[:, 9:10], in0=DC[:, 9:10], in1=DC[:, 11:12], op=ALU.add),
                waits=[(s.sGP, n11)])
            n13 = gop(lambda e: e.tensor_tensor(
                out=DC[:, 9:10], in0=DC[:, 9:10], in1=DC[:, 4:5], op=ALU.mult),
                waits=[(s.sGP, n12), (s.sGP, v6)])
            n14 = gop(lambda e: e.tensor_scalar(
                out=DC[:, 11:12], in0=DC[:, 4:5], scalar1=-1.0, scalar2=1.0,
                op0=ALU.mult, op1=ALU.add), waits=[(s.sGP, n13)])
            n15 = gop(lambda e: e.tensor_scalar(
                out=DC[:, 10:11], in0=DC[:, 7:8], scalar1=float(c.BIG), scalar2=None,
                op0=ALU.add), waits=[(s.sGP, v9), (s.sGP, n9)])
            n16 = gop(lambda e: e.tensor_tensor(
                out=DC[:, 10:11], in0=DC[:, 10:11], in1=DC[:, 11:12], op=ALU.mult),
                waits=[(s.sGP, n15), (s.sGP, n14)])
            n17 = gop(lambda e: e.tensor_tensor(
                out=DC[:, 9:10], in0=DC[:, 9:10], in1=DC[:, 10:11], op=ALU.add),
                waits=[(s.sGP, n16)])
            v22 = gop(lambda e, t=t: e.tensor_copy(out=SIDX[:, t:t + 1], in_=DC[:, 9:10]),
                      waits=[(s.sGP, n17)])
            v23 = gop(lambda e, t=t: e.tensor_copy(out=RECT[:, 2 * t:2 * t + 1],
                                                   in_=DC[:, 7:8]), waits=[(s.sGP, v8)])
            v24 = gop(lambda e, t=t: e.tensor_copy(out=RECT[:, 2 * t + 1:2 * t + 2],
                                                   in_=DC[:, 6:7]))
            s.op("gpsimd", lambda e, t=t: e.indirect_dma_start(
                out=s.rec_in,
                out_offset=bass.IndirectOffsetOnAxis(ap=SIDX[:, t:t + 1], axis=0),
                in_=RECT[:, 2 * t:2 * t + 2], in_offset=None,
                bounds_check=s.breg(e, c.E * c.SLOTS - 1), oob_is_err=False),
                waits=[(s.sGP, v24), (s.sGP, v22), (s.sXL[0], s.v_zrec)],
                inc=s.sSC, incn=16)
        s.v_scat = s.sSC.n

    def phase_a2(self):
        s, c = self, self.c
        KFM = s.sb("KFM", [128, 4096], BF16)
        VTM = [s.sb(f"VTM{i}", [128, 1024], BF16) for i in range(4)]
        SQB = [s.sb("SQB0", [128, 512], BF16), s.sb("SQB1", [128, 512], BF16)]
        ATL = [s.sb(f"ATL{i}", [128, 512], BF16) for i in range(2)]
        s.DKTM = s.sb("DKTM", [128, 64])
        MXKG = s.sb("MXKG", [128, 64])
        LNMX = s.sb("LNMX", [128, 64])
        MXR = s.sb("MXR", [128, 2])
        s.UCOMP = s.sb("UCOMP", [128, 2048])
        s.USB = s.sb("USB", [8, 512])
        pKV = [s.PF[0], s.PF[1]]
        pDD = [s.PF[2], s.PF[3]]
        pDT = s.PF[2][:, 0:64]
        pU = s.PF[4]
        pu8 = s.PF[5][0:8, :]
        s.pu8 = pu8

        s.op("vector", lambda e: e.memset(s.UCOMP[:], 0.0), inc=s.sDVE)
        s.op("vector", lambda e: e.memset(MXR[:], -1e30), inc=s.sDVE)
        s.op("vector", lambda e: e.memset(s.USB[:], 0.0), inc=s.sDVE)
        v_z = s.sDVE.n

        kv_free = [s.v_pbd, s.v_pbd]
        dd_free_act = [0, 0]
        sq_free = [0, 0]
        at_free = [0, 0]
        at_free_dve = [0, 0]
        u_guard = v_z
        usb_ms = v_z
        dkt_ms = 0
        for g in range(c.GA):
            gb = g * 4096
            for oc in range(8):
                par = oc % 2
                w0 = [(s.sDVE, kv_free[par]), (s.sWA, s.v_wa),
                      (s.sDVE, s.fm_done[g * 4 + 3]), (s.sDVE, dkt_ms)]
                for kc in range(8):
                    s.op("tensor", lambda e, oc=oc, kc=kc, par=par, gb=gb: e.matmul(
                        out=pKV[par][:],
                        lhsT=s.WKB[:, kc * 1024 + oc * 128: kc * 1024 + (oc + 1) * 128],
                        rhs=s.XNFM[:, gb + kc * 512: gb + (kc + 1) * 512],
                        start=(kc == 0), stop=(kc == 7)),
                        waits=w0 if kc == 0 else (), inc=s.sPE if kc == 7 else None)
                vk = s.sPE.n
                s.op("vector", lambda e, oc=oc, par=par: e.tensor_copy(
                    out=KFM[:, oc * 512:(oc + 1) * 512], in_=pKV[par][:]),
                    waits=[(s.sPE, vk), (s.sDVE, s.sDVE.n)], inc=s.sDVE)
                kv_free[par] = s.sDVE.n
            kfm_ms = s.sDVE.n
            for oc in range(8):
                par = oc % 2
                s.op("vector", lambda e, oc=oc, par=par: e.tensor_tensor(
                    out=SQB[par][:], in0=KFM[:, oc * 512:(oc + 1) * 512],
                    in1=KFM[:, oc * 512:(oc + 1) * 512], op=ALU.mult),
                    waits=[(s.sDVE, kfm_ms), (s.sPE, sq_free[par])], inc=s.sDVE)
                vsq = s.sDVE.n
                for s2 in range(4):
                    s.op("tensor", lambda e, oc=oc, s2=s2, par=par: e.matmul(
                        out=pDT[:, s2 * 16:(s2 + 1) * 16],
                        lhsT=SQB[par][:, s2 * 128:(s2 + 1) * 128],
                        rhs=s.OD16N[:, oc * 16:(oc + 1) * 16],
                        start=(oc == 0 and s2 == 0), stop=(oc == 7 and s2 == 3)),
                        waits=[(s.sDVE, vsq)] if s2 == 0 else (),
                        inc=s.sPE if s2 == 3 else None)
                sq_free[par] = s.sPE.n
            vdt = s.sPE.n
            s.op("vector", lambda e: e.tensor_copy(
                out=s.DKTM[:].rearrange("p (h q) -> p h q", h=16, q=4),
                in_=pDT.rearrange("p (q h) -> p h q", q=4, h=16)),
                waits=[(s.sPE, vdt), (s.sDVE, s.sDVE.n)], inc=s.sDVE)
            dkt_ms = s.sDVE.n
            vdk = dkt_ms
            for s2 in range(4):
                for half in range(2):
                    par = (s2 * 2 + half) % 2
                    w0 = [(s.sDVE, kv_free[par])]
                    for kc in range(8):
                        s.op("tensor", lambda e, s2=s2, half=half, kc=kc, par=par, gb=gb: e.matmul(
                            out=pKV[par][:],
                            lhsT=s.XNFM[:, gb + kc * 512 + s2 * 128: gb + kc * 512 + (s2 + 1) * 128],
                            rhs=s.WVB[:, kc * 1024 + half * 512: kc * 1024 + (half + 1) * 512],
                            start=(kc == 0), stop=(kc == 7)),
                            waits=w0 if kc == 0 else (), inc=s.sPE if kc == 7 else None)
                    vv = s.sPE.n
                    s.op("vector", lambda e, s2=s2, half=half, par=par: e.tensor_copy(
                        out=VTM[s2][:, half * 512:(half + 1) * 512], in_=pKV[par][:]),
                        waits=[(s.sPE, vv), (s.sDVE, s.sDVE.n)], inc=s.sDVE)
                    kv_free[par] = s.sDVE.n
            vtm_ms = s.sDVE.n
            for hp in range(8):
                for s2 in range(4):
                    par = s2 % 2
                    s.op("tensor", lambda e, hp=hp, s2=s2, par=par: e.matmul(
                        out=pDD[par][:],
                        lhsT=KFM[:, hp * 512 + s2 * 128: hp * 512 + (s2 + 1) * 128],
                        rhs=s.PBD[:], start=True, stop=True),
                        waits=[(s.sACT, dd_free_act[par]), (s.sDVE, vtm_ms),
                               (s.sDVE, vdk), (s.sDVE, s.v_pbd)],
                        inc=s.sPE)
                    vdd = s.sPE.n
                    apar = s2 % 2
                    for hh in range(2):
                        col = (2 * hp + hh) * 4 + s2
                        s.op("scalar", lambda e, col=col, hh=hh, par=par, apar=apar: e.activation(
                            out=ATL[apar][:, hh * 256:(hh + 1) * 256],
                            in_=pDD[par][:, hh * 256:(hh + 1) * 256], func=AF.Exp,
                            bias=s.DKTM[:, col:col + 1], scale=1.0),
                            waits=[(s.sPE, vdd), (s.sPE, at_free[apar]),
                                   (s.sDVE, at_free_dve[apar]),
                                   (s.sACT, s.sACT.n)] if hh == 0 else (),
                            inc=s.sACT)
                    vexp = s.sACT.n
                    dd_free_act[par] = vexp
                    for hh in range(2):
                        col = (2 * hp + hh) * 4 + s2
                        s.op("vector", lambda e, col=col, hh=hh, apar=apar: e.tensor_reduce(
                            out=MXKG[:, col:col + 1], in_=ATL[apar][:, hh * 256:(hh + 1) * 256],
                            axis=AX.X, op=ALU.max),
                            waits=[(s.sACT, vexp)], inc=s.sDVE)
                    at_free_dve[apar] = s.sDVE.n
                    s.op("tensor", lambda e, hp=hp, s2=s2, apar=apar: e.matmul(
                        out=pU[:], lhsT=VTM[s2][:, hp * 128:(hp + 1) * 128], rhs=ATL[apar][:],
                        start=(s2 == 0), stop=(s2 == 3)),
                        waits=[(s.sACT, vexp), (s.sDVE, u_guard)])
                    s.op("tensor", lambda e, hp=hp, apar=apar, s2=s2: e.matmul(
                        out=pu8, lhsT=s.OC8[:, hp * 8:(hp + 1) * 8], rhs=ATL[apar][:],
                        start=(s2 == 0), stop=(s2 == 3)),
                        waits=[(s.sDVE, usb_ms)] if s2 == 0 else (), inc=s.sPE)
                    at_free[apar] = s.sPE.n
                vU = s.sPE.n
                s.op("vector", lambda e, hp=hp: e.tensor_tensor(
                    out=s.UCOMP[0:64, hp * 256:(hp + 1) * 256],
                    in0=s.UCOMP[0:64, hp * 256:(hp + 1) * 256],
                    in1=pU[0:64, 0:256], op=ALU.add),
                    waits=[(s.sPE, vU), (s.sDVE, s.sDVE.n)], inc=s.sDVE)
                s.op("vector", lambda e, hp=hp: e.tensor_tensor(
                    out=s.UCOMP[64:128, hp * 256:(hp + 1) * 256],
                    in0=s.UCOMP[64:128, hp * 256:(hp + 1) * 256],
                    in1=pU[64:128, 256:512], op=ALU.add), inc=s.sDVE)
                u_guard = s.sDVE.n
                s.op("vector", lambda e: e.tensor_tensor(
                    out=s.USB[0:8, :], in0=s.USB[0:8, :], in1=pu8, op=ALU.add),
                    waits=[(s.sPE, vU), (s.sDVE, usb_ms)], inc=s.sDVE)
                usb_ms = s.sDVE.n
            vgd = s.sDVE.n
            s.op("scalar", lambda e: e.activation(
                out=LNMX[:], in_=MXKG[:], func=AF.Ln, bias=0.0, scale=1.0),
                waits=[(s.sDVE, vgd), (s.sACT, s.sACT.n)], inc=s.sACT)
            vln = s.sACT.n
            s.op("vector", lambda e: e.tensor_tensor(
                out=LNMX[:], in0=LNMX[:], in1=s.DKTM[:], op=ALU.subtract),
                waits=[(s.sACT, vln), (s.sDVE, s.sDVE.n)], inc=s.sDVE)
            vsub = s.sDVE.n
            s.op("vector", lambda e: e.tensor_reduce(
                out=MXR[:, 0:1], in_=LNMX[:], axis=AX.X, op=ALU.max),
                waits=[(s.sDVE, vsub)], inc=s.sDVE)
            vm0 = s.sDVE.n
            s.op("vector", lambda e: e.tensor_tensor(
                out=MXR[:, 1:2], in0=MXR[:, 1:2], in1=MXR[:, 0:1], op=ALU.max),
                waits=[(s.sDVE, vm0)], inc=s.sDVE)
        s.a2_pe = s.sPE.n
        s.a2_dve = s.sDVE.n
        s.MXR = MXR

    def phase_coll2(self):
        s, c = self, self.c
        pMX = s.PF[0][0:1, 0:128]
        MXROW = s.sb("MXROW", [1, 128])
        s.MXCOL = s.sb("MXCOL", [128, 1])
        s.op("tensor", lambda e: e.transpose(out=pMX, in_=s.MXR[:, 1:2], identity=s.IDE[:]),
             waits=[(s.sDVE, s.a2_dve), (s.sPE, s.a2_pe)], inc=s.sPE)
        v = s.sPE.n
        s.copy("vector", MXROW[:], pMX, waits=[(s.sPE, v)])
        s.op("vector", lambda e: e.nop(), inc=s.sDVE)
        v2 = s.sDVE.n
        s.op("vector", lambda e: e.tensor_reduce(
            out=MXROW[:, 0:1], in_=MXROW[:], axis=AX.X, op=ALU.max),
            waits=[(s.sDVE, v2)], inc=s.sDVE)
        v3 = s.sDVE.n
        pMB = s.PF[1][:, 0:1]
        s.op("tensor", lambda e: e.matmul(out=pMB, lhsT=s.O1x128f[:], rhs=MXROW[:, 0:1],
                                          start=True, stop=True),
             waits=[(s.sDVE, v3)], inc=s.sPE)
        v4 = s.sPE.n
        s.copy("vector", s.MXCOL[:], pMB, waits=[(s.sPE, v4)])
        s.op("vector", lambda e: e.nop(), inc=s.sDVE)
        v_mx = s.sDVE.n

        UTM = s.sb("UTM", [128, 32])
        USUM = s.sb("USUM", [1, 16])
        pUT = s.PF[2][:, 0:32]
        for ch in range(4):
            s.op("tensor", lambda e, ch=ch: e.transpose(
                out=pUT[:, ch * 8:(ch + 1) * 8], in_=s.USB[:, ch * 128:(ch + 1) * 128],
                identity=s.IDE[0:8, 0:8]),
                waits=[(s.sDVE, s.a2_dve)] if ch == 0 else (),
                inc=s.sPE if ch == 3 else None)
        vut = s.sPE.n
        s.copy("vector", UTM[:], pUT, waits=[(s.sPE, vut)])
        s.op("vector", lambda e: e.nop(), inc=s.sDVE)
        v_utm = s.sDVE.n
        pUS = s.PF[3][0:32, 0:1]
        USR = s.sb("USR", [32, 1])
        pUSR = s.PF[0][0:1, 200:232]
        USROW = s.sb("USROW", [1, 32])
        s.op("tensor", lambda e: e.matmul(out=pUS, lhsT=UTM[:], rhs=s.O128x1f[:],
                                          start=True, stop=True),
             waits=[(s.sDVE, v_utm)], inc=s.sPE)
        v5 = s.sPE.n
        s.copy("vector", USR[:], pUS, waits=[(s.sPE, v5)])
        s.op("vector", lambda e: e.nop(), inc=s.sDVE)
        v6 = s.sDVE.n
        s.op("tensor", lambda e: e.transpose(out=pUSR, in_=USR[:], identity=s.IDE[0:32, 0:32]),
             waits=[(s.sDVE, v6)], inc=s.sPE)
        v7 = s.sPE.n
        s.copy("vector", USROW[:], pUSR, waits=[(s.sPE, v7)])
        s.op("vector", lambda e: e.nop(), inc=s.sDVE)
        v8 = s.sDVE.n
        for pr in range(2):
            s.op("vector", lambda e, pr=pr: e.tensor_tensor(
                out=USUM[:, pr:16:2], in0=USROW[:, pr * 16:pr * 16 + 8],
                in1=USROW[:, pr * 16 + 8:pr * 16 + 16], op=ALU.add),
                waits=[(s.sDVE, v8)], inc=s.sDVE)
        v_usum = s.sDVE.n

        SX = s.sb("SX", [128, 8])
        SXP = s.sb("SXP", [128, 8 * c.GA])
        for ch in range(8):
            for g in range(c.GA):
                s.op("vector", lambda e, ch=ch, g=g: e.tensor_reduce(
                    out=SXP[:, ch * c.GA + g:ch * c.GA + g + 1],
                    in_=s.XNFM[:, g * 4096 + ch * 512:g * 4096 + (ch + 1) * 512],
                    axis=AX.X, op=ALU.add),
                    waits=[(s.sDVE, s.a2_dve)], inc=s.sDVE)
        v9 = s.sDVE.n
        for ch in range(8):
            s.op("vector", lambda e, ch=ch: e.tensor_reduce(
                out=SX[:, ch:ch + 1], in_=SXP[:, ch * c.GA:(ch + 1) * c.GA],
                axis=AX.X, op=ALU.add), waits=[(s.sDVE, v9)], inc=s.sDVE)
        v10 = s.sDVE.n
        SXB = s.sb("SXB", [128, 8], BF16)
        s.copy("vector", SXB[:], SX[:], waits=[(s.sDVE, v10)])
        s.op("vector", lambda e: e.nop(), inc=s.sDVE)
        v11 = s.sDVE.n
        pS = [s.PF[4][0:1, :], s.PF[5][0:1, :]]
        SROW = s.sb("SROW", [1, 1024])
        for half in range(2):
            for ch in range(8):
                s.op("tensor", lambda e, half=half, ch=ch: e.matmul(
                    out=pS[half], lhsT=SXB[:, ch:ch + 1],
                    rhs=s.WVB[:, ch * 1024 + half * 512: ch * 1024 + (half + 1) * 512],
                    start=(ch == 0), stop=(ch == 7)),
                    waits=[(s.sDVE, v11)] if (half == 0 and ch == 0) else (),
                    inc=s.sPE if ch == 7 else None)
            vs = s.sPE.n
            s.copy("vector", SROW[:, half * 512:(half + 1) * 512], pS[half],
                   waits=[(s.sPE, vs)])
        s.op("vector", lambda e: e.nop(), inc=s.sDVE)
        v_srow = s.sDVE.n
        s.S8 = s.sb("S8", [128, 8])
        s.op("sync", lambda e: e.dma_start(out=s.srow_dram, in_=SROW[:]),
             waits=[(s.sDVE, v_srow)], inc=s.sSR, incn=16)
        vsr1 = s.sSR.n
        s.op("sync", lambda e: e.dma_start(
            out=s.S8[:], in_=s.srow_dram.rearrange("o (p f) -> (o p) f", p=128)),
            waits=[(s.sSR, vsr1)], inc=s.sSR, incn=16)
        v_s8 = s.sSR.n

        s.op("gpsimd", lambda e: e.dma_start(out=s.pk_in[:, 0:2048], in_=s.UCOMP[:]),
             waits=[(s.sDVE, s.a2_dve)], inc=s.sCD, incn=16)
        s.op("gpsimd", lambda e: e.dma_start(out=s.pk_in[:, 2048:2080], in_=UTM[:]),
             waits=[(s.sDVE, v_utm)], inc=s.sCD, incn=16)
        PKF = s.sb("PKF", [128, 16])
        s.op("vector", lambda e: e.memset(PKF[:], 0.0), waits=[(s.sDVE, s.sDVE.n)],
             inc=s.sDVE)
        vpkz = s.sDVE.n
        s.op("vector", lambda e: e.tensor_copy(out=PKF[0:1, :], in_=USUM[:]),
             waits=[(s.sDVE, vpkz), (s.sDVE, v_usum)], inc=s.sDVE)
        vpkf = s.sDVE.n
        s.op("gpsimd", lambda e: e.dma_start(out=s.pk_in[:, 2080:2096], in_=PKF[:]),
             waits=[(s.sDVE, vpkf)], inc=s.sCD, incn=16)
        s.op("gpsimd", lambda e: e.dma_start(out=s.pk_in[:, 2096:2097], in_=s.MXCOL[:], allow_slow_non_contiguous=True),
             waits=[(s.sDVE, v_mx)], inc=s.sCD, incn=16)
        s.op("gpsimd", lambda e: e.dma_start(out=s.pk_in[:, 2097:2105], in_=s.S8[:]),
             waits=[(s.sSR, v_s8)], inc=s.sCD, incn=16)
        v_pk = s.sCD.n
        s.op("gpsimd", lambda e: e.collective_compute(
            "AllGather", ALU.bypass, replica_groups=[list(range(8))],
            ins=[s.rec_in], outs=[s.rec_out]),
            waits=[(s.sSC, s.v_scat)], inc=s.sC2a)
        s.v_c2a = s.sC2a.n
        s.op("gpsimd", lambda e: e.collective_compute(
            "AllGather", ALU.bypass, replica_groups=[list(range(8))],
            ins=[s.pk_in], outs=[s.pk_out]),
            waits=[(s.sCD, v_pk)], inc=s.sC2b)
        s.v_c2b = s.sC2b.n

    def phase_merge(self):
        s, c = self, self.c
        W = 2 * c.TILES
        MRG = s.sb("MRG", [128, W])
        s.RTID = s.sb("RTID", [128, c.TILES])
        s.IDXO = s.sb("IDXO", [128, c.TILES], I32)
        IDXF = s.sb("IDXF", [128, c.TILES])
        GIF = s.sb("GIF", [128, c.TILES])
        MSK = s.sb("MSK", [128, c.TILES])
        NMSK = s.sb("NMSK", [128, c.TILES])
        RIDX = s.sb("RIDX", [128, 8], U32)
        RIF = s.sb("RIF", [128, 1])
        MSL = [s.sb(f"MSL{i}", [128, W]) for i in range(2)]

        for r in range(8):
            v0a = s.op("gpsimd", lambda e, r=r: e.tensor_scalar(
                out=RIF[:], in0=s.IOTP[:], scalar1=float(r * c.E * 128),
                scalar2=None, op0=ALU.add),
                waits=[(s.sXL[0], s.v_consts), (s.sGP, s.sGP.n)], inc=s.sGP)
            v0 = s.op("gpsimd", lambda e: e.tensor_tensor(
                out=RIF[:], in0=RIF[:], in1=s.META[:, 5:6], op=ALU.add),
                waits=[(s.sGP, v0a)], inc=s.sGP)
            s.op("gpsimd", lambda e, r=r: e.tensor_copy(out=RIDX[:, r:r + 1], in_=RIF[:]),
                 waits=[(s.sGP, v0)], inc=s.sGP)
        v_ridx = s.sGP.n
        recv = s.rec_out.rearrange("(a j) w -> a (j w)", j=c.TILES)
        vml = [0, 0]
        for r in range(8):
            par = r % 2
            s.op("gpsimd", lambda e, r=r, par=par: e.indirect_dma_start(
                out=MSL[par][:], out_offset=None, in_=recv,
                in_offset=bass.IndirectOffsetOnAxis(ap=RIDX[:, r:r + 1], axis=0)),
                waits=[(s.sGP, v_ridx), (s.sC2a, s.v_c2a), (s.sDVE, vml[par])],
                inc=s.sEG[par], incn=16)
            vg = s.sEG[par].n
            if r == 0:
                s.copy("vector", MRG[:], MSL[par][:], waits=[(s.sEG[par], vg)])
            else:
                s.op("vector", lambda e, par=par: e.tensor_tensor(
                    out=MRG[:], in0=MRG[:], in1=MSL[par][:], op=ALU.add),
                    waits=[(s.sEG[par], vg), (s.sDVE, s.sDVE.n)], inc=s.sDVE)
            vml[par] = s.op("vector", lambda e: e.nop(), inc=s.sDVE)
        v_mrg = s.sDVE.n
        s.op("vector", lambda e: e.tensor_copy(
            out=s.RTID[:], in_=MRG[:].rearrange("p (t w) -> p w t", w=2)[:, 0, :]),
            waits=[(s.sDVE, v_mrg)], inc=s.sDVE)
        s.op("vector", lambda e: e.tensor_copy(
            out=s.RSCL[:], in_=MRG[:].rearrange("p (t w) -> p w t", w=2)[:, 1, :]),
            inc=s.sDVE)
        v_sp = s.sDVE.n
        s.op("vector", lambda e: e.tensor_scalar(
            out=IDXF[:], in0=s.RTID[:], scalar1=-1.0, scalar2=None, op0=ALU.add),
            waits=[(s.sDVE, v_sp)], inc=s.sDVE)
        v_idxf = s.sDVE.n
        s.op("vector", lambda e: e.tensor_copy(out=s.IDXO[:], in_=IDXF[:]),
             waits=[(s.sDVE, v_idxf)], inc=s.sDVE)
        v_idxo = s.sDVE.n
        s.op("vector", lambda e: e.tensor_scalar(
            out=MSK[:], in0=s.RTID[:], scalar1=0.5, scalar2=None, op0=ALU.is_gt),
            waits=[(s.sDVE, v_sp)], inc=s.sDVE)
        vmsk = s.sDVE.n
        s.op("vector", lambda e: e.tensor_scalar(
            out=NMSK[:], in0=s.RTID[:], scalar1=0.5, scalar2=None, op0=ALU.is_le),
            inc=s.sDVE)
        vnmsk = s.sDVE.n
        s.op("vector", lambda e: e.tensor_tensor(
            out=GIF[:], in0=MSK[:], in1=IDXF[:], op=ALU.mult),
            waits=[(s.sDVE, vmsk), (s.sDVE, v_idxf)], inc=s.sDVE)
        vg1 = s.sDVE.n
        s.op("vector", lambda e: e.scalar_tensor_tensor(
            out=GIF[:], in0=NMSK[:], scalar=float(c.BIG), in1=GIF[:],
            op0=ALU.mult, op1=ALU.add),
            waits=[(s.sDVE, vnmsk), (s.sDVE, vg1)], inc=s.sDVE)
        vgp = s.sDVE.n
        s.op("vector", lambda e: e.tensor_copy(out=s.GIDX[:], in_=GIF[:]),
             waits=[(s.sDVE, vgp)], inc=s.sDVE)
        s.v_gidx = s.sDVE.n
        s.op("sync", lambda e: e.dma_start(out=s.o_idx, in_=s.IDXO[:]),
             waits=[(s.sDVE, v_idxo)], inc=s.sOUT, incn=16)

    def phase_expert(self):
        s, c = self, self.c
        FQ = s.FQ
        NFC = FQ // 128
        KQ = FQ // 128
        EGT = c.EGT
        TW = EGT * 128
        XGT = [s.sb(f"XGT{i}", [128, 1024]) for i in range(EGT)]
        XNN = s.sb("XNN", [128, 1024])
        XNB = [s.sb("XNB0", [128, 1024], BF16), s.sb("XNB1", [128, 1024], BF16)]
        ELN = [s.sb("ELN0", [128, 8]), s.sb("ELN1", [128, 8])]
        ETT = XNN
        XEG2 = [s.sb("XEGa", [128, 8 * TW], BF16), s.sb("XEGb", [128, 8 * TW], BF16)]
        HFMG2 = [s.sb("HFMGa", [128, NFC * TW], BF16),
                 s.sb("HFMGb", [128, NFC * TW], BF16)]
        YE2 = [s.sb("YEa", [128, 1024]), s.sb("YEb", [128, 1024])]
        pE = [s.PF[0], s.PF[1]]
        pY = [s.PF[2], s.PF[3]]
        pT2 = [s.PB[0], s.PB[1]]
        scr = s.xefm_scr.rearrange("(gg p) f -> gg p f", p=128)

        xg_free = [0] * EGT
        xg_dma = [None] * EGT
        ye_rel2 = [[(self.sYW, 0)], [(self.sYW, 0)]]
        xnn_free = 0
        eln1_free = [0, 0]
        xnb_free = [0, 0]
        t2_free = [0, 0]
        pe_free = [0, 0]
        py_free = [0, 0]
        xeg_free2 = [0, 0]
        xeg_w2 = [0, 0]
        hfm_free2 = [0, 0]
        eact = s.sACT.n
        v_h0w = []

        for q in range(4):
            wbuf = q % 2
            ggi_base = q * c.EG
            if q == 1:
                v_h0w = [(s.sYW, s.sYW.n), (s.sYWb, s.sYWb.n)]
                v_xf = [(s.sXF2[0], s.sXF2[0].n), (s.sXF2[1], s.sXF2[1].n)]
            if 2 <= q + 1 <= 3:
                pass
            if q >= 1:
                if q + 1 <= 3:
                    self.load_w_quarter(q + 1, wait_pe=s.sPE.n)
            elif q == 0:
                if False:
                    pass
            for gi in range(c.EG):
                tiles = list(range(gi * EGT, (gi + 1) * EGT))
                ggi = ggi_base + gi
                xb = ggi % 2
                XEG = XEG2[xb]
                HFMG = HFMG2[xb]
                if q == 0:
                    for ti, t in enumerate(tiles):
                        vz = s.op("vector", lambda e, ti=ti: e.memset(XGT[ti][:], 0.0),
                                  waits=[(s.sDVE, xg_free[ti])] +
                                        ([xg_dma[ti]] if xg_dma[ti] else []),
                                  inc=s.sDVE)
                        par = ti % 2
                        s.op("gpsimd", lambda e, t=t, ti=ti: e.indirect_dma_start(
                            out=XGT[ti][:], out_offset=None, in_=s.xfull,
                            in_offset=bass.IndirectOffsetOnAxis(ap=s.GIDX[:, t:t + 1], axis=0),
                            bounds_check=s.breg(e, c.T - 1), oob_is_err=False),
                            waits=[(s.sDVE, vz), (s.sDVE, s.v_gidx)],
                            inc=s.sEG[par], incn=16)
                        vg = s.sEG[par].n
                        LN = ELN[ti % 2]
                        s.op("vector", lambda e, ti=ti, LN=LN: e.tensor_reduce(
                            out=LN[:, 0:1], in_=XGT[ti][:], axis=AX.X, op=ALU.add),
                            waits=[(s.sEG[par], vg), (s.sDVE, s.sDVE.n)], inc=s.sDVE)
                        vst = s.sDVE.n
                        s.op("scalar", lambda e, ti=ti, LN=LN: e.activation(
                            out=ETT[:], in_=XGT[ti][:], func=AF.Square, bias=0.0,
                            scale=1.0, accum_out=LN[:, 1:2]),
                            waits=[(s.sEG[par], vg), (s.sDVE, xnn_free),
                                   (s.sDVE, eln1_free[ti % 2]), (s.sACT, eact)],
                            inc=s.sACT)
                        vsq = s.sACT.n
                        s.op("vector", lambda e, LN=LN: e.tensor_scalar(
                            out=LN[:, 2:3], in0=LN[:, 0:1], scalar1=1.0 / c.D, scalar2=None,
                            op0=ALU.mult), waits=[(s.sDVE, vst)], inc=s.sDVE)
                        vmean = s.sDVE.n
                        s.op("vector", lambda e, LN=LN: e.tensor_tensor(
                            out=LN[:, 3:4], in0=LN[:, 2:3], in1=LN[:, 2:3], op=ALU.mult),
                            waits=[(s.sDVE, vmean)], inc=s.sDVE)
                        vm2 = s.sDVE.n
                        s.op("vector", lambda e, LN=LN: e.scalar_tensor_tensor(
                            out=LN[:, 4:5], in0=LN[:, 1:2], scalar=1.0 / c.D, in1=LN[:, 3:4],
                            op0=ALU.mult, op1=ALU.subtract),
                            waits=[(s.sDVE, vm2), (s.sACT, vsq)], inc=s.sDVE)
                        vvar = s.sDVE.n
                        eln1_free[ti % 2] = vvar
                        s.op("scalar", lambda e, LN=LN: e.activation(
                            out=LN[:, 5:6], in_=LN[:, 4:5], func=AF.Ln, bias=1e-5, scale=1.0),
                            waits=[(s.sDVE, vvar), (s.sACT, eact)], inc=s.sACT)
                        va = s.sACT.n
                        s.op("scalar", lambda e, LN=LN: e.activation(
                            out=LN[:, 6:7], in_=LN[:, 5:6], func=AF.Exp, bias=0.0, scale=-0.5),
                            waits=[(s.sACT, va)], inc=s.sACT)
                        vrstd = s.sACT.n
                        eact = vrstd
                        s.op("vector", lambda e, ti=ti, LN=LN: e.tensor_scalar(
                            out=XNN[:], in0=XGT[ti][:], scalar1=LN[:, 2:3], scalar2=LN[:, 6:7],
                            op0=ALU.subtract, op1=ALU.mult),
                            waits=[(s.sACT, vrstd), (s.sDVE, s.sDVE.n)], inc=s.sDVE)
                        vxn = s.sDVE.n
                        bpar = ti % 2
                        s.op("vector", lambda e, bpar=bpar: e.tensor_copy(
                            out=XNB[bpar][:], in_=XNN[:]),
                            waits=[(s.sPE, xnb_free[bpar]), (s.sDVE, vxn)], inc=s.sDVE)
                        vhi = s.sDVE.n
                        xnn_free = vhi
                        for cp in range(4):
                            tp = cp % 2
                            s.op("tensor", lambda e, bpar=bpar, cp=cp, tp=tp: e.transpose(
                                out=pT2[tp][:, 0:128],
                                in_=XNB[bpar][:, (2 * cp) * 128:(2 * cp + 1) * 128],
                                identity=s.IDEB[:]),
                                waits=[(s.sDVE, vhi), (s.sDVE, t2_free[tp]),
                                       (s.sPE, xeg_free2[xb]),
                                       (s.sXF2[xb], xeg_w2[xb])])
                            s.op("tensor", lambda e, bpar=bpar, cp=cp, tp=tp: e.transpose(
                                out=pT2[tp][:, 128:256],
                                in_=XNB[bpar][:, (2 * cp + 1) * 128:(2 * cp + 2) * 128],
                                identity=s.IDEB[:]), inc=s.sPE)
                            vtp = s.sPE.n
                            b0 = (2 * cp) * TW + ti * 128
                            b1 = (2 * cp + 1) * TW + ti * 128
                            s.copy("vector", XEG[:, b0:b0 + 128], pT2[tp][:, 0:128],
                                   waits=[(s.sPE, vtp)])
                            t2_free[tp] = s.copy("vector", XEG[:, b1:b1 + 128],
                                                 pT2[tp][:, 128:256])
                        xnb_free[bpar] = s.sPE.n
                    vxeg = s.op("vector", lambda e: e.nop(), inc=s.sDVE)
                    s.op("sync", lambda e, gi=gi, XEG=XEG: e.dma_start(
                        out=scr[gi], in_=XEG[:]), waits=[(s.sDVE, vxeg)],
                        inc=s.sXF2[xb], incn=16)
                    xeg_w2[xb] = s.sXF2[xb].n
                    w_xeg = [(s.sDVE, vxeg)]
                else:
                    s.op("sync", lambda e, gi=gi, XEG=XEG: e.dma_start(
                        out=XEG[:], in_=scr[gi]),
                        waits=[(s.sPE, xeg_free2[xb]),
                               (s.sXE[xb], s.sXE[xb].n)] + v_xf,
                        inc=s.sXE[xb], incn=16)
                    w_xeg = [(s.sXE[xb], s.sXE[xb].n)]
                for fc in range(NFC):
                    par = fc % 2
                    w0 = list(w_xeg) + [(s.sWE[q % 2], s.v_wq[q]), (s.sACT, pe_free[par]),
                                        (s.sPE, hfm_free2[xb])]
                    for kc in range(8):
                        s.op("tensor", lambda e, fc=fc, kc=kc, par=par, wbuf=wbuf, TW=TW, FQ=FQ, XEG=XEG: e.matmul(
                            out=pE[par][:, 0:TW],
                            lhsT=s.W1Q[wbuf][:, kc * FQ + fc * 128: kc * FQ + (fc + 1) * 128],
                            rhs=XEG[:, kc * TW:(kc + 1) * TW],
                            start=(kc == 0), stop=(kc == 7)),
                            waits=w0 if kc == 0 else (), inc=s.sPE if kc == 7 else None)
                    vh = s.sPE.n
                    s.op("scalar", lambda e, fc=fc, par=par, TW=TW, HFMG=HFMG: e.activation(
                        out=HFMG[:, fc * TW:(fc + 1) * TW], in_=pE[par][:, 0:TW],
                        func=AF.Relu, bias=0.0, scale=1.0),
                        waits=[(s.sPE, vh), (s.sACT, eact)], inc=s.sACT)
                    pe_free[par] = s.sACT.n
                    eact = s.sACT.n
                xeg_free2[xb] = s.sPE.n
                vrelu = s.sACT.n
                for ti, t in enumerate(tiles):
                    for dh in range(2):
                        par = dh
                        for kq in range(KQ):
                            s.op("tensor", lambda e, ti=ti, dh=dh, kq=kq, par=par, wbuf=wbuf, TW=TW, HFMG=HFMG: e.matmul(
                                out=pY[par][:],
                                lhsT=HFMG[:, kq * TW + ti * 128: kq * TW + (ti + 1) * 128],
                                rhs=s.W2Q[wbuf][:, kq * 1024 + dh * 512: kq * 1024 + (dh + 1) * 512],
                                start=(kq == 0), stop=(kq == KQ - 1)),
                                waits=[(s.sACT, vrelu), (s.sDVE, py_free[par])] if kq == 0 else (),
                                inc=s.sPE if kq == KQ - 1 else None)
                        vy = s.sPE.n
                        tp2 = ti % 2
                        YE = YE2[tp2]
                        if q == 0:
                            s.op("vector", lambda e, t=t, ti=ti, dh=dh, par=par, YE=YE: e.scalar_tensor_tensor(
                                out=YE[:, dh * 512:(dh + 1) * 512], in0=pY[par][:],
                                scalar=s.RSCL[:, t:t + 1],
                                in1=XGT[ti][:, dh * 512:(dh + 1) * 512],
                                op0=ALU.mult, op1=ALU.add),
                                waits=[(s.sPE, vy), (s.sDVE, s.sDVE.n)] + ye_rel2[tp2],
                                inc=s.sDVE)
                        else:
                            s.op("vector", lambda e, t=t, dh=dh, par=par, YE=YE: e.tensor_scalar(
                                out=YE[:, dh * 512:(dh + 1) * 512], in0=pY[par][:],
                                scalar1=s.RSCL[:, t:t + 1], scalar2=None, op0=ALU.mult),
                                waits=[(s.sPE, vy), (s.sDVE, s.sDVE.n)] + ye_rel2[tp2],
                                inc=s.sDVE)
                        py_free[par] = s.sDVE.n
                    vye = s.sDVE.n
                    tp2 = ti % 2
                    YE = YE2[tp2]
                    if q == 0:
                        xg_free[ti] = vye
                        ysem = s.sYW if tp2 == 0 else s.sYWb
                        s.op("sync", lambda e, t=t, YE=YE: e.dma_start(
                            out=s.xye_scr[t * 128:(t + 1) * 128, :], in_=YE[:]),
                            waits=[(s.sDVE, vye)], inc=ysem, incn=16)
                        xg_dma[ti] = (ysem, ysem.n)
                        ye_rel2[tp2] = [(ysem, ysem.n)]
                    elif tp2 == 0:
                        s.op("gpsimd", lambda e, t=t, YE=YE: e.dma_start(
                            out=s.xye_scr[t * 128:(t + 1) * 128, :], in_=YE[:],
                            accum_op=ALU.add),
                            waits=[(s.sDVE, vye)] + v_h0w, inc=s.sYB, incn=16)
                        ye_rel2[tp2] = [(s.sYB, s.sYB.n)]
                    else:
                        s.op("gpsimd", lambda e, t=t, YE=YE: e.dma_start(
                            out=s.xye_scr[t * 128:(t + 1) * 128, :], in_=YE[:],
                            accum_op=ALU.add),
                            waits=[(s.sDVE, vye)] + v_h0w, inc=s.sYA, incn=16)
                        ye_rel2[tp2] = [(s.sYA, s.sYA.n)]
                hfm_free2[xb] = s.sPE.n
        s.v_ya = s.sYA.n
        s.v_yb = s.sYB.n
        s.v_yw = s.sYW.n
        s.v_ywb = s.sYWb.n
        s.e_pe = s.sPE.n
        s.e_dve = s.sDVE.n
        s.e_act = s.sACT.n

    def phase_ctx(self):
        s, c = self, self.c
        H, NB = c.H, c.NB
        UCG = s.sb("UCG", [128, 2112])
        UCG2 = s.sb("UCG2", [128, 2112])
        EMX = s.sb("EMX", [128, 1])
        MX8 = s.sb("MX8", [128, 8])
        SBC = s.sb("SBC", [128, 1024])
        S8C = s.sb("S8C", [128, 8])
        TRS = s.sb("TRS", [64, 128])
        pCT = [s.PF[0][:, 0:64], s.PF[1][:, 0:64]]
        pB0 = s.PF[2][0:65, 0:1]
        B0C = s.sb("B0C", [65, 1])
        pB0T = s.PF[3][0:1, 0:65]
        pSB = [s.PF[4][:, 0:512], s.PF[5][:, 0:512]]
        SRWS = s.sb("SRWS", [1, 1024])

        def ctxa(b, h, mc):
            off = ((b * H + h) * 2 + mc) * 65
            return s.CTXA[:, off:off + 65]

        def b0t(b, h):
            off = (b * H + h) * 65
            return s.B0T[:, off:off + 65]
        s.ctxa_view = ctxa
        s.b0t_view = b0t

        bar = [(s.sPE, s.e_pe), (s.sDVE, s.e_dve), (s.sACT, s.e_act)]
        pko = s.pk_out.rearrange("(r p) w -> r p w", p=128)
        for r in range(8):
            s.op("sync", lambda e, r=r: e.dma_start(
                out=MX8[:, r:r + 1], in_=pko[r, :, 2096:2097],
                allow_slow_non_contiguous=True),
                 waits=[(s.sC2b, s.v_c2b)], inc=s.sMG, incn=16)
        vmx8 = s.sMG.n
        s.op("vector", lambda e: e.tensor_reduce(out=EMX[:], in_=MX8[:], axis=AX.X,
                                                 op=ALU.max),
             waits=[(s.sMG, vmx8), (s.sDVE, s.e_dve)], inc=s.sDVE)
        vmx = s.sDVE.n
        s.op("scalar", lambda e: e.activation(out=EMX[:], in_=EMX[:], func=AF.Exp,
                                              bias=math.log(c.ratio), scale=-1.0),
             waits=[(s.sDVE, vmx), (s.sACT, s.e_act)], inc=s.sACT)
        s.v_emx = s.sACT.n

        ct_free = [0, 0]
        trs_free = 0
        ucg_free = 0
        sbc_ms = 0
        for b in range(NB):
            s.op("sync", lambda e, b=b: e.dma_start(out=UCG[:, 0:2105], in_=pko[2 * b, :, 0:2105]),
                 waits=[(s.sC2b, s.v_c2b), (s.sDVE, ucg_free), (s.sMG, s.sMG.n)],
                 inc=s.sMG, incn=16)
            s.op("sync", lambda e, b=b: e.dma_start(out=UCG2[:, 0:2105], in_=pko[2 * b + 1, :, 0:2105]),
                 inc=s.sMG, incn=16)
            v1 = s.sMG.n
            s.op("vector", lambda e: e.tensor_tensor(out=UCG[:, 0:2105], in0=UCG[:, 0:2105],
                                                     in1=UCG2[:, 0:2105], op=ALU.add),
                 waits=[(s.sMG, v1), (s.sDVE, s.sDVE.n)], inc=s.sDVE)
            v_cmb = s.sDVE.n
            s.op("vector", lambda e: e.tensor_scalar(
                out=S8C[:], in0=UCG[:, 2097:2105], scalar1=c.ratio * c.EPS, scalar2=None,
                op0=ALU.mult), waits=[(s.sDVE, v_cmb)], inc=s.sDVE)
            v_s8c = s.sDVE.n
            s.op("sync", lambda e: e.dma_start(
                out=s.srow_dram.rearrange("o (p f) -> (o p) f", p=128), in_=S8C[:]),
                waits=[(s.sDVE, v_s8c), (s.sSR, s.sSR.n)], inc=s.sSR, incn=16)
            vsr = s.sSR.n
            SRW = SRWS
            s.op("sync", lambda e, SRW=SRW: e.dma_start(out=SRW[:], in_=s.srow_dram),
                 waits=[(s.sSR, vsr), (s.sPE, s.sPE.n)], inc=s.sSR, incn=16)
            vsr2 = s.sSR.n
            for hf in range(2):
                s.op("tensor", lambda e, hf=hf, SRW=SRW: e.matmul(
                    out=pSB[hf], lhsT=s.O1x128f[:], rhs=SRW[:, hf * 512:(hf + 1) * 512],
                    start=True, stop=True),
                    waits=[(s.sSR, vsr2), (s.sDVE, sbc_ms)] + (bar if b == 0 and hf == 0 else []),
                    inc=s.sPE)
                vps = s.sPE.n
                s.copy("vector", SBC[:, hf * 512:(hf + 1) * 512], pSB[hf],
                       waits=[(s.sPE, vps)])
            s.op("vector", lambda e: e.nop(), inc=s.sDVE)
            v_sbc = s.sDVE.n
            for h in range(H):
                hh, hp = h % 2, h // 2
                for mc in range(2):
                    par = mc
                    s.op("vector", lambda e, hh=hh, hp=hp, mc=mc: e.tensor_copy(
                        out=TRS[:], in_=UCG[hh * 64:hh * 64 + 64,
                                          hp * 256 + mc * 128: hp * 256 + (mc + 1) * 128]),
                        waits=[(s.sDVE, v_cmb), (s.sPE, trs_free)], inc=s.sDVE)
                    vq = s.sDVE.n
                    s.op("tensor", lambda e, par=par: e.transpose(
                        out=pCT[par], in_=TRS[:], identity=s.IDE[0:64, 0:64]),
                        waits=[(s.sDVE, vq), (s.sDVE, ct_free[par])], inc=s.sPE)
                    vt = s.sPE.n
                    trs_free = vt
                    s.op("vector", lambda e, b=b, h=h, mc=mc, par=par: e.scalar_tensor_tensor(
                        out=ctxa(b, h, mc)[:, 0:64], in0=pCT[par],
                        scalar=EMX[:, 0:1], in1=SBC[:, h * 64:(h + 1) * 64],
                        op0=ALU.mult, op1=ALU.add),
                        waits=[(s.sPE, vt), (s.sACT, s.v_emx), (s.sDVE, v_sbc)], inc=s.sDVE)
                    ct_free[par] = s.sDVE.n
                    ucol = 2048 + (hh * 2 + mc) * 8 + hp
                    s.op("vector", lambda e, b=b, h=h, mc=mc, ucol=ucol: e.tensor_scalar(
                        out=ctxa(b, h, mc)[:, 64:65], in0=UCG[:, ucol:ucol + 1],
                        scalar1=EMX[:, 0:1], scalar2=c.ratio * c.EPS * c.N,
                        op0=ALU.mult, op1=ALU.add), inc=s.sDVE)
                vcx = s.sDVE.n
                for mc in range(2):
                    s.op("tensor", lambda e, b=b, h=h, mc=mc: e.matmul(
                        out=pB0, lhsT=ctxa(b, h, mc)[:], rhs=s.O128x1b[:],
                        start=(mc == 0), stop=(mc == 1)),
                        waits=[(s.sDVE, vcx)] if mc == 0 else (),
                        inc=s.sPE if mc == 1 else None)
                vb0 = s.sPE.n
                s.copy("vector", B0C[:], pB0, waits=[(s.sPE, vb0)])
                vb0c = s.op("vector", lambda e: e.nop(), inc=s.sDVE)
                s.op("tensor", lambda e: e.transpose(out=pB0T, in_=B0C[:], identity=s.IDE[0:65, 0:65]),
                     waits=[(s.sDVE, vb0c)], inc=s.sPE)
                vb0t = s.sPE.n
                s.copy("vector", b0t(b, h), pB0T, waits=[(s.sPE, vb0t)])
                s.op("vector", lambda e: e.nop(), inc=s.sDVE)
            ucg_free = s.sDVE.n
            sbc_ms = s.sDVE.n
        s.v_ctx = s.sDVE.n
        s.ctx_pe = s.sPE.n

    def phase_b(self):
        s, c = self, self.c
        H = c.H
        TW = c.BGT * 128
        XEGB = [s.sb("XEGB0", [128, 8 * TW], BF16), s.sb("XEGB1", [128, 8 * TW], BF16)]
        QFM = s.sb("QFM", [128, 8 * TW], BF16)
        SQ2 = [s.sb("SQ20", [128, TW], BF16), s.sb("SQ21", [128, TW], BF16)]
        DIAGQ = s.sb("DIAGQ", [128, H * c.BGT])
        MXQG = s.sb("MXQG", [128, H * c.BGT])
        SUMW = s.sb("SUMW", [128, H * c.BGT])
        WTM = s.sb("WTM", [128, H * c.BGT], BF16)
        WTS = s.sb("WTS", [H * c.BGT, 128], BF16)
        WROWS = [s.sb("WROWS0", [1, H * TW], BF16), s.sb("WROWS1", [1, H * TW], BF16)]
        E0B = s.sb("E0B", [128, 2 * TW], BF16)
        RCP = s.sb("RCP", [1, 512])
        RCPB = s.sb("RCPB", [1, 512], BF16)
        RBS = s.sb("RBS", [64, 512])
        ATTN = s.sb("ATTN", [128, 8 * TW], BF16)
        XYET = [s.sb("XYET0", [128, 1024]), s.sb("XYET1", [128, 1024])]
        OUTT = [s.sb("OUTT0", [128, 1024]), s.sb("OUTT1", [128, 1024])]
        pQ = [s.PF[0], s.PF[1]]
        pDGQ = s.PF[2][:, 0:64]
        pD2 = [s.PF[2], s.PF[3]]
        pO = [s.PF[4][0:65, :], s.PF[5][0:65, :]]
        pWT = s.PB[0][0:H * c.BGT, 0:128]
        pRB = s.PF[0][0:64, :]
        wr = s.wrow_dram.rearrange("(g r) f -> g r f", r=H * c.BGT)
        scr = s.xefm_scr.rearrange("(gg p) f -> gg p f", p=128)

        bar = [(s.sPE, s.ctx_pe), (s.sDVE, s.v_ctx)]
        q_free = [0, 0]
        sq_free = [0, 0]
        d2_free_act = [0, 0]
        d2_free_dve = [0, 0]
        o_free = [0, 0]
        xegb_ms = [0, 0]
        xyet_ms = [0, 0]
        out_free = [0, 0]
        outt_dma = [0, 0]
        rbs_free = 0
        qfm_free = 0
        e0_free = 0
        attn_free = 0
        dgq_ms = 0
        bact = s.sACT.n

        for gi in range(c.BG):
            b = (gi * c.BGT) // c.CELLT
            tiles = list(range(gi * c.BGT, (gi + 1) * c.BGT))
            par = gi % 2
            for ti, t in enumerate(tiles):
                ge, tt = t // c.EGT, t % c.EGT
                ETW = c.EGT * 128
                s.op("sync", lambda e, ge=ge, tt=tt, ti=ti, par=par, ETW=ETW, TW=TW: e.dma_start(
                    out=XEGB[par][:].rearrange("p (ch w) -> p ch w", ch=8)[:, :, ti * 128:(ti + 1) * 128],
                    in_=scr[ge].rearrange("p (ch w) -> p ch w", ch=8)[:, :, tt * 128:(tt + 1) * 128]),
                    waits=[(s.sXF2[0], s.sXF2[0].n), (s.sXF2[1], s.sXF2[1].n),
                           (s.sPE, xegb_ms[par]), (s.sYA, s.v_ya)],
                    inc=s.sBR[1], incn=16)
            vxb = s.sBR[1].n
            for oc in range(8):
                qp_ = oc % 2
                w0 = [(s.sBR[1], vxb), (s.sWA, s.v_wqo), (s.sDVE, q_free[qp_]),
                      (s.sPE, qfm_free), (s.sDVE, dgq_ms)] + (bar if gi == 0 and oc == 0 else [])
                for kc in range(8):
                    s.op("tensor", lambda e, oc=oc, kc=kc, qp_=qp_, par=par, TW=TW: e.matmul(
                        out=pQ[qp_][:, 0:TW],
                        lhsT=s.WQB[:, kc * 1024 + oc * 128: kc * 1024 + (oc + 1) * 128],
                        rhs=XEGB[par][:, kc * TW:(kc + 1) * TW],
                        start=(kc == 0), stop=(kc == 7)),
                        waits=w0 if kc == 0 else (), inc=s.sPE if kc == 7 else None)
                vq = s.sPE.n
                s.op("vector", lambda e, oc=oc, qp_=qp_, TW=TW: e.tensor_copy(
                    out=QFM[:, oc * TW:(oc + 1) * TW], in_=pQ[qp_][:, 0:TW]),
                    waits=[(s.sPE, vq), (s.sDVE, s.sDVE.n)], inc=s.sDVE)
                q_free[qp_] = s.sDVE.n
                vqe = s.sDVE.n
                s.op("vector", lambda e, oc=oc, qp_=qp_, TW=TW: e.tensor_tensor(
                    out=SQ2[qp_][:], in0=QFM[:, oc * TW:(oc + 1) * TW],
                    in1=QFM[:, oc * TW:(oc + 1) * TW], op=ALU.mult),
                    waits=[(s.sDVE, vqe), (s.sPE, sq_free[qp_])], inc=s.sDVE)
                vsq = s.sDVE.n
                for s2 in range(c.BGT):
                    s.op("tensor", lambda e, oc=oc, s2=s2, qp_=qp_: e.matmul(
                        out=pDGQ[:, s2 * 16:(s2 + 1) * 16],
                        lhsT=SQ2[qp_][:, s2 * 128:(s2 + 1) * 128],
                        rhs=s.OD16P[:, oc * 16:(oc + 1) * 16],
                        start=(oc == 0 and s2 == 0), stop=(oc == 7 and s2 == c.BGT - 1)),
                        waits=[(s.sDVE, vsq)] if s2 == 0 else (),
                        inc=s.sPE if s2 == c.BGT - 1 else None)
                sq_free[qp_] = s.sPE.n
            vqf = s.sDVE.n
            vdq = s.sPE.n
            s.op("vector", lambda e: e.tensor_copy(
                out=DIAGQ[:, 0:16 * c.BGT].rearrange("p (h q) -> p h q", h=16, q=c.BGT),
                in_=pDGQ[:, 0:16 * c.BGT].rearrange("p (q h) -> p h q", q=c.BGT, h=16)),
                waits=[(s.sPE, vdq), (s.sDVE, s.sDVE.n)], inc=s.sDVE)
            vdqr = s.sDVE.n
            for hp in range(8):
                for s2 in range(c.BGT):
                    dpar = (hp * c.BGT + s2) % 2
                    s.op("tensor", lambda e, hp=hp, s2=s2, dpar=dpar, TW=TW: e.matmul(
                        out=pD2[dpar][:], lhsT=QFM[:, hp * TW + s2 * 128: hp * TW + (s2 + 1) * 128],
                        rhs=s.PBD[:], start=True, stop=True),
                        waits=[(s.sDVE, vqf), (s.sDVE, d2_free_dve[dpar]),
                               (s.sACT, d2_free_act[dpar]), (s.sDVE, vdqr)],
                        inc=s.sPE)
                    vdd = s.sPE.n
                    for hh in range(2):
                        col = (2 * hp + hh) * c.BGT + s2
                        s.op("vector", lambda e, col=col, hh=hh, dpar=dpar: e.tensor_reduce(
                            out=MXQG[:, col:col + 1], in_=pD2[dpar][:, hh * 256:(hh + 1) * 256],
                            axis=AX.X, op=ALU.max),
                            waits=[(s.sPE, vdd)], inc=s.sDVE)
                    d2_free_dve[dpar] = s.sDVE.n
            vmxq = s.sDVE.n
            s.op("vector", lambda e: e.tensor_tensor(out=SUMW[:], in0=DIAGQ[:], in1=MXQG[:],
                                                     op=ALU.add),
                 waits=[(s.sDVE, vmxq)], inc=s.sDVE)
            vsw = s.sDVE.n
            s.op("scalar", lambda e: e.activation(out=WTM[:], in_=SUMW[:], func=AF.Exp,
                                                  bias=c.LNEPS, scale=1.0),
                 waits=[(s.sDVE, vsw), (s.sACT, bact)], inc=s.sACT)
            vwtm = s.sACT.n
            bact = vwtm
            s.op("tensor", lambda e: e.transpose(out=pWT, in_=WTM[:], identity=s.IDEB[:]),
                 waits=[(s.sACT, vwtm)], inc=s.sPE)
            vwt = s.sPE.n
            s.copy("vector", WTS[:], pWT, waits=[(s.sPE, vwt)])
            vwts = s.op("vector", lambda e: e.nop(), inc=s.sDVE)
            s.op("sync", lambda e, gi=gi: e.dma_start(out=wr[gi], in_=WTS[:]),
                 waits=[(s.sDVE, vwts), (s.sWR[par], s.sWR[par].n)], inc=s.sWR[par], incn=16)
            vw1 = s.sWR[par].n
            s.op("sync", lambda e, gi=gi, par=par: e.dma_start(
                out=WROWS[par][:], in_=wr[gi].rearrange("r f -> (r f)").rearrange("(o q) -> o q", o=1)),
                waits=[(s.sWR[par], vw1)], inc=s.sWR[par], incn=16)
            vw2 = s.sWR[par].n
            for h in range(H):
                hh, hp = h % 2, h // 2
                opar = h % 2
                for mc in range(2):
                    dpar = mc
                    s.op("tensor", lambda e, hh=hh, hp=hp, mc=mc, dpar=dpar, TW=TW: e.matmul(
                        out=pD2[dpar][:, 0:TW],
                        lhsT=s.PBDT[:, (hh * 2 + mc) * 128:(hh * 2 + mc + 1) * 128],
                        rhs=QFM[:, hp * TW:(hp + 1) * TW], start=True, stop=True),
                        waits=[(s.sDVE, d2_free_dve[dpar]), (s.sACT, d2_free_act[dpar]),
                               (s.sPE, e0_free)],
                        inc=s.sPE)
                    vdd = s.sPE.n
                    s.op("scalar", lambda e, mc=mc, dpar=dpar, TW=TW: e.activation(
                        out=E0B[:, mc * TW:(mc + 1) * TW], in_=pD2[dpar][:, 0:TW],
                        func=AF.Exp, bias=0.0, scale=1.0),
                        waits=[(s.sPE, vdd), (s.sACT, bact)], inc=s.sACT)
                    d2_free_act[dpar] = s.sACT.n
                    bact = s.sACT.n
                ve0 = s.sACT.n
                s.op("tensor", lambda e, b=b, h=h, opar=opar, TW=TW: e.matmul(
                    out=pO[opar][:, 0:TW], lhsT=s.ctxa_view(b, h, 0)[:], rhs=E0B[:, 0:TW],
                    start=True, stop=False),
                    waits=[(s.sACT, ve0), (s.sDVE, o_free[opar]), (s.sDVE, s.v_ctx)])
                s.op("tensor", lambda e, b=b, h=h, opar=opar, TW=TW: e.matmul(
                    out=pO[opar][:, 0:TW], lhsT=s.ctxa_view(b, h, 1)[:],
                    rhs=E0B[:, TW:2 * TW], start=False, stop=False))
                s.op("tensor", lambda e, b=b, h=h, opar=opar, par=par, TW=TW: e.matmul(
                    out=pO[opar][:, 0:TW], lhsT=s.b0t_view(b, h)[:],
                    rhs=WROWS[par][:, h * TW:(h + 1) * TW],
                    start=False, stop=True),
                    waits=[(s.sWR[par], vw2)], inc=s.sPE)
                e0_free = s.sPE.n
                vO = s.sPE.n
                s.op("vector", lambda e, opar=opar, TW=TW: e.tensor_copy(
                    out=RCPB[:, 0:TW], in_=pO[opar][64:65, 0:TW]),
                    waits=[(s.sPE, vO), (s.sDVE, s.sDVE.n)], inc=s.sDVE)
                vrcb = s.sDVE.n
                s.op("tensor", lambda e, TW=TW: e.matmul(
                    out=pRB[:, 0:TW], lhsT=s.O1x64b[:], rhs=RCPB[:, 0:TW],
                    start=True, stop=True),
                    waits=[(s.sDVE, vrcb), (s.sDVE, rbs_free)], inc=s.sPE)
                vrb = s.sPE.n
                s.op("scalar", lambda e, TW=TW: e.activation(
                    out=RBS[0:64, 0:TW], in_=pRB[0:64, 0:TW], func=AF.Ln,
                    bias=0.0, scale=1.0),
                    waits=[(s.sPE, vrb), (s.sDVE, rbs_free), (s.sACT, s.sACT.n)],
                    inc=s.sACT)
                vln = s.sACT.n
                s.op("scalar", lambda e, TW=TW: e.activation(
                    out=RBS[0:64, 0:TW], in_=RBS[0:64, 0:TW], func=AF.Exp,
                    bias=0.0, scale=-1.0),
                    waits=[(s.sACT, vln)], inc=s.sACT)
                vrbs = s.sACT.n
                bact = vrbs
                s.op("vector", lambda e, hh=hh, hp=hp, opar=opar, TW=TW: e.scalar_tensor_tensor(
                    out=ATTN[hh * 64:hh * 64 + 64, hp * TW:(hp + 1) * TW],
                    in0=pO[opar][0:64, 0:TW], scalar=1.0, in1=RBS[0:64, 0:TW],
                    op0=ALU.mult, op1=ALU.mult),
                    waits=[(s.sACT, vrbs), (s.sPE, attn_free), (s.sDVE, s.sDVE.n)],
                    inc=s.sDVE)
                o_free[opar] = s.sDVE.n
                rbs_free = s.sDVE.n
            vattn = s.sDVE.n
            for ti, t in enumerate(tiles):
                tp = ti % 2
                s.op("sync", lambda e, t=t, tp=tp: e.dma_start(
                    out=XYET[tp][:], in_=s.xye_scr[t * 128:(t + 1) * 128, :]),
                    waits=[(s.sYA, s.v_ya), (s.sYB, s.v_yb), (s.sYW, s.v_yw),
                           (s.sYWb, s.v_ywb),
                           (s.sDVE, xyet_ms[tp]), (s.sBR[1], s.sBR[1].n)],
                    inc=s.sBR[1], incn=16)
                vxy = s.sBR[1].n
                for dh in range(2):
                    wpar = dh
                    for kc in range(8):
                        s.op("tensor", lambda e, ti=ti, dh=dh, kc=kc, wpar=wpar, TW=TW: e.matmul(
                            out=pQ[wpar][:],
                            lhsT=ATTN[:, kc * TW + ti * 128: kc * TW + (ti + 1) * 128],
                            rhs=s.WOB[:, kc * 1024 + dh * 512: kc * 1024 + (dh + 1) * 512],
                            start=(kc == 0), stop=False),
                            waits=[(s.sDVE, vattn), (s.sDVE, q_free[wpar]),
                                   (s.sDVE, out_free[wpar])] if kc == 0 else ())
                    s.op("tensor", lambda e, dh=dh, wpar=wpar: e.matmul(
                        out=pQ[wpar][:], lhsT=s.OB1x128[:], rhs=s.BO2[:, dh * 512:(dh + 1) * 512],
                        start=False, stop=True), waits=[(s.sDVE, s.v_bo2)], inc=s.sPE)
                    vwo = s.sPE.n
                    s.op("vector", lambda e, dh=dh, tp=tp, wpar=wpar: e.tensor_tensor(
                        out=OUTT[tp][:, dh * 512:(dh + 1) * 512], in0=pQ[wpar][:],
                        in1=XYET[tp][:, dh * 512:(dh + 1) * 512], op=ALU.add),
                        waits=[(s.sPE, vwo), (s.sBR[1], vxy), (s.sDVE, s.sDVE.n),
                               (s.sOT[tp], outt_dma[tp])],
                        inc=s.sDVE)
                    q_free[wpar] = s.sDVE.n
                    out_free[wpar] = s.sDVE.n
                vout = s.sDVE.n
                xyet_ms[tp] = vout
                s.op("sync", lambda e, t=t, tp=tp: e.dma_start(
                    out=s.o_rows[t * 128:(t + 1) * 128, :], in_=OUTT[tp][:]),
                    waits=[(s.sDVE, vout)], inc=s.sOT[tp], incn=16)
                outt_dma[tp] = s.sOT[tp].n
            attn_free = s.sPE.n
            qfm_free = s.sPE.n
            xegb_ms[par] = s.sPE.n
            dgq_ms = s.sDVE.n
        s.op("sync", lambda e: e.nop(),
             waits=[(s.sOUT, s.sOUT.n), (s.sOT[0], s.sOT[0].n),
                    (s.sOT[1], s.sOT[1].n)])


def _consts(cfg):
    c = cfg
    dn2 = 0.5 * c.dn * c.dn
    e16n = np.zeros((128, 128), np.float32)
    e16p = np.zeros((128, 128), np.float32)
    for ch in range(8):
        for r in range(128):
            h = 2 * ch + (1 if r >= 64 else 0)
            e16n[r, ch * 16 + h] = -dn2
            e16p[r, ch * 16 + h] = dn2
    oc8 = np.zeros((128, 64), np.float32)
    for hp in range(8):
        oc8[:, hp * 8 + hp] = 1.0
    return {
        "cident": np.eye(128, dtype=np.float32),
        "cidentb": np.eye(128, dtype=np.float32).astype(ml_dtypes.bfloat16),
        "custrict": np.triu(np.ones((128, 128), np.float32), 1),
        "conesd16n": e16n.astype(ml_dtypes.bfloat16),
        "conesd16p": e16p.astype(ml_dtypes.bfloat16),
        "conescol8": oc8.astype(ml_dtypes.bfloat16),
        "cones1x128f": np.ones((1, 128), np.float32),
        "cones1x128b": np.ones((1, 128), ml_dtypes.bfloat16),
        "cones1x64b": np.ones((1, 64), ml_dtypes.bfloat16),
        "cones128x1f": np.ones((128, 1), np.float32),
        "cones128x1b": np.ones((128, 1), ml_dtypes.bfloat16),
        "ciota8": np.tile(np.arange(8, dtype=np.float32)[None, :], (128, 1)),
        "ciotap": np.arange(128, dtype=np.float32)[:, None].copy(),
        "cconstf": np.tile(np.array([0.0, 1e-5, math.log(cfg.EPS),
                                     math.log(cfg.ratio)], np.float32)[None, :], (128, 1)),
    }


_BUILD_CACHE = {}


def _get_nc(T):
    if T not in _BUILD_CACHE:
        _BUILD_CACHE[T] = Builder(Cfg(T)).build()
    return _BUILD_CACHE[T]


def make_in_maps(c, xf, Wq, Wk, Wv, Wo, bo, P, Wr, W1, W2):
    consts = _consts(c)
    in_maps = []
    for core in range(8):
        meta = np.zeros((128, 8), np.float32)
        meta[:, 0] = core
        meta[:, 1] = core * c.TPC
        meta[:, 3] = core - 1 if core % 2 == 1 else 999.0
        meta[:, 4] = (core // 2) * c.CELL
        meta[:, 5] = core * 128
        m = {
            "xmy": np.ascontiguousarray(xf[core * c.TPC:(core + 1) * c.TPC]),
            "xfull": xf,
            "wq": Wq, "wk": Wk, "wv": Wv, "wo": Wo,
            "boi": bo.reshape(1, -1),
            "ppi": P, "wri": Wr,
            "w1i": np.ascontiguousarray(W1[core]),
            "w2i": np.ascontiguousarray(W2[core]),
            "meta": meta,
        }
        m.update(consts)
        in_maps.append(m)
    return in_maps


def kernel(L0_attn_x, L0_attn_Wq, L0_attn_Wk, L0_attn_Wv, L0_attn_Wo, L0_attn_bo,
           L0_attn_P, L0_ffn_Wr, L0_ffn_W1, L0_ffn_W2):
    x = np.ascontiguousarray(np.asarray(L0_attn_x, np.float32))
    NBv, Nv, Dv = x.shape
    T = NBv * Nv
    c = Cfg(T)
    nc = _get_nc(T)
    xf = np.ascontiguousarray(x.reshape(T, Dv))
    in_maps = make_in_maps(
        c, xf,
        np.ascontiguousarray(np.asarray(L0_attn_Wq, np.float32)),
        np.ascontiguousarray(np.asarray(L0_attn_Wk, np.float32)),
        np.ascontiguousarray(np.asarray(L0_attn_Wv, np.float32)),
        np.ascontiguousarray(np.asarray(L0_attn_Wo, np.float32)),
        np.asarray(L0_attn_bo, np.float32),
        np.ascontiguousarray(np.asarray(L0_attn_P, np.float32)),
        np.ascontiguousarray(np.asarray(L0_ffn_Wr, np.float32)),
        np.asarray(L0_ffn_W1, np.float32),
        np.asarray(L0_ffn_W2, np.float32))
    res = run_bass_kernel_spmd(nc, in_maps, core_ids=list(range(8)))
    y = np.zeros((T, Dv), np.float32)
    for core in range(8):
        idx = np.asarray(res.results[core]["o_idx"]).T.reshape(-1)
        rows = np.asarray(res.results[core]["o_rows"])
        v = idx >= 0
        y[idx[v]] = rows[v]
    return y.reshape(NBv, Nv, Dv)


# version 06
# speedup vs baseline: 1.0312x; 1.0312x over previous
import math
import numpy as np
import ml_dtypes
import concourse.bass as bass
import concourse.mybir as mybir
from concourse.bass_utils import run_bass_kernel_spmd
from contextlib import ExitStack

F32 = mybir.dt.float32
BF16 = mybir.dt.bfloat16
U32 = mybir.dt.uint32
I32 = mybir.dt.int32
AF = mybir.ActivationFunctionType
ALU = mybir.AluOpType
AX = mybir.AxisListType

ENGS = ("sync", "vector", "scalar", "tensor", "gpsimd")


class Cfg:
    def __init__(self, T):
        self.D, self.H, self.DH, self.M, self.E, self.F, self.NB = 1024, 16, 64, 256, 8, 4096, 4
        self.CORES = 8
        self.T = T
        self.TPC = T // self.CORES
        self.N = T // self.NB
        self.CAP = T // self.E
        self.CELL = self.CAP // self.NB + 128
        self.SLOTS = self.CELL * self.NB
        self.TILES = self.SLOTS // 128
        self.CELLT = self.CELL // 128
        self.NT = self.TPC // 128
        self.GA = self.TPC // 512
        assert self.TPC % 512 == 0 and self.CELL % 128 == 0
        self.BGT = 3 if self.CELLT % 3 == 0 else (2 if self.CELLT % 2 == 0 else 1)
        self.BG = self.TILES // self.BGT
        self.EGT = 4 if self.TILES % 4 == 0 else self.BGT
        self.EG = self.TILES // self.EGT
        self.FH = self.F // 2
        self.EPS = 1e-4
        self.dn = float(self.DH) ** (-0.25)
        self.ratio = float(self.M) ** (-0.5)
        self.LNEPS = math.log(self.EPS)
        self.BIG = 1 << 28


class Sem:
    __slots__ = ("h", "name", "n")

    def __init__(self, h, name):
        self.h, self.name, self.n = h, name, 0


class Builder:
    def __init__(self, cfg):
        self.c = cfg
        self.nc = bass.Bass("TRN2", target_bir_lowering=False, debug=False)
        self.sched = {e: [] for e in ENGS}
        self.hw = {}
        self.scopes = []
        self.waited = set()
        self.sem_registry = []
        self.gp_regs = {}
        import os as _os
        self.opcut = int(_os.environ.get("MK_OPCUT", "1000000000"))
        self.opcount = 0
        self.cur_phase = "init"

    def sem(self, name):
        sm = Sem(self.st.enter_context(self.nc.semaphore(name)), name)
        self.sem_registry.append(sm)
        return sm

    def push_scope(self):
        self.scopes.append(ExitStack())
        self.scopes[-1].__enter__()

    def pop_scope(self):
        self.scopes.pop().__exit__(None, None, None)

    def fence(self):
        sems = {"tensor": self.sPE, "vector": self.sDVE, "scalar": self.sACT,
                "gpsimd": self.sGP}
        engset = set(sems.values())
        extra = [(sm, sm.n) for sm in self.sem_registry
                 if sm not in engset and sm.n > 0]
        tot = {}
        for eng, sm in sems.items():
            tot[eng] = self.op(eng, lambda e: e.nop(), inc=sm)
        for eng in ("tensor", "vector", "scalar", "gpsimd", "sync"):
            ws = [(sems[nm], tot[nm]) for nm in sems if nm != eng] + extra
            if eng == "sync":
                ws = [(sems[nm], tot[nm]) for nm in sems] + extra
                self.op("sync", lambda e: e.nop(), waits=ws)
            else:
                self.op(eng, lambda e: e.nop(), waits=ws, inc=sems[eng])

    def op(self, eng, fn, waits=(), inc=None, incn=1):
        ws = []
        for s, v in waits:
            if v <= 0:
                continue
            self.waited.add((s.name, v))
            key = (eng, s.name)
            if self.hw.get(key, 0) >= v:
                continue
            self.hw[key] = v
            ws.append((s, v))
        out = None
        if inc is not None:
            inc.n += incn
            out = inc.n
        self.opcount += 1
        if self.opcount <= self.opcut:
            self.sched[eng].append((ws, fn, inc, incn, out, self.cur_phase))
        return out

    def breg(self, e, val):
        r = self.gp_regs.get(val)
        if r is None:
            r = e.to_reg(val)
            self.gp_regs[val] = r
        return r

    def copy(self, eng, dst, src, waits=()):
        if eng == "scalar":
            return self.op(eng, lambda e: e.activation(out=dst, in_=src, func=AF.Copy,
                                                       bias=0.0, scale=1.0),
                           waits=waits, inc=self.sACT)
        return self.op(eng, lambda e: e.tensor_copy(out=dst, in_=src), waits=waits,
                       inc=self.sDVE)

    def sb(self, name, shape, dt=F32):
        st = self.scopes[-1] if self.scopes else self.st
        return st.enter_context(self.nc.sbuf_tensor(name, shape, dt))

    def sb_outer(self, name, shape, dt=F32):
        return self.st.enter_context(self.nc.sbuf_tensor(name, shape, dt))

    def ps(self, name, shape, dt=F32):
        return self.st.enter_context(self.nc.psum_tensor(name, shape, dt))

    def din(self, n, s, d=F32):
        return self.nc.dram_tensor(n, s, d, kind="ExternalInput").ap()

    def dout(self, n, s, d=F32):
        return self.nc.dram_tensor(n, s, d, kind="ExternalOutput").ap()

    def dint(self, n, s, d=F32):
        return self.nc.dram_tensor(n, s, d).ap()

    def build(self):
        c = self.c
        D, H, DH, M, E = c.D, c.H, c.DH, c.M, c.E

        self.xmy = self.din("xmy", [c.TPC, D])
        self.xfull = self.din("xfull", [c.T, D])
        self.wq = self.din("wq", [D, D]); self.wk = self.din("wk", [D, D])
        self.wv = self.din("wv", [D, D]); self.wo = self.din("wo", [D, D])
        self.boi = self.din("boi", [1, D])
        self.ppi = self.din("ppi", [M, DH])
        self.wri = self.din("wri", [D, E])
        self.w1i = self.din("w1i", [D, c.F])
        self.w2i = self.din("w2i", [c.F, D])
        self.meta = self.din("meta", [128, 8])
        self.cid = self.din("cident", [128, 128])
        self.cidb = self.din("cidentb", [128, 128], BF16)
        self.cust = self.din("custrict", [128, 128])
        self.cd16n = self.din("conesd16n", [128, 128], BF16)
        self.cd16p = self.din("conesd16p", [128, 128], BF16)
        self.coc8 = self.din("conescol8", [128, 64], BF16)
        self.c1x128f = self.din("cones1x128f", [1, 128])
        self.c1x128b = self.din("cones1x128b", [1, 128], BF16)
        self.c1x64b = self.din("cones1x64b", [1, 64], BF16)
        self.c128x1f = self.din("cones128x1f", [128, 1])
        self.c128x1b = self.din("cones128x1b", [128, 1], BF16)
        self.ciota8 = self.din("ciota8", [128, 8])
        self.ciotap = self.din("ciotap", [128, 1])
        self.cconstf = self.din("cconstf", [128, 4])

        self.o_rows = self.dout("o_rows", [c.SLOTS, D])
        self.o_idx = self.dout("o_idx", [128, c.TILES], I32)

        self.rec_in = self.dint("rec_in", [E * c.SLOTS, 2])
        self.rec_out = self.dint("rec_out", [8 * E * c.SLOTS, 2])
        self.cc1_in = self.dint("cc1_in", [1, 8])
        self.cc1_out = self.dint("cc1_out", [8, 8])
        self.PKW = 2048 + 32 + 16 + 1 + 8
        self.pk_in = self.dint("pk_in", [128, self.PKW])
        self.pk_out = self.dint("pk_out", [8 * 128, self.PKW])
        self.xye_scr = self.dint("xye_scr", [c.SLOTS, D])
        self.xefm_scr = self.dint("xefm_scr", [c.EG * 128, 8 * c.EGT * 128], BF16)
        self.wrow_dram = self.dint("wrow_dram", [c.BG * H * c.BGT, 128], BF16)
        self.srow_dram = self.dint("srow_dram", [1, 1024])

        with ExitStack() as st:
            self.st = st
            s = self
            s.sXL = [s.sem("sXL0"), s.sem("sXL1")]
            s.sWA = s.sem("sWA"); s.sSC = s.sem("sSC")
            s.sWE = [s.sem("sWE0"), s.sem("sWE1")]
            s.sC1 = s.sem("sC1"); s.sC2a = s.sem("sC2a"); s.sC2b = s.sem("sC2b")
            s.sCD = s.sem("sCD"); s.sMG = s.sem("sMG")
            s.sEG = [s.sem("sEG0"), s.sem("sEG1")]
            s.sXF = s.sem("sXF"); s.sYW = s.sem("sYW"); s.sYA = s.sem("sYA")
            s.sBR = [s.sem("sBR0"), s.sem("sBR1")]
            s.sWR = [s.sem("sWR0"), s.sem("sWR1")]
            s.sOUT = s.sem("sOUT"); s.sSR = s.sem("sSR")
            s.sOT = [s.sem("sOT0"), s.sem("sOT1")]
            s.sXE = [s.sem("sXE0"), s.sem("sXE1")]
            s.sYB = s.sem("sYB")
            s.sYWb = s.sem("sYWb")
            s.sXF2 = [s.sem("sXF0"), s.sem("sXF1")]
            s.sPE = s.sem("sPE"); s.sDVE = s.sem("sDVE")
            s.sACT = s.sem("sACT"); s.sGP = s.sem("sGP")
            s.sONE = s.sem("sONE")
            s.op("vector", lambda e: e.nop(), inc=s.sONE)

            s.PF = [s.ps(f"PF{i}", [128, 512]) for i in range(6)]
            s.PB = [s.ps(f"PBb{i}", [128, 256], BF16) for i in range(2)]

            FQ = c.F // 4
            s.FQ = FQ
            s.W1Q = [s.sb("W1Q0", [128, 8 * FQ], BF16)]
            s.W2Q = [s.sb("W2Q0", [128, (FQ // 128) * 1024], BF16)]
            s.v_wq = [0, 0, 0, 0]

            s.RSCL = s.sb_outer("RSCL", [128, c.TILES])
            s.GIDX = s.sb_outer("GIDX", [128, c.TILES], U32)
            steps = [
                self.phase0_consts,
                lambda: self.load_w_quarter(0),
                self.push_scope,
                self.load_wkv,
                self.phase_a1,
                self.phase_coll1,
                self.phase_scatter,
                self.phase_a2,
                self.phase_coll2,
                self.phase_merge,
                self.pop_scope,
                self.fence,
                self._alloc_post_a,
                self.load_wq_wo,
                lambda: self.load_w_quarter(1),
                self.push_scope,
                self.phase_expert,
                self.pop_scope,
                self.fence,
                self.push_scope,
                self.phase_ctx,
                self.pop_scope,
                self.fence,
                self.push_scope,
                self.phase_b,
                self.pop_scope,
            ]
            import os as _os
            nsteps = int(_os.environ.get("MK_STEPS", "99"))
            for st_ in steps[:nsteps]:
                self.cur_phase = getattr(st_, "__name__", "misc")
                st_()
            self.cur_phase = "tail"
            while self.scopes:
                self.pop_scope()

            with self.nc.Block() as block:
                def mk(name):
                    def body(e):
                        prev = None
                        cur_scope = None
                        scope_cm = None
                        for ws, fn, inc, incn, val, ph in self.sched[name]:
                            if ph != cur_scope:
                                if scope_cm is not None:
                                    scope_cm.__exit__(None, None, None)
                                scope_cm = self.nc.named_scope(f"{ph}")
                                scope_cm.__enter__()
                                cur_scope = ph
                            if prev is not None and not ws and name != "sync":
                                e.wait_ge(prev[0].h, prev[1])
                            for sm, v in ws:
                                e.wait_ge(sm.h, v)
                            ins = fn(e)
                            if inc is not None:
                                assert ins is not None, f"then_inc on None ({name})"
                                ins.then_inc(inc.h, incn)
                            prev = ((inc, val)
                                    if inc is not None and
                                    (inc.name, val) in self.waited else None)
                        if scope_cm is not None:
                            scope_cm.__exit__(None, None, None)
                    return body
                block.sync(mk("sync"))
                block.vector(mk("vector"))
                block.scalar(mk("scalar"))
                block.tensor(mk("tensor"))
                block.gpsimd(mk("gpsimd"))
        return self.nc

    def _alloc_post_a(self):
        s, c = self, self.c
        FQ = s.FQ
        s.WQB = s.sb("WQB", [128, 8192], BF16)
        s.WOB = s.sb("WOB", [128, 8192], BF16)
        s.CTXA = s.sb("CTXA", [128, c.NB * c.H * 2 * 65], BF16)
        s.B0T = s.sb("B0T", [1, c.NB * c.H * 65], BF16)
        s.W1Q.append(s.sb("W1Q1", [128, 8 * FQ], BF16))
        s.W2Q.append(s.sb("W2Q1", [128, (FQ // 128) * 1024], BF16))

    def phase0_consts(self):
        s, c = self, self.c
        s.IDE = s.sb("IDE", [128, 128])
        s.IDEB = s.sb("IDEB", [128, 128], BF16)
        s.USTR = s.sb("USTR", [128, 128])
        s.OD16N = s.sb("OD16N", [128, 128], BF16)
        s.OD16P = s.sb("OD16P", [128, 128], BF16)
        s.OC8 = s.sb("OC8", [128, 64], BF16)
        s.O1x128f = s.sb("O1x128f", [1, 128])
        s.OB1x128 = s.sb("OB1x128", [1, 128], BF16)
        s.O1x64b = s.sb("O1x64b", [1, 64], BF16)
        s.O128x1f = s.sb("O128x1f", [128, 1])
        s.O128x1b = s.sb("O128x1b", [128, 1], BF16)
        s.IOT8 = s.sb("IOT8", [128, 8])
        s.IOTP = s.sb("IOTP", [128, 1])
        s.META = s.sb("META", [128, 8])
        s.CONSTF = s.sb("CONSTF", [128, 4])
        for dst, src in [(s.IDE, s.cid), (s.IDEB, s.cidb), (s.USTR, s.cust),
                         (s.OD16N, s.cd16n), (s.OD16P, s.cd16p), (s.OC8, s.coc8),
                         (s.O1x128f, s.c1x128f), (s.OB1x128, s.c1x128b),
                         (s.O1x64b, s.c1x64b), (s.O128x1f, s.c128x1f),
                         (s.O128x1b, s.c128x1b), (s.IOT8, s.ciota8),
                         (s.IOTP, s.ciotap), (s.META, s.meta),
                         (s.CONSTF, s.cconstf)]:
            s.op("sync", lambda e, d=dst, sr=src: e.dma_start(out=d[:], in_=sr[:]),
                 inc=s.sXL[0], incn=16)
        s.v_consts = s.sXL[0].n
        s.nc.const_aps.register(F32, 0.0, s.CONSTF[:, 0:1])
        s.nc.const_aps.register(F32, 1e-5, s.CONSTF[:, 1:2])
        s.nc.const_aps.register(F32, c.LNEPS, s.CONSTF[:, 2:3])
        s.nc.const_aps.register(F32, math.log(c.ratio), s.CONSTF[:, 3:4])

        s.PTB = s.sb("PTB", [64, 256], BF16)
        s.PBD = s.sb("PBD", [128, 512], BF16)
        s.PBDT = s.sb("PBDT", [128, 512], BF16)
        s.BO2 = s.sb("BO2", [1, 1024], BF16)
        self.push_scope()
        PFt = s.sb("PFt", [128, 128])
        BOF = s.sb("BOF", [1, 1024])
        s.op("sync", lambda e: e.dma_start(out=PFt[:, 0:64], in_=s.ppi[0:128, :]),
             inc=s.sXL[0], incn=16)
        s.op("sync", lambda e: e.dma_start(out=PFt[:, 64:128], in_=s.ppi[128:256, :]),
             inc=s.sXL[0], incn=16)
        s.op("sync", lambda e: e.dma_start(out=BOF[:], in_=s.boi[:]), inc=s.sXL[0], incn=16)
        v_ph0 = s.sXL[0].n
        s.v_consts = v_ph0
        pP = s.PF[0][0:64, 0:256]
        s.op("tensor", lambda e: e.transpose(out=pP[:, 0:128], in_=PFt[:, 0:64],
                                             identity=s.IDE[:]),
             waits=[(s.sXL[0], v_ph0)])
        s.op("tensor", lambda e: e.transpose(out=pP[:, 128:256], in_=PFt[:, 64:128],
                                             identity=s.IDE[:]), inc=s.sPE)
        v_ptr = s.sPE.n
        s.op("scalar", lambda e: e.activation(out=s.PTB[:], in_=pP, func=AF.Copy,
                                              scale=c.dn, bias=0.0),
             waits=[(s.sPE, v_ptr)], inc=s.sACT)
        v_ptb = s.sACT.n
        s.op("vector", lambda e: e.memset(s.PBD[:], 0.0), inc=s.sDVE)
        s.op("vector", lambda e: e.memset(s.PBDT[:], 0.0), inc=s.sDVE)
        vz = s.sDVE.n
        s.copy("vector", s.PBD[0:64, 0:256], s.PTB[:], waits=[(s.sACT, v_ptb), (s.sDVE, vz)])
        s.copy("vector", s.PBD[64:128, 256:512], s.PTB[:])
        for hh in range(2):
            for mc in range(2):
                j = hh * 2 + mc
                s.copy("vector", s.PBDT[hh * 64:hh * 64 + 64, j * 128:(j + 1) * 128],
                       s.PTB[:, mc * 128:(mc + 1) * 128])
        s.op("vector", lambda e: e.nop(), inc=s.sDVE)
        s.v_pbd = s.sDVE.n

        s.copy("vector", s.BO2[:], BOF[:], waits=[(s.sXL[0], v_ph0)])
        s.op("vector", lambda e: e.nop(), inc=s.sDVE)
        s.v_bo2 = s.sDVE.n

        nzc = 2 * c.E * c.SLOTS // 128
        ZREC = s.sb("ZREC", [128, nzc])
        s.op("vector", lambda e: e.memset(ZREC[:], 0.0), inc=s.sDVE)
        vzr = s.sDVE.n
        s.op("sync", lambda e: e.dma_start(
            out=s.rec_in.rearrange("(p a) w -> p (a w)", p=128), in_=ZREC[:]),
            waits=[(s.sDVE, vzr)], inc=s.sXL[0], incn=16)
        s.v_zrec = s.sXL[0].n
        s.ph0_gate = [(s.sDVE, s.sDVE.n), (s.sPE, s.sPE.n), (s.sACT, s.sACT.n),
                      (s.sXL[0], s.sXL[0].n)]
        self.pop_scope()

    def load_wkv(self):
        s = self
        s.WKB = s.sb("WKB", [128, 8192], BF16)
        s.WVB = s.sb("WVB", [128, 8192], BF16)
        s.WRB = s.sb("WRB", [128, 64], BF16)
        for kc in range(8):
            s.op("gpsimd", lambda e, kc=kc: e.dma_start(
                out=s.WKB[:, kc * 1024:(kc + 1) * 1024], in_=s.wk[kc * 128:(kc + 1) * 128, :]),
                waits=s.ph0_gate if kc == 0 else (), inc=s.sWA, incn=16)
            s.op("gpsimd", lambda e, kc=kc: e.dma_start(
                out=s.WVB[:, kc * 1024:(kc + 1) * 1024], in_=s.wv[kc * 128:(kc + 1) * 128, :]),
                inc=s.sWA, incn=16)
            s.op("gpsimd", lambda e, kc=kc: e.dma_start(
                out=s.WRB[:, kc * 8:(kc + 1) * 8], in_=s.wri[kc * 128:(kc + 1) * 128, :]),
                inc=s.sWA, incn=16)
        s.v_wa = s.sWA.n

    def load_w_quarter(self, q, wait_pe=0):
        s, c = self, self.c
        FQ = s.FQ
        buf = q % 2
        KQ = FQ // 128
        sm = s.sWE[q % 2]
        for kc in range(8):
            s.op("gpsimd", lambda e, q=q, kc=kc, buf=buf, FQ=FQ: e.dma_start(
                out=s.W1Q[buf][:, kc * FQ:(kc + 1) * FQ],
                in_=s.w1i[kc * 128:(kc + 1) * 128, q * FQ:(q + 1) * FQ]),
                waits=[(s.sPE, wait_pe)] if kc == 0 else (), inc=sm, incn=16)
        for kq in range(KQ):
            s.op("gpsimd", lambda e, q=q, kq=kq, buf=buf, FQ=FQ: e.dma_start(
                out=s.W2Q[buf][:, kq * 1024:(kq + 1) * 1024],
                in_=s.w2i[q * FQ + kq * 128: q * FQ + (kq + 1) * 128, :]),
                inc=sm, incn=16)
        s.v_wq[q] = sm.n

    def load_wq_wo(self):
        s = self
        for kc in range(8):
            s.op("gpsimd", lambda e, kc=kc: e.dma_start(
                out=s.WQB[:, kc * 1024:(kc + 1) * 1024], in_=s.wq[kc * 128:(kc + 1) * 128, :]),
                inc=s.sWA, incn=16)
            s.op("gpsimd", lambda e, kc=kc: e.dma_start(
                out=s.WOB[:, kc * 1024:(kc + 1) * 1024], in_=s.wo[kc * 128:(kc + 1) * 128, :]),
                inc=s.sWA, incn=16)
        s.v_wqo = s.sWA.n

    def phase_a1(self):
        s, c = self, self.c
        D = c.D
        s.XNFM = s.sb("XNFM", [128, c.TPC * 8], BF16)
        XL = [s.sb("XL0", [128, 1024]), s.sb("XL1", [128, 1024])]
        XNF = s.sb("XNF", [128, 1024])
        XNH = [s.sb("XNH0", [128, 1024], BF16), s.sb("XNH1", [128, 1024], BF16)]
        LNS2 = [s.sb("LNS0", [128, 8]), s.sb("LNS1", [128, 8])]
        TTRS = XNF
        pTR = [s.PB[0], s.PB[1]]
        pRT = s.PF[0][0:8, :]
        RTF = s.sb("RTF", [8, 512])
        pRTT = s.PF[1][:, 0:32]
        LGT = s.sb("LGT", [128, 32])
        TOP8 = s.sb("TOP8", [128, 8])
        TI8 = s.sb("TI8", [128, 8], U32)
        EXPS = s.sb("EXPS", [128, 8])
        pLR = s.PF[2][:, 0:8]
        pCN = s.PF[2][0:1, 8:16]
        TMP8 = s.sb("TMP8", [128, 8])
        s.SEID = s.sb("SEID", [128, c.NT])
        s.SP = s.sb("SP", [128, c.NT])
        s.SLR = s.sb("SLR", [128, c.NT])
        s.SOH = s.sb("SOH", [128, 8 * c.NT])
        s.BASEC = [s.sb("BASEC0", [1, 8]), s.sb("BASEC1", [1, 8])]

        s.op("vector", lambda e: e.memset(s.BASEC[0][:], 0.0), inc=s.sDVE)
        base_ms = s.sDVE.n
        act_chain = 0
        xnf_free = 0
        lns1_free = [0, 0]

        xl_free = [0, 0]
        xnh_free = [0, 0]
        tr_free = [(s.sDVE, 0), (s.sDVE, 0)]
        rt_free = 0
        s.fm_done = {}
        for ts in range(c.NT):
            par = ts % 2
            g, sub = ts // 4, ts % 4
            s.op("sync", lambda e, ts=ts, par=par: e.dma_start(
                out=XL[par][:], in_=s.xmy[ts * 128:(ts + 1) * 128, :]),
                waits=[(s.sDVE, xl_free[par])] + (s.ph0_gate if ts < 2 else []),
                inc=s.sXL[par], incn=16)
            vload = s.sXL[par].n
            LNS = LNS2[par]
            s.op("vector", lambda e, par=par, LNS=LNS: e.tensor_reduce(
                out=LNS[:, 0:1], in_=XL[par][:], axis=AX.X, op=ALU.add),
                waits=[(s.sXL[par], vload), (s.sDVE, s.sDVE.n)], inc=s.sDVE)
            vst = s.sDVE.n
            s.op("scalar", lambda e, par=par, LNS=LNS: e.activation(
                out=TTRS[:], in_=XL[par][:], func=AF.Square, bias=0.0, scale=1.0,
                accum_out=LNS[:, 1:2]),
                waits=[(s.sXL[par], vload), (s.sDVE, xnf_free),
                       (s.sDVE, lns1_free[par]), (s.sACT, act_chain)], inc=s.sACT)
            vsq = s.sACT.n
            s.op("vector", lambda e, LNS=LNS: e.tensor_scalar(
                out=LNS[:, 2:3], in0=LNS[:, 0:1], scalar1=1.0 / D, scalar2=None,
                op0=ALU.mult), waits=[(s.sDVE, vst)], inc=s.sDVE)
            vmean = s.sDVE.n
            s.op("vector", lambda e, LNS=LNS: e.tensor_tensor(
                out=LNS[:, 3:4], in0=LNS[:, 2:3], in1=LNS[:, 2:3], op=ALU.mult),
                waits=[(s.sDVE, vmean)], inc=s.sDVE)
            vm2 = s.sDVE.n
            s.op("vector", lambda e, LNS=LNS: e.scalar_tensor_tensor(
                out=LNS[:, 4:5], in0=LNS[:, 1:2], scalar=1.0 / D, in1=LNS[:, 3:4],
                op0=ALU.mult, op1=ALU.subtract),
                waits=[(s.sDVE, vm2), (s.sACT, vsq)], inc=s.sDVE)
            vvar = s.sDVE.n
            lns1_free[par] = vvar
            s.op("scalar", lambda e, LNS=LNS: e.activation(
                out=LNS[:, 5:6], in_=LNS[:, 4:5], func=AF.Ln, bias=1e-5, scale=1.0),
                waits=[(s.sDVE, vvar), (s.sACT, act_chain)], inc=s.sACT)
            va1 = s.sACT.n
            s.op("scalar", lambda e, LNS=LNS: e.activation(
                out=LNS[:, 6:7], in_=LNS[:, 5:6], func=AF.Exp, bias=0.0, scale=-0.5),
                waits=[(s.sACT, va1)], inc=s.sACT)
            vrstd = s.sACT.n
            act_chain = vrstd
            s.op("vector", lambda e, par=par, LNS=LNS: e.tensor_scalar(
                out=XNF[:], in0=XL[par][:], scalar1=LNS[:, 2:3], scalar2=LNS[:, 6:7],
                op0=ALU.subtract, op1=ALU.mult),
                waits=[(s.sACT, vrstd), (s.sDVE, s.sDVE.n)], inc=s.sDVE)
            vxn = s.sDVE.n
            xl_free[par] = vxn
            s.op("vector", lambda e, par=par: e.tensor_copy(out=XNH[par][:], in_=XNF[:]),
                 waits=[(s.sPE, xnh_free[par]), (s.sDVE, vxn)], inc=s.sDVE)
            vhi = s.sDVE.n
            xnf_free = vhi
            for cp in range(4):
                tp = cp % 2
                trs_, trv_ = tr_free[tp]
                s.op("tensor", lambda e, par=par, cp=cp, tp=tp: e.transpose(
                    out=pTR[tp][:, 0:128], in_=XNH[par][:, (2 * cp) * 128:(2 * cp + 1) * 128],
                    identity=s.IDEB[:]),
                    waits=[(s.sDVE, vhi), (trs_, trv_), (s.sXL[0], s.v_consts)])
                s.op("tensor", lambda e, par=par, cp=cp, tp=tp: e.transpose(
                    out=pTR[tp][:, 128:256],
                    in_=XNH[par][:, (2 * cp + 1) * 128:(2 * cp + 2) * 128],
                    identity=s.IDEB[:]), inc=s.sPE)
                vtp = s.sPE.n
                eng = "vector" if cp % 2 == 0 else "scalar"
                b0 = g * 4096 + (2 * cp) * 512 + sub * 128
                b1 = g * 4096 + (2 * cp + 1) * 512 + sub * 128
                s.copy(eng, s.XNFM[:, b0:b0 + 128], pTR[tp][:, 0:128], waits=[(s.sPE, vtp)])
                lastc = s.copy(eng, s.XNFM[:, b1:b1 + 128], pTR[tp][:, 128:256])
                tr_free[tp] = (s.sDVE, lastc) if eng == "vector" else (s.sACT, lastc)
            xnh_free[par] = s.sPE.n
            vfa, vfd = s.sACT.n, s.sDVE.n
            s.op("vector", lambda e: e.nop(), waits=[(s.sACT, vfa), (s.sDVE, vfd)],
                 inc=s.sDVE)
            s.fm_done[ts] = s.sDVE.n

            if sub == 3:
                for ch in range(8):
                    s.op("tensor", lambda e, g=g, ch=ch: e.matmul(
                        out=pRT, lhsT=s.WRB[:, ch * 8:(ch + 1) * 8],
                        rhs=s.XNFM[:, g * 4096 + ch * 512: g * 4096 + (ch + 1) * 512],
                        start=(ch == 0), stop=(ch == 7)),
                        waits=[(s.sDVE, s.fm_done[ts]), (s.sWA, s.v_wa),
                               (s.sDVE, rt_free)] if ch == 0 else (),
                        inc=s.sPE if ch == 7 else None)
                vrt = s.sPE.n
                s.copy("vector", RTF[:], pRT, waits=[(s.sPE, vrt)])
                s.op("vector", lambda e: e.nop(), inc=s.sDVE)
                vrtf = s.sDVE.n
                for s2 in range(4):
                    s.op("tensor", lambda e, s2=s2: e.transpose(
                        out=pRTT[:, s2 * 8:(s2 + 1) * 8], in_=RTF[:, s2 * 128:(s2 + 1) * 128],
                        identity=s.IDE[0:8, 0:8]),
                        waits=[(s.sDVE, vrtf)] if s2 == 0 else (),
                        inc=s.sPE if s2 == 3 else None)
                vrtt = s.sPE.n
                s.copy("vector", LGT[:], pRTT, waits=[(s.sPE, vrtt)])
                s.op("vector", lambda e: e.nop(), inc=s.sDVE)
                vlgt = s.sDVE.n
                rt_free = vlgt
                for s2 in range(4):
                    ts2 = g * 4 + s2
                    LNR = LNS2[s2 % 2]
                    L = LGT[:, s2 * 8:(s2 + 1) * 8]
                    s.op("vector", lambda e, L=L, LNR=LNR: e.tensor_reduce(
                        out=LNR[:, 6:7], in_=L, axis=AX.X, op=ALU.max),
                        waits=[(s.sDVE, vlgt)], inc=s.sDVE)
                    vmx = s.sDVE.n
                    s.op("vector", lambda e, LNR=LNR: e.tensor_scalar(
                        out=LNR[:, 7:8], in0=LNR[:, 6:7], scalar1=-1.0, scalar2=None,
                        op0=ALU.mult), waits=[(s.sDVE, vmx)], inc=s.sDVE)
                    vnm = s.sDVE.n
                    s.op("scalar", lambda e, L=L, LNR=LNR: e.activation(
                        out=EXPS[:], in_=L, func=AF.Exp, bias=LNR[:, 7:8], scale=1.0,
                        accum_out=LNR[:, 0:1]),
                        waits=[(s.sDVE, vnm), (s.sACT, act_chain)], inc=s.sACT)
                    vse = s.sACT.n
                    act_chain = vse
                    s.op("vector", lambda e, ts2=ts2, LNR=LNR: e.reciprocal(
                        out=s.SP[:, ts2:ts2 + 1], in_=LNR[:, 0:1]),
                        waits=[(s.sACT, vse)], inc=s.sDVE)
                    s.op("vector", lambda e, L=L: e.max(out=TOP8[:], in_=L), inc=s.sDVE)
                    vt8 = s.sDVE.n
                    s.op("vector", lambda e, L=L: e.max_index(
                        out=TI8[:], in_max=TOP8[:], in_values=L),
                        waits=[(s.sDVE, vt8)], inc=s.sDVE)
                    vti = s.sDVE.n
                    s.op("vector", lambda e, ts2=ts2: e.tensor_copy(
                        out=s.SEID[:, ts2:ts2 + 1], in_=TI8[:, 0:1]),
                        waits=[(s.sDVE, vti)], inc=s.sDVE)
                    veid = s.sDVE.n
                    s.op("vector", lambda e, ts2=ts2: e.tensor_scalar(
                        out=s.SOH[:, ts2 * 8:(ts2 + 1) * 8], in0=s.IOT8[:],
                        scalar1=s.SEID[:, ts2:ts2 + 1], scalar2=None, op0=ALU.is_equal),
                        waits=[(s.sDVE, veid)], inc=s.sDVE)
                    voh = s.sDVE.n
                    bsrc = s.BASEC[ts2 % 2]
                    bdst = s.BASEC[(ts2 + 1) % 2]
                    s.op("tensor", lambda e, ts2=ts2: e.matmul(
                        out=pLR, lhsT=s.USTR[:], rhs=s.SOH[:, ts2 * 8:(ts2 + 1) * 8],
                        start=True, stop=False),
                        waits=[(s.sDVE, voh), (s.sDVE, base_ms)])
                    s.op("tensor", lambda e, bsrc=bsrc: e.matmul(
                        out=pLR, lhsT=s.O1x128f[:], rhs=bsrc[:], start=False, stop=True))
                    s.op("tensor", lambda e, ts2=ts2: e.matmul(
                        out=pCN, lhsT=s.O128x1f[:], rhs=s.SOH[:, ts2 * 8:(ts2 + 1) * 8],
                        start=True, stop=True), inc=s.sPE)
                    vlr = s.sPE.n
                    s.op("vector", lambda e, ts2=ts2: e.tensor_tensor(
                        out=TMP8[:], in0=s.SOH[:, ts2 * 8:(ts2 + 1) * 8], in1=pLR,
                        op=ALU.mult), waits=[(s.sPE, vlr)], inc=s.sDVE)
                    vt8b = s.sDVE.n
                    s.op("vector", lambda e, ts2=ts2: e.tensor_reduce(
                        out=s.SLR[:, ts2:ts2 + 1], in_=TMP8[:], axis=AX.X, op=ALU.add),
                        waits=[(s.sDVE, vt8b)], inc=s.sDVE)
                    s.op("vector", lambda e, bsrc=bsrc, bdst=bdst: e.tensor_tensor(
                        out=bdst[:], in0=bsrc[:], in1=pCN, op=ALU.add), inc=s.sDVE)
                    base_ms = s.sDVE.n
        s.a1_dve = s.sDVE.n
        s.fincnt = s.BASEC[c.NT % 2]

    def phase_coll1(self):
        s, c = self, self.c
        s.CNT8 = s.sb("CNT8", [8, 8])
        s.op("gpsimd", lambda e: e.dma_start(out=s.cc1_in, in_=s.fincnt[:]),
             waits=[(s.sDVE, s.a1_dve)], inc=s.sCD, incn=16)
        v1 = s.sCD.n
        s.op("gpsimd", lambda e: e.collective_compute(
            "AllGather", ALU.bypass, replica_groups=[list(range(8))],
            ins=[s.cc1_in], outs=[s.cc1_out]), waits=[(s.sCD, v1)], inc=s.sC1)
        vc1 = s.sC1.n
        s.op("gpsimd", lambda e: e.dma_start(out=s.CNT8[:], in_=s.cc1_out),
             waits=[(s.sC1, vc1)], inc=s.sCD, incn=16)
        s.v_cnt8 = s.sCD.n

        WCOL = s.sb("WCOL", [8, 2])
        pOFF = s.PF[3][0:1, 0:16]
        s.OFFG = s.sb("OFFG", [1, 8]); s.OFFEB = s.sb("OFFEB", [1, 8])
        s.op("gpsimd", lambda e: e.tensor_tensor(
            out=WCOL[:, 0:1], in0=s.IOTP[0:8, :], in1=s.META[0:8, 0:1],
            op=ALU.subtract), waits=[(s.sCD, s.v_cnt8), (s.sXL[0], s.v_consts)],
            inc=s.sGP)
        vws = s.sGP.n
        s.op("gpsimd", lambda e: e.tensor_scalar(
            out=WCOL[:, 0:1], in0=WCOL[:, 0:1], scalar1=0.0, scalar2=None,
            op0=ALU.is_lt), waits=[(s.sGP, vws)], inc=s.sGP)
        s.op("gpsimd", lambda e: e.tensor_tensor(
            out=WCOL[:, 1:2], in0=s.IOTP[0:8, :], in1=s.META[0:8, 3:4],
            op=ALU.subtract), inc=s.sGP)
        vws2 = s.sGP.n
        s.op("gpsimd", lambda e: e.tensor_scalar(
            out=WCOL[:, 1:2], in0=WCOL[:, 1:2], scalar1=0.0, scalar2=None,
            op0=ALU.is_equal), waits=[(s.sGP, vws2)], inc=s.sGP)
        vw = s.sGP.n
        s.op("tensor", lambda e: e.matmul(out=pOFF[:, 0:8], lhsT=WCOL[:, 0:1], rhs=s.CNT8[:],
                                          start=True, stop=True), waits=[(s.sGP, vw)])
        s.op("tensor", lambda e: e.matmul(out=pOFF[:, 8:16], lhsT=WCOL[:, 1:2], rhs=s.CNT8[:],
                                          start=True, stop=True), inc=s.sPE)
        voff = s.sPE.n
        s.copy("vector", s.OFFG[:], pOFF[:, 0:8], waits=[(s.sPE, voff)])
        s.copy("vector", s.OFFEB[:], pOFF[:, 8:16])
        s.op("vector", lambda e: e.nop(), inc=s.sDVE)
        voc = s.sDVE.n
        s.OFFGB = s.sb("OFFGB", [128, 8]); s.OFFEBB = s.sb("OFFEBB", [128, 8])
        pOFB = s.PF[4][:, 0:16]
        s.op("tensor", lambda e: e.matmul(out=pOFB[:, 0:8], lhsT=s.O1x128f[:], rhs=s.OFFG[:],
                                          start=True, stop=True), waits=[(s.sDVE, voc)])
        s.op("tensor", lambda e: e.matmul(out=pOFB[:, 8:16], lhsT=s.O1x128f[:], rhs=s.OFFEB[:],
                                          start=True, stop=True), inc=s.sPE)
        vofb = s.sPE.n
        s.copy("vector", s.OFFGB[:], pOFB[:, 0:8], waits=[(s.sPE, vofb)])
        s.copy("vector", s.OFFEBB[:], pOFB[:, 8:16])
        s.op("vector", lambda e: e.nop(), inc=s.sDVE)
        s.v_offb = s.sDVE.n

    def phase_scatter(self):
        s, c = self, self.c
        TMP8 = s.sb("STMP8", [128, 8])
        DC = s.sb("DC", [128, 16])
        SIDX = s.sb("SIDX", [128, c.NT], U32)
        RECT = s.sb("RECT", [128, 2 * c.NT])

        def gop(fn, waits=(), inc=None):
            return s.op("gpsimd", fn, waits=waits, inc=inc if inc is not None else s.sGP)

        for t in range(c.NT):
            oh = s.SOH[:, t * 8:(t + 1) * 8]
            def tree8(dst, src_ms):
                va = gop(lambda e: e.tensor_tensor(out=TMP8[:, 0:4], in0=TMP8[:, 0:4],
                                                   in1=TMP8[:, 4:8], op=ALU.add),
                         waits=[(s.sGP, src_ms)])
                vb = gop(lambda e: e.tensor_tensor(out=TMP8[:, 0:2], in0=TMP8[:, 0:2],
                                                   in1=TMP8[:, 2:4], op=ALU.add),
                         waits=[(s.sGP, va)])
                return gop(lambda e, dst=dst: e.tensor_tensor(out=dst, in0=TMP8[:, 0:1],
                                                              in1=TMP8[:, 1:2], op=ALU.add),
                           waits=[(s.sGP, vb)])
            v0 = gop(lambda e, oh=oh: e.tensor_tensor(out=TMP8[:], in0=oh, in1=s.OFFGB[:],
                                                      op=ALU.mult),
                     waits=[(s.sDVE, s.v_offb), (s.sGP, s.sGP.n)])
            v1 = tree8(DC[:, 0:1], v0)
            v2 = gop(lambda e, oh=oh: e.tensor_tensor(out=TMP8[:], in0=oh, in1=s.OFFEBB[:],
                                                      op=ALU.mult), waits=[(s.sGP, v1)])
            v3 = tree8(DC[:, 1:2], v2)
            v4 = gop(lambda e, t=t: e.tensor_tensor(out=DC[:, 2:3], in0=s.SLR[:, t:t + 1],
                                                    in1=DC[:, 1:2], op=ALU.add),
                     waits=[(s.sGP, v3)])
            v5 = gop(lambda e, t=t: e.tensor_tensor(out=DC[:, 3:4], in0=s.SLR[:, t:t + 1],
                                                    in1=DC[:, 0:1], op=ALU.add))
            v6 = gop(lambda e: e.tensor_scalar(out=DC[:, 4:5], in0=DC[:, 2:3],
                                               scalar1=float(c.CELL), scalar2=None,
                                               op0=ALU.is_lt), waits=[(s.sGP, v5)])
            v7 = gop(lambda e: e.tensor_scalar(out=DC[:, 5:6], in0=DC[:, 3:4],
                                               scalar1=float(c.CAP), scalar2=None,
                                               op0=ALU.is_lt))
            v8 = gop(lambda e, t=t: e.tensor_tensor(out=DC[:, 6:7], in0=s.SP[:, t:t + 1],
                                                    in1=DC[:, 5:6], op=ALU.mult),
                     waits=[(s.sGP, v7)])
            v9a = gop(lambda e, t=t: e.tensor_scalar(out=DC[:, 7:8], in0=s.IOTP[:],
                                                     scalar1=float(t * 128 + 1),
                                                     scalar2=None, op0=ALU.add))
            v9 = gop(lambda e: e.tensor_tensor(out=DC[:, 7:8], in0=DC[:, 7:8],
                                               in1=s.META[:, 1:2], op=ALU.add),
                     waits=[(s.sGP, v9a)])
            v10 = gop(lambda e: e.tensor_tensor(out=DC[:, 8:9], in0=DC[:, 2:3],
                                                in1=s.META[:, 4:5], op=ALU.add),
                      waits=[(s.sGP, v6)])
            M23 = float(1 << 23)
            n1 = gop(lambda e, M23=M23: e.tensor_scalar(
                out=DC[:, 9:10], in0=DC[:, 8:9], scalar1=1.0 / 128.0, scalar2=M23,
                op0=ALU.mult, op1=ALU.add), waits=[(s.sGP, v10)])
            n2 = gop(lambda e, M23=M23: e.tensor_scalar(
                out=DC[:, 9:10], in0=DC[:, 9:10], scalar1=-M23, scalar2=None,
                op0=ALU.add), waits=[(s.sGP, n1)])
            n3 = gop(lambda e: e.tensor_scalar(
                out=DC[:, 10:11], in0=DC[:, 9:10], scalar1=128.0, scalar2=None,
                op0=ALU.mult), waits=[(s.sGP, n2)])
            n4 = gop(lambda e: e.tensor_tensor(
                out=DC[:, 10:11], in0=DC[:, 8:9], in1=DC[:, 10:11], op=ALU.subtract),
                waits=[(s.sGP, n3)])
            n5 = gop(lambda e: e.tensor_scalar(
                out=DC[:, 11:12], in0=DC[:, 10:11], scalar1=0.0, scalar2=None,
                op0=ALU.is_lt), waits=[(s.sGP, n4)])
            n6 = gop(lambda e: e.tensor_tensor(
                out=DC[:, 9:10], in0=DC[:, 9:10], in1=DC[:, 11:12], op=ALU.subtract),
                waits=[(s.sGP, n5)])
            n7 = gop(lambda e: e.tensor_scalar(
                out=DC[:, 11:12], in0=DC[:, 11:12], scalar1=128.0, scalar2=None,
                op0=ALU.mult), waits=[(s.sGP, n6)])
            n8 = gop(lambda e: e.tensor_tensor(
                out=DC[:, 10:11], in0=DC[:, 10:11], in1=DC[:, 11:12], op=ALU.add),
                waits=[(s.sGP, n7)])
            n9 = gop(lambda e: e.tensor_scalar(
                out=DC[:, 10:11], in0=DC[:, 10:11], scalar1=float(c.TILES),
                scalar2=None, op0=ALU.mult), waits=[(s.sGP, n8)])
            n10 = gop(lambda e: e.tensor_tensor(
                out=DC[:, 9:10], in0=DC[:, 9:10], in1=DC[:, 10:11], op=ALU.add),
                waits=[(s.sGP, n9)])
            n11 = gop(lambda e, t=t: e.tensor_scalar(
                out=DC[:, 11:12], in0=s.SEID[:, t:t + 1], scalar1=float(c.SLOTS),
                scalar2=None, op0=ALU.mult), waits=[(s.sGP, n10)])
            n12 = gop(lambda e: e.tensor_tensor(
                out=DC[:, 9:10], in0=DC[:, 9:10], in1=DC[:, 11:12], op=ALU.add),
                waits=[(s.sGP, n11)])
            n13 = gop(lambda e: e.tensor_tensor(
                out=DC[:, 9:10], in0=DC[:, 9:10], in1=DC[:, 4:5], op=ALU.mult),
                waits=[(s.sGP, n12), (s.sGP, v6)])
            n14 = gop(lambda e: e.tensor_scalar(
                out=DC[:, 11:12], in0=DC[:, 4:5], scalar1=-1.0, scalar2=1.0,
                op0=ALU.mult, op1=ALU.add), waits=[(s.sGP, n13)])
            n15 = gop(lambda e: e.tensor_scalar(
                out=DC[:, 10:11], in0=DC[:, 7:8], scalar1=float(c.BIG), scalar2=None,
                op0=ALU.add), waits=[(s.sGP, v9), (s.sGP, n9)])
            n16 = gop(lambda e: e.tensor_tensor(
                out=DC[:, 10:11], in0=DC[:, 10:11], in1=DC[:, 11:12], op=ALU.mult),
                waits=[(s.sGP, n15), (s.sGP, n14)])
            n17 = gop(lambda e: e.tensor_tensor(
                out=DC[:, 9:10], in0=DC[:, 9:10], in1=DC[:, 10:11], op=ALU.add),
                waits=[(s.sGP, n16)])
            v22 = gop(lambda e, t=t: e.tensor_copy(out=SIDX[:, t:t + 1], in_=DC[:, 9:10]),
                      waits=[(s.sGP, n17)])
            v23 = gop(lambda e, t=t: e.tensor_copy(out=RECT[:, 2 * t:2 * t + 1],
                                                   in_=DC[:, 7:8]), waits=[(s.sGP, v8)])
            v24 = gop(lambda e, t=t: e.tensor_copy(out=RECT[:, 2 * t + 1:2 * t + 2],
                                                   in_=DC[:, 6:7]))
            s.op("gpsimd", lambda e, t=t: e.indirect_dma_start(
                out=s.rec_in,
                out_offset=bass.IndirectOffsetOnAxis(ap=SIDX[:, t:t + 1], axis=0),
                in_=RECT[:, 2 * t:2 * t + 2], in_offset=None,
                bounds_check=s.breg(e, c.E * c.SLOTS - 1), oob_is_err=False),
                waits=[(s.sGP, v24), (s.sGP, v22), (s.sXL[0], s.v_zrec)],
                inc=s.sSC, incn=16)
        s.v_scat = s.sSC.n

    def phase_a2(self):
        s, c = self, self.c
        KFM = s.sb("KFM", [128, 4096], BF16)
        VTM = [s.sb(f"VTM{i}", [128, 1024], BF16) for i in range(4)]
        SQB = [s.sb("SQB0", [128, 512], BF16), s.sb("SQB1", [128, 512], BF16)]
        ATL = [s.sb(f"ATL{i}", [128, 512], BF16) for i in range(2)]
        s.DKTM = s.sb("DKTM", [128, 64])
        MXKG = s.sb("MXKG", [128, 64])
        LNMX = s.sb("LNMX", [128, 64])
        MXR = s.sb("MXR", [128, 2])
        s.UCOMP = s.sb("UCOMP", [128, 2048])
        s.USB = s.sb("USB", [8, 512])
        pKV = [s.PF[0], s.PF[1]]
        pDD = [s.PF[2], s.PF[3]]
        pDT = s.PF[2][:, 0:64]
        pU = s.PF[4]
        pu8 = s.PF[5][0:8, :]
        s.pu8 = pu8

        s.op("vector", lambda e: e.memset(s.UCOMP[:], 0.0), inc=s.sDVE)
        s.op("vector", lambda e: e.memset(MXR[:], -1e30), inc=s.sDVE)
        s.op("vector", lambda e: e.memset(s.USB[:], 0.0), inc=s.sDVE)
        v_z = s.sDVE.n

        kv_free = [s.v_pbd, s.v_pbd]
        dd_free_act = [0, 0]
        sq_free = [0, 0]
        at_free = [0, 0]
        at_free_dve = [0, 0]
        u_guard = v_z
        usb_ms = v_z
        dkt_ms = 0
        for g in range(c.GA):
            gb = g * 4096
            for oc in range(8):
                par = oc % 2
                w0 = [(s.sDVE, kv_free[par]), (s.sWA, s.v_wa),
                      (s.sDVE, s.fm_done[g * 4 + 3]), (s.sDVE, dkt_ms)]
                for kc in range(8):
                    s.op("tensor", lambda e, oc=oc, kc=kc, par=par, gb=gb: e.matmul(
                        out=pKV[par][:],
                        lhsT=s.WKB[:, kc * 1024 + oc * 128: kc * 1024 + (oc + 1) * 128],
                        rhs=s.XNFM[:, gb + kc * 512: gb + (kc + 1) * 512],
                        start=(kc == 0), stop=(kc == 7)),
                        waits=w0 if kc == 0 else (), inc=s.sPE if kc == 7 else None)
                vk = s.sPE.n
                s.op("vector", lambda e, oc=oc, par=par: e.tensor_copy(
                    out=KFM[:, oc * 512:(oc + 1) * 512], in_=pKV[par][:]),
                    waits=[(s.sPE, vk), (s.sDVE, s.sDVE.n)], inc=s.sDVE)
                kv_free[par] = s.sDVE.n
            kfm_ms = s.sDVE.n
            for oc in range(8):
                par = oc % 2
                s.op("vector", lambda e, oc=oc, par=par: e.tensor_tensor(
                    out=SQB[par][:], in0=KFM[:, oc * 512:(oc + 1) * 512],
                    in1=KFM[:, oc * 512:(oc + 1) * 512], op=ALU.mult),
                    waits=[(s.sDVE, kfm_ms), (s.sPE, sq_free[par])], inc=s.sDVE)
                vsq = s.sDVE.n
                for s2 in range(4):
                    s.op("tensor", lambda e, oc=oc, s2=s2, par=par: e.matmul(
                        out=pDT[:, s2 * 16:(s2 + 1) * 16],
                        lhsT=SQB[par][:, s2 * 128:(s2 + 1) * 128],
                        rhs=s.OD16N[:, oc * 16:(oc + 1) * 16],
                        start=(oc == 0 and s2 == 0), stop=(oc == 7 and s2 == 3)),
                        waits=[(s.sDVE, vsq)] if s2 == 0 else (),
                        inc=s.sPE if s2 == 3 else None)
                sq_free[par] = s.sPE.n
            vdt = s.sPE.n
            s.op("vector", lambda e: e.tensor_copy(
                out=s.DKTM[:].rearrange("p (h q) -> p h q", h=16, q=4),
                in_=pDT.rearrange("p (q h) -> p h q", q=4, h=16)),
                waits=[(s.sPE, vdt), (s.sDVE, s.sDVE.n)], inc=s.sDVE)
            dkt_ms = s.sDVE.n
            vdk = dkt_ms
            for s2 in range(4):
                for half in range(2):
                    par = (s2 * 2 + half) % 2
                    w0 = [(s.sDVE, kv_free[par])]
                    for kc in range(8):
                        s.op("tensor", lambda e, s2=s2, half=half, kc=kc, par=par, gb=gb: e.matmul(
                            out=pKV[par][:],
                            lhsT=s.XNFM[:, gb + kc * 512 + s2 * 128: gb + kc * 512 + (s2 + 1) * 128],
                            rhs=s.WVB[:, kc * 1024 + half * 512: kc * 1024 + (half + 1) * 512],
                            start=(kc == 0), stop=(kc == 7)),
                            waits=w0 if kc == 0 else (), inc=s.sPE if kc == 7 else None)
                    vv = s.sPE.n
                    s.op("vector", lambda e, s2=s2, half=half, par=par: e.tensor_copy(
                        out=VTM[s2][:, half * 512:(half + 1) * 512], in_=pKV[par][:]),
                        waits=[(s.sPE, vv), (s.sDVE, s.sDVE.n)], inc=s.sDVE)
                    kv_free[par] = s.sDVE.n
            vtm_ms = s.sDVE.n
            for hp in range(8):
                for s2 in range(4):
                    par = s2 % 2
                    s.op("tensor", lambda e, hp=hp, s2=s2, par=par: e.matmul(
                        out=pDD[par][:],
                        lhsT=KFM[:, hp * 512 + s2 * 128: hp * 512 + (s2 + 1) * 128],
                        rhs=s.PBD[:], start=True, stop=True),
                        waits=[(s.sACT, dd_free_act[par]), (s.sDVE, vtm_ms),
                               (s.sDVE, vdk), (s.sDVE, s.v_pbd)],
                        inc=s.sPE)
                    vdd = s.sPE.n
                    apar = s2 % 2
                    for hh in range(2):
                        col = (2 * hp + hh) * 4 + s2
                        s.op("scalar", lambda e, col=col, hh=hh, par=par, apar=apar: e.activation(
                            out=ATL[apar][:, hh * 256:(hh + 1) * 256],
                            in_=pDD[par][:, hh * 256:(hh + 1) * 256], func=AF.Exp,
                            bias=s.DKTM[:, col:col + 1], scale=1.0),
                            waits=[(s.sPE, vdd), (s.sPE, at_free[apar]),
                                   (s.sDVE, at_free_dve[apar]),
                                   (s.sACT, s.sACT.n)] if hh == 0 else (),
                            inc=s.sACT)
                    vexp = s.sACT.n
                    dd_free_act[par] = vexp
                    for hh in range(2):
                        col = (2 * hp + hh) * 4 + s2
                        s.op("vector", lambda e, col=col, hh=hh, apar=apar: e.tensor_reduce(
                            out=MXKG[:, col:col + 1], in_=ATL[apar][:, hh * 256:(hh + 1) * 256],
                            axis=AX.X, op=ALU.max),
                            waits=[(s.sACT, vexp)], inc=s.sDVE)
                    at_free_dve[apar] = s.sDVE.n
                    s.op("tensor", lambda e, hp=hp, s2=s2, apar=apar: e.matmul(
                        out=pU[:], lhsT=VTM[s2][:, hp * 128:(hp + 1) * 128], rhs=ATL[apar][:],
                        start=(s2 == 0), stop=(s2 == 3)),
                        waits=[(s.sACT, vexp), (s.sDVE, u_guard)])
                    s.op("tensor", lambda e, hp=hp, apar=apar, s2=s2: e.matmul(
                        out=pu8, lhsT=s.OC8[:, hp * 8:(hp + 1) * 8], rhs=ATL[apar][:],
                        start=(s2 == 0), stop=(s2 == 3)),
                        waits=[(s.sDVE, usb_ms)] if s2 == 0 else (), inc=s.sPE)
                    at_free[apar] = s.sPE.n
                vU = s.sPE.n
                s.op("vector", lambda e, hp=hp: e.tensor_tensor(
                    out=s.UCOMP[0:64, hp * 256:(hp + 1) * 256],
                    in0=s.UCOMP[0:64, hp * 256:(hp + 1) * 256],
                    in1=pU[0:64, 0:256], op=ALU.add),
                    waits=[(s.sPE, vU), (s.sDVE, s.sDVE.n)], inc=s.sDVE)
                s.op("vector", lambda e, hp=hp: e.tensor_tensor(
                    out=s.UCOMP[64:128, hp * 256:(hp + 1) * 256],
                    in0=s.UCOMP[64:128, hp * 256:(hp + 1) * 256],
                    in1=pU[64:128, 256:512], op=ALU.add), inc=s.sDVE)
                u_guard = s.sDVE.n
                s.op("vector", lambda e: e.tensor_tensor(
                    out=s.USB[0:8, :], in0=s.USB[0:8, :], in1=pu8, op=ALU.add),
                    waits=[(s.sPE, vU), (s.sDVE, usb_ms)], inc=s.sDVE)
                usb_ms = s.sDVE.n
            vgd = s.sDVE.n
            s.op("scalar", lambda e: e.activation(
                out=LNMX[:], in_=MXKG[:], func=AF.Ln, bias=0.0, scale=1.0),
                waits=[(s.sDVE, vgd), (s.sACT, s.sACT.n)], inc=s.sACT)
            vln = s.sACT.n
            s.op("vector", lambda e: e.tensor_tensor(
                out=LNMX[:], in0=LNMX[:], in1=s.DKTM[:], op=ALU.subtract),
                waits=[(s.sACT, vln), (s.sDVE, s.sDVE.n)], inc=s.sDVE)
            vsub = s.sDVE.n
            s.op("vector", lambda e: e.tensor_reduce(
                out=MXR[:, 0:1], in_=LNMX[:], axis=AX.X, op=ALU.max),
                waits=[(s.sDVE, vsub)], inc=s.sDVE)
            vm0 = s.sDVE.n
            s.op("vector", lambda e: e.tensor_tensor(
                out=MXR[:, 1:2], in0=MXR[:, 1:2], in1=MXR[:, 0:1], op=ALU.max),
                waits=[(s.sDVE, vm0)], inc=s.sDVE)
        s.a2_pe = s.sPE.n
        s.a2_dve = s.sDVE.n
        s.MXR = MXR

    def phase_coll2(self):
        s, c = self, self.c
        pMX = s.PF[0][0:1, 0:128]
        MXROW = s.sb("MXROW", [1, 128])
        s.MXCOL = s.sb("MXCOL", [128, 1])
        s.op("tensor", lambda e: e.transpose(out=pMX, in_=s.MXR[:, 1:2], identity=s.IDE[:]),
             waits=[(s.sDVE, s.a2_dve), (s.sPE, s.a2_pe)], inc=s.sPE)
        v = s.sPE.n
        s.copy("vector", MXROW[:], pMX, waits=[(s.sPE, v)])
        s.op("vector", lambda e: e.nop(), inc=s.sDVE)
        v2 = s.sDVE.n
        s.op("vector", lambda e: e.tensor_reduce(
            out=MXROW[:, 0:1], in_=MXROW[:], axis=AX.X, op=ALU.max),
            waits=[(s.sDVE, v2)], inc=s.sDVE)
        v3 = s.sDVE.n
        pMB = s.PF[1][:, 0:1]
        s.op("tensor", lambda e: e.matmul(out=pMB, lhsT=s.O1x128f[:], rhs=MXROW[:, 0:1],
                                          start=True, stop=True),
             waits=[(s.sDVE, v3)], inc=s.sPE)
        v4 = s.sPE.n
        s.copy("vector", s.MXCOL[:], pMB, waits=[(s.sPE, v4)])
        s.op("vector", lambda e: e.nop(), inc=s.sDVE)
        v_mx = s.sDVE.n

        UTM = s.sb("UTM", [128, 32])
        USUM = s.sb("USUM", [1, 16])
        pUT = s.PF[2][:, 0:32]
        for ch in range(4):
            s.op("tensor", lambda e, ch=ch: e.transpose(
                out=pUT[:, ch * 8:(ch + 1) * 8], in_=s.USB[:, ch * 128:(ch + 1) * 128],
                identity=s.IDE[0:8, 0:8]),
                waits=[(s.sDVE, s.a2_dve)] if ch == 0 else (),
                inc=s.sPE if ch == 3 else None)
        vut = s.sPE.n
        s.copy("vector", UTM[:], pUT, waits=[(s.sPE, vut)])
        s.op("vector", lambda e: e.nop(), inc=s.sDVE)
        v_utm = s.sDVE.n
        pUS = s.PF[3][0:32, 0:1]
        USR = s.sb("USR", [32, 1])
        pUSR = s.PF[0][0:1, 200:232]
        USROW = s.sb("USROW", [1, 32])
        s.op("tensor", lambda e: e.matmul(out=pUS, lhsT=UTM[:], rhs=s.O128x1f[:],
                                          start=True, stop=True),
             waits=[(s.sDVE, v_utm)], inc=s.sPE)
        v5 = s.sPE.n
        s.copy("vector", USR[:], pUS, waits=[(s.sPE, v5)])
        s.op("vector", lambda e: e.nop(), inc=s.sDVE)
        v6 = s.sDVE.n
        s.op("tensor", lambda e: e.transpose(out=pUSR, in_=USR[:], identity=s.IDE[0:32, 0:32]),
             waits=[(s.sDVE, v6)], inc=s.sPE)
        v7 = s.sPE.n
        s.copy("vector", USROW[:], pUSR, waits=[(s.sPE, v7)])
        s.op("vector", lambda e: e.nop(), inc=s.sDVE)
        v8 = s.sDVE.n
        for pr in range(2):
            s.op("vector", lambda e, pr=pr: e.tensor_tensor(
                out=USUM[:, pr:16:2], in0=USROW[:, pr * 16:pr * 16 + 8],
                in1=USROW[:, pr * 16 + 8:pr * 16 + 16], op=ALU.add),
                waits=[(s.sDVE, v8)], inc=s.sDVE)
        v_usum = s.sDVE.n

        SX = s.sb("SX", [128, 8])
        SXP = s.sb("SXP", [128, 8 * c.GA])
        for ch in range(8):
            for g in range(c.GA):
                s.op("vector", lambda e, ch=ch, g=g: e.tensor_reduce(
                    out=SXP[:, ch * c.GA + g:ch * c.GA + g + 1],
                    in_=s.XNFM[:, g * 4096 + ch * 512:g * 4096 + (ch + 1) * 512],
                    axis=AX.X, op=ALU.add),
                    waits=[(s.sDVE, s.a2_dve)], inc=s.sDVE)
        v9 = s.sDVE.n
        for ch in range(8):
            s.op("vector", lambda e, ch=ch: e.tensor_reduce(
                out=SX[:, ch:ch + 1], in_=SXP[:, ch * c.GA:(ch + 1) * c.GA],
                axis=AX.X, op=ALU.add), waits=[(s.sDVE, v9)], inc=s.sDVE)
        v10 = s.sDVE.n
        SXB = s.sb("SXB", [128, 8], BF16)
        s.copy("vector", SXB[:], SX[:], waits=[(s.sDVE, v10)])
        s.op("vector", lambda e: e.nop(), inc=s.sDVE)
        v11 = s.sDVE.n
        pS = [s.PF[4][0:1, :], s.PF[5][0:1, :]]
        SROW = s.sb("SROW", [1, 1024])
        for half in range(2):
            for ch in range(8):
                s.op("tensor", lambda e, half=half, ch=ch: e.matmul(
                    out=pS[half], lhsT=SXB[:, ch:ch + 1],
                    rhs=s.WVB[:, ch * 1024 + half * 512: ch * 1024 + (half + 1) * 512],
                    start=(ch == 0), stop=(ch == 7)),
                    waits=[(s.sDVE, v11)] if (half == 0 and ch == 0) else (),
                    inc=s.sPE if ch == 7 else None)
            vs = s.sPE.n
            s.copy("vector", SROW[:, half * 512:(half + 1) * 512], pS[half],
                   waits=[(s.sPE, vs)])
        s.op("vector", lambda e: e.nop(), inc=s.sDVE)
        v_srow = s.sDVE.n
        s.S8 = s.sb("S8", [128, 8])
        s.op("sync", lambda e: e.dma_start(out=s.srow_dram, in_=SROW[:]),
             waits=[(s.sDVE, v_srow)], inc=s.sSR, incn=16)
        vsr1 = s.sSR.n
        s.op("sync", lambda e: e.dma_start(
            out=s.S8[:], in_=s.srow_dram.rearrange("o (p f) -> (o p) f", p=128)),
            waits=[(s.sSR, vsr1)], inc=s.sSR, incn=16)
        v_s8 = s.sSR.n

        s.op("gpsimd", lambda e: e.dma_start(out=s.pk_in[:, 0:2048], in_=s.UCOMP[:]),
             waits=[(s.sDVE, s.a2_dve)], inc=s.sCD, incn=16)
        s.op("gpsimd", lambda e: e.dma_start(out=s.pk_in[:, 2048:2080], in_=UTM[:]),
             waits=[(s.sDVE, v_utm)], inc=s.sCD, incn=16)
        PKF = s.sb("PKF", [128, 16])
        s.op("vector", lambda e: e.memset(PKF[:], 0.0), waits=[(s.sDVE, s.sDVE.n)],
             inc=s.sDVE)
        vpkz = s.sDVE.n
        s.op("vector", lambda e: e.tensor_copy(out=PKF[0:1, :], in_=USUM[:]),
             waits=[(s.sDVE, vpkz), (s.sDVE, v_usum)], inc=s.sDVE)
        vpkf = s.sDVE.n
        s.op("gpsimd", lambda e: e.dma_start(out=s.pk_in[:, 2080:2096], in_=PKF[:]),
             waits=[(s.sDVE, vpkf)], inc=s.sCD, incn=16)
        s.op("gpsimd", lambda e: e.dma_start(out=s.pk_in[:, 2096:2097], in_=s.MXCOL[:], allow_slow_non_contiguous=True),
             waits=[(s.sDVE, v_mx)], inc=s.sCD, incn=16)
        s.op("gpsimd", lambda e: e.dma_start(out=s.pk_in[:, 2097:2105], in_=s.S8[:]),
             waits=[(s.sSR, v_s8)], inc=s.sCD, incn=16)
        v_pk = s.sCD.n
        s.op("gpsimd", lambda e: e.collective_compute(
            "AllGather", ALU.bypass, replica_groups=[list(range(8))],
            ins=[s.rec_in], outs=[s.rec_out]),
            waits=[(s.sSC, s.v_scat)], inc=s.sC2a)
        s.v_c2a = s.sC2a.n
        s.op("gpsimd", lambda e: e.collective_compute(
            "AllGather", ALU.bypass, replica_groups=[list(range(8))],
            ins=[s.pk_in], outs=[s.pk_out]),
            waits=[(s.sCD, v_pk)], inc=s.sC2b)
        s.v_c2b = s.sC2b.n

    def phase_merge(self):
        s, c = self, self.c
        W = 2 * c.TILES
        MRG = s.sb("MRG", [128, W])
        s.RTID = s.sb("RTID", [128, c.TILES])
        s.IDXO = s.sb("IDXO", [128, c.TILES], I32)
        IDXF = s.sb("IDXF", [128, c.TILES])
        GIF = s.sb("GIF", [128, c.TILES])
        MSK = s.sb("MSK", [128, c.TILES])
        NMSK = s.sb("NMSK", [128, c.TILES])
        RIDX = s.sb("RIDX", [128, 8], U32)
        RIF = s.sb("RIF", [128, 1])
        MSL = [s.sb(f"MSL{i}", [128, W]) for i in range(2)]

        for r in range(8):
            v0a = s.op("gpsimd", lambda e, r=r: e.tensor_scalar(
                out=RIF[:], in0=s.IOTP[:], scalar1=float(r * c.E * 128),
                scalar2=None, op0=ALU.add),
                waits=[(s.sXL[0], s.v_consts), (s.sGP, s.sGP.n)], inc=s.sGP)
            v0 = s.op("gpsimd", lambda e: e.tensor_tensor(
                out=RIF[:], in0=RIF[:], in1=s.META[:, 5:6], op=ALU.add),
                waits=[(s.sGP, v0a)], inc=s.sGP)
            s.op("gpsimd", lambda e, r=r: e.tensor_copy(out=RIDX[:, r:r + 1], in_=RIF[:]),
                 waits=[(s.sGP, v0)], inc=s.sGP)
        v_ridx = s.sGP.n
        recv = s.rec_out.rearrange("(a j) w -> a (j w)", j=c.TILES)
        vml = [0, 0]
        for r in range(8):
            par = r % 2
            s.op("gpsimd", lambda e, r=r, par=par: e.indirect_dma_start(
                out=MSL[par][:], out_offset=None, in_=recv,
                in_offset=bass.IndirectOffsetOnAxis(ap=RIDX[:, r:r + 1], axis=0)),
                waits=[(s.sGP, v_ridx), (s.sC2a, s.v_c2a), (s.sDVE, vml[par])],
                inc=s.sEG[par], incn=16)
            vg = s.sEG[par].n
            if r == 0:
                s.copy("vector", MRG[:], MSL[par][:], waits=[(s.sEG[par], vg)])
            else:
                s.op("vector", lambda e, par=par: e.tensor_tensor(
                    out=MRG[:], in0=MRG[:], in1=MSL[par][:], op=ALU.add),
                    waits=[(s.sEG[par], vg), (s.sDVE, s.sDVE.n)], inc=s.sDVE)
            vml[par] = s.op("vector", lambda e: e.nop(), inc=s.sDVE)
        v_mrg = s.sDVE.n
        s.op("vector", lambda e: e.tensor_copy(
            out=s.RTID[:], in_=MRG[:].rearrange("p (t w) -> p w t", w=2)[:, 0, :]),
            waits=[(s.sDVE, v_mrg)], inc=s.sDVE)
        s.op("vector", lambda e: e.tensor_copy(
            out=s.RSCL[:], in_=MRG[:].rearrange("p (t w) -> p w t", w=2)[:, 1, :]),
            inc=s.sDVE)
        v_sp = s.sDVE.n
        s.op("vector", lambda e: e.tensor_scalar(
            out=IDXF[:], in0=s.RTID[:], scalar1=-1.0, scalar2=None, op0=ALU.add),
            waits=[(s.sDVE, v_sp)], inc=s.sDVE)
        v_idxf = s.sDVE.n
        s.op("vector", lambda e: e.tensor_copy(out=s.IDXO[:], in_=IDXF[:]),
             waits=[(s.sDVE, v_idxf)], inc=s.sDVE)
        v_idxo = s.sDVE.n
        s.op("vector", lambda e: e.tensor_scalar(
            out=MSK[:], in0=s.RTID[:], scalar1=0.5, scalar2=None, op0=ALU.is_gt),
            waits=[(s.sDVE, v_sp)], inc=s.sDVE)
        vmsk = s.sDVE.n
        s.op("vector", lambda e: e.tensor_scalar(
            out=NMSK[:], in0=s.RTID[:], scalar1=0.5, scalar2=None, op0=ALU.is_le),
            inc=s.sDVE)
        vnmsk = s.sDVE.n
        s.op("vector", lambda e: e.tensor_tensor(
            out=GIF[:], in0=MSK[:], in1=IDXF[:], op=ALU.mult),
            waits=[(s.sDVE, vmsk), (s.sDVE, v_idxf)], inc=s.sDVE)
        vg1 = s.sDVE.n
        s.op("vector", lambda e: e.scalar_tensor_tensor(
            out=GIF[:], in0=NMSK[:], scalar=float(c.BIG), in1=GIF[:],
            op0=ALU.mult, op1=ALU.add),
            waits=[(s.sDVE, vnmsk), (s.sDVE, vg1)], inc=s.sDVE)
        vgp = s.sDVE.n
        s.op("vector", lambda e: e.tensor_copy(out=s.GIDX[:], in_=GIF[:]),
             waits=[(s.sDVE, vgp)], inc=s.sDVE)
        s.v_gidx = s.sDVE.n
        s.op("sync", lambda e: e.dma_start(out=s.o_idx, in_=s.IDXO[:]),
             waits=[(s.sDVE, v_idxo)], inc=s.sOUT, incn=16)

    def phase_expert(self):
        s, c = self, self.c
        FQ = s.FQ
        NFC = FQ // 128
        KQ = FQ // 128
        EGT = c.EGT
        TW = EGT * 128
        XGT = [s.sb(f"XGT{i}", [128, 1024]) for i in range(EGT)]
        XNN = s.sb("XNN", [128, 1024])
        XNB = [s.sb("XNB0", [128, 1024], BF16), s.sb("XNB1", [128, 1024], BF16)]
        ELN = [s.sb("ELN0", [128, 8]), s.sb("ELN1", [128, 8])]
        ETT = XNN
        XEG2 = [s.sb("XEGa", [128, 8 * TW], BF16), s.sb("XEGb", [128, 8 * TW], BF16)]
        HFMG2 = [s.sb("HFMGa", [128, NFC * TW], BF16),
                 s.sb("HFMGb", [128, NFC * TW], BF16)]
        YE2 = [s.sb("YEa", [128, 1024]), s.sb("YEb", [128, 1024])]
        pE = [s.PF[0], s.PF[1]]
        pY = [s.PF[2], s.PF[3]]
        pT2 = [s.PB[0], s.PB[1]]
        scr = s.xefm_scr.rearrange("(gg p) f -> gg p f", p=128)

        xg_free = [0] * EGT
        xg_dma = [None] * EGT
        ye_rel2 = [[(self.sYW, 0)], [(self.sYW, 0)]]
        xnn_free = 0
        eln1_free = [0, 0]
        xnb_free = [0, 0]
        t2_free = [0, 0]
        pe_free = [0, 0]
        py_free = [0, 0]
        xeg_free2 = [0, 0]
        xeg_w2 = [0, 0]
        hfm_free2 = [0, 0]
        eact = s.sACT.n
        v_h0w = []

        for q in range(4):
            wbuf = q % 2
            ggi_base = q * c.EG
            if q == 1:
                v_h0w = [(s.sYW, s.sYW.n), (s.sYWb, s.sYWb.n)]
                v_xf = [(s.sXF2[0], s.sXF2[0].n), (s.sXF2[1], s.sXF2[1].n)]
            if 2 <= q + 1 <= 3:
                pass
            if q >= 1:
                if q + 1 <= 3:
                    self.load_w_quarter(q + 1, wait_pe=s.sPE.n)
            elif q == 0:
                if False:
                    pass
            for gi in range(c.EG):
                tiles = list(range(gi * EGT, (gi + 1) * EGT))
                ggi = ggi_base + gi
                xb = ggi % 2
                XEG = XEG2[xb]
                HFMG = HFMG2[xb]
                if q == 0:
                    for ti, t in enumerate(tiles):
                        vz = s.op("vector", lambda e, ti=ti: e.memset(XGT[ti][:], 0.0),
                                  waits=[(s.sDVE, xg_free[ti])] +
                                        ([xg_dma[ti]] if xg_dma[ti] else []),
                                  inc=s.sDVE)
                        par = ti % 2
                        s.op("gpsimd", lambda e, t=t, ti=ti: e.indirect_dma_start(
                            out=XGT[ti][:], out_offset=None, in_=s.xfull,
                            in_offset=bass.IndirectOffsetOnAxis(ap=s.GIDX[:, t:t + 1], axis=0),
                            bounds_check=s.breg(e, c.T - 1), oob_is_err=False),
                            waits=[(s.sDVE, vz), (s.sDVE, s.v_gidx)],
                            inc=s.sEG[par], incn=16)
                        vg = s.sEG[par].n
                        LN = ELN[ti % 2]
                        s.op("vector", lambda e, ti=ti, LN=LN: e.tensor_reduce(
                            out=LN[:, 0:1], in_=XGT[ti][:], axis=AX.X, op=ALU.add),
                            waits=[(s.sEG[par], vg), (s.sDVE, s.sDVE.n)], inc=s.sDVE)
                        vst = s.sDVE.n
                        s.op("scalar", lambda e, ti=ti, LN=LN: e.activation(
                            out=ETT[:], in_=XGT[ti][:], func=AF.Square, bias=0.0,
                            scale=1.0, accum_out=LN[:, 1:2]),
                            waits=[(s.sEG[par], vg), (s.sDVE, xnn_free),
                                   (s.sDVE, eln1_free[ti % 2]), (s.sACT, eact)],
                            inc=s.sACT)
                        vsq = s.sACT.n
                        s.op("vector", lambda e, LN=LN: e.tensor_scalar(
                            out=LN[:, 2:3], in0=LN[:, 0:1], scalar1=1.0 / c.D, scalar2=None,
                            op0=ALU.mult), waits=[(s.sDVE, vst)], inc=s.sDVE)
                        vmean = s.sDVE.n
                        s.op("vector", lambda e, LN=LN: e.tensor_tensor(
                            out=LN[:, 3:4], in0=LN[:, 2:3], in1=LN[:, 2:3], op=ALU.mult),
                            waits=[(s.sDVE, vmean)], inc=s.sDVE)
                        vm2 = s.sDVE.n
                        s.op("vector", lambda e, LN=LN: e.scalar_tensor_tensor(
                            out=LN[:, 4:5], in0=LN[:, 1:2], scalar=1.0 / c.D, in1=LN[:, 3:4],
                            op0=ALU.mult, op1=ALU.subtract),
                            waits=[(s.sDVE, vm2), (s.sACT, vsq)], inc=s.sDVE)
                        vvar = s.sDVE.n
                        eln1_free[ti % 2] = vvar
                        s.op("scalar", lambda e, LN=LN: e.activation(
                            out=LN[:, 5:6], in_=LN[:, 4:5], func=AF.Ln, bias=1e-5, scale=1.0),
                            waits=[(s.sDVE, vvar), (s.sACT, eact)], inc=s.sACT)
                        va = s.sACT.n
                        s.op("scalar", lambda e, LN=LN: e.activation(
                            out=LN[:, 6:7], in_=LN[:, 5:6], func=AF.Exp, bias=0.0, scale=-0.5),
                            waits=[(s.sACT, va)], inc=s.sACT)
                        vrstd = s.sACT.n
                        eact = vrstd
                        s.op("vector", lambda e, ti=ti, LN=LN: e.tensor_scalar(
                            out=XNN[:], in0=XGT[ti][:], scalar1=LN[:, 2:3], scalar2=LN[:, 6:7],
                            op0=ALU.subtract, op1=ALU.mult),
                            waits=[(s.sACT, vrstd), (s.sDVE, s.sDVE.n)], inc=s.sDVE)
                        vxn = s.sDVE.n
                        bpar = ti % 2
                        s.op("vector", lambda e, bpar=bpar: e.tensor_copy(
                            out=XNB[bpar][:], in_=XNN[:]),
                            waits=[(s.sPE, xnb_free[bpar]), (s.sDVE, vxn)], inc=s.sDVE)
                        vhi = s.sDVE.n
                        xnn_free = vhi
                        for cp in range(4):
                            tp = cp % 2
                            s.op("tensor", lambda e, bpar=bpar, cp=cp, tp=tp: e.transpose(
                                out=pT2[tp][:, 0:128],
                                in_=XNB[bpar][:, (2 * cp) * 128:(2 * cp + 1) * 128],
                                identity=s.IDEB[:]),
                                waits=[(s.sDVE, vhi), (s.sDVE, t2_free[tp]),
                                       (s.sPE, xeg_free2[xb]),
                                       (s.sXF2[xb], xeg_w2[xb])])
                            s.op("tensor", lambda e, bpar=bpar, cp=cp, tp=tp: e.transpose(
                                out=pT2[tp][:, 128:256],
                                in_=XNB[bpar][:, (2 * cp + 1) * 128:(2 * cp + 2) * 128],
                                identity=s.IDEB[:]), inc=s.sPE)
                            vtp = s.sPE.n
                            b0 = (2 * cp) * TW + ti * 128
                            b1 = (2 * cp + 1) * TW + ti * 128
                            s.copy("vector", XEG[:, b0:b0 + 128], pT2[tp][:, 0:128],
                                   waits=[(s.sPE, vtp)])
                            t2_free[tp] = s.copy("vector", XEG[:, b1:b1 + 128],
                                                 pT2[tp][:, 128:256])
                        xnb_free[bpar] = s.sPE.n
                    vxeg = s.op("vector", lambda e: e.nop(), inc=s.sDVE)
                    s.op("sync", lambda e, gi=gi, XEG=XEG: e.dma_start(
                        out=scr[gi], in_=XEG[:]), waits=[(s.sDVE, vxeg)],
                        inc=s.sXF2[xb], incn=16)
                    xeg_w2[xb] = s.sXF2[xb].n
                    w_xeg = [(s.sDVE, vxeg)]
                else:
                    s.op("sync", lambda e, gi=gi, XEG=XEG: e.dma_start(
                        out=XEG[:], in_=scr[gi]),
                        waits=[(s.sPE, xeg_free2[xb]),
                               (s.sXE[xb], s.sXE[xb].n)] + v_xf,
                        inc=s.sXE[xb], incn=16)
                    w_xeg = [(s.sXE[xb], s.sXE[xb].n)]
                for fc in range(NFC):
                    par = fc % 2
                    w0 = list(w_xeg) + [(s.sWE[q % 2], s.v_wq[q]), (s.sACT, pe_free[par]),
                                        (s.sPE, hfm_free2[xb])]
                    for kc in range(8):
                        s.op("tensor", lambda e, fc=fc, kc=kc, par=par, wbuf=wbuf, TW=TW, FQ=FQ, XEG=XEG: e.matmul(
                            out=pE[par][:, 0:TW],
                            lhsT=s.W1Q[wbuf][:, kc * FQ + fc * 128: kc * FQ + (fc + 1) * 128],
                            rhs=XEG[:, kc * TW:(kc + 1) * TW],
                            start=(kc == 0), stop=(kc == 7)),
                            waits=w0 if kc == 0 else (), inc=s.sPE if kc == 7 else None)
                    vh = s.sPE.n
                    s.op("scalar", lambda e, fc=fc, par=par, TW=TW, HFMG=HFMG: e.activation(
                        out=HFMG[:, fc * TW:(fc + 1) * TW], in_=pE[par][:, 0:TW],
                        func=AF.Relu, bias=0.0, scale=1.0),
                        waits=[(s.sPE, vh), (s.sACT, eact)], inc=s.sACT)
                    pe_free[par] = s.sACT.n
                    eact = s.sACT.n
                xeg_free2[xb] = s.sPE.n
                vrelu = s.sACT.n
                for ti, t in enumerate(tiles):
                    for dh in range(2):
                        par = dh
                        for kq in range(KQ):
                            s.op("tensor", lambda e, ti=ti, dh=dh, kq=kq, par=par, wbuf=wbuf, TW=TW, HFMG=HFMG: e.matmul(
                                out=pY[par][:],
                                lhsT=HFMG[:, kq * TW + ti * 128: kq * TW + (ti + 1) * 128],
                                rhs=s.W2Q[wbuf][:, kq * 1024 + dh * 512: kq * 1024 + (dh + 1) * 512],
                                start=(kq == 0), stop=(kq == KQ - 1)),
                                waits=[(s.sACT, vrelu), (s.sDVE, py_free[par])] if kq == 0 else (),
                                inc=s.sPE if kq == KQ - 1 else None)
                        vy = s.sPE.n
                        tp2 = ti % 2
                        YE = YE2[tp2]
                        if q == 0:
                            s.op("vector", lambda e, t=t, ti=ti, dh=dh, par=par, YE=YE: e.scalar_tensor_tensor(
                                out=YE[:, dh * 512:(dh + 1) * 512], in0=pY[par][:],
                                scalar=s.RSCL[:, t:t + 1],
                                in1=XGT[ti][:, dh * 512:(dh + 1) * 512],
                                op0=ALU.mult, op1=ALU.add),
                                waits=[(s.sPE, vy), (s.sDVE, s.sDVE.n)] + ye_rel2[tp2],
                                inc=s.sDVE)
                        else:
                            s.op("vector", lambda e, t=t, dh=dh, par=par, YE=YE: e.tensor_scalar(
                                out=YE[:, dh * 512:(dh + 1) * 512], in0=pY[par][:],
                                scalar1=s.RSCL[:, t:t + 1], scalar2=None, op0=ALU.mult),
                                waits=[(s.sPE, vy), (s.sDVE, s.sDVE.n)] + ye_rel2[tp2],
                                inc=s.sDVE)
                        py_free[par] = s.sDVE.n
                    vye = s.sDVE.n
                    tp2 = ti % 2
                    YE = YE2[tp2]
                    if q == 0:
                        xg_free[ti] = vye
                        ysem = s.sYW if tp2 == 0 else s.sYWb
                        s.op("sync", lambda e, t=t, YE=YE: e.dma_start(
                            out=s.xye_scr[t * 128:(t + 1) * 128, :], in_=YE[:]),
                            waits=[(s.sDVE, vye)], inc=ysem, incn=16)
                        xg_dma[ti] = (ysem, ysem.n)
                        ye_rel2[tp2] = [(ysem, ysem.n)]
                    elif tp2 == 0:
                        s.op("gpsimd", lambda e, t=t, YE=YE: e.dma_start(
                            out=s.xye_scr[t * 128:(t + 1) * 128, :], in_=YE[:],
                            accum_op=ALU.add),
                            waits=[(s.sDVE, vye)] + v_h0w, inc=s.sYB, incn=16)
                        ye_rel2[tp2] = [(s.sYB, s.sYB.n)]
                    else:
                        s.op("gpsimd", lambda e, t=t, YE=YE: e.dma_start(
                            out=s.xye_scr[t * 128:(t + 1) * 128, :], in_=YE[:],
                            accum_op=ALU.add),
                            waits=[(s.sDVE, vye)] + v_h0w, inc=s.sYA, incn=16)
                        ye_rel2[tp2] = [(s.sYA, s.sYA.n)]
                hfm_free2[xb] = s.sPE.n
        s.v_ya = s.sYA.n
        s.v_yb = s.sYB.n
        s.v_yw = s.sYW.n
        s.v_ywb = s.sYWb.n
        s.e_pe = s.sPE.n
        s.e_dve = s.sDVE.n
        s.e_act = s.sACT.n

    def phase_ctx(self):
        s, c = self, self.c
        H, NB = c.H, c.NB
        UCG = s.sb("UCG", [128, 2112])
        UCG2 = s.sb("UCG2", [128, 2112])
        EMX = s.sb("EMX", [128, 1])
        MX8 = s.sb("MX8", [128, 8])
        SBC = s.sb("SBC", [128, 1024])
        S8C = s.sb("S8C", [128, 8])
        pCT = [s.PF[0][:, 0:128], s.PF[1][:, 0:128]]
        pB0 = s.PF[2][0:65, 0:1]
        B0C = s.sb("B0C", [65, 1])
        pB0T = s.PF[3][0:1, 0:65]
        pSB = [s.PF[4][:, 0:512], s.PF[5][:, 0:512]]
        SRWS = s.sb("SRWS", [1, 1024])

        def ctxa(b, h, mc):
            off = ((b * H + h) * 2 + mc) * 65
            return s.CTXA[:, off:off + 65]

        def b0t(b, h):
            off = (b * H + h) * 65
            return s.B0T[:, off:off + 65]
        s.ctxa_view = ctxa
        s.b0t_view = b0t

        bar = [(s.sPE, s.e_pe), (s.sDVE, s.e_dve), (s.sACT, s.e_act)]
        pko = s.pk_out.rearrange("(r p) w -> r p w", p=128)
        for r in range(8):
            s.op("sync", lambda e, r=r: e.dma_start(
                out=MX8[:, r:r + 1], in_=pko[r, :, 2096:2097],
                allow_slow_non_contiguous=True),
                 waits=[(s.sC2b, s.v_c2b)], inc=s.sMG, incn=16)
        vmx8 = s.sMG.n
        s.op("vector", lambda e: e.tensor_reduce(out=EMX[:], in_=MX8[:], axis=AX.X,
                                                 op=ALU.max),
             waits=[(s.sMG, vmx8), (s.sDVE, s.e_dve)], inc=s.sDVE)
        vmx = s.sDVE.n
        s.op("scalar", lambda e: e.activation(out=EMX[:], in_=EMX[:], func=AF.Exp,
                                              bias=math.log(c.ratio), scale=-1.0),
             waits=[(s.sDVE, vmx), (s.sACT, s.e_act)], inc=s.sACT)
        s.v_emx = s.sACT.n

        ct_free = [0, 0]
        ucg_free = 0
        sbc_ms = 0
        for b in range(NB):
            s.op("sync", lambda e, b=b: e.dma_start(out=UCG[:, 0:2105], in_=pko[2 * b, :, 0:2105]),
                 waits=[(s.sC2b, s.v_c2b), (s.sDVE, ucg_free), (s.sMG, s.sMG.n)],
                 inc=s.sMG, incn=16)
            s.op("sync", lambda e, b=b: e.dma_start(out=UCG2[:, 0:2105], in_=pko[2 * b + 1, :, 0:2105]),
                 inc=s.sMG, incn=16)
            v1 = s.sMG.n
            s.op("vector", lambda e: e.tensor_tensor(out=UCG[:, 0:2105], in0=UCG[:, 0:2105],
                                                     in1=UCG2[:, 0:2105], op=ALU.add),
                 waits=[(s.sMG, v1), (s.sDVE, s.sDVE.n)], inc=s.sDVE)
            v_cmb = s.sDVE.n
            s.op("vector", lambda e: e.tensor_scalar(
                out=S8C[:], in0=UCG[:, 2097:2105], scalar1=c.ratio * c.EPS, scalar2=None,
                op0=ALU.mult), waits=[(s.sDVE, v_cmb)], inc=s.sDVE)
            v_s8c = s.sDVE.n
            s.op("sync", lambda e: e.dma_start(
                out=s.srow_dram.rearrange("o (p f) -> (o p) f", p=128), in_=S8C[:]),
                waits=[(s.sDVE, v_s8c), (s.sSR, s.sSR.n)], inc=s.sSR, incn=16)
            vsr = s.sSR.n
            SRW = SRWS
            s.op("sync", lambda e, SRW=SRW: e.dma_start(out=SRW[:], in_=s.srow_dram),
                 waits=[(s.sSR, vsr), (s.sPE, s.sPE.n)], inc=s.sSR, incn=16)
            vsr2 = s.sSR.n
            for hf in range(2):
                s.op("tensor", lambda e, hf=hf, SRW=SRW: e.matmul(
                    out=pSB[hf], lhsT=s.O1x128f[:], rhs=SRW[:, hf * 512:(hf + 1) * 512],
                    start=True, stop=True),
                    waits=[(s.sSR, vsr2), (s.sDVE, sbc_ms)] + (bar if b == 0 and hf == 0 else []),
                    inc=s.sPE)
                vps = s.sPE.n
                s.copy("vector", SBC[:, hf * 512:(hf + 1) * 512], pSB[hf],
                       waits=[(s.sPE, vps)])
            s.op("vector", lambda e: e.nop(), inc=s.sDVE)
            v_sbc = s.sDVE.n
            for hp in range(H // 2):
                for mc in range(2):
                    par = mc
                    s.op("tensor", lambda e, hp=hp, mc=mc, par=par: e.transpose(
                        out=pCT[par],
                        in_=UCG[0:128, hp * 256 + mc * 128: hp * 256 + (mc + 1) * 128],
                        identity=s.IDE[:]),
                        waits=[(s.sDVE, v_cmb), (s.sDVE, ct_free[par])], inc=s.sPE)
                    vt = s.sPE.n
                    for hh in range(2):
                        h = hp * 2 + hh
                        s.op("vector", lambda e, b=b, h=h, hh=hh, mc=mc, par=par: e.scalar_tensor_tensor(
                            out=ctxa(b, h, mc)[:, 0:64],
                            in0=pCT[par][:, hh * 64:(hh + 1) * 64],
                            scalar=EMX[:, 0:1], in1=SBC[:, h * 64:(h + 1) * 64],
                            op0=ALU.mult, op1=ALU.add),
                            waits=[(s.sPE, vt), (s.sACT, s.v_emx), (s.sDVE, v_sbc)],
                            inc=s.sDVE)
                        ucol = 2048 + (hh * 2 + mc) * 8 + hp
                        s.op("vector", lambda e, b=b, h=h, mc=mc, ucol=ucol: e.tensor_scalar(
                            out=ctxa(b, h, mc)[:, 64:65], in0=UCG[:, ucol:ucol + 1],
                            scalar1=EMX[:, 0:1], scalar2=c.ratio * c.EPS * c.N,
                            op0=ALU.mult, op1=ALU.add), inc=s.sDVE)
                    ct_free[par] = s.sDVE.n
                vcx = s.sDVE.n
                for hh in range(2):
                    h = hp * 2 + hh
                    for mc in range(2):
                        s.op("tensor", lambda e, b=b, h=h, mc=mc: e.matmul(
                            out=pB0, lhsT=ctxa(b, h, mc)[:], rhs=s.O128x1b[:],
                            start=(mc == 0), stop=(mc == 1)),
                            waits=[(s.sDVE, vcx)] if mc == 0 else (),
                            inc=s.sPE if mc == 1 else None)
                    vb0 = s.sPE.n
                    s.copy("vector", B0C[:], pB0, waits=[(s.sPE, vb0)])
                    vb0c = s.op("vector", lambda e: e.nop(), inc=s.sDVE)
                    s.op("tensor", lambda e: e.transpose(out=pB0T, in_=B0C[:], identity=s.IDE[0:65, 0:65]),
                         waits=[(s.sDVE, vb0c)], inc=s.sPE)
                    vb0t = s.sPE.n
                    s.copy("vector", b0t(b, h), pB0T, waits=[(s.sPE, vb0t)])
                    s.op("vector", lambda e: e.nop(), inc=s.sDVE)
            ucg_free = s.sDVE.n
            sbc_ms = s.sDVE.n
        s.v_ctx = s.sDVE.n
        s.ctx_pe = s.sPE.n

    def phase_b(self):
        s, c = self, self.c
        H = c.H
        TW = c.BGT * 128
        XEGB = [s.sb("XEGB0", [128, 8 * TW], BF16), s.sb("XEGB1", [128, 8 * TW], BF16)]
        QFM = s.sb("QFM", [128, 8 * TW], BF16)
        SQ2 = [s.sb("SQ20", [128, TW], BF16), s.sb("SQ21", [128, TW], BF16)]
        DIAGQ = s.sb("DIAGQ", [128, H * c.BGT])
        MXQG = s.sb("MXQG", [128, H * c.BGT])
        SUMW = s.sb("SUMW", [128, H * c.BGT])
        WTM = s.sb("WTM", [128, H * c.BGT], BF16)
        WTS = s.sb("WTS", [H * c.BGT, 128], BF16)
        WROWS = [s.sb("WROWS0", [1, H * TW], BF16), s.sb("WROWS1", [1, H * TW], BF16)]
        E0B = s.sb("E0B", [128, 2 * TW], BF16)
        RCP = s.sb("RCP", [1, 512])
        RCPB = s.sb("RCPB", [1, 512], BF16)
        RBS = s.sb("RBS", [64, 512])
        ATTN = s.sb("ATTN", [128, 8 * TW], BF16)
        XYET = [s.sb("XYET0", [128, 1024]), s.sb("XYET1", [128, 1024])]
        OUTT = [s.sb("OUTT0", [128, 1024]), s.sb("OUTT1", [128, 1024])]
        pQ = [s.PF[0], s.PF[1]]
        pDGQ = s.PF[2][:, 0:64]
        pD2 = [s.PF[2], s.PF[3]]
        pO = [s.PF[4][0:65, :], s.PF[5][0:65, :]]
        pWT = s.PB[0][0:H * c.BGT, 0:128]
        pRB = s.PF[0][0:64, :]
        wr = s.wrow_dram.rearrange("(g r) f -> g r f", r=H * c.BGT)
        scr = s.xefm_scr.rearrange("(gg p) f -> gg p f", p=128)

        bar = [(s.sPE, s.ctx_pe), (s.sDVE, s.v_ctx)]
        q_free = [0, 0]
        sq_free = [0, 0]
        d2_free_act = [0, 0]
        d2_free_dve = [0, 0]
        o_free = [0, 0]
        xegb_ms = [0, 0]
        xyet_ms = [0, 0]
        out_free = [0, 0]
        outt_dma = [0, 0]
        rbs_free = 0
        qfm_free = 0
        e0_free = 0
        attn_free = 0
        dgq_ms = 0
        bact = s.sACT.n

        for gi in range(c.BG):
            b = (gi * c.BGT) // c.CELLT
            tiles = list(range(gi * c.BGT, (gi + 1) * c.BGT))
            par = gi % 2
            for ti, t in enumerate(tiles):
                ge, tt = t // c.EGT, t % c.EGT
                ETW = c.EGT * 128
                s.op("sync", lambda e, ge=ge, tt=tt, ti=ti, par=par, ETW=ETW, TW=TW: e.dma_start(
                    out=XEGB[par][:].rearrange("p (ch w) -> p ch w", ch=8)[:, :, ti * 128:(ti + 1) * 128],
                    in_=scr[ge].rearrange("p (ch w) -> p ch w", ch=8)[:, :, tt * 128:(tt + 1) * 128]),
                    waits=[(s.sXF2[0], s.sXF2[0].n), (s.sXF2[1], s.sXF2[1].n),
                           (s.sPE, xegb_ms[par]), (s.sYA, s.v_ya)],
                    inc=s.sBR[1], incn=16)
            vxb = s.sBR[1].n
            for oc in range(8):
                qp_ = oc % 2
                w0 = [(s.sBR[1], vxb), (s.sWA, s.v_wqo), (s.sDVE, q_free[qp_]),
                      (s.sPE, qfm_free), (s.sDVE, dgq_ms)] + (bar if gi == 0 and oc == 0 else [])
                for kc in range(8):
                    s.op("tensor", lambda e, oc=oc, kc=kc, qp_=qp_, par=par, TW=TW: e.matmul(
                        out=pQ[qp_][:, 0:TW],
                        lhsT=s.WQB[:, kc * 1024 + oc * 128: kc * 1024 + (oc + 1) * 128],
                        rhs=XEGB[par][:, kc * TW:(kc + 1) * TW],
                        start=(kc == 0), stop=(kc == 7)),
                        waits=w0 if kc == 0 else (), inc=s.sPE if kc == 7 else None)
                vq = s.sPE.n
                s.op("vector", lambda e, oc=oc, qp_=qp_, TW=TW: e.tensor_copy(
                    out=QFM[:, oc * TW:(oc + 1) * TW], in_=pQ[qp_][:, 0:TW]),
                    waits=[(s.sPE, vq), (s.sDVE, s.sDVE.n)], inc=s.sDVE)
                q_free[qp_] = s.sDVE.n
                vqe = s.sDVE.n
                s.op("vector", lambda e, oc=oc, qp_=qp_, TW=TW: e.tensor_tensor(
                    out=SQ2[qp_][:], in0=QFM[:, oc * TW:(oc + 1) * TW],
                    in1=QFM[:, oc * TW:(oc + 1) * TW], op=ALU.mult),
                    waits=[(s.sDVE, vqe), (s.sPE, sq_free[qp_])], inc=s.sDVE)
                vsq = s.sDVE.n
                for s2 in range(c.BGT):
                    s.op("tensor", lambda e, oc=oc, s2=s2, qp_=qp_: e.matmul(
                        out=pDGQ[:, s2 * 16:(s2 + 1) * 16],
                        lhsT=SQ2[qp_][:, s2 * 128:(s2 + 1) * 128],
                        rhs=s.OD16P[:, oc * 16:(oc + 1) * 16],
                        start=(oc == 0 and s2 == 0), stop=(oc == 7 and s2 == c.BGT - 1)),
                        waits=[(s.sDVE, vsq)] if s2 == 0 else (),
                        inc=s.sPE if s2 == c.BGT - 1 else None)
                sq_free[qp_] = s.sPE.n
            vqf = s.sDVE.n
            vdq = s.sPE.n
            s.op("vector", lambda e: e.tensor_copy(
                out=DIAGQ[:, 0:16 * c.BGT].rearrange("p (h q) -> p h q", h=16, q=c.BGT),
                in_=pDGQ[:, 0:16 * c.BGT].rearrange("p (q h) -> p h q", q=c.BGT, h=16)),
                waits=[(s.sPE, vdq), (s.sDVE, s.sDVE.n)], inc=s.sDVE)
            vdqr = s.sDVE.n
            for hp in range(8):
                for s2 in range(c.BGT):
                    dpar = (hp * c.BGT + s2) % 2
                    s.op("tensor", lambda e, hp=hp, s2=s2, dpar=dpar, TW=TW: e.matmul(
                        out=pD2[dpar][:], lhsT=QFM[:, hp * TW + s2 * 128: hp * TW + (s2 + 1) * 128],
                        rhs=s.PBD[:], start=True, stop=True),
                        waits=[(s.sDVE, vqf), (s.sDVE, d2_free_dve[dpar]),
                               (s.sACT, d2_free_act[dpar]), (s.sDVE, vdqr)],
                        inc=s.sPE)
                    vdd = s.sPE.n
                    for hh in range(2):
                        col = (2 * hp + hh) * c.BGT + s2
                        s.op("vector", lambda e, col=col, hh=hh, dpar=dpar: e.tensor_reduce(
                            out=MXQG[:, col:col + 1], in_=pD2[dpar][:, hh * 256:(hh + 1) * 256],
                            axis=AX.X, op=ALU.max),
                            waits=[(s.sPE, vdd)], inc=s.sDVE)
                    d2_free_dve[dpar] = s.sDVE.n
            vmxq = s.sDVE.n
            s.op("vector", lambda e: e.tensor_tensor(out=SUMW[:], in0=DIAGQ[:], in1=MXQG[:],
                                                     op=ALU.add),
                 waits=[(s.sDVE, vmxq)], inc=s.sDVE)
            vsw = s.sDVE.n
            s.op("scalar", lambda e: e.activation(out=WTM[:], in_=SUMW[:], func=AF.Exp,
                                                  bias=c.LNEPS, scale=1.0),
                 waits=[(s.sDVE, vsw), (s.sACT, bact)], inc=s.sACT)
            vwtm = s.sACT.n
            bact = vwtm
            s.op("tensor", lambda e: e.transpose(out=pWT, in_=WTM[:], identity=s.IDEB[:]),
                 waits=[(s.sACT, vwtm)], inc=s.sPE)
            vwt = s.sPE.n
            s.copy("vector", WTS[:], pWT, waits=[(s.sPE, vwt)])
            vwts = s.op("vector", lambda e: e.nop(), inc=s.sDVE)
            s.op("sync", lambda e, gi=gi: e.dma_start(out=wr[gi], in_=WTS[:]),
                 waits=[(s.sDVE, vwts), (s.sWR[par], s.sWR[par].n)], inc=s.sWR[par], incn=16)
            vw1 = s.sWR[par].n
            s.op("sync", lambda e, gi=gi, par=par: e.dma_start(
                out=WROWS[par][:], in_=wr[gi].rearrange("r f -> (r f)").rearrange("(o q) -> o q", o=1)),
                waits=[(s.sWR[par], vw1)], inc=s.sWR[par], incn=16)
            vw2 = s.sWR[par].n
            for h in range(H):
                hh, hp = h % 2, h // 2
                opar = h % 2
                for mc in range(2):
                    dpar = mc
                    s.op("tensor", lambda e, hh=hh, hp=hp, mc=mc, dpar=dpar, TW=TW: e.matmul(
                        out=pD2[dpar][:, 0:TW],
                        lhsT=s.PBDT[:, (hh * 2 + mc) * 128:(hh * 2 + mc + 1) * 128],
                        rhs=QFM[:, hp * TW:(hp + 1) * TW], start=True, stop=True),
                        waits=[(s.sDVE, d2_free_dve[dpar]), (s.sACT, d2_free_act[dpar]),
                               (s.sPE, e0_free)],
                        inc=s.sPE)
                    vdd = s.sPE.n
                    s.op("scalar", lambda e, mc=mc, dpar=dpar, TW=TW: e.activation(
                        out=E0B[:, mc * TW:(mc + 1) * TW], in_=pD2[dpar][:, 0:TW],
                        func=AF.Exp, bias=0.0, scale=1.0),
                        waits=[(s.sPE, vdd), (s.sACT, bact)], inc=s.sACT)
                    d2_free_act[dpar] = s.sACT.n
                    bact = s.sACT.n
                ve0 = s.sACT.n
                s.op("tensor", lambda e, b=b, h=h, opar=opar, TW=TW: e.matmul(
                    out=pO[opar][:, 0:TW], lhsT=s.ctxa_view(b, h, 0)[:], rhs=E0B[:, 0:TW],
                    start=True, stop=False),
                    waits=[(s.sACT, ve0), (s.sDVE, o_free[opar]), (s.sDVE, s.v_ctx)])
                s.op("tensor", lambda e, b=b, h=h, opar=opar, TW=TW: e.matmul(
                    out=pO[opar][:, 0:TW], lhsT=s.ctxa_view(b, h, 1)[:],
                    rhs=E0B[:, TW:2 * TW], start=False, stop=False))
                s.op("tensor", lambda e, b=b, h=h, opar=opar, par=par, TW=TW: e.matmul(
                    out=pO[opar][:, 0:TW], lhsT=s.b0t_view(b, h)[:],
                    rhs=WROWS[par][:, h * TW:(h + 1) * TW],
                    start=False, stop=True),
                    waits=[(s.sWR[par], vw2)], inc=s.sPE)
                e0_free = s.sPE.n
                vO = s.sPE.n
                s.op("vector", lambda e, opar=opar, TW=TW: e.tensor_copy(
                    out=RCPB[:, 0:TW], in_=pO[opar][64:65, 0:TW]),
                    waits=[(s.sPE, vO), (s.sDVE, s.sDVE.n)], inc=s.sDVE)
                vrcb = s.sDVE.n
                s.op("tensor", lambda e, TW=TW: e.matmul(
                    out=pRB[:, 0:TW], lhsT=s.O1x64b[:], rhs=RCPB[:, 0:TW],
                    start=True, stop=True),
                    waits=[(s.sDVE, vrcb), (s.sDVE, rbs_free)], inc=s.sPE)
                vrb = s.sPE.n
                s.op("scalar", lambda e, TW=TW: e.activation(
                    out=RBS[0:64, 0:TW], in_=pRB[0:64, 0:TW], func=AF.Ln,
                    bias=0.0, scale=1.0),
                    waits=[(s.sPE, vrb), (s.sDVE, rbs_free), (s.sACT, s.sACT.n)],
                    inc=s.sACT)
                vln = s.sACT.n
                s.op("scalar", lambda e, TW=TW: e.activation(
                    out=RBS[0:64, 0:TW], in_=RBS[0:64, 0:TW], func=AF.Exp,
                    bias=0.0, scale=-1.0),
                    waits=[(s.sACT, vln)], inc=s.sACT)
                vrbs = s.sACT.n
                bact = vrbs
                s.op("vector", lambda e, hh=hh, hp=hp, opar=opar, TW=TW: e.scalar_tensor_tensor(
                    out=ATTN[hh * 64:hh * 64 + 64, hp * TW:(hp + 1) * TW],
                    in0=pO[opar][0:64, 0:TW], scalar=1.0, in1=RBS[0:64, 0:TW],
                    op0=ALU.mult, op1=ALU.mult),
                    waits=[(s.sACT, vrbs), (s.sPE, attn_free), (s.sDVE, s.sDVE.n)],
                    inc=s.sDVE)
                o_free[opar] = s.sDVE.n
                rbs_free = s.sDVE.n
            vattn = s.sDVE.n
            for ti, t in enumerate(tiles):
                tp = ti % 2
                s.op("sync", lambda e, t=t, tp=tp: e.dma_start(
                    out=XYET[tp][:], in_=s.xye_scr[t * 128:(t + 1) * 128, :]),
                    waits=[(s.sYA, s.v_ya), (s.sYB, s.v_yb), (s.sYW, s.v_yw),
                           (s.sYWb, s.v_ywb),
                           (s.sDVE, xyet_ms[tp]), (s.sBR[1], s.sBR[1].n)],
                    inc=s.sBR[1], incn=16)
                vxy = s.sBR[1].n
                for dh in range(2):
                    wpar = dh
                    for kc in range(8):
                        s.op("tensor", lambda e, ti=ti, dh=dh, kc=kc, wpar=wpar, TW=TW: e.matmul(
                            out=pQ[wpar][:],
                            lhsT=ATTN[:, kc * TW + ti * 128: kc * TW + (ti + 1) * 128],
                            rhs=s.WOB[:, kc * 1024 + dh * 512: kc * 1024 + (dh + 1) * 512],
                            start=(kc == 0), stop=False),
                            waits=[(s.sDVE, vattn), (s.sDVE, q_free[wpar]),
                                   (s.sDVE, out_free[wpar])] if kc == 0 else ())
                    s.op("tensor", lambda e, dh=dh, wpar=wpar: e.matmul(
                        out=pQ[wpar][:], lhsT=s.OB1x128[:], rhs=s.BO2[:, dh * 512:(dh + 1) * 512],
                        start=False, stop=True), waits=[(s.sDVE, s.v_bo2)], inc=s.sPE)
                    vwo = s.sPE.n
                    s.op("vector", lambda e, dh=dh, tp=tp, wpar=wpar: e.tensor_tensor(
                        out=OUTT[tp][:, dh * 512:(dh + 1) * 512], in0=pQ[wpar][:],
                        in1=XYET[tp][:, dh * 512:(dh + 1) * 512], op=ALU.add),
                        waits=[(s.sPE, vwo), (s.sBR[1], vxy), (s.sDVE, s.sDVE.n),
                               (s.sOT[tp], outt_dma[tp])],
                        inc=s.sDVE)
                    q_free[wpar] = s.sDVE.n
                    out_free[wpar] = s.sDVE.n
                vout = s.sDVE.n
                xyet_ms[tp] = vout
                s.op("sync", lambda e, t=t, tp=tp: e.dma_start(
                    out=s.o_rows[t * 128:(t + 1) * 128, :], in_=OUTT[tp][:]),
                    waits=[(s.sDVE, vout)], inc=s.sOT[tp], incn=16)
                outt_dma[tp] = s.sOT[tp].n
            attn_free = s.sPE.n
            qfm_free = s.sPE.n
            xegb_ms[par] = s.sPE.n
            dgq_ms = s.sDVE.n
        s.op("sync", lambda e: e.nop(),
             waits=[(s.sOUT, s.sOUT.n), (s.sOT[0], s.sOT[0].n),
                    (s.sOT[1], s.sOT[1].n)])


def _consts(cfg):
    c = cfg
    dn2 = 0.5 * c.dn * c.dn
    e16n = np.zeros((128, 128), np.float32)
    e16p = np.zeros((128, 128), np.float32)
    for ch in range(8):
        for r in range(128):
            h = 2 * ch + (1 if r >= 64 else 0)
            e16n[r, ch * 16 + h] = -dn2
            e16p[r, ch * 16 + h] = dn2
    oc8 = np.zeros((128, 64), np.float32)
    for hp in range(8):
        oc8[:, hp * 8 + hp] = 1.0
    return {
        "cident": np.eye(128, dtype=np.float32),
        "cidentb": np.eye(128, dtype=np.float32).astype(ml_dtypes.bfloat16),
        "custrict": np.triu(np.ones((128, 128), np.float32), 1),
        "conesd16n": e16n.astype(ml_dtypes.bfloat16),
        "conesd16p": e16p.astype(ml_dtypes.bfloat16),
        "conescol8": oc8.astype(ml_dtypes.bfloat16),
        "cones1x128f": np.ones((1, 128), np.float32),
        "cones1x128b": np.ones((1, 128), ml_dtypes.bfloat16),
        "cones1x64b": np.ones((1, 64), ml_dtypes.bfloat16),
        "cones128x1f": np.ones((128, 1), np.float32),
        "cones128x1b": np.ones((128, 1), ml_dtypes.bfloat16),
        "ciota8": np.tile(np.arange(8, dtype=np.float32)[None, :], (128, 1)),
        "ciotap": np.arange(128, dtype=np.float32)[:, None].copy(),
        "cconstf": np.tile(np.array([0.0, 1e-5, math.log(cfg.EPS),
                                     math.log(cfg.ratio)], np.float32)[None, :], (128, 1)),
    }


_BUILD_CACHE = {}


def _get_nc(T):
    if T not in _BUILD_CACHE:
        _BUILD_CACHE[T] = Builder(Cfg(T)).build()
    return _BUILD_CACHE[T]


def make_in_maps(c, xf, Wq, Wk, Wv, Wo, bo, P, Wr, W1, W2):
    consts = _consts(c)
    in_maps = []
    for core in range(8):
        meta = np.zeros((128, 8), np.float32)
        meta[:, 0] = core
        meta[:, 1] = core * c.TPC
        meta[:, 3] = core - 1 if core % 2 == 1 else 999.0
        meta[:, 4] = (core // 2) * c.CELL
        meta[:, 5] = core * 128
        m = {
            "xmy": np.ascontiguousarray(xf[core * c.TPC:(core + 1) * c.TPC]),
            "xfull": xf,
            "wq": Wq, "wk": Wk, "wv": Wv, "wo": Wo,
            "boi": bo.reshape(1, -1),
            "ppi": P, "wri": Wr,
            "w1i": np.ascontiguousarray(W1[core]),
            "w2i": np.ascontiguousarray(W2[core]),
            "meta": meta,
        }
        m.update(consts)
        in_maps.append(m)
    return in_maps


def kernel(L0_attn_x, L0_attn_Wq, L0_attn_Wk, L0_attn_Wv, L0_attn_Wo, L0_attn_bo,
           L0_attn_P, L0_ffn_Wr, L0_ffn_W1, L0_ffn_W2):
    x = np.ascontiguousarray(np.asarray(L0_attn_x, np.float32))
    NBv, Nv, Dv = x.shape
    T = NBv * Nv
    c = Cfg(T)
    nc = _get_nc(T)
    xf = np.ascontiguousarray(x.reshape(T, Dv))
    in_maps = make_in_maps(
        c, xf,
        np.ascontiguousarray(np.asarray(L0_attn_Wq, np.float32)),
        np.ascontiguousarray(np.asarray(L0_attn_Wk, np.float32)),
        np.ascontiguousarray(np.asarray(L0_attn_Wv, np.float32)),
        np.ascontiguousarray(np.asarray(L0_attn_Wo, np.float32)),
        np.asarray(L0_attn_bo, np.float32),
        np.ascontiguousarray(np.asarray(L0_attn_P, np.float32)),
        np.ascontiguousarray(np.asarray(L0_ffn_Wr, np.float32)),
        np.asarray(L0_ffn_W1, np.float32),
        np.asarray(L0_ffn_W2, np.float32))
    res = run_bass_kernel_spmd(nc, in_maps, core_ids=list(range(8)))
    y = np.zeros((T, Dv), np.float32)
    for core in range(8):
        idx = np.asarray(res.results[core]["o_idx"]).T.reshape(-1)
        rows = np.asarray(res.results[core]["o_rows"])
        v = idx >= 0
        y[idx[v]] = rows[v]
    return y.reshape(NBv, Nv, Dv)


# version 07
# speedup vs baseline: 1.0422x; 1.0107x over previous
import math
import numpy as np
import ml_dtypes
import concourse.bass as bass
import concourse.mybir as mybir
from concourse.bass_utils import run_bass_kernel_spmd
from contextlib import ExitStack

F32 = mybir.dt.float32
BF16 = mybir.dt.bfloat16
U32 = mybir.dt.uint32
I32 = mybir.dt.int32
AF = mybir.ActivationFunctionType
ALU = mybir.AluOpType
AX = mybir.AxisListType

ENGS = ("sync", "vector", "scalar", "tensor", "gpsimd")


class Cfg:
    def __init__(self, T):
        self.D, self.H, self.DH, self.M, self.E, self.F, self.NB = 1024, 16, 64, 256, 8, 4096, 4
        self.CORES = 8
        self.T = T
        self.TPC = T // self.CORES
        self.N = T // self.NB
        self.CAP = T // self.E
        self.CELL = self.CAP // self.NB + 128
        self.SLOTS = self.CELL * self.NB
        self.TILES = self.SLOTS // 128
        self.CELLT = self.CELL // 128
        self.NT = self.TPC // 128
        self.GA = self.TPC // 512
        assert self.TPC % 512 == 0 and self.CELL % 128 == 0
        self.BGT = 3 if self.CELLT % 3 == 0 else (2 if self.CELLT % 2 == 0 else 1)
        self.BG = self.TILES // self.BGT
        self.EGT = 4 if self.TILES % 4 == 0 else self.BGT
        self.EG = self.TILES // self.EGT
        self.FH = self.F // 2
        self.EPS = 1e-4
        self.dn = float(self.DH) ** (-0.25)
        self.ratio = float(self.M) ** (-0.5)
        self.LNEPS = math.log(self.EPS)
        self.BIG = 1 << 28


class Sem:
    __slots__ = ("h", "name", "n")

    def __init__(self, h, name):
        self.h, self.name, self.n = h, name, 0


class Builder:
    def __init__(self, cfg):
        self.c = cfg
        self.nc = bass.Bass("TRN2", target_bir_lowering=False, debug=False)
        self.sched = {e: [] for e in ENGS}
        self.hw = {}
        self.scopes = []
        self.waited = set()
        self.sem_registry = []
        self.gp_regs = {}
        import os as _os
        self.opcut = int(_os.environ.get("MK_OPCUT", "1000000000"))
        self.opcount = 0
        self.cur_phase = "init"

    def sem(self, name):
        sm = Sem(self.st.enter_context(self.nc.semaphore(name)), name)
        self.sem_registry.append(sm)
        return sm

    def push_scope(self):
        self.scopes.append(ExitStack())
        self.scopes[-1].__enter__()

    def pop_scope(self):
        self.scopes.pop().__exit__(None, None, None)

    def fence(self):
        sems = {"tensor": self.sPE, "vector": self.sDVE, "scalar": self.sACT,
                "gpsimd": self.sGP}
        engset = set(sems.values())
        extra = [(sm, sm.n) for sm in self.sem_registry
                 if sm not in engset and sm.n > 0]
        tot = {}
        for eng, sm in sems.items():
            tot[eng] = self.op(eng, lambda e: e.nop(), inc=sm)
        for eng in ("tensor", "vector", "scalar", "gpsimd", "sync"):
            ws = [(sems[nm], tot[nm]) for nm in sems if nm != eng] + extra
            if eng == "sync":
                ws = [(sems[nm], tot[nm]) for nm in sems] + extra
                self.op("sync", lambda e: e.nop(), waits=ws)
            else:
                self.op(eng, lambda e: e.nop(), waits=ws, inc=sems[eng])

    def op(self, eng, fn, waits=(), inc=None, incn=1):
        ws = []
        for s, v in waits:
            if v <= 0:
                continue
            self.waited.add((s.name, v))
            key = (eng, s.name)
            if self.hw.get(key, 0) >= v:
                continue
            self.hw[key] = v
            ws.append((s, v))
        out = None
        if inc is not None:
            inc.n += incn
            out = inc.n
        self.opcount += 1
        if self.opcount <= self.opcut:
            self.sched[eng].append((ws, fn, inc, incn, out, self.cur_phase))
        return out

    def breg(self, e, val):
        r = self.gp_regs.get(val)
        if r is None:
            r = e.to_reg(val)
            self.gp_regs[val] = r
        return r

    def copy(self, eng, dst, src, waits=()):
        if eng == "scalar":
            return self.op(eng, lambda e: e.activation(out=dst, in_=src, func=AF.Copy,
                                                       bias=0.0, scale=1.0),
                           waits=waits, inc=self.sACT)
        return self.op(eng, lambda e: e.tensor_copy(out=dst, in_=src), waits=waits,
                       inc=self.sDVE)

    def sb(self, name, shape, dt=F32):
        st = self.scopes[-1] if self.scopes else self.st
        return st.enter_context(self.nc.sbuf_tensor(name, shape, dt))

    def sb_outer(self, name, shape, dt=F32):
        return self.st.enter_context(self.nc.sbuf_tensor(name, shape, dt))

    def ps(self, name, shape, dt=F32):
        return self.st.enter_context(self.nc.psum_tensor(name, shape, dt))

    def din(self, n, s, d=F32):
        return self.nc.dram_tensor(n, s, d, kind="ExternalInput").ap()

    def dout(self, n, s, d=F32):
        return self.nc.dram_tensor(n, s, d, kind="ExternalOutput").ap()

    def dint(self, n, s, d=F32):
        return self.nc.dram_tensor(n, s, d).ap()

    def build(self):
        c = self.c
        D, H, DH, M, E = c.D, c.H, c.DH, c.M, c.E

        self.xmy = self.din("xmy", [c.TPC, D])
        self.xfull = self.din("xfull", [c.T, D])
        self.wq = self.din("wq", [D, D]); self.wk = self.din("wk", [D, D])
        self.wv = self.din("wv", [D, D]); self.wo = self.din("wo", [D, D])
        self.boi = self.din("boi", [1, D])
        self.ppi = self.din("ppi", [M, DH])
        self.wri = self.din("wri", [D, E])
        self.w1i = self.din("w1i", [D, c.F])
        self.w2i = self.din("w2i", [c.F, D])
        self.meta = self.din("meta", [128, 8])
        self.cid = self.din("cident", [128, 128])
        self.cidb = self.din("cidentb", [128, 128], BF16)
        self.cust = self.din("custrict", [128, 128])
        self.cd16n = self.din("conesd16n", [128, 128], BF16)
        self.cd16p = self.din("conesd16p", [128, 128], BF16)
        self.coc8 = self.din("conescol8", [128, 64], BF16)
        self.c1x128f = self.din("cones1x128f", [1, 128])
        self.c1x128b = self.din("cones1x128b", [1, 128], BF16)
        self.c1x64b = self.din("cones1x64b", [1, 64], BF16)
        self.c128x1f = self.din("cones128x1f", [128, 1])
        self.c128x1b = self.din("cones128x1b", [128, 1], BF16)
        self.ciota8 = self.din("ciota8", [128, 8])
        self.ciotap = self.din("ciotap", [128, 1])
        self.cconstf = self.din("cconstf", [128, 4])

        self.o_rows = self.dout("o_rows", [c.SLOTS, D])
        self.o_idx = self.dout("o_idx", [128, c.TILES], I32)

        self.rec_in = self.dint("rec_in", [E * c.SLOTS, 2])
        self.rec_out = self.dint("rec_out", [8 * E * c.SLOTS, 2])
        self.cc1_in = self.dint("cc1_in", [1, 8])
        self.cc1_out = self.dint("cc1_out", [8, 8])
        self.PKW = 2048 + 32 + 16 + 1 + 8
        self.pk_in = self.dint("pk_in", [128, self.PKW])
        self.pk_out = self.dint("pk_out", [8 * 128, self.PKW])
        self.xye_scr = self.dint("xye_scr", [c.SLOTS, D])
        self.xefm_scr = self.dint("xefm_scr", [c.EG * 128, 8 * c.EGT * 128], BF16)
        self.wrow_dram = self.dint("wrow_dram", [c.BG * H * c.BGT, 128], BF16)
        self.srow_dram = self.dint("srow_dram", [1, 1024])

        with ExitStack() as st:
            self.st = st
            s = self
            s.sXL = [s.sem("sXL0"), s.sem("sXL1")]
            s.sWA = s.sem("sWA"); s.sSC = s.sem("sSC")
            s.sWE = [s.sem("sWE0"), s.sem("sWE1")]
            s.sC1 = s.sem("sC1"); s.sC2a = s.sem("sC2a"); s.sC2b = s.sem("sC2b")
            s.sCD = s.sem("sCD"); s.sMG = s.sem("sMG")
            s.sEG = [s.sem("sEG0"), s.sem("sEG1")]
            s.sXF = s.sem("sXF"); s.sYW = s.sem("sYW"); s.sYA = s.sem("sYA")
            s.sBR = [s.sem("sBR0"), s.sem("sBR1")]
            s.sWR = [s.sem("sWR0"), s.sem("sWR1")]
            s.sOUT = s.sem("sOUT"); s.sSR = s.sem("sSR")
            s.sOT = [s.sem("sOT0"), s.sem("sOT1")]
            s.sXE = [s.sem("sXE0"), s.sem("sXE1")]
            s.sYB = s.sem("sYB")
            s.sYWb = s.sem("sYWb")
            s.sXF2 = [s.sem("sXF0"), s.sem("sXF1")]
            s.sPE = s.sem("sPE"); s.sDVE = s.sem("sDVE")
            s.sACT = s.sem("sACT"); s.sGP = s.sem("sGP")
            s.sONE = s.sem("sONE")
            s.op("vector", lambda e: e.nop(), inc=s.sONE)

            s.PF = [s.ps(f"PF{i}", [128, 512]) for i in range(6)]
            s.PB = [s.ps(f"PBb{i}", [128, 256], BF16) for i in range(2)]

            FQ = c.F // 4
            s.FQ = FQ
            s.W1Q = [s.sb("W1Q0", [128, 8 * FQ], BF16)]
            s.W2Q = [s.sb("W2Q0", [128, (FQ // 128) * 1024], BF16)]
            s.v_wq = [0, 0, 0, 0]

            s.RSCL = s.sb_outer("RSCL", [128, c.TILES])
            s.GIDX = s.sb_outer("GIDX", [128, c.TILES], U32)
            steps = [
                self.phase0_consts,
                lambda: self.load_w_quarter(0),
                self.push_scope,
                self.load_wkv,
                self.phase_a1,
                self.phase_coll1,
                self.phase_scatter,
                self.phase_a2,
                self.phase_coll2,
                self.phase_merge,
                self.pop_scope,
                self.fence,
                self._alloc_post_a,
                self.load_wq_wo,
                lambda: self.load_w_quarter(1),
                self.push_scope,
                self.phase_expert,
                self.pop_scope,
                self.fence,
                self.push_scope,
                self.phase_ctx,
                self.pop_scope,
                self.fence,
                self.push_scope,
                self.phase_b,
                self.pop_scope,
            ]
            import os as _os
            nsteps = int(_os.environ.get("MK_STEPS", "99"))
            for st_ in steps[:nsteps]:
                self.cur_phase = getattr(st_, "__name__", "misc")
                st_()
            self.cur_phase = "tail"
            while self.scopes:
                self.pop_scope()

            with self.nc.Block() as block:
                def mk(name):
                    def body(e):
                        prev = None
                        cur_scope = None
                        scope_cm = None
                        for ws, fn, inc, incn, val, ph in self.sched[name]:
                            if ph != cur_scope:
                                if scope_cm is not None:
                                    scope_cm.__exit__(None, None, None)
                                scope_cm = self.nc.named_scope(f"{ph}")
                                scope_cm.__enter__()
                                cur_scope = ph
                            if prev is not None and not ws and name != "sync":
                                e.wait_ge(prev[0].h, prev[1])
                            for sm, v in ws:
                                e.wait_ge(sm.h, v)
                            ins = fn(e)
                            if inc is not None:
                                assert ins is not None, f"then_inc on None ({name})"
                                ins.then_inc(inc.h, incn)
                            prev = ((inc, val)
                                    if inc is not None and
                                    (inc.name, val) in self.waited else None)
                        if scope_cm is not None:
                            scope_cm.__exit__(None, None, None)
                    return body
                block.sync(mk("sync"))
                block.vector(mk("vector"))
                block.scalar(mk("scalar"))
                block.tensor(mk("tensor"))
                block.gpsimd(mk("gpsimd"))
        return self.nc

    def _alloc_post_a(self):
        s, c = self, self.c
        FQ = s.FQ
        s.WQB = s.sb("WQB", [128, 8192], BF16)
        s.WOB = s.sb("WOB", [128, 8192], BF16)
        s.CTXA = s.sb("CTXA", [128, c.NB * c.H * 2 * 65], BF16)
        s.B0T = s.sb("B0T", [1, c.NB * c.H * 65], BF16)
        s.W1Q.append(s.sb("W1Q1", [128, 8 * FQ], BF16))
        s.W2Q.append(s.sb("W2Q1", [128, (FQ // 128) * 1024], BF16))

    def phase0_consts(self):
        s, c = self, self.c
        s.IDE = s.sb("IDE", [128, 128])
        s.IDEB = s.sb("IDEB", [128, 128], BF16)
        s.USTR = s.sb("USTR", [128, 128])
        s.OD16N = s.sb("OD16N", [128, 128], BF16)
        s.OD16P = s.sb("OD16P", [128, 128], BF16)
        s.OC8 = s.sb("OC8", [128, 64], BF16)
        s.O1x128f = s.sb("O1x128f", [1, 128])
        s.OB1x128 = s.sb("OB1x128", [1, 128], BF16)
        s.O1x64b = s.sb("O1x64b", [1, 64], BF16)
        s.O128x1f = s.sb("O128x1f", [128, 1])
        s.O128x1b = s.sb("O128x1b", [128, 1], BF16)
        s.IOT8 = s.sb("IOT8", [128, 8])
        s.IOTP = s.sb("IOTP", [128, 1])
        s.META = s.sb("META", [128, 8])
        s.CONSTF = s.sb("CONSTF", [128, 4])
        for dst, src in [(s.IDE, s.cid), (s.IDEB, s.cidb), (s.USTR, s.cust),
                         (s.OD16N, s.cd16n), (s.OD16P, s.cd16p), (s.OC8, s.coc8),
                         (s.O1x128f, s.c1x128f), (s.OB1x128, s.c1x128b),
                         (s.O1x64b, s.c1x64b), (s.O128x1f, s.c128x1f),
                         (s.O128x1b, s.c128x1b), (s.IOT8, s.ciota8),
                         (s.IOTP, s.ciotap), (s.META, s.meta),
                         (s.CONSTF, s.cconstf)]:
            s.op("sync", lambda e, d=dst, sr=src: e.dma_start(out=d[:], in_=sr[:]),
                 inc=s.sXL[0], incn=16)
        s.v_consts = s.sXL[0].n
        s.nc.const_aps.register(F32, 0.0, s.CONSTF[:, 0:1])
        s.nc.const_aps.register(F32, 1e-5, s.CONSTF[:, 1:2])
        s.nc.const_aps.register(F32, c.LNEPS, s.CONSTF[:, 2:3])
        s.nc.const_aps.register(F32, math.log(c.ratio), s.CONSTF[:, 3:4])

        s.PTB = s.sb("PTB", [64, 256], BF16)
        s.PBD = s.sb("PBD", [128, 512], BF16)
        s.PBDT = s.sb("PBDT", [128, 512], BF16)
        s.BO2 = s.sb("BO2", [1, 1024], BF16)
        self.push_scope()
        PFt = s.sb("PFt", [128, 128])
        BOF = s.sb("BOF", [1, 1024])
        s.op("sync", lambda e: e.dma_start(out=PFt[:, 0:64], in_=s.ppi[0:128, :]),
             inc=s.sXL[0], incn=16)
        s.op("sync", lambda e: e.dma_start(out=PFt[:, 64:128], in_=s.ppi[128:256, :]),
             inc=s.sXL[0], incn=16)
        s.op("sync", lambda e: e.dma_start(out=BOF[:], in_=s.boi[:]), inc=s.sXL[0], incn=16)
        v_ph0 = s.sXL[0].n
        s.v_consts = v_ph0
        pP = s.PF[0][0:64, 0:256]
        s.op("tensor", lambda e: e.transpose(out=pP[:, 0:128], in_=PFt[:, 0:64],
                                             identity=s.IDE[:]),
             waits=[(s.sXL[0], v_ph0)])
        s.op("tensor", lambda e: e.transpose(out=pP[:, 128:256], in_=PFt[:, 64:128],
                                             identity=s.IDE[:]), inc=s.sPE)
        v_ptr = s.sPE.n
        s.op("scalar", lambda e: e.activation(out=s.PTB[:], in_=pP, func=AF.Copy,
                                              scale=c.dn, bias=0.0),
             waits=[(s.sPE, v_ptr)], inc=s.sACT)
        v_ptb = s.sACT.n
        s.op("vector", lambda e: e.memset(s.PBD[:], 0.0), inc=s.sDVE)
        s.op("vector", lambda e: e.memset(s.PBDT[:], 0.0), inc=s.sDVE)
        vz = s.sDVE.n
        s.copy("vector", s.PBD[0:64, 0:256], s.PTB[:], waits=[(s.sACT, v_ptb), (s.sDVE, vz)])
        s.copy("vector", s.PBD[64:128, 256:512], s.PTB[:])
        for hh in range(2):
            for mc in range(2):
                j = hh * 2 + mc
                s.copy("vector", s.PBDT[hh * 64:hh * 64 + 64, j * 128:(j + 1) * 128],
                       s.PTB[:, mc * 128:(mc + 1) * 128])
        s.op("vector", lambda e: e.nop(), inc=s.sDVE)
        s.v_pbd = s.sDVE.n

        s.copy("vector", s.BO2[:], BOF[:], waits=[(s.sXL[0], v_ph0)])
        s.op("vector", lambda e: e.nop(), inc=s.sDVE)
        s.v_bo2 = s.sDVE.n

        nzc = 2 * c.E * c.SLOTS // 128
        ZREC = s.sb("ZREC", [128, nzc])
        s.op("vector", lambda e: e.memset(ZREC[:], 0.0), inc=s.sDVE)
        vzr = s.sDVE.n
        s.op("sync", lambda e: e.dma_start(
            out=s.rec_in.rearrange("(p a) w -> p (a w)", p=128), in_=ZREC[:]),
            waits=[(s.sDVE, vzr)], inc=s.sXL[0], incn=16)
        s.v_zrec = s.sXL[0].n
        s.ph0_gate = [(s.sDVE, s.sDVE.n), (s.sPE, s.sPE.n), (s.sACT, s.sACT.n),
                      (s.sXL[0], s.sXL[0].n)]
        self.pop_scope()

    def load_wkv(self):
        s = self
        s.WKB = s.sb("WKB", [128, 8192], BF16)
        s.WVB = s.sb("WVB", [128, 8192], BF16)
        s.WRB = s.sb("WRB", [128, 64], BF16)
        for kc in range(8):
            s.op("gpsimd", lambda e, kc=kc: e.dma_start(
                out=s.WKB[:, kc * 1024:(kc + 1) * 1024], in_=s.wk[kc * 128:(kc + 1) * 128, :]),
                waits=s.ph0_gate if kc == 0 else (), inc=s.sWA, incn=16)
            s.op("gpsimd", lambda e, kc=kc: e.dma_start(
                out=s.WVB[:, kc * 1024:(kc + 1) * 1024], in_=s.wv[kc * 128:(kc + 1) * 128, :]),
                inc=s.sWA, incn=16)
            s.op("gpsimd", lambda e, kc=kc: e.dma_start(
                out=s.WRB[:, kc * 8:(kc + 1) * 8], in_=s.wri[kc * 128:(kc + 1) * 128, :]),
                inc=s.sWA, incn=16)
        s.v_wa = s.sWA.n

    def load_w_quarter(self, q, wait_pe=0):
        s, c = self, self.c
        FQ = s.FQ
        buf = q % 2
        KQ = FQ // 128
        sm = s.sWE[q % 2]
        for kc in range(8):
            s.op("gpsimd", lambda e, q=q, kc=kc, buf=buf, FQ=FQ: e.dma_start(
                out=s.W1Q[buf][:, kc * FQ:(kc + 1) * FQ],
                in_=s.w1i[kc * 128:(kc + 1) * 128, q * FQ:(q + 1) * FQ]),
                waits=[(s.sPE, wait_pe)] if kc == 0 else (), inc=sm, incn=16)
        for kq in range(KQ):
            s.op("gpsimd", lambda e, q=q, kq=kq, buf=buf, FQ=FQ: e.dma_start(
                out=s.W2Q[buf][:, kq * 1024:(kq + 1) * 1024],
                in_=s.w2i[q * FQ + kq * 128: q * FQ + (kq + 1) * 128, :]),
                inc=sm, incn=16)
        s.v_wq[q] = sm.n

    def load_wq_wo(self):
        s = self
        for kc in range(8):
            s.op("gpsimd", lambda e, kc=kc: e.dma_start(
                out=s.WQB[:, kc * 1024:(kc + 1) * 1024], in_=s.wq[kc * 128:(kc + 1) * 128, :]),
                inc=s.sWA, incn=16)
            s.op("gpsimd", lambda e, kc=kc: e.dma_start(
                out=s.WOB[:, kc * 1024:(kc + 1) * 1024], in_=s.wo[kc * 128:(kc + 1) * 128, :]),
                inc=s.sWA, incn=16)
        s.v_wqo = s.sWA.n

    def phase_a1(self):
        s, c = self, self.c
        D = c.D
        s.XNFM = s.sb("XNFM", [128, c.TPC * 8], BF16)
        XL = [s.sb("XL0", [128, 1024]), s.sb("XL1", [128, 1024])]
        XNF = s.sb("XNF", [128, 1024])
        XNH = [s.sb("XNH0", [128, 1024], BF16), s.sb("XNH1", [128, 1024], BF16)]
        LNS2 = [s.sb("LNS0", [128, 8]), s.sb("LNS1", [128, 8])]
        TTRS = XNF
        pTR = [s.PB[0], s.PB[1]]
        pRT = s.PF[0][0:8, :]
        RTF = s.sb("RTF", [8, 512])
        pRTT = s.PF[1][:, 0:32]
        LGT = s.sb("LGT", [128, 32])
        TOP8 = s.sb("TOP8", [128, 8])
        TI8 = s.sb("TI8", [128, 8], U32)
        EXPS = s.sb("EXPS", [128, 8])
        pLR = s.PF[2][:, 0:8]
        pCN = s.PF[2][0:1, 8:16]
        TMP8 = s.sb("TMP8", [128, 8])
        s.SEID = s.sb("SEID", [128, c.NT])
        s.SP = s.sb("SP", [128, c.NT])
        s.SLR = s.sb("SLR", [128, c.NT])
        s.SOH = s.sb("SOH", [128, 8 * c.NT])
        s.BASEC = [s.sb("BASEC0", [1, 8]), s.sb("BASEC1", [1, 8])]

        s.op("vector", lambda e: e.memset(s.BASEC[0][:], 0.0), inc=s.sDVE)
        base_ms = s.sDVE.n
        act_chain = 0
        xnf_free = 0
        lns1_free = [0, 0]

        xl_free = [0, 0]
        xnh_free = [0, 0]
        tr_free = [(s.sDVE, 0), (s.sDVE, 0)]
        rt_free = 0
        s.fm_done = {}
        for ts in range(c.NT):
            par = ts % 2
            g, sub = ts // 4, ts % 4
            s.op("sync", lambda e, ts=ts, par=par: e.dma_start(
                out=XL[par][:], in_=s.xmy[ts * 128:(ts + 1) * 128, :]),
                waits=[(s.sDVE, xl_free[par])] + (s.ph0_gate if ts < 2 else []),
                inc=s.sXL[par], incn=16)
            vload = s.sXL[par].n
            LNS = LNS2[par]
            s.op("vector", lambda e, par=par, LNS=LNS: e.tensor_reduce(
                out=LNS[:, 0:1], in_=XL[par][:], axis=AX.X, op=ALU.add),
                waits=[(s.sXL[par], vload), (s.sDVE, s.sDVE.n)], inc=s.sDVE)
            vst = s.sDVE.n
            s.op("scalar", lambda e, par=par, LNS=LNS: e.activation(
                out=TTRS[:], in_=XL[par][:], func=AF.Square, bias=0.0, scale=1.0,
                accum_out=LNS[:, 1:2]),
                waits=[(s.sXL[par], vload), (s.sDVE, xnf_free),
                       (s.sDVE, lns1_free[par]), (s.sACT, act_chain)], inc=s.sACT)
            vsq = s.sACT.n
            s.op("vector", lambda e, LNS=LNS: e.tensor_scalar(
                out=LNS[:, 2:3], in0=LNS[:, 0:1], scalar1=1.0 / D, scalar2=None,
                op0=ALU.mult), waits=[(s.sDVE, vst)], inc=s.sDVE)
            vmean = s.sDVE.n
            s.op("vector", lambda e, LNS=LNS: e.tensor_tensor(
                out=LNS[:, 3:4], in0=LNS[:, 2:3], in1=LNS[:, 2:3], op=ALU.mult),
                waits=[(s.sDVE, vmean)], inc=s.sDVE)
            vm2 = s.sDVE.n
            s.op("vector", lambda e, LNS=LNS: e.scalar_tensor_tensor(
                out=LNS[:, 4:5], in0=LNS[:, 1:2], scalar=1.0 / D, in1=LNS[:, 3:4],
                op0=ALU.mult, op1=ALU.subtract),
                waits=[(s.sDVE, vm2), (s.sACT, vsq)], inc=s.sDVE)
            vvar = s.sDVE.n
            lns1_free[par] = vvar
            s.op("scalar", lambda e, LNS=LNS: e.activation(
                out=LNS[:, 5:6], in_=LNS[:, 4:5], func=AF.Ln, bias=1e-5, scale=1.0),
                waits=[(s.sDVE, vvar), (s.sACT, act_chain)], inc=s.sACT)
            va1 = s.sACT.n
            s.op("scalar", lambda e, LNS=LNS: e.activation(
                out=LNS[:, 6:7], in_=LNS[:, 5:6], func=AF.Exp, bias=0.0, scale=-0.5),
                waits=[(s.sACT, va1)], inc=s.sACT)
            vrstd = s.sACT.n
            act_chain = vrstd
            s.op("vector", lambda e, par=par, LNS=LNS: e.tensor_scalar(
                out=XNF[:], in0=XL[par][:], scalar1=LNS[:, 2:3], scalar2=LNS[:, 6:7],
                op0=ALU.subtract, op1=ALU.mult),
                waits=[(s.sACT, vrstd), (s.sDVE, s.sDVE.n)], inc=s.sDVE)
            vxn = s.sDVE.n
            xl_free[par] = vxn
            s.op("vector", lambda e, par=par: e.tensor_copy(out=XNH[par][:], in_=XNF[:]),
                 waits=[(s.sPE, xnh_free[par]), (s.sDVE, vxn)], inc=s.sDVE)
            vhi = s.sDVE.n
            xnf_free = vhi
            for cp in range(4):
                tp = cp % 2
                trs_, trv_ = tr_free[tp]
                s.op("tensor", lambda e, par=par, cp=cp, tp=tp: e.transpose(
                    out=pTR[tp][:, 0:128], in_=XNH[par][:, (2 * cp) * 128:(2 * cp + 1) * 128],
                    identity=s.IDEB[:]),
                    waits=[(s.sDVE, vhi), (trs_, trv_), (s.sXL[0], s.v_consts)])
                s.op("tensor", lambda e, par=par, cp=cp, tp=tp: e.transpose(
                    out=pTR[tp][:, 128:256],
                    in_=XNH[par][:, (2 * cp + 1) * 128:(2 * cp + 2) * 128],
                    identity=s.IDEB[:]), inc=s.sPE)
                vtp = s.sPE.n
                eng = "vector" if cp % 2 == 0 else "scalar"
                b0 = g * 4096 + (2 * cp) * 512 + sub * 128
                b1 = g * 4096 + (2 * cp + 1) * 512 + sub * 128
                s.copy(eng, s.XNFM[:, b0:b0 + 128], pTR[tp][:, 0:128], waits=[(s.sPE, vtp)])
                lastc = s.copy(eng, s.XNFM[:, b1:b1 + 128], pTR[tp][:, 128:256])
                tr_free[tp] = (s.sDVE, lastc) if eng == "vector" else (s.sACT, lastc)
            xnh_free[par] = s.sPE.n
            vfa, vfd = s.sACT.n, s.sDVE.n
            s.op("vector", lambda e: e.nop(), waits=[(s.sACT, vfa), (s.sDVE, vfd)],
                 inc=s.sDVE)
            s.fm_done[ts] = s.sDVE.n

            if sub == 3:
                for ch in range(8):
                    s.op("tensor", lambda e, g=g, ch=ch: e.matmul(
                        out=pRT, lhsT=s.WRB[:, ch * 8:(ch + 1) * 8],
                        rhs=s.XNFM[:, g * 4096 + ch * 512: g * 4096 + (ch + 1) * 512],
                        start=(ch == 0), stop=(ch == 7)),
                        waits=[(s.sDVE, s.fm_done[ts]), (s.sWA, s.v_wa),
                               (s.sDVE, rt_free)] if ch == 0 else (),
                        inc=s.sPE if ch == 7 else None)
                vrt = s.sPE.n
                s.copy("vector", RTF[:], pRT, waits=[(s.sPE, vrt)])
                s.op("vector", lambda e: e.nop(), inc=s.sDVE)
                vrtf = s.sDVE.n
                for s2 in range(4):
                    s.op("tensor", lambda e, s2=s2: e.transpose(
                        out=pRTT[:, s2 * 8:(s2 + 1) * 8], in_=RTF[:, s2 * 128:(s2 + 1) * 128],
                        identity=s.IDE[0:8, 0:8]),
                        waits=[(s.sDVE, vrtf)] if s2 == 0 else (),
                        inc=s.sPE if s2 == 3 else None)
                vrtt = s.sPE.n
                s.copy("vector", LGT[:], pRTT, waits=[(s.sPE, vrtt)])
                s.op("vector", lambda e: e.nop(), inc=s.sDVE)
                vlgt = s.sDVE.n
                rt_free = vlgt
                for s2 in range(4):
                    ts2 = g * 4 + s2
                    LNR = LNS2[s2 % 2]
                    L = LGT[:, s2 * 8:(s2 + 1) * 8]
                    s.op("vector", lambda e, L=L, LNR=LNR: e.tensor_reduce(
                        out=LNR[:, 6:7], in_=L, axis=AX.X, op=ALU.max),
                        waits=[(s.sDVE, vlgt)], inc=s.sDVE)
                    vmx = s.sDVE.n
                    s.op("vector", lambda e, LNR=LNR: e.tensor_scalar(
                        out=LNR[:, 7:8], in0=LNR[:, 6:7], scalar1=-1.0, scalar2=None,
                        op0=ALU.mult), waits=[(s.sDVE, vmx)], inc=s.sDVE)
                    vnm = s.sDVE.n
                    s.op("scalar", lambda e, L=L, LNR=LNR: e.activation(
                        out=EXPS[:], in_=L, func=AF.Exp, bias=LNR[:, 7:8], scale=1.0,
                        accum_out=LNR[:, 0:1]),
                        waits=[(s.sDVE, vnm), (s.sACT, act_chain)], inc=s.sACT)
                    vse = s.sACT.n
                    act_chain = vse
                    s.op("vector", lambda e, ts2=ts2, LNR=LNR: e.reciprocal(
                        out=s.SP[:, ts2:ts2 + 1], in_=LNR[:, 0:1]),
                        waits=[(s.sACT, vse)], inc=s.sDVE)
                    s.op("vector", lambda e, L=L: e.max(out=TOP8[:], in_=L), inc=s.sDVE)
                    vt8 = s.sDVE.n
                    s.op("vector", lambda e, L=L: e.max_index(
                        out=TI8[:], in_max=TOP8[:], in_values=L),
                        waits=[(s.sDVE, vt8)], inc=s.sDVE)
                    vti = s.sDVE.n
                    s.op("vector", lambda e, ts2=ts2: e.tensor_copy(
                        out=s.SEID[:, ts2:ts2 + 1], in_=TI8[:, 0:1]),
                        waits=[(s.sDVE, vti)], inc=s.sDVE)
                    veid = s.sDVE.n
                    s.op("vector", lambda e, ts2=ts2: e.tensor_scalar(
                        out=s.SOH[:, ts2 * 8:(ts2 + 1) * 8], in0=s.IOT8[:],
                        scalar1=s.SEID[:, ts2:ts2 + 1], scalar2=None, op0=ALU.is_equal),
                        waits=[(s.sDVE, veid)], inc=s.sDVE)
                    voh = s.sDVE.n
                    bsrc = s.BASEC[ts2 % 2]
                    bdst = s.BASEC[(ts2 + 1) % 2]
                    s.op("tensor", lambda e, ts2=ts2: e.matmul(
                        out=pLR, lhsT=s.USTR[:], rhs=s.SOH[:, ts2 * 8:(ts2 + 1) * 8],
                        start=True, stop=False),
                        waits=[(s.sDVE, voh), (s.sDVE, base_ms)])
                    s.op("tensor", lambda e, bsrc=bsrc: e.matmul(
                        out=pLR, lhsT=s.O1x128f[:], rhs=bsrc[:], start=False, stop=True))
                    s.op("tensor", lambda e, ts2=ts2: e.matmul(
                        out=pCN, lhsT=s.O128x1f[:], rhs=s.SOH[:, ts2 * 8:(ts2 + 1) * 8],
                        start=True, stop=True), inc=s.sPE)
                    vlr = s.sPE.n
                    s.op("vector", lambda e, ts2=ts2: e.tensor_tensor(
                        out=TMP8[:], in0=s.SOH[:, ts2 * 8:(ts2 + 1) * 8], in1=pLR,
                        op=ALU.mult), waits=[(s.sPE, vlr)], inc=s.sDVE)
                    vt8b = s.sDVE.n
                    s.op("vector", lambda e, ts2=ts2: e.tensor_reduce(
                        out=s.SLR[:, ts2:ts2 + 1], in_=TMP8[:], axis=AX.X, op=ALU.add),
                        waits=[(s.sDVE, vt8b)], inc=s.sDVE)
                    s.op("vector", lambda e, bsrc=bsrc, bdst=bdst: e.tensor_tensor(
                        out=bdst[:], in0=bsrc[:], in1=pCN, op=ALU.add), inc=s.sDVE)
                    base_ms = s.sDVE.n
        s.a1_dve = s.sDVE.n
        s.fincnt = s.BASEC[c.NT % 2]

    def phase_coll1(self):
        s, c = self, self.c
        s.CNT8 = s.sb("CNT8", [8, 8])
        s.op("gpsimd", lambda e: e.dma_start(out=s.cc1_in, in_=s.fincnt[:]),
             waits=[(s.sDVE, s.a1_dve)], inc=s.sCD, incn=16)
        v1 = s.sCD.n
        s.op("gpsimd", lambda e: e.collective_compute(
            "AllGather", ALU.bypass, replica_groups=[list(range(8))],
            ins=[s.cc1_in], outs=[s.cc1_out]), waits=[(s.sCD, v1)], inc=s.sC1)
        vc1 = s.sC1.n
        s.op("gpsimd", lambda e: e.dma_start(out=s.CNT8[:], in_=s.cc1_out),
             waits=[(s.sC1, vc1)], inc=s.sCD, incn=16)
        s.v_cnt8 = s.sCD.n

        WCOL = s.sb("WCOL", [8, 2])
        pOFF = s.PF[3][0:1, 0:16]
        s.OFFG = s.sb("OFFG", [1, 8]); s.OFFEB = s.sb("OFFEB", [1, 8])
        s.op("gpsimd", lambda e: e.tensor_tensor(
            out=WCOL[:, 0:1], in0=s.IOTP[0:8, :], in1=s.META[0:8, 0:1],
            op=ALU.subtract), waits=[(s.sCD, s.v_cnt8), (s.sXL[0], s.v_consts)],
            inc=s.sGP)
        vws = s.sGP.n
        s.op("gpsimd", lambda e: e.tensor_scalar(
            out=WCOL[:, 0:1], in0=WCOL[:, 0:1], scalar1=0.0, scalar2=None,
            op0=ALU.is_lt), waits=[(s.sGP, vws)], inc=s.sGP)
        s.op("gpsimd", lambda e: e.tensor_tensor(
            out=WCOL[:, 1:2], in0=s.IOTP[0:8, :], in1=s.META[0:8, 3:4],
            op=ALU.subtract), inc=s.sGP)
        vws2 = s.sGP.n
        s.op("gpsimd", lambda e: e.tensor_scalar(
            out=WCOL[:, 1:2], in0=WCOL[:, 1:2], scalar1=0.0, scalar2=None,
            op0=ALU.is_equal), waits=[(s.sGP, vws2)], inc=s.sGP)
        vw = s.sGP.n
        s.op("tensor", lambda e: e.matmul(out=pOFF[:, 0:8], lhsT=WCOL[:, 0:1], rhs=s.CNT8[:],
                                          start=True, stop=True), waits=[(s.sGP, vw)])
        s.op("tensor", lambda e: e.matmul(out=pOFF[:, 8:16], lhsT=WCOL[:, 1:2], rhs=s.CNT8[:],
                                          start=True, stop=True), inc=s.sPE)
        voff = s.sPE.n
        s.copy("vector", s.OFFG[:], pOFF[:, 0:8], waits=[(s.sPE, voff)])
        s.copy("vector", s.OFFEB[:], pOFF[:, 8:16])
        s.op("vector", lambda e: e.nop(), inc=s.sDVE)
        voc = s.sDVE.n
        s.OFFGB = s.sb("OFFGB", [128, 8]); s.OFFEBB = s.sb("OFFEBB", [128, 8])
        pOFB = s.PF[4][:, 0:16]
        s.op("tensor", lambda e: e.matmul(out=pOFB[:, 0:8], lhsT=s.O1x128f[:], rhs=s.OFFG[:],
                                          start=True, stop=True), waits=[(s.sDVE, voc)])
        s.op("tensor", lambda e: e.matmul(out=pOFB[:, 8:16], lhsT=s.O1x128f[:], rhs=s.OFFEB[:],
                                          start=True, stop=True), inc=s.sPE)
        vofb = s.sPE.n
        s.copy("vector", s.OFFGB[:], pOFB[:, 0:8], waits=[(s.sPE, vofb)])
        s.copy("vector", s.OFFEBB[:], pOFB[:, 8:16])
        s.op("vector", lambda e: e.nop(), inc=s.sDVE)
        s.v_offb = s.sDVE.n

    def phase_scatter(self):
        s, c = self, self.c
        TMP8 = s.sb("STMP8", [128, 8])
        DC = s.sb("DC", [128, 16])
        SIDX = s.sb("SIDX", [128, c.NT], U32)
        RECT = s.sb("RECT", [128, 2 * c.NT])

        def gop(fn, waits=(), inc=None):
            return s.op("gpsimd", fn, waits=waits, inc=inc if inc is not None else s.sGP)

        for t in range(c.NT):
            oh = s.SOH[:, t * 8:(t + 1) * 8]
            def tree8(dst, src_ms):
                va = gop(lambda e: e.tensor_tensor(out=TMP8[:, 0:4], in0=TMP8[:, 0:4],
                                                   in1=TMP8[:, 4:8], op=ALU.add),
                         waits=[(s.sGP, src_ms)])
                vb = gop(lambda e: e.tensor_tensor(out=TMP8[:, 0:2], in0=TMP8[:, 0:2],
                                                   in1=TMP8[:, 2:4], op=ALU.add),
                         waits=[(s.sGP, va)])
                return gop(lambda e, dst=dst: e.tensor_tensor(out=dst, in0=TMP8[:, 0:1],
                                                              in1=TMP8[:, 1:2], op=ALU.add),
                           waits=[(s.sGP, vb)])
            v0 = gop(lambda e, oh=oh: e.tensor_tensor(out=TMP8[:], in0=oh, in1=s.OFFGB[:],
                                                      op=ALU.mult),
                     waits=[(s.sDVE, s.v_offb), (s.sGP, s.sGP.n)])
            v1 = tree8(DC[:, 0:1], v0)
            v2 = gop(lambda e, oh=oh: e.tensor_tensor(out=TMP8[:], in0=oh, in1=s.OFFEBB[:],
                                                      op=ALU.mult), waits=[(s.sGP, v1)])
            v3 = tree8(DC[:, 1:2], v2)
            v4 = gop(lambda e, t=t: e.tensor_tensor(out=DC[:, 2:3], in0=s.SLR[:, t:t + 1],
                                                    in1=DC[:, 1:2], op=ALU.add),
                     waits=[(s.sGP, v3)])
            v5 = gop(lambda e, t=t: e.tensor_tensor(out=DC[:, 3:4], in0=s.SLR[:, t:t + 1],
                                                    in1=DC[:, 0:1], op=ALU.add))
            v6 = gop(lambda e: e.tensor_scalar(out=DC[:, 4:5], in0=DC[:, 2:3],
                                               scalar1=float(c.CELL), scalar2=None,
                                               op0=ALU.is_lt), waits=[(s.sGP, v5)])
            v7 = gop(lambda e: e.tensor_scalar(out=DC[:, 5:6], in0=DC[:, 3:4],
                                               scalar1=float(c.CAP), scalar2=None,
                                               op0=ALU.is_lt))
            v8 = gop(lambda e, t=t: e.tensor_tensor(out=DC[:, 6:7], in0=s.SP[:, t:t + 1],
                                                    in1=DC[:, 5:6], op=ALU.mult),
                     waits=[(s.sGP, v7)])
            v9a = gop(lambda e, t=t: e.tensor_scalar(out=DC[:, 7:8], in0=s.IOTP[:],
                                                     scalar1=float(t * 128 + 1),
                                                     scalar2=None, op0=ALU.add))
            v9 = gop(lambda e: e.tensor_tensor(out=DC[:, 7:8], in0=DC[:, 7:8],
                                               in1=s.META[:, 1:2], op=ALU.add),
                     waits=[(s.sGP, v9a)])
            v10 = gop(lambda e: e.tensor_tensor(out=DC[:, 8:9], in0=DC[:, 2:3],
                                                in1=s.META[:, 4:5], op=ALU.add),
                      waits=[(s.sGP, v6)])
            M23 = float(1 << 23)
            n1 = gop(lambda e, M23=M23: e.tensor_scalar(
                out=DC[:, 9:10], in0=DC[:, 8:9], scalar1=1.0 / 128.0, scalar2=M23,
                op0=ALU.mult, op1=ALU.add), waits=[(s.sGP, v10)])
            n2 = gop(lambda e, M23=M23: e.tensor_scalar(
                out=DC[:, 9:10], in0=DC[:, 9:10], scalar1=-M23, scalar2=None,
                op0=ALU.add), waits=[(s.sGP, n1)])
            n3 = gop(lambda e: e.tensor_scalar(
                out=DC[:, 10:11], in0=DC[:, 9:10], scalar1=128.0, scalar2=None,
                op0=ALU.mult), waits=[(s.sGP, n2)])
            n4 = gop(lambda e: e.tensor_tensor(
                out=DC[:, 10:11], in0=DC[:, 8:9], in1=DC[:, 10:11], op=ALU.subtract),
                waits=[(s.sGP, n3)])
            n5 = gop(lambda e: e.tensor_scalar(
                out=DC[:, 11:12], in0=DC[:, 10:11], scalar1=0.0, scalar2=None,
                op0=ALU.is_lt), waits=[(s.sGP, n4)])
            n6 = gop(lambda e: e.tensor_tensor(
                out=DC[:, 9:10], in0=DC[:, 9:10], in1=DC[:, 11:12], op=ALU.subtract),
                waits=[(s.sGP, n5)])
            n7 = gop(lambda e: e.tensor_scalar(
                out=DC[:, 11:12], in0=DC[:, 11:12], scalar1=128.0, scalar2=None,
                op0=ALU.mult), waits=[(s.sGP, n6)])
            n8 = gop(lambda e: e.tensor_tensor(
                out=DC[:, 10:11], in0=DC[:, 10:11], in1=DC[:, 11:12], op=ALU.add),
                waits=[(s.sGP, n7)])
            n9 = gop(lambda e: e.tensor_scalar(
                out=DC[:, 10:11], in0=DC[:, 10:11], scalar1=float(c.TILES),
                scalar2=None, op0=ALU.mult), waits=[(s.sGP, n8)])
            n10 = gop(lambda e: e.tensor_tensor(
                out=DC[:, 9:10], in0=DC[:, 9:10], in1=DC[:, 10:11], op=ALU.add),
                waits=[(s.sGP, n9)])
            n11 = gop(lambda e, t=t: e.tensor_scalar(
                out=DC[:, 11:12], in0=s.SEID[:, t:t + 1], scalar1=float(c.SLOTS),
                scalar2=None, op0=ALU.mult), waits=[(s.sGP, n10)])
            n12 = gop(lambda e: e.tensor_tensor(
                out=DC[:, 9:10], in0=DC[:, 9:10], in1=DC[:, 11:12], op=ALU.add),
                waits=[(s.sGP, n11)])
            n13 = gop(lambda e: e.tensor_tensor(
                out=DC[:, 9:10], in0=DC[:, 9:10], in1=DC[:, 4:5], op=ALU.mult),
                waits=[(s.sGP, n12), (s.sGP, v6)])
            n14 = gop(lambda e: e.tensor_scalar(
                out=DC[:, 11:12], in0=DC[:, 4:5], scalar1=-1.0, scalar2=1.0,
                op0=ALU.mult, op1=ALU.add), waits=[(s.sGP, n13)])
            n15 = gop(lambda e: e.tensor_scalar(
                out=DC[:, 10:11], in0=DC[:, 7:8], scalar1=float(c.BIG), scalar2=None,
                op0=ALU.add), waits=[(s.sGP, v9), (s.sGP, n9)])
            n16 = gop(lambda e: e.tensor_tensor(
                out=DC[:, 10:11], in0=DC[:, 10:11], in1=DC[:, 11:12], op=ALU.mult),
                waits=[(s.sGP, n15), (s.sGP, n14)])
            n17 = gop(lambda e: e.tensor_tensor(
                out=DC[:, 9:10], in0=DC[:, 9:10], in1=DC[:, 10:11], op=ALU.add),
                waits=[(s.sGP, n16)])
            v22 = gop(lambda e, t=t: e.tensor_copy(out=SIDX[:, t:t + 1], in_=DC[:, 9:10]),
                      waits=[(s.sGP, n17)])
            v23 = gop(lambda e, t=t: e.tensor_copy(out=RECT[:, 2 * t:2 * t + 1],
                                                   in_=DC[:, 7:8]), waits=[(s.sGP, v8)])
            v24 = gop(lambda e, t=t: e.tensor_copy(out=RECT[:, 2 * t + 1:2 * t + 2],
                                                   in_=DC[:, 6:7]))
            s.op("gpsimd", lambda e, t=t: e.indirect_dma_start(
                out=s.rec_in,
                out_offset=bass.IndirectOffsetOnAxis(ap=SIDX[:, t:t + 1], axis=0),
                in_=RECT[:, 2 * t:2 * t + 2], in_offset=None,
                bounds_check=s.breg(e, c.E * c.SLOTS - 1), oob_is_err=False),
                waits=[(s.sGP, v24), (s.sGP, v22), (s.sXL[0], s.v_zrec)],
                inc=s.sSC, incn=16)
        s.v_scat = s.sSC.n

    def phase_a2(self):
        s, c = self, self.c
        KFM = s.sb("KFM", [128, 4096], BF16)
        VTM = [s.sb(f"VTM{i}", [128, 1024], BF16) for i in range(4)]
        SQB = [s.sb("SQB0", [128, 512], BF16), s.sb("SQB1", [128, 512], BF16)]
        ATL = [s.sb(f"ATL{i}", [128, 512], BF16) for i in range(2)]
        s.DKTM = s.sb("DKTM", [128, 64])
        MXKG = s.sb("MXKG", [128, 64])
        LNMX = s.sb("LNMX", [128, 64])
        MXR = s.sb("MXR", [128, 2])
        s.UCOMP = s.sb("UCOMP", [128, 2048])
        s.USB = s.sb("USB", [8, 512])
        pKV = [s.PF[0], s.PF[1]]
        pDD = [s.PF[2], s.PF[3]]
        pDT = s.PF[2][:, 0:64]
        pU = s.PF[4]
        pu8 = s.PF[5][0:8, :]
        s.pu8 = pu8

        s.op("vector", lambda e: e.memset(s.UCOMP[:], 0.0), inc=s.sDVE)
        s.op("vector", lambda e: e.memset(MXR[:], -1e30), inc=s.sDVE)
        s.op("vector", lambda e: e.memset(s.USB[:], 0.0), inc=s.sDVE)
        v_z = s.sDVE.n

        kv_free = [s.v_pbd, s.v_pbd]
        dd_free_act = [0, 0]
        sq_free = [0, 0]
        at_free = [0, 0]
        at_free_dve = [0, 0]
        u_guard = v_z
        usb_ms = v_z
        dkt_ms = 0
        for g in range(c.GA):
            gb = g * 4096
            for oc in range(8):
                par = oc % 2
                w0 = [(s.sDVE, kv_free[par]), (s.sWA, s.v_wa),
                      (s.sDVE, s.fm_done[g * 4 + 3]), (s.sDVE, dkt_ms)]
                for kc in range(8):
                    s.op("tensor", lambda e, oc=oc, kc=kc, par=par, gb=gb: e.matmul(
                        out=pKV[par][:],
                        lhsT=s.WKB[:, kc * 1024 + oc * 128: kc * 1024 + (oc + 1) * 128],
                        rhs=s.XNFM[:, gb + kc * 512: gb + (kc + 1) * 512],
                        start=(kc == 0), stop=(kc == 7)),
                        waits=w0 if kc == 0 else (), inc=s.sPE if kc == 7 else None)
                vk = s.sPE.n
                s.op("vector", lambda e, oc=oc, par=par: e.tensor_copy(
                    out=KFM[:, oc * 512:(oc + 1) * 512], in_=pKV[par][:]),
                    waits=[(s.sPE, vk), (s.sDVE, s.sDVE.n)], inc=s.sDVE)
                kv_free[par] = s.sDVE.n
            kfm_ms = s.sDVE.n
            for oc in range(8):
                par = oc % 2
                s.op("vector", lambda e, oc=oc, par=par: e.tensor_tensor(
                    out=SQB[par][:], in0=KFM[:, oc * 512:(oc + 1) * 512],
                    in1=KFM[:, oc * 512:(oc + 1) * 512], op=ALU.mult),
                    waits=[(s.sDVE, kfm_ms), (s.sPE, sq_free[par])], inc=s.sDVE)
                vsq = s.sDVE.n
                for s2 in range(4):
                    s.op("tensor", lambda e, oc=oc, s2=s2, par=par: e.matmul(
                        out=pDT[:, s2 * 16:(s2 + 1) * 16],
                        lhsT=SQB[par][:, s2 * 128:(s2 + 1) * 128],
                        rhs=s.OD16N[:, oc * 16:(oc + 1) * 16],
                        start=(oc == 0 and s2 == 0), stop=(oc == 7 and s2 == 3)),
                        waits=[(s.sDVE, vsq)] if s2 == 0 else (),
                        inc=s.sPE if s2 == 3 else None)
                sq_free[par] = s.sPE.n
            vdt = s.sPE.n
            s.op("vector", lambda e: e.tensor_copy(
                out=s.DKTM[:].rearrange("p (h q) -> p h q", h=16, q=4),
                in_=pDT.rearrange("p (q h) -> p h q", q=4, h=16)),
                waits=[(s.sPE, vdt), (s.sDVE, s.sDVE.n)], inc=s.sDVE)
            dkt_ms = s.sDVE.n
            vdk = dkt_ms
            for s2 in range(4):
                for half in range(2):
                    par = (s2 * 2 + half) % 2
                    w0 = [(s.sDVE, kv_free[par])]
                    for kc in range(8):
                        s.op("tensor", lambda e, s2=s2, half=half, kc=kc, par=par, gb=gb: e.matmul(
                            out=pKV[par][:],
                            lhsT=s.XNFM[:, gb + kc * 512 + s2 * 128: gb + kc * 512 + (s2 + 1) * 128],
                            rhs=s.WVB[:, kc * 1024 + half * 512: kc * 1024 + (half + 1) * 512],
                            start=(kc == 0), stop=(kc == 7)),
                            waits=w0 if kc == 0 else (), inc=s.sPE if kc == 7 else None)
                    vv = s.sPE.n
                    s.op("vector", lambda e, s2=s2, half=half, par=par: e.tensor_copy(
                        out=VTM[s2][:, half * 512:(half + 1) * 512], in_=pKV[par][:]),
                        waits=[(s.sPE, vv), (s.sDVE, s.sDVE.n)], inc=s.sDVE)
                    kv_free[par] = s.sDVE.n
            vtm_ms = s.sDVE.n
            for hp in range(8):
                for s2 in range(4):
                    par = s2 % 2
                    s.op("tensor", lambda e, hp=hp, s2=s2, par=par: e.matmul(
                        out=pDD[par][:],
                        lhsT=KFM[:, hp * 512 + s2 * 128: hp * 512 + (s2 + 1) * 128],
                        rhs=s.PBD[:], start=True, stop=True),
                        waits=[(s.sACT, dd_free_act[par]), (s.sDVE, vtm_ms),
                               (s.sDVE, vdk), (s.sDVE, s.v_pbd)],
                        inc=s.sPE)
                    vdd = s.sPE.n
                    apar = s2 % 2
                    for hh in range(2):
                        col = (2 * hp + hh) * 4 + s2
                        s.op("scalar", lambda e, col=col, hh=hh, par=par, apar=apar: e.activation(
                            out=ATL[apar][:, hh * 256:(hh + 1) * 256],
                            in_=pDD[par][:, hh * 256:(hh + 1) * 256], func=AF.Exp,
                            bias=s.DKTM[:, col:col + 1], scale=1.0),
                            waits=[(s.sPE, vdd), (s.sPE, at_free[apar]),
                                   (s.sDVE, at_free_dve[apar]),
                                   (s.sACT, s.sACT.n)] if hh == 0 else (),
                            inc=s.sACT)
                    vexp = s.sACT.n
                    dd_free_act[par] = vexp
                    for hh in range(2):
                        col = (2 * hp + hh) * 4 + s2
                        s.op("vector", lambda e, col=col, hh=hh, apar=apar: e.tensor_reduce(
                            out=MXKG[:, col:col + 1], in_=ATL[apar][:, hh * 256:(hh + 1) * 256],
                            axis=AX.X, op=ALU.max),
                            waits=[(s.sACT, vexp)], inc=s.sDVE)
                    at_free_dve[apar] = s.sDVE.n
                    s.op("tensor", lambda e, hp=hp, s2=s2, apar=apar: e.matmul(
                        out=pU[:], lhsT=VTM[s2][:, hp * 128:(hp + 1) * 128], rhs=ATL[apar][:],
                        start=(s2 == 0), stop=(s2 == 3)),
                        waits=[(s.sACT, vexp), (s.sDVE, u_guard)])
                    s.op("tensor", lambda e, hp=hp, apar=apar, s2=s2: e.matmul(
                        out=pu8, lhsT=s.OC8[:, hp * 8:(hp + 1) * 8], rhs=ATL[apar][:],
                        start=(s2 == 0), stop=(s2 == 3)),
                        waits=[(s.sDVE, usb_ms)] if s2 == 0 else (), inc=s.sPE)
                    at_free[apar] = s.sPE.n
                vU = s.sPE.n
                s.op("vector", lambda e, hp=hp: e.tensor_tensor(
                    out=s.UCOMP[0:64, hp * 256:(hp + 1) * 256],
                    in0=s.UCOMP[0:64, hp * 256:(hp + 1) * 256],
                    in1=pU[0:64, 0:256], op=ALU.add),
                    waits=[(s.sPE, vU), (s.sDVE, s.sDVE.n)], inc=s.sDVE)
                s.op("vector", lambda e, hp=hp: e.tensor_tensor(
                    out=s.UCOMP[64:128, hp * 256:(hp + 1) * 256],
                    in0=s.UCOMP[64:128, hp * 256:(hp + 1) * 256],
                    in1=pU[64:128, 256:512], op=ALU.add), inc=s.sDVE)
                u_guard = s.sDVE.n
                s.op("vector", lambda e: e.tensor_tensor(
                    out=s.USB[0:8, :], in0=s.USB[0:8, :], in1=pu8, op=ALU.add),
                    waits=[(s.sPE, vU), (s.sDVE, usb_ms)], inc=s.sDVE)
                usb_ms = s.sDVE.n
            vgd = s.sDVE.n
            s.op("scalar", lambda e: e.activation(
                out=LNMX[:], in_=MXKG[:], func=AF.Ln, bias=0.0, scale=1.0),
                waits=[(s.sDVE, vgd), (s.sACT, s.sACT.n)], inc=s.sACT)
            vln = s.sACT.n
            s.op("vector", lambda e: e.tensor_tensor(
                out=LNMX[:], in0=LNMX[:], in1=s.DKTM[:], op=ALU.subtract),
                waits=[(s.sACT, vln), (s.sDVE, s.sDVE.n)], inc=s.sDVE)
            vsub = s.sDVE.n
            s.op("vector", lambda e: e.tensor_reduce(
                out=MXR[:, 0:1], in_=LNMX[:], axis=AX.X, op=ALU.max),
                waits=[(s.sDVE, vsub)], inc=s.sDVE)
            vm0 = s.sDVE.n
            s.op("vector", lambda e: e.tensor_tensor(
                out=MXR[:, 1:2], in0=MXR[:, 1:2], in1=MXR[:, 0:1], op=ALU.max),
                waits=[(s.sDVE, vm0)], inc=s.sDVE)
        s.a2_pe = s.sPE.n
        s.a2_dve = s.sDVE.n
        s.MXR = MXR

    def phase_coll2(self):
        s, c = self, self.c
        pMX = s.PF[0][0:1, 0:128]
        MXROW = s.sb("MXROW", [1, 128])
        s.MXCOL = s.sb("MXCOL", [128, 1])
        s.op("tensor", lambda e: e.transpose(out=pMX, in_=s.MXR[:, 1:2], identity=s.IDE[:]),
             waits=[(s.sDVE, s.a2_dve), (s.sPE, s.a2_pe)], inc=s.sPE)
        v = s.sPE.n
        s.copy("vector", MXROW[:], pMX, waits=[(s.sPE, v)])
        s.op("vector", lambda e: e.nop(), inc=s.sDVE)
        v2 = s.sDVE.n
        s.op("vector", lambda e: e.tensor_reduce(
            out=MXROW[:, 0:1], in_=MXROW[:], axis=AX.X, op=ALU.max),
            waits=[(s.sDVE, v2)], inc=s.sDVE)
        v3 = s.sDVE.n
        pMB = s.PF[1][:, 0:1]
        s.op("tensor", lambda e: e.matmul(out=pMB, lhsT=s.O1x128f[:], rhs=MXROW[:, 0:1],
                                          start=True, stop=True),
             waits=[(s.sDVE, v3)], inc=s.sPE)
        v4 = s.sPE.n
        s.copy("vector", s.MXCOL[:], pMB, waits=[(s.sPE, v4)])
        s.op("vector", lambda e: e.nop(), inc=s.sDVE)
        v_mx = s.sDVE.n

        UTM = s.sb("UTM", [128, 32])
        USUM = s.sb("USUM", [1, 16])
        pUT = s.PF[2][:, 0:32]
        for ch in range(4):
            s.op("tensor", lambda e, ch=ch: e.transpose(
                out=pUT[:, ch * 8:(ch + 1) * 8], in_=s.USB[:, ch * 128:(ch + 1) * 128],
                identity=s.IDE[0:8, 0:8]),
                waits=[(s.sDVE, s.a2_dve)] if ch == 0 else (),
                inc=s.sPE if ch == 3 else None)
        vut = s.sPE.n
        s.copy("vector", UTM[:], pUT, waits=[(s.sPE, vut)])
        s.op("vector", lambda e: e.nop(), inc=s.sDVE)
        v_utm = s.sDVE.n
        pUS = s.PF[3][0:32, 0:1]
        USR = s.sb("USR", [32, 1])
        pUSR = s.PF[0][0:1, 200:232]
        USROW = s.sb("USROW", [1, 32])
        s.op("tensor", lambda e: e.matmul(out=pUS, lhsT=UTM[:], rhs=s.O128x1f[:],
                                          start=True, stop=True),
             waits=[(s.sDVE, v_utm)], inc=s.sPE)
        v5 = s.sPE.n
        s.copy("vector", USR[:], pUS, waits=[(s.sPE, v5)])
        s.op("vector", lambda e: e.nop(), inc=s.sDVE)
        v6 = s.sDVE.n
        s.op("tensor", lambda e: e.transpose(out=pUSR, in_=USR[:], identity=s.IDE[0:32, 0:32]),
             waits=[(s.sDVE, v6)], inc=s.sPE)
        v7 = s.sPE.n
        s.copy("vector", USROW[:], pUSR, waits=[(s.sPE, v7)])
        s.op("vector", lambda e: e.nop(), inc=s.sDVE)
        v8 = s.sDVE.n
        for pr in range(2):
            s.op("vector", lambda e, pr=pr: e.tensor_tensor(
                out=USUM[:, pr:16:2], in0=USROW[:, pr * 16:pr * 16 + 8],
                in1=USROW[:, pr * 16 + 8:pr * 16 + 16], op=ALU.add),
                waits=[(s.sDVE, v8)], inc=s.sDVE)
        v_usum = s.sDVE.n

        SX = s.sb("SX", [128, 8])
        SXP = s.sb("SXP", [128, 8 * c.GA])
        for ch in range(8):
            for g in range(c.GA):
                s.op("vector", lambda e, ch=ch, g=g: e.tensor_reduce(
                    out=SXP[:, ch * c.GA + g:ch * c.GA + g + 1],
                    in_=s.XNFM[:, g * 4096 + ch * 512:g * 4096 + (ch + 1) * 512],
                    axis=AX.X, op=ALU.add),
                    waits=[(s.sDVE, s.a2_dve)], inc=s.sDVE)
        v9 = s.sDVE.n
        for ch in range(8):
            s.op("vector", lambda e, ch=ch: e.tensor_reduce(
                out=SX[:, ch:ch + 1], in_=SXP[:, ch * c.GA:(ch + 1) * c.GA],
                axis=AX.X, op=ALU.add), waits=[(s.sDVE, v9)], inc=s.sDVE)
        v10 = s.sDVE.n
        SXB = s.sb("SXB", [128, 8], BF16)
        s.copy("vector", SXB[:], SX[:], waits=[(s.sDVE, v10)])
        s.op("vector", lambda e: e.nop(), inc=s.sDVE)
        v11 = s.sDVE.n
        pS = [s.PF[4][0:1, :], s.PF[5][0:1, :]]
        SROW = s.sb("SROW", [1, 1024])
        for half in range(2):
            for ch in range(8):
                s.op("tensor", lambda e, half=half, ch=ch: e.matmul(
                    out=pS[half], lhsT=SXB[:, ch:ch + 1],
                    rhs=s.WVB[:, ch * 1024 + half * 512: ch * 1024 + (half + 1) * 512],
                    start=(ch == 0), stop=(ch == 7)),
                    waits=[(s.sDVE, v11)] if (half == 0 and ch == 0) else (),
                    inc=s.sPE if ch == 7 else None)
            vs = s.sPE.n
            s.copy("vector", SROW[:, half * 512:(half + 1) * 512], pS[half],
                   waits=[(s.sPE, vs)])
        s.op("vector", lambda e: e.nop(), inc=s.sDVE)
        v_srow = s.sDVE.n
        s.S8 = s.sb("S8", [128, 8])
        s.op("sync", lambda e: e.dma_start(out=s.srow_dram, in_=SROW[:]),
             waits=[(s.sDVE, v_srow)], inc=s.sSR, incn=16)
        vsr1 = s.sSR.n
        s.op("sync", lambda e: e.dma_start(
            out=s.S8[:], in_=s.srow_dram.rearrange("o (p f) -> (o p) f", p=128)),
            waits=[(s.sSR, vsr1)], inc=s.sSR, incn=16)
        v_s8 = s.sSR.n

        s.op("gpsimd", lambda e: e.dma_start(out=s.pk_in[:, 0:2048], in_=s.UCOMP[:]),
             waits=[(s.sDVE, s.a2_dve)], inc=s.sCD, incn=16)
        s.op("gpsimd", lambda e: e.dma_start(out=s.pk_in[:, 2048:2080], in_=UTM[:]),
             waits=[(s.sDVE, v_utm)], inc=s.sCD, incn=16)
        PKF = s.sb("PKF", [128, 16])
        s.op("vector", lambda e: e.memset(PKF[:], 0.0), waits=[(s.sDVE, s.sDVE.n)],
             inc=s.sDVE)
        vpkz = s.sDVE.n
        s.op("vector", lambda e: e.tensor_copy(out=PKF[0:1, :], in_=USUM[:]),
             waits=[(s.sDVE, vpkz), (s.sDVE, v_usum)], inc=s.sDVE)
        vpkf = s.sDVE.n
        s.op("gpsimd", lambda e: e.dma_start(out=s.pk_in[:, 2080:2096], in_=PKF[:]),
             waits=[(s.sDVE, vpkf)], inc=s.sCD, incn=16)
        s.op("gpsimd", lambda e: e.dma_start(out=s.pk_in[:, 2096:2097], in_=s.MXCOL[:], allow_slow_non_contiguous=True),
             waits=[(s.sDVE, v_mx)], inc=s.sCD, incn=16)
        s.op("gpsimd", lambda e: e.dma_start(out=s.pk_in[:, 2097:2105], in_=s.S8[:]),
             waits=[(s.sSR, v_s8)], inc=s.sCD, incn=16)
        v_pk = s.sCD.n
        s.op("gpsimd", lambda e: e.collective_compute(
            "AllGather", ALU.bypass, replica_groups=[list(range(8))],
            ins=[s.rec_in], outs=[s.rec_out]),
            waits=[(s.sSC, s.v_scat)], inc=s.sC2a)
        s.v_c2a = s.sC2a.n
        s.op("gpsimd", lambda e: e.collective_compute(
            "AllGather", ALU.bypass, replica_groups=[list(range(8))],
            ins=[s.pk_in], outs=[s.pk_out]),
            waits=[(s.sCD, v_pk)], inc=s.sC2b)
        s.v_c2b = s.sC2b.n

    def phase_merge(self):
        s, c = self, self.c
        W = 2 * c.TILES
        MRG = s.sb("MRG", [128, W])
        s.RTID = s.sb("RTID", [128, c.TILES])
        s.IDXO = s.sb("IDXO", [128, c.TILES], I32)
        IDXF = s.sb("IDXF", [128, c.TILES])
        GIF = s.sb("GIF", [128, c.TILES])
        MSK = s.sb("MSK", [128, c.TILES])
        NMSK = s.sb("NMSK", [128, c.TILES])
        RIDX = s.sb("RIDX", [128, 8], U32)
        RIF = s.sb("RIF", [128, 1])
        MSL = [s.sb(f"MSL{i}", [128, W]) for i in range(2)]

        for r in range(8):
            v0a = s.op("gpsimd", lambda e, r=r: e.tensor_scalar(
                out=RIF[:], in0=s.IOTP[:], scalar1=float(r * c.E * 128),
                scalar2=None, op0=ALU.add),
                waits=[(s.sXL[0], s.v_consts), (s.sGP, s.sGP.n)], inc=s.sGP)
            v0 = s.op("gpsimd", lambda e: e.tensor_tensor(
                out=RIF[:], in0=RIF[:], in1=s.META[:, 5:6], op=ALU.add),
                waits=[(s.sGP, v0a)], inc=s.sGP)
            s.op("gpsimd", lambda e, r=r: e.tensor_copy(out=RIDX[:, r:r + 1], in_=RIF[:]),
                 waits=[(s.sGP, v0)], inc=s.sGP)
        v_ridx = s.sGP.n
        recv = s.rec_out.rearrange("(a j) w -> a (j w)", j=c.TILES)
        vml = [0, 0]
        for r in range(8):
            par = r % 2
            s.op("gpsimd", lambda e, r=r, par=par: e.indirect_dma_start(
                out=MSL[par][:], out_offset=None, in_=recv,
                in_offset=bass.IndirectOffsetOnAxis(ap=RIDX[:, r:r + 1], axis=0)),
                waits=[(s.sGP, v_ridx), (s.sC2a, s.v_c2a), (s.sDVE, vml[par])],
                inc=s.sEG[par], incn=16)
            vg = s.sEG[par].n
            if r == 0:
                s.copy("vector", MRG[:], MSL[par][:], waits=[(s.sEG[par], vg)])
            else:
                s.op("vector", lambda e, par=par: e.tensor_tensor(
                    out=MRG[:], in0=MRG[:], in1=MSL[par][:], op=ALU.add),
                    waits=[(s.sEG[par], vg), (s.sDVE, s.sDVE.n)], inc=s.sDVE)
            vml[par] = s.op("vector", lambda e: e.nop(), inc=s.sDVE)
        v_mrg = s.sDVE.n
        s.op("vector", lambda e: e.tensor_copy(
            out=s.RTID[:], in_=MRG[:].rearrange("p (t w) -> p w t", w=2)[:, 0, :]),
            waits=[(s.sDVE, v_mrg)], inc=s.sDVE)
        s.op("vector", lambda e: e.tensor_copy(
            out=s.RSCL[:], in_=MRG[:].rearrange("p (t w) -> p w t", w=2)[:, 1, :]),
            inc=s.sDVE)
        v_sp = s.sDVE.n
        s.op("vector", lambda e: e.tensor_scalar(
            out=IDXF[:], in0=s.RTID[:], scalar1=-1.0, scalar2=None, op0=ALU.add),
            waits=[(s.sDVE, v_sp)], inc=s.sDVE)
        v_idxf = s.sDVE.n
        s.op("vector", lambda e: e.tensor_copy(out=s.IDXO[:], in_=IDXF[:]),
             waits=[(s.sDVE, v_idxf)], inc=s.sDVE)
        v_idxo = s.sDVE.n
        s.op("vector", lambda e: e.tensor_scalar(
            out=MSK[:], in0=s.RTID[:], scalar1=0.5, scalar2=None, op0=ALU.is_gt),
            waits=[(s.sDVE, v_sp)], inc=s.sDVE)
        vmsk = s.sDVE.n
        s.op("vector", lambda e: e.tensor_scalar(
            out=NMSK[:], in0=s.RTID[:], scalar1=0.5, scalar2=None, op0=ALU.is_le),
            inc=s.sDVE)
        vnmsk = s.sDVE.n
        s.op("vector", lambda e: e.tensor_tensor(
            out=GIF[:], in0=MSK[:], in1=IDXF[:], op=ALU.mult),
            waits=[(s.sDVE, vmsk), (s.sDVE, v_idxf)], inc=s.sDVE)
        vg1 = s.sDVE.n
        s.op("vector", lambda e: e.scalar_tensor_tensor(
            out=GIF[:], in0=NMSK[:], scalar=float(c.BIG), in1=GIF[:],
            op0=ALU.mult, op1=ALU.add),
            waits=[(s.sDVE, vnmsk), (s.sDVE, vg1)], inc=s.sDVE)
        vgp = s.sDVE.n
        s.op("vector", lambda e: e.tensor_copy(out=s.GIDX[:], in_=GIF[:]),
             waits=[(s.sDVE, vgp)], inc=s.sDVE)
        s.v_gidx = s.sDVE.n
        s.op("sync", lambda e: e.dma_start(out=s.o_idx, in_=s.IDXO[:]),
             waits=[(s.sDVE, v_idxo)], inc=s.sOUT, incn=16)

    def phase_expert(self):
        s, c = self, self.c
        FQ = s.FQ
        NFC = FQ // 128
        KQ = FQ // 128
        EGT = c.EGT
        TW = EGT * 128
        XGT = [s.sb(f"XGT{i}", [128, 1024]) for i in range(EGT)]
        XNN = s.sb("XNN", [128, 1024])
        XNB = [s.sb("XNB0", [128, 1024], BF16), s.sb("XNB1", [128, 1024], BF16)]
        ELN = [s.sb("ELN0", [128, 8]), s.sb("ELN1", [128, 8])]
        ETT = XNN
        XEG2 = [s.sb("XEGa", [128, 8 * TW], BF16), s.sb("XEGb", [128, 8 * TW], BF16)]
        HFMG2 = [s.sb("HFMGa", [128, NFC * TW], BF16),
                 s.sb("HFMGb", [128, NFC * TW], BF16)]
        YE2 = [s.sb("YEa", [128, 1024]), s.sb("YEb", [128, 1024])]
        pE = [s.PF[0], s.PF[1]]
        pY = [s.PF[2], s.PF[3]]
        pT2 = [s.PB[0], s.PB[1]]
        scr = s.xefm_scr.rearrange("(gg p) f -> gg p f", p=128)

        xg_free = [0] * EGT
        xg_dma = [None] * EGT
        ye_rel2 = [[(self.sYW, 0)], [(self.sYW, 0)]]
        xnn_free = 0
        eln1_free = [0, 0]
        xnb_free = [0, 0]
        t2_free = [0, 0]
        pe_free = [0, 0]
        py_free = [0, 0]
        xeg_free2 = [0, 0]
        xeg_w2 = [0, 0]
        hfm_free2 = [0, 0]
        eact = s.sACT.n
        v_h0w = []

        for q in range(4):
            wbuf = q % 2
            ggi_base = q * c.EG
            if q == 1:
                v_h0w = [(s.sYW, s.sYW.n), (s.sYWb, s.sYWb.n)]
                v_xf = [(s.sXF2[0], s.sXF2[0].n), (s.sXF2[1], s.sXF2[1].n)]
            if 2 <= q + 1 <= 3:
                pass
            if q >= 1:
                if q + 1 <= 3:
                    self.load_w_quarter(q + 1, wait_pe=s.sPE.n)
            elif q == 0:
                if False:
                    pass
            for gi in range(c.EG):
                tiles = list(range(gi * EGT, (gi + 1) * EGT))
                ggi = ggi_base + gi
                xb = ggi % 2
                XEG = XEG2[xb]
                HFMG = HFMG2[xb]
                if q == 0:
                    for ti, t in enumerate(tiles):
                        vz = s.op("vector", lambda e, ti=ti: e.memset(XGT[ti][:], 0.0),
                                  waits=[(s.sDVE, xg_free[ti])] +
                                        ([xg_dma[ti]] if xg_dma[ti] else []),
                                  inc=s.sDVE)
                        par = ti % 2
                        s.op("gpsimd", lambda e, t=t, ti=ti: e.indirect_dma_start(
                            out=XGT[ti][:], out_offset=None, in_=s.xfull,
                            in_offset=bass.IndirectOffsetOnAxis(ap=s.GIDX[:, t:t + 1], axis=0),
                            bounds_check=s.breg(e, c.T - 1), oob_is_err=False),
                            waits=[(s.sDVE, vz), (s.sDVE, s.v_gidx)],
                            inc=s.sEG[par], incn=16)
                        vg = s.sEG[par].n
                        LN = ELN[ti % 2]
                        s.op("vector", lambda e, ti=ti, LN=LN: e.tensor_reduce(
                            out=LN[:, 0:1], in_=XGT[ti][:], axis=AX.X, op=ALU.add),
                            waits=[(s.sEG[par], vg), (s.sDVE, s.sDVE.n)], inc=s.sDVE)
                        vst = s.sDVE.n
                        s.op("scalar", lambda e, ti=ti, LN=LN: e.activation(
                            out=ETT[:], in_=XGT[ti][:], func=AF.Square, bias=0.0,
                            scale=1.0, accum_out=LN[:, 1:2]),
                            waits=[(s.sEG[par], vg), (s.sDVE, xnn_free),
                                   (s.sDVE, eln1_free[ti % 2]), (s.sACT, eact)],
                            inc=s.sACT)
                        vsq = s.sACT.n
                        s.op("vector", lambda e, LN=LN: e.tensor_scalar(
                            out=LN[:, 2:3], in0=LN[:, 0:1], scalar1=1.0 / c.D, scalar2=None,
                            op0=ALU.mult), waits=[(s.sDVE, vst)], inc=s.sDVE)
                        vmean = s.sDVE.n
                        s.op("vector", lambda e, LN=LN: e.tensor_tensor(
                            out=LN[:, 3:4], in0=LN[:, 2:3], in1=LN[:, 2:3], op=ALU.mult),
                            waits=[(s.sDVE, vmean)], inc=s.sDVE)
                        vm2 = s.sDVE.n
                        s.op("vector", lambda e, LN=LN: e.scalar_tensor_tensor(
                            out=LN[:, 4:5], in0=LN[:, 1:2], scalar=1.0 / c.D, in1=LN[:, 3:4],
                            op0=ALU.mult, op1=ALU.subtract),
                            waits=[(s.sDVE, vm2), (s.sACT, vsq)], inc=s.sDVE)
                        vvar = s.sDVE.n
                        eln1_free[ti % 2] = vvar
                        s.op("scalar", lambda e, LN=LN: e.activation(
                            out=LN[:, 5:6], in_=LN[:, 4:5], func=AF.Ln, bias=1e-5, scale=1.0),
                            waits=[(s.sDVE, vvar), (s.sACT, eact)], inc=s.sACT)
                        va = s.sACT.n
                        s.op("scalar", lambda e, LN=LN: e.activation(
                            out=LN[:, 6:7], in_=LN[:, 5:6], func=AF.Exp, bias=0.0, scale=-0.5),
                            waits=[(s.sACT, va)], inc=s.sACT)
                        vrstd = s.sACT.n
                        eact = vrstd
                        s.op("vector", lambda e, ti=ti, LN=LN: e.tensor_scalar(
                            out=XNN[:], in0=XGT[ti][:], scalar1=LN[:, 2:3], scalar2=LN[:, 6:7],
                            op0=ALU.subtract, op1=ALU.mult),
                            waits=[(s.sACT, vrstd), (s.sDVE, s.sDVE.n)], inc=s.sDVE)
                        vxn = s.sDVE.n
                        bpar = ti % 2
                        s.op("vector", lambda e, bpar=bpar: e.tensor_copy(
                            out=XNB[bpar][:], in_=XNN[:]),
                            waits=[(s.sPE, xnb_free[bpar]), (s.sDVE, vxn)], inc=s.sDVE)
                        vhi = s.sDVE.n
                        xnn_free = vhi
                        for cp in range(4):
                            tp = cp % 2
                            s.op("tensor", lambda e, bpar=bpar, cp=cp, tp=tp: e.transpose(
                                out=pT2[tp][:, 0:128],
                                in_=XNB[bpar][:, (2 * cp) * 128:(2 * cp + 1) * 128],
                                identity=s.IDEB[:]),
                                waits=[(s.sDVE, vhi), (s.sDVE, t2_free[tp]),
                                       (s.sPE, xeg_free2[xb]),
                                       (s.sXF2[xb], xeg_w2[xb])])
                            s.op("tensor", lambda e, bpar=bpar, cp=cp, tp=tp: e.transpose(
                                out=pT2[tp][:, 128:256],
                                in_=XNB[bpar][:, (2 * cp + 1) * 128:(2 * cp + 2) * 128],
                                identity=s.IDEB[:]), inc=s.sPE)
                            vtp = s.sPE.n
                            b0 = (2 * cp) * TW + ti * 128
                            b1 = (2 * cp + 1) * TW + ti * 128
                            s.copy("vector", XEG[:, b0:b0 + 128], pT2[tp][:, 0:128],
                                   waits=[(s.sPE, vtp)])
                            t2_free[tp] = s.copy("vector", XEG[:, b1:b1 + 128],
                                                 pT2[tp][:, 128:256])
                        xnb_free[bpar] = s.sPE.n
                    vxeg = s.op("vector", lambda e: e.nop(), inc=s.sDVE)
                    s.op("sync", lambda e, gi=gi, XEG=XEG: e.dma_start(
                        out=scr[gi], in_=XEG[:]), waits=[(s.sDVE, vxeg)],
                        inc=s.sXF2[xb], incn=16)
                    xeg_w2[xb] = s.sXF2[xb].n
                    w_xeg = [(s.sDVE, vxeg)]
                else:
                    s.op("sync", lambda e, gi=gi, XEG=XEG: e.dma_start(
                        out=XEG[:], in_=scr[gi]),
                        waits=[(s.sPE, xeg_free2[xb]),
                               (s.sXE[xb], s.sXE[xb].n)] + v_xf,
                        inc=s.sXE[xb], incn=16)
                    w_xeg = [(s.sXE[xb], s.sXE[xb].n)]
                for fc in range(NFC):
                    par = fc % 2
                    w0 = list(w_xeg) + [(s.sWE[q % 2], s.v_wq[q]), (s.sACT, pe_free[par]),
                                        (s.sPE, hfm_free2[xb])]
                    for kc in range(8):
                        s.op("tensor", lambda e, fc=fc, kc=kc, par=par, wbuf=wbuf, TW=TW, FQ=FQ, XEG=XEG: e.matmul(
                            out=pE[par][:, 0:TW],
                            lhsT=s.W1Q[wbuf][:, kc * FQ + fc * 128: kc * FQ + (fc + 1) * 128],
                            rhs=XEG[:, kc * TW:(kc + 1) * TW],
                            start=(kc == 0), stop=(kc == 7)),
                            waits=w0 if kc == 0 else (), inc=s.sPE if kc == 7 else None)
                    vh = s.sPE.n
                    s.op("scalar", lambda e, fc=fc, par=par, TW=TW, HFMG=HFMG: e.activation(
                        out=HFMG[:, fc * TW:(fc + 1) * TW], in_=pE[par][:, 0:TW],
                        func=AF.Relu, bias=0.0, scale=1.0),
                        waits=[(s.sPE, vh), (s.sACT, eact)], inc=s.sACT)
                    pe_free[par] = s.sACT.n
                    eact = s.sACT.n
                xeg_free2[xb] = s.sPE.n
                vrelu = s.sACT.n
                for ti, t in enumerate(tiles):
                    for dh in range(2):
                        par = dh
                        for kq in range(KQ):
                            s.op("tensor", lambda e, ti=ti, dh=dh, kq=kq, par=par, wbuf=wbuf, TW=TW, HFMG=HFMG: e.matmul(
                                out=pY[par][:],
                                lhsT=HFMG[:, kq * TW + ti * 128: kq * TW + (ti + 1) * 128],
                                rhs=s.W2Q[wbuf][:, kq * 1024 + dh * 512: kq * 1024 + (dh + 1) * 512],
                                start=(kq == 0), stop=(kq == KQ - 1)),
                                waits=[(s.sACT, vrelu), (s.sDVE, py_free[par])] if kq == 0 else (),
                                inc=s.sPE if kq == KQ - 1 else None)
                        vy = s.sPE.n
                        tp2 = ti % 2
                        YE = YE2[tp2]
                        if q == 0:
                            s.op("vector", lambda e, t=t, ti=ti, dh=dh, par=par, YE=YE: e.scalar_tensor_tensor(
                                out=YE[:, dh * 512:(dh + 1) * 512], in0=pY[par][:],
                                scalar=s.RSCL[:, t:t + 1],
                                in1=XGT[ti][:, dh * 512:(dh + 1) * 512],
                                op0=ALU.mult, op1=ALU.add),
                                waits=[(s.sPE, vy), (s.sDVE, s.sDVE.n)] + ye_rel2[tp2],
                                inc=s.sDVE)
                        else:
                            s.op("vector", lambda e, t=t, dh=dh, par=par, YE=YE: e.tensor_scalar(
                                out=YE[:, dh * 512:(dh + 1) * 512], in0=pY[par][:],
                                scalar1=s.RSCL[:, t:t + 1], scalar2=None, op0=ALU.mult),
                                waits=[(s.sPE, vy), (s.sDVE, s.sDVE.n)] + ye_rel2[tp2],
                                inc=s.sDVE)
                        py_free[par] = s.sDVE.n
                    vye = s.sDVE.n
                    tp2 = ti % 2
                    YE = YE2[tp2]
                    if q == 0:
                        xg_free[ti] = vye
                        ysem = s.sYW if tp2 == 0 else s.sYWb
                        s.op("sync", lambda e, t=t, YE=YE: e.dma_start(
                            out=s.xye_scr[t * 128:(t + 1) * 128, :], in_=YE[:]),
                            waits=[(s.sDVE, vye)], inc=ysem, incn=16)
                        xg_dma[ti] = (ysem, ysem.n)
                        ye_rel2[tp2] = [(ysem, ysem.n)]
                    elif tp2 == 0:
                        s.op("gpsimd", lambda e, t=t, YE=YE: e.dma_start(
                            out=s.xye_scr[t * 128:(t + 1) * 128, :], in_=YE[:],
                            accum_op=ALU.add),
                            waits=[(s.sDVE, vye)] + v_h0w, inc=s.sYB, incn=16)
                        ye_rel2[tp2] = [(s.sYB, s.sYB.n)]
                    else:
                        s.op("gpsimd", lambda e, t=t, YE=YE: e.dma_start(
                            out=s.xye_scr[t * 128:(t + 1) * 128, :], in_=YE[:],
                            accum_op=ALU.add),
                            waits=[(s.sDVE, vye)] + v_h0w, inc=s.sYA, incn=16)
                        ye_rel2[tp2] = [(s.sYA, s.sYA.n)]
                hfm_free2[xb] = s.sPE.n
        s.v_ya = s.sYA.n
        s.v_yb = s.sYB.n
        s.v_yw = s.sYW.n
        s.v_ywb = s.sYWb.n
        s.e_pe = s.sPE.n
        s.e_dve = s.sDVE.n
        s.e_act = s.sACT.n

    def phase_ctx(self):
        s, c = self, self.c
        H, NB = c.H, c.NB
        UCG = s.sb("UCG", [128, 2112])
        UCG2 = s.sb("UCG2", [128, 2112])
        EMX = s.sb("EMX", [128, 1])
        MX8 = s.sb("MX8", [128, 8])
        SBC = s.sb("SBC", [128, 1024])
        S8C = s.sb("S8C", [128, 8])
        pCT = [s.PF[0][:, 0:128], s.PF[1][:, 0:128]]
        pB0 = s.PF[2][0:65, 0:1]
        B0C = s.sb("B0C", [65, 1])
        pB0T = s.PF[3][0:1, 0:65]
        pSB = [s.PF[4][:, 0:512], s.PF[5][:, 0:512]]
        SRWS = s.sb("SRWS", [1, 1024])

        def ctxa(b, h, mc):
            off = ((b * H + h) * 2 + mc) * 65
            return s.CTXA[:, off:off + 65]

        def b0t(b, h):
            off = (b * H + h) * 65
            return s.B0T[:, off:off + 65]
        s.ctxa_view = ctxa
        s.b0t_view = b0t

        bar = [(s.sPE, s.e_pe), (s.sDVE, s.e_dve), (s.sACT, s.e_act)]
        pko = s.pk_out.rearrange("(r p) w -> r p w", p=128)
        for r in range(8):
            s.op("sync", lambda e, r=r: e.dma_start(
                out=MX8[:, r:r + 1], in_=pko[r, :, 2096:2097],
                allow_slow_non_contiguous=True),
                 waits=[(s.sC2b, s.v_c2b)], inc=s.sMG, incn=16)
        vmx8 = s.sMG.n
        s.op("vector", lambda e: e.tensor_reduce(out=EMX[:], in_=MX8[:], axis=AX.X,
                                                 op=ALU.max),
             waits=[(s.sMG, vmx8), (s.sDVE, s.e_dve)], inc=s.sDVE)
        vmx = s.sDVE.n
        s.op("scalar", lambda e: e.activation(out=EMX[:], in_=EMX[:], func=AF.Exp,
                                              bias=math.log(c.ratio), scale=-1.0),
             waits=[(s.sDVE, vmx), (s.sACT, s.e_act)], inc=s.sACT)
        s.v_emx = s.sACT.n

        ct_free = [0, 0]
        ucg_free = 0
        sbc_ms = 0
        for b in range(NB):
            s.op("sync", lambda e, b=b: e.dma_start(out=UCG[:, 0:2105], in_=pko[2 * b, :, 0:2105]),
                 waits=[(s.sC2b, s.v_c2b), (s.sDVE, ucg_free), (s.sMG, s.sMG.n)],
                 inc=s.sMG, incn=16)
            s.op("sync", lambda e, b=b: e.dma_start(out=UCG2[:, 0:2105], in_=pko[2 * b + 1, :, 0:2105]),
                 inc=s.sMG, incn=16)
            v1 = s.sMG.n
            s.op("vector", lambda e: e.tensor_tensor(out=UCG[:, 0:2105], in0=UCG[:, 0:2105],
                                                     in1=UCG2[:, 0:2105], op=ALU.add),
                 waits=[(s.sMG, v1), (s.sDVE, s.sDVE.n)], inc=s.sDVE)
            v_cmb = s.sDVE.n
            s.op("vector", lambda e: e.tensor_scalar(
                out=S8C[:], in0=UCG[:, 2097:2105], scalar1=c.ratio * c.EPS, scalar2=None,
                op0=ALU.mult), waits=[(s.sDVE, v_cmb)], inc=s.sDVE)
            v_s8c = s.sDVE.n
            s.op("sync", lambda e: e.dma_start(
                out=s.srow_dram.rearrange("o (p f) -> (o p) f", p=128), in_=S8C[:]),
                waits=[(s.sDVE, v_s8c), (s.sSR, s.sSR.n)], inc=s.sSR, incn=16)
            vsr = s.sSR.n
            SRW = SRWS
            s.op("sync", lambda e, SRW=SRW: e.dma_start(out=SRW[:], in_=s.srow_dram),
                 waits=[(s.sSR, vsr), (s.sPE, s.sPE.n)], inc=s.sSR, incn=16)
            vsr2 = s.sSR.n
            for hf in range(2):
                s.op("tensor", lambda e, hf=hf, SRW=SRW: e.matmul(
                    out=pSB[hf], lhsT=s.O1x128f[:], rhs=SRW[:, hf * 512:(hf + 1) * 512],
                    start=True, stop=True),
                    waits=[(s.sSR, vsr2), (s.sDVE, sbc_ms)] + (bar if b == 0 and hf == 0 else []),
                    inc=s.sPE)
                vps = s.sPE.n
                s.copy("vector", SBC[:, hf * 512:(hf + 1) * 512], pSB[hf],
                       waits=[(s.sPE, vps)])
            s.op("vector", lambda e: e.nop(), inc=s.sDVE)
            v_sbc = s.sDVE.n
            for hp in range(H // 2):
                for mc in range(2):
                    par = mc
                    s.op("tensor", lambda e, hp=hp, mc=mc, par=par: e.transpose(
                        out=pCT[par],
                        in_=UCG[0:128, hp * 256 + mc * 128: hp * 256 + (mc + 1) * 128],
                        identity=s.IDE[:]),
                        waits=[(s.sDVE, v_cmb), (s.sDVE, ct_free[par])], inc=s.sPE)
                    vt = s.sPE.n
                    for hh in range(2):
                        h = hp * 2 + hh
                        s.op("vector", lambda e, b=b, h=h, hh=hh, mc=mc, par=par: e.scalar_tensor_tensor(
                            out=ctxa(b, h, mc)[:, 0:64],
                            in0=pCT[par][:, hh * 64:(hh + 1) * 64],
                            scalar=EMX[:, 0:1], in1=SBC[:, h * 64:(h + 1) * 64],
                            op0=ALU.mult, op1=ALU.add),
                            waits=[(s.sPE, vt), (s.sACT, s.v_emx), (s.sDVE, v_sbc)],
                            inc=s.sDVE)
                        ucol = 2048 + (hh * 2 + mc) * 8 + hp
                        s.op("vector", lambda e, b=b, h=h, mc=mc, ucol=ucol: e.tensor_scalar(
                            out=ctxa(b, h, mc)[:, 64:65], in0=UCG[:, ucol:ucol + 1],
                            scalar1=EMX[:, 0:1], scalar2=c.ratio * c.EPS * c.N,
                            op0=ALU.mult, op1=ALU.add), inc=s.sDVE)
                    ct_free[par] = s.sDVE.n
                vcx = s.sDVE.n
                for hh in range(2):
                    h = hp * 2 + hh
                    for mc in range(2):
                        s.op("tensor", lambda e, b=b, h=h, mc=mc: e.matmul(
                            out=pB0, lhsT=ctxa(b, h, mc)[:], rhs=s.O128x1b[:],
                            start=(mc == 0), stop=(mc == 1)),
                            waits=[(s.sDVE, vcx)] if mc == 0 else (),
                            inc=s.sPE if mc == 1 else None)
                    vb0 = s.sPE.n
                    s.copy("vector", B0C[:], pB0, waits=[(s.sPE, vb0)])
                    vb0c = s.op("vector", lambda e: e.nop(), inc=s.sDVE)
                    s.op("tensor", lambda e: e.transpose(out=pB0T, in_=B0C[:], identity=s.IDE[0:65, 0:65]),
                         waits=[(s.sDVE, vb0c)], inc=s.sPE)
                    vb0t = s.sPE.n
                    s.copy("vector", b0t(b, h), pB0T, waits=[(s.sPE, vb0t)])
                    s.op("vector", lambda e: e.nop(), inc=s.sDVE)
            ucg_free = s.sDVE.n
            sbc_ms = s.sDVE.n
        s.v_ctx = s.sDVE.n
        s.ctx_pe = s.sPE.n

    def phase_b(self):
        s, c = self, self.c
        H = c.H
        TW = c.BGT * 128
        XEGB = [s.sb("XEGB0", [128, 8 * TW], BF16), s.sb("XEGB1", [128, 8 * TW], BF16)]
        QFM = s.sb("QFM", [128, 8 * TW], BF16)
        SQ2 = [s.sb("SQ20", [128, TW], BF16), s.sb("SQ21", [128, TW], BF16)]
        DIAGQ = s.sb("DIAGQ", [128, H * c.BGT])
        MXQG = s.sb("MXQG", [128, H * c.BGT])
        SUMW = s.sb("SUMW", [128, H * c.BGT])
        WTM = s.sb("WTM", [128, H * c.BGT], BF16)
        WTS = s.sb("WTS", [H * c.BGT, 128], BF16)
        WROWS = [s.sb("WROWS0", [1, H * TW], BF16), s.sb("WROWS1", [1, H * TW], BF16)]
        E0B = s.sb("E0B", [128, 2 * TW], BF16)
        RCP = s.sb("RCP", [1, 512])
        RCPB = s.sb("RCPB", [1, 512], BF16)
        RBS = s.sb("RBS", [64, 512])
        ATTN = s.sb("ATTN", [128, 8 * TW], BF16)
        XYET = [s.sb("XYET0", [128, 1024]), s.sb("XYET1", [128, 1024])]
        OUTT = [s.sb("OUTT0", [128, 1024]), s.sb("OUTT1", [128, 1024])]
        pQ = [s.PF[0], s.PF[1]]
        pDGQ = s.PF[2][:, 0:64]
        pD2 = [s.PF[2], s.PF[3]]
        pO = [s.PF[4][0:65, :], s.PF[5][0:65, :]]
        pWT = s.PB[0][0:H * c.BGT, 0:128]
        pRB = s.PF[0][0:64, :]
        wr = s.wrow_dram.rearrange("(g r) f -> g r f", r=H * c.BGT)
        scr = s.xefm_scr.rearrange("(gg p) f -> gg p f", p=128)

        bar = [(s.sPE, s.ctx_pe), (s.sDVE, s.v_ctx)]
        q_free = [0, 0]
        sq_free = [0, 0]
        d2_free_act = [0, 0]
        d2_free_dve = [0, 0]
        o_free = [0, 0]
        xegb_ms = [0, 0]
        xyet_ms = [0, 0]
        out_free = [0, 0]
        outt_dma = [0, 0]
        rbs_free = 0
        qfm_free = 0
        e0_free = 0
        attn_free = 0
        dgq_ms = 0
        bact = s.sACT.n

        for gi in range(c.BG):
            b = (gi * c.BGT) // c.CELLT
            tiles = list(range(gi * c.BGT, (gi + 1) * c.BGT))
            par = gi % 2
            for ti, t in enumerate(tiles):
                ge, tt = t // c.EGT, t % c.EGT
                ETW = c.EGT * 128
                s.op("sync", lambda e, ge=ge, tt=tt, ti=ti, par=par, ETW=ETW, TW=TW: e.dma_start(
                    out=XEGB[par][:].rearrange("p (ch w) -> p ch w", ch=8)[:, :, ti * 128:(ti + 1) * 128],
                    in_=scr[ge].rearrange("p (ch w) -> p ch w", ch=8)[:, :, tt * 128:(tt + 1) * 128]),
                    waits=[(s.sXF2[0], s.sXF2[0].n), (s.sXF2[1], s.sXF2[1].n),
                           (s.sPE, xegb_ms[par]), (s.sYA, s.v_ya)],
                    inc=s.sBR[1], incn=16)
            vxb = s.sBR[1].n
            for oc in range(8):
                qp_ = oc % 2
                w0 = [(s.sBR[1], vxb), (s.sWA, s.v_wqo), (s.sDVE, q_free[qp_]),
                      (s.sPE, qfm_free), (s.sDVE, dgq_ms)] + (bar if gi == 0 and oc == 0 else [])
                for kc in range(8):
                    s.op("tensor", lambda e, oc=oc, kc=kc, qp_=qp_, par=par, TW=TW: e.matmul(
                        out=pQ[qp_][:, 0:TW],
                        lhsT=s.WQB[:, kc * 1024 + oc * 128: kc * 1024 + (oc + 1) * 128],
                        rhs=XEGB[par][:, kc * TW:(kc + 1) * TW],
                        start=(kc == 0), stop=(kc == 7)),
                        waits=w0 if kc == 0 else (), inc=s.sPE if kc == 7 else None)
                vq = s.sPE.n
                s.op("vector", lambda e, oc=oc, qp_=qp_, TW=TW: e.tensor_copy(
                    out=QFM[:, oc * TW:(oc + 1) * TW], in_=pQ[qp_][:, 0:TW]),
                    waits=[(s.sPE, vq), (s.sDVE, s.sDVE.n)], inc=s.sDVE)
                q_free[qp_] = s.sDVE.n
                vqe = s.sDVE.n
                s.op("vector", lambda e, oc=oc, qp_=qp_, TW=TW: e.tensor_tensor(
                    out=SQ2[qp_][:], in0=QFM[:, oc * TW:(oc + 1) * TW],
                    in1=QFM[:, oc * TW:(oc + 1) * TW], op=ALU.mult),
                    waits=[(s.sDVE, vqe), (s.sPE, sq_free[qp_])], inc=s.sDVE)
                vsq = s.sDVE.n
                for s2 in range(c.BGT):
                    s.op("tensor", lambda e, oc=oc, s2=s2, qp_=qp_: e.matmul(
                        out=pDGQ[:, s2 * 16:(s2 + 1) * 16],
                        lhsT=SQ2[qp_][:, s2 * 128:(s2 + 1) * 128],
                        rhs=s.OD16P[:, oc * 16:(oc + 1) * 16],
                        start=(oc == 0 and s2 == 0), stop=(oc == 7 and s2 == c.BGT - 1)),
                        waits=[(s.sDVE, vsq)] if s2 == 0 else (),
                        inc=s.sPE if s2 == c.BGT - 1 else None)
                sq_free[qp_] = s.sPE.n
            vqf = s.sDVE.n
            vdq = s.sPE.n
            s.op("vector", lambda e: e.tensor_copy(
                out=DIAGQ[:, 0:16 * c.BGT].rearrange("p (h q) -> p h q", h=16, q=c.BGT),
                in_=pDGQ[:, 0:16 * c.BGT].rearrange("p (q h) -> p h q", q=c.BGT, h=16)),
                waits=[(s.sPE, vdq), (s.sDVE, s.sDVE.n)], inc=s.sDVE)
            vdqr = s.sDVE.n
            for hp in range(8):
                for s2 in range(c.BGT):
                    dpar = (hp * c.BGT + s2) % 2
                    s.op("tensor", lambda e, hp=hp, s2=s2, dpar=dpar, TW=TW: e.matmul(
                        out=pD2[dpar][:], lhsT=QFM[:, hp * TW + s2 * 128: hp * TW + (s2 + 1) * 128],
                        rhs=s.PBD[:], start=True, stop=True),
                        waits=[(s.sDVE, vqf), (s.sDVE, d2_free_dve[dpar]),
                               (s.sACT, d2_free_act[dpar]), (s.sDVE, vdqr)],
                        inc=s.sPE)
                    vdd = s.sPE.n
                    for hh in range(2):
                        col = (2 * hp + hh) * c.BGT + s2
                        s.op("vector", lambda e, col=col, hh=hh, dpar=dpar: e.tensor_reduce(
                            out=MXQG[:, col:col + 1], in_=pD2[dpar][:, hh * 256:(hh + 1) * 256],
                            axis=AX.X, op=ALU.max),
                            waits=[(s.sPE, vdd)], inc=s.sDVE)
                    d2_free_dve[dpar] = s.sDVE.n
            vmxq = s.sDVE.n
            s.op("vector", lambda e: e.tensor_tensor(out=SUMW[:], in0=DIAGQ[:], in1=MXQG[:],
                                                     op=ALU.add),
                 waits=[(s.sDVE, vmxq)], inc=s.sDVE)
            vsw = s.sDVE.n
            s.op("scalar", lambda e: e.activation(out=WTM[:], in_=SUMW[:], func=AF.Exp,
                                                  bias=c.LNEPS, scale=1.0),
                 waits=[(s.sDVE, vsw), (s.sACT, bact)], inc=s.sACT)
            vwtm = s.sACT.n
            bact = vwtm
            s.op("tensor", lambda e: e.transpose(out=pWT, in_=WTM[:], identity=s.IDEB[:]),
                 waits=[(s.sACT, vwtm)], inc=s.sPE)
            vwt = s.sPE.n
            s.copy("vector", WTS[:], pWT, waits=[(s.sPE, vwt)])
            vwts = s.op("vector", lambda e: e.nop(), inc=s.sDVE)
            s.op("sync", lambda e, gi=gi: e.dma_start(out=wr[gi], in_=WTS[:]),
                 waits=[(s.sDVE, vwts), (s.sWR[par], s.sWR[par].n)], inc=s.sWR[par], incn=16)
            vw1 = s.sWR[par].n
            s.op("sync", lambda e, gi=gi, par=par: e.dma_start(
                out=WROWS[par][:], in_=wr[gi].rearrange("r f -> (r f)").rearrange("(o q) -> o q", o=1)),
                waits=[(s.sWR[par], vw1)], inc=s.sWR[par], incn=16)
            vw2 = s.sWR[par].n
            for h in range(H):
                hh, hp = h % 2, h // 2
                opar = h % 2
                for mc in range(2):
                    dpar = mc
                    s.op("tensor", lambda e, hh=hh, hp=hp, mc=mc, dpar=dpar, TW=TW: e.matmul(
                        out=pD2[dpar][:, 0:TW],
                        lhsT=s.PBDT[:, (hh * 2 + mc) * 128:(hh * 2 + mc + 1) * 128],
                        rhs=QFM[:, hp * TW:(hp + 1) * TW], start=True, stop=True),
                        waits=[(s.sDVE, d2_free_dve[dpar]), (s.sACT, d2_free_act[dpar]),
                               (s.sPE, e0_free)],
                        inc=s.sPE)
                    vdd = s.sPE.n
                    s.op("scalar", lambda e, mc=mc, dpar=dpar, TW=TW: e.activation(
                        out=E0B[:, mc * TW:(mc + 1) * TW], in_=pD2[dpar][:, 0:TW],
                        func=AF.Exp, bias=0.0, scale=1.0),
                        waits=[(s.sPE, vdd), (s.sACT, bact)], inc=s.sACT)
                    d2_free_act[dpar] = s.sACT.n
                    bact = s.sACT.n
                    if mc == 0:
                        ve0a = s.sACT.n
                ve0 = s.sACT.n
                s.op("tensor", lambda e, b=b, h=h, opar=opar, TW=TW: e.matmul(
                    out=pO[opar][:, 0:TW], lhsT=s.ctxa_view(b, h, 0)[:], rhs=E0B[:, 0:TW],
                    start=True, stop=False),
                    waits=[(s.sACT, ve0a), (s.sDVE, o_free[opar]), (s.sDVE, s.v_ctx)])
                s.op("tensor", lambda e, b=b, h=h, opar=opar, TW=TW: e.matmul(
                    out=pO[opar][:, 0:TW], lhsT=s.ctxa_view(b, h, 1)[:],
                    rhs=E0B[:, TW:2 * TW], start=False, stop=False),
                    waits=[(s.sACT, ve0)])
                s.op("tensor", lambda e, b=b, h=h, opar=opar, par=par, TW=TW: e.matmul(
                    out=pO[opar][:, 0:TW], lhsT=s.b0t_view(b, h)[:],
                    rhs=WROWS[par][:, h * TW:(h + 1) * TW],
                    start=False, stop=True),
                    waits=[(s.sWR[par], vw2)], inc=s.sPE)
                e0_free = s.sPE.n
                vO = s.sPE.n
                s.op("vector", lambda e, opar=opar, TW=TW: e.tensor_copy(
                    out=RCPB[:, 0:TW], in_=pO[opar][64:65, 0:TW]),
                    waits=[(s.sPE, vO), (s.sDVE, s.sDVE.n)], inc=s.sDVE)
                vrcb = s.sDVE.n
                s.op("tensor", lambda e, TW=TW: e.matmul(
                    out=pRB[:, 0:TW], lhsT=s.O1x64b[:], rhs=RCPB[:, 0:TW],
                    start=True, stop=True),
                    waits=[(s.sDVE, vrcb), (s.sDVE, rbs_free)], inc=s.sPE)
                vrb = s.sPE.n
                s.op("scalar", lambda e, TW=TW: e.activation(
                    out=RBS[0:64, 0:TW], in_=pRB[0:64, 0:TW], func=AF.Ln,
                    bias=0.0, scale=1.0),
                    waits=[(s.sPE, vrb), (s.sDVE, rbs_free), (s.sACT, s.sACT.n)],
                    inc=s.sACT)
                vln = s.sACT.n
                s.op("scalar", lambda e, TW=TW: e.activation(
                    out=RBS[0:64, 0:TW], in_=RBS[0:64, 0:TW], func=AF.Exp,
                    bias=0.0, scale=-1.0),
                    waits=[(s.sACT, vln)], inc=s.sACT)
                vrbs = s.sACT.n
                bact = vrbs
                s.op("vector", lambda e, hh=hh, hp=hp, opar=opar, TW=TW: e.scalar_tensor_tensor(
                    out=ATTN[hh * 64:hh * 64 + 64, hp * TW:(hp + 1) * TW],
                    in0=pO[opar][0:64, 0:TW], scalar=1.0, in1=RBS[0:64, 0:TW],
                    op0=ALU.mult, op1=ALU.mult),
                    waits=[(s.sACT, vrbs), (s.sPE, attn_free), (s.sDVE, s.sDVE.n)],
                    inc=s.sDVE)
                o_free[opar] = s.sDVE.n
                rbs_free = s.sDVE.n
            vattn = s.sDVE.n
            for ti, t in enumerate(tiles):
                tp = ti % 2
                s.op("sync", lambda e, t=t, tp=tp: e.dma_start(
                    out=XYET[tp][:], in_=s.xye_scr[t * 128:(t + 1) * 128, :]),
                    waits=[(s.sYA, s.v_ya), (s.sYB, s.v_yb), (s.sYW, s.v_yw),
                           (s.sYWb, s.v_ywb),
                           (s.sDVE, xyet_ms[tp]), (s.sBR[1], s.sBR[1].n)],
                    inc=s.sBR[1], incn=16)
                vxy = s.sBR[1].n
                for dh in range(2):
                    wpar = dh
                    for kc in range(8):
                        s.op("tensor", lambda e, ti=ti, dh=dh, kc=kc, wpar=wpar, TW=TW: e.matmul(
                            out=pQ[wpar][:],
                            lhsT=ATTN[:, kc * TW + ti * 128: kc * TW + (ti + 1) * 128],
                            rhs=s.WOB[:, kc * 1024 + dh * 512: kc * 1024 + (dh + 1) * 512],
                            start=(kc == 0), stop=False),
                            waits=[(s.sDVE, vattn), (s.sDVE, q_free[wpar]),
                                   (s.sDVE, out_free[wpar])] if kc == 0 else ())
                    s.op("tensor", lambda e, dh=dh, wpar=wpar: e.matmul(
                        out=pQ[wpar][:], lhsT=s.OB1x128[:], rhs=s.BO2[:, dh * 512:(dh + 1) * 512],
                        start=False, stop=True), waits=[(s.sDVE, s.v_bo2)], inc=s.sPE)
                    vwo = s.sPE.n
                    s.op("vector", lambda e, dh=dh, tp=tp, wpar=wpar: e.tensor_tensor(
                        out=OUTT[tp][:, dh * 512:(dh + 1) * 512], in0=pQ[wpar][:],
                        in1=XYET[tp][:, dh * 512:(dh + 1) * 512], op=ALU.add),
                        waits=[(s.sPE, vwo), (s.sBR[1], vxy), (s.sDVE, s.sDVE.n),
                               (s.sOT[tp], outt_dma[tp])],
                        inc=s.sDVE)
                    q_free[wpar] = s.sDVE.n
                    out_free[wpar] = s.sDVE.n
                vout = s.sDVE.n
                xyet_ms[tp] = vout
                s.op("sync", lambda e, t=t, tp=tp: e.dma_start(
                    out=s.o_rows[t * 128:(t + 1) * 128, :], in_=OUTT[tp][:]),
                    waits=[(s.sDVE, vout)], inc=s.sOT[tp], incn=16)
                outt_dma[tp] = s.sOT[tp].n
            attn_free = s.sPE.n
            qfm_free = s.sPE.n
            xegb_ms[par] = s.sPE.n
            dgq_ms = s.sDVE.n
        s.op("sync", lambda e: e.nop(),
             waits=[(s.sOUT, s.sOUT.n), (s.sOT[0], s.sOT[0].n),
                    (s.sOT[1], s.sOT[1].n)])


def _consts(cfg):
    c = cfg
    dn2 = 0.5 * c.dn * c.dn
    e16n = np.zeros((128, 128), np.float32)
    e16p = np.zeros((128, 128), np.float32)
    for ch in range(8):
        for r in range(128):
            h = 2 * ch + (1 if r >= 64 else 0)
            e16n[r, ch * 16 + h] = -dn2
            e16p[r, ch * 16 + h] = dn2
    oc8 = np.zeros((128, 64), np.float32)
    for hp in range(8):
        oc8[:, hp * 8 + hp] = 1.0
    return {
        "cident": np.eye(128, dtype=np.float32),
        "cidentb": np.eye(128, dtype=np.float32).astype(ml_dtypes.bfloat16),
        "custrict": np.triu(np.ones((128, 128), np.float32), 1),
        "conesd16n": e16n.astype(ml_dtypes.bfloat16),
        "conesd16p": e16p.astype(ml_dtypes.bfloat16),
        "conescol8": oc8.astype(ml_dtypes.bfloat16),
        "cones1x128f": np.ones((1, 128), np.float32),
        "cones1x128b": np.ones((1, 128), ml_dtypes.bfloat16),
        "cones1x64b": np.ones((1, 64), ml_dtypes.bfloat16),
        "cones128x1f": np.ones((128, 1), np.float32),
        "cones128x1b": np.ones((128, 1), ml_dtypes.bfloat16),
        "ciota8": np.tile(np.arange(8, dtype=np.float32)[None, :], (128, 1)),
        "ciotap": np.arange(128, dtype=np.float32)[:, None].copy(),
        "cconstf": np.tile(np.array([0.0, 1e-5, math.log(cfg.EPS),
                                     math.log(cfg.ratio)], np.float32)[None, :], (128, 1)),
    }


_BUILD_CACHE = {}


def _get_nc(T):
    if T not in _BUILD_CACHE:
        _BUILD_CACHE[T] = Builder(Cfg(T)).build()
    return _BUILD_CACHE[T]


def make_in_maps(c, xf, Wq, Wk, Wv, Wo, bo, P, Wr, W1, W2):
    consts = _consts(c)
    in_maps = []
    for core in range(8):
        meta = np.zeros((128, 8), np.float32)
        meta[:, 0] = core
        meta[:, 1] = core * c.TPC
        meta[:, 3] = core - 1 if core % 2 == 1 else 999.0
        meta[:, 4] = (core // 2) * c.CELL
        meta[:, 5] = core * 128
        m = {
            "xmy": np.ascontiguousarray(xf[core * c.TPC:(core + 1) * c.TPC]),
            "xfull": xf,
            "wq": Wq, "wk": Wk, "wv": Wv, "wo": Wo,
            "boi": bo.reshape(1, -1),
            "ppi": P, "wri": Wr,
            "w1i": np.ascontiguousarray(W1[core]),
            "w2i": np.ascontiguousarray(W2[core]),
            "meta": meta,
        }
        m.update(consts)
        in_maps.append(m)
    return in_maps


def kernel(L0_attn_x, L0_attn_Wq, L0_attn_Wk, L0_attn_Wv, L0_attn_Wo, L0_attn_bo,
           L0_attn_P, L0_ffn_Wr, L0_ffn_W1, L0_ffn_W2):
    x = np.ascontiguousarray(np.asarray(L0_attn_x, np.float32))
    NBv, Nv, Dv = x.shape
    T = NBv * Nv
    c = Cfg(T)
    nc = _get_nc(T)
    xf = np.ascontiguousarray(x.reshape(T, Dv))
    in_maps = make_in_maps(
        c, xf,
        np.ascontiguousarray(np.asarray(L0_attn_Wq, np.float32)),
        np.ascontiguousarray(np.asarray(L0_attn_Wk, np.float32)),
        np.ascontiguousarray(np.asarray(L0_attn_Wv, np.float32)),
        np.ascontiguousarray(np.asarray(L0_attn_Wo, np.float32)),
        np.asarray(L0_attn_bo, np.float32),
        np.ascontiguousarray(np.asarray(L0_attn_P, np.float32)),
        np.ascontiguousarray(np.asarray(L0_ffn_Wr, np.float32)),
        np.asarray(L0_ffn_W1, np.float32),
        np.asarray(L0_ffn_W2, np.float32))
    res = run_bass_kernel_spmd(nc, in_maps, core_ids=list(range(8)))
    y = np.zeros((T, Dv), np.float32)
    for core in range(8):
        idx = np.asarray(res.results[core]["o_idx"]).T.reshape(-1)
        rows = np.asarray(res.results[core]["o_rows"])
        v = idx >= 0
        y[idx[v]] = rows[v]
    return y.reshape(NBv, Nv, Dv)


# version 08 (final)
# speedup vs baseline: 1.0461x; 1.0037x over previous
import math
import numpy as np
import ml_dtypes
import concourse.bass as bass
import concourse.mybir as mybir
from concourse.bass_utils import run_bass_kernel_spmd
from contextlib import ExitStack

F32 = mybir.dt.float32
BF16 = mybir.dt.bfloat16
U32 = mybir.dt.uint32
I32 = mybir.dt.int32
AF = mybir.ActivationFunctionType
ALU = mybir.AluOpType
AX = mybir.AxisListType

ENGS = ("sync", "vector", "scalar", "tensor", "gpsimd")


class Cfg:
    def __init__(self, T):
        self.D, self.H, self.DH, self.M, self.E, self.F, self.NB = 1024, 16, 64, 256, 8, 4096, 4
        self.CORES = 8
        self.T = T
        self.TPC = T // self.CORES
        self.N = T // self.NB
        self.CAP = T // self.E
        self.CELL = self.CAP // self.NB + 128
        self.SLOTS = self.CELL * self.NB
        self.TILES = self.SLOTS // 128
        self.CELLT = self.CELL // 128
        self.NT = self.TPC // 128
        self.GA = self.TPC // 512
        assert self.TPC % 512 == 0 and self.CELL % 128 == 0
        self.BGT = 3 if self.CELLT % 3 == 0 else (2 if self.CELLT % 2 == 0 else 1)
        self.BG = self.TILES // self.BGT
        self.EGT = 4 if self.TILES % 4 == 0 else self.BGT
        self.EG = self.TILES // self.EGT
        self.FH = self.F // 2
        self.EPS = 1e-4
        self.dn = float(self.DH) ** (-0.25)
        self.ratio = float(self.M) ** (-0.5)
        self.LNEPS = math.log(self.EPS)
        self.BIG = 1 << 28


class Sem:
    __slots__ = ("h", "name", "n")

    def __init__(self, h, name):
        self.h, self.name, self.n = h, name, 0


class Builder:
    def __init__(self, cfg):
        self.c = cfg
        self.nc = bass.Bass("TRN2", target_bir_lowering=False, debug=False)
        self.sched = {e: [] for e in ENGS}
        self.hw = {}
        self.scopes = []
        self.waited = set()
        self.sem_registry = []
        self.gp_regs = {}
        import os as _os
        self.opcut = int(_os.environ.get("MK_OPCUT", "1000000000"))
        self.opcount = 0
        self.cur_phase = "init"

    def sem(self, name):
        sm = Sem(self.st.enter_context(self.nc.semaphore(name)), name)
        self.sem_registry.append(sm)
        return sm

    def push_scope(self):
        self.scopes.append(ExitStack())
        self.scopes[-1].__enter__()

    def pop_scope(self):
        self.scopes.pop().__exit__(None, None, None)

    def fence(self):
        sems = {"tensor": self.sPE, "vector": self.sDVE, "scalar": self.sACT,
                "gpsimd": self.sGP}
        engset = set(sems.values())
        extra = [(sm, sm.n) for sm in self.sem_registry
                 if sm not in engset and sm.n > 0]
        tot = {}
        for eng, sm in sems.items():
            tot[eng] = self.op(eng, lambda e: e.nop(), inc=sm)
        for eng in ("tensor", "vector", "scalar", "gpsimd", "sync"):
            ws = [(sems[nm], tot[nm]) for nm in sems if nm != eng] + extra
            if eng == "sync":
                ws = [(sems[nm], tot[nm]) for nm in sems] + extra
                self.op("sync", lambda e: e.nop(), waits=ws)
            else:
                self.op(eng, lambda e: e.nop(), waits=ws, inc=sems[eng])

    def op(self, eng, fn, waits=(), inc=None, incn=1):
        ws = []
        for s, v in waits:
            if v <= 0:
                continue
            self.waited.add((s.name, v))
            key = (eng, s.name)
            if self.hw.get(key, 0) >= v:
                continue
            self.hw[key] = v
            ws.append((s, v))
        out = None
        if inc is not None:
            inc.n += incn
            out = inc.n
        self.opcount += 1
        if self.opcount <= self.opcut:
            self.sched[eng].append((ws, fn, inc, incn, out, self.cur_phase))
        return out

    def breg(self, e, val):
        r = self.gp_regs.get(val)
        if r is None:
            r = e.to_reg(val)
            self.gp_regs[val] = r
        return r

    def copy(self, eng, dst, src, waits=()):
        if eng == "scalar":
            return self.op(eng, lambda e: e.activation(out=dst, in_=src, func=AF.Copy,
                                                       bias=0.0, scale=1.0),
                           waits=waits, inc=self.sACT)
        return self.op(eng, lambda e: e.tensor_copy(out=dst, in_=src), waits=waits,
                       inc=self.sDVE)

    def sb(self, name, shape, dt=F32):
        st = self.scopes[-1] if self.scopes else self.st
        return st.enter_context(self.nc.sbuf_tensor(name, shape, dt))

    def sb_outer(self, name, shape, dt=F32):
        return self.st.enter_context(self.nc.sbuf_tensor(name, shape, dt))

    def ps(self, name, shape, dt=F32):
        return self.st.enter_context(self.nc.psum_tensor(name, shape, dt))

    def din(self, n, s, d=F32):
        return self.nc.dram_tensor(n, s, d, kind="ExternalInput").ap()

    def dout(self, n, s, d=F32):
        return self.nc.dram_tensor(n, s, d, kind="ExternalOutput").ap()

    def dint(self, n, s, d=F32):
        return self.nc.dram_tensor(n, s, d).ap()

    def build(self):
        c = self.c
        D, H, DH, M, E = c.D, c.H, c.DH, c.M, c.E

        self.xmy = self.din("xmy", [c.TPC, D])
        self.xfull = self.din("xfull", [c.T, D])
        self.wq = self.din("wq", [D, D]); self.wk = self.din("wk", [D, D])
        self.wv = self.din("wv", [D, D]); self.wo = self.din("wo", [D, D])
        self.boi = self.din("boi", [1, D])
        self.ppi = self.din("ppi", [M, DH])
        self.wri = self.din("wri", [D, E])
        self.w1i = self.din("w1i", [D, c.F])
        self.w2i = self.din("w2i", [c.F, D])
        self.meta = self.din("meta", [128, 8])
        self.cid = self.din("cident", [128, 128])
        self.cidb = self.din("cidentb", [128, 128], BF16)
        self.cust = self.din("custrict", [128, 128])
        self.cd16n = self.din("conesd16n", [128, 128], BF16)
        self.cd16p = self.din("conesd16p", [128, 128], BF16)
        self.coc8 = self.din("conescol8", [128, 64], BF16)
        self.c1x128f = self.din("cones1x128f", [1, 128])
        self.c1x128b = self.din("cones1x128b", [1, 128], BF16)
        self.c1x64b = self.din("cones1x64b", [1, 64], BF16)
        self.c128x1f = self.din("cones128x1f", [128, 1])
        self.c128x1b = self.din("cones128x1b", [128, 1], BF16)
        self.ciota8 = self.din("ciota8", [128, 8])
        self.ciotap = self.din("ciotap", [128, 1])
        self.cconstf = self.din("cconstf", [128, 4])

        self.o_rows = self.dout("o_rows", [c.SLOTS, D])
        self.o_idx = self.dout("o_idx", [128, c.TILES], I32)

        self.rec_in = self.dint("rec_in", [E * c.SLOTS, 2])
        self.rec_out = self.dint("rec_out", [8 * E * c.SLOTS, 2])
        self.cc1_in = self.dint("cc1_in", [1, 8])
        self.cc1_out = self.dint("cc1_out", [8, 8])
        self.PKW = 2048 + 32 + 16 + 1 + 8
        self.pk_in = self.dint("pk_in", [128, self.PKW])
        self.pk_out = self.dint("pk_out", [8 * 128, self.PKW])
        self.xye_scr = self.dint("xye_scr", [c.SLOTS, D])
        self.xefm_scr = self.dint("xefm_scr", [c.EG * 128, 8 * c.EGT * 128], BF16)
        self.wrow_dram = self.dint("wrow_dram", [c.BG * H * c.BGT, 128], BF16)
        self.srow_dram = self.dint("srow_dram", [1, 1024])

        with ExitStack() as st:
            self.st = st
            s = self
            s.sXL = [s.sem("sXL0"), s.sem("sXL1")]
            s.sWA = s.sem("sWA"); s.sSC = s.sem("sSC")
            s.sWE = [s.sem("sWE0"), s.sem("sWE1")]
            s.sC1 = s.sem("sC1"); s.sC2a = s.sem("sC2a"); s.sC2b = s.sem("sC2b")
            s.sCD = s.sem("sCD"); s.sMG = s.sem("sMG")
            s.sEG = [s.sem("sEG0"), s.sem("sEG1")]
            s.sXF = s.sem("sXF"); s.sYW = s.sem("sYW"); s.sYA = s.sem("sYA")
            s.sBR = [s.sem("sBR0"), s.sem("sBR1")]
            s.sWR = [s.sem("sWR0"), s.sem("sWR1")]
            s.sOUT = s.sem("sOUT"); s.sSR = s.sem("sSR")
            s.sOT = [s.sem("sOT0"), s.sem("sOT1")]
            s.sXE = [s.sem("sXE0"), s.sem("sXE1")]
            s.sYB = s.sem("sYB")
            s.sYWb = s.sem("sYWb")
            s.sXF2 = [s.sem("sXF0"), s.sem("sXF1")]
            s.sPE = s.sem("sPE"); s.sDVE = s.sem("sDVE")
            s.sACT = s.sem("sACT"); s.sGP = s.sem("sGP")
            s.sONE = s.sem("sONE")
            s.op("vector", lambda e: e.nop(), inc=s.sONE)

            s.PF = [s.ps(f"PF{i}", [128, 512]) for i in range(6)]
            s.PB = [s.ps(f"PBb{i}", [128, 256], BF16) for i in range(2)]

            FQ = c.F // 4
            s.FQ = FQ
            s.W1Q = [s.sb("W1Q0", [128, 8 * FQ], BF16)]
            s.W2Q = [s.sb("W2Q0", [128, (FQ // 128) * 1024], BF16)]
            s.v_wq = [0, 0, 0, 0]

            s.RSCL = s.sb_outer("RSCL", [128, c.TILES])
            s.GIDX = s.sb_outer("GIDX", [128, c.TILES], U32)
            steps = [
                self.phase0_consts,
                lambda: self.load_w_quarter(0),
                self.push_scope,
                self.load_wkv,
                self.phase_a1,
                self.phase_coll1,
                self.phase_scatter,
                self.phase_a2,
                self.phase_coll2,
                self.phase_merge,
                self.pop_scope,
                self.fence,
                self._alloc_post_a,
                self.load_wq_wo,
                lambda: self.load_w_quarter(1),
                self.push_scope,
                self.phase_expert,
                self.pop_scope,
                self.fence,
                self.push_scope,
                self.phase_ctx,
                self.pop_scope,
                self.fence,
                self.push_scope,
                self.phase_b,
                self.pop_scope,
            ]
            import os as _os
            nsteps = int(_os.environ.get("MK_STEPS", "99"))
            for st_ in steps[:nsteps]:
                self.cur_phase = getattr(st_, "__name__", "misc")
                st_()
            self.cur_phase = "tail"
            while self.scopes:
                self.pop_scope()

            with self.nc.Block() as block:
                def mk(name):
                    def body(e):
                        prev = None
                        cur_scope = None
                        scope_cm = None
                        for ws, fn, inc, incn, val, ph in self.sched[name]:
                            if ph != cur_scope:
                                if scope_cm is not None:
                                    scope_cm.__exit__(None, None, None)
                                scope_cm = self.nc.named_scope(f"{ph}")
                                scope_cm.__enter__()
                                cur_scope = ph
                            if prev is not None and not ws and name != "sync":
                                e.wait_ge(prev[0].h, prev[1])
                            for sm, v in ws:
                                e.wait_ge(sm.h, v)
                            ins = fn(e)
                            if inc is not None:
                                assert ins is not None, f"then_inc on None ({name})"
                                ins.then_inc(inc.h, incn)
                            prev = ((inc, val)
                                    if inc is not None and
                                    (inc.name, val) in self.waited else None)
                        if scope_cm is not None:
                            scope_cm.__exit__(None, None, None)
                    return body
                block.sync(mk("sync"))
                block.vector(mk("vector"))
                block.scalar(mk("scalar"))
                block.tensor(mk("tensor"))
                block.gpsimd(mk("gpsimd"))
        return self.nc

    def _alloc_post_a(self):
        s, c = self, self.c
        FQ = s.FQ
        s.WQB = s.sb("WQB", [128, 8192], BF16)
        s.WOB = s.sb("WOB", [128, 8192], BF16)
        s.CTXA = s.sb("CTXA", [128, c.NB * c.H * 2 * 65], BF16)
        s.B0T = s.sb("B0T", [1, c.NB * c.H * 65], BF16)
        s.W1Q.append(s.sb("W1Q1", [128, 8 * FQ], BF16))
        s.W2Q.append(s.sb("W2Q1", [128, (FQ // 128) * 1024], BF16))

    def phase0_consts(self):
        s, c = self, self.c
        s.IDE = s.sb("IDE", [128, 128])
        s.IDEB = s.sb("IDEB", [128, 128], BF16)
        s.USTR = s.sb("USTR", [128, 128])
        s.OD16N = s.sb("OD16N", [128, 128], BF16)
        s.OD16P = s.sb("OD16P", [128, 128], BF16)
        s.OC8 = s.sb("OC8", [128, 64], BF16)
        s.O1x128f = s.sb("O1x128f", [1, 128])
        s.OB1x128 = s.sb("OB1x128", [1, 128], BF16)
        s.O1x64b = s.sb("O1x64b", [1, 64], BF16)
        s.O128x1f = s.sb("O128x1f", [128, 1])
        s.O128x1b = s.sb("O128x1b", [128, 1], BF16)
        s.IOT8 = s.sb("IOT8", [128, 8])
        s.IOTP = s.sb("IOTP", [128, 1])
        s.META = s.sb("META", [128, 8])
        s.CONSTF = s.sb("CONSTF", [128, 4])
        for dst, src in [(s.IDE, s.cid), (s.IDEB, s.cidb), (s.USTR, s.cust),
                         (s.OD16N, s.cd16n), (s.OD16P, s.cd16p), (s.OC8, s.coc8),
                         (s.O1x128f, s.c1x128f), (s.OB1x128, s.c1x128b),
                         (s.O1x64b, s.c1x64b), (s.O128x1f, s.c128x1f),
                         (s.O128x1b, s.c128x1b), (s.IOT8, s.ciota8),
                         (s.IOTP, s.ciotap), (s.META, s.meta),
                         (s.CONSTF, s.cconstf)]:
            s.op("sync", lambda e, d=dst, sr=src: e.dma_start(out=d[:], in_=sr[:]),
                 inc=s.sXL[0], incn=16)
        s.v_consts = s.sXL[0].n
        s.nc.const_aps.register(F32, 0.0, s.CONSTF[:, 0:1])
        s.nc.const_aps.register(F32, 1e-5, s.CONSTF[:, 1:2])
        s.nc.const_aps.register(F32, c.LNEPS, s.CONSTF[:, 2:3])
        s.nc.const_aps.register(F32, math.log(c.ratio), s.CONSTF[:, 3:4])

        s.PTB = s.sb("PTB", [64, 256], BF16)
        s.PBD = s.sb("PBD", [128, 512], BF16)
        s.PBDT = s.sb("PBDT", [128, 512], BF16)
        s.BO2 = s.sb("BO2", [1, 1024], BF16)
        self.push_scope()
        PFt = s.sb("PFt", [128, 128])
        BOF = s.sb("BOF", [1, 1024])
        s.op("sync", lambda e: e.dma_start(out=PFt[:, 0:64], in_=s.ppi[0:128, :]),
             inc=s.sXL[0], incn=16)
        s.op("sync", lambda e: e.dma_start(out=PFt[:, 64:128], in_=s.ppi[128:256, :]),
             inc=s.sXL[0], incn=16)
        s.op("sync", lambda e: e.dma_start(out=BOF[:], in_=s.boi[:]), inc=s.sXL[0], incn=16)
        v_ph0 = s.sXL[0].n
        s.v_consts = v_ph0
        pP = s.PF[0][0:64, 0:256]
        s.op("tensor", lambda e: e.transpose(out=pP[:, 0:128], in_=PFt[:, 0:64],
                                             identity=s.IDE[:]),
             waits=[(s.sXL[0], v_ph0)])
        s.op("tensor", lambda e: e.transpose(out=pP[:, 128:256], in_=PFt[:, 64:128],
                                             identity=s.IDE[:]), inc=s.sPE)
        v_ptr = s.sPE.n
        s.op("scalar", lambda e: e.activation(out=s.PTB[:], in_=pP, func=AF.Copy,
                                              scale=c.dn, bias=0.0),
             waits=[(s.sPE, v_ptr)], inc=s.sACT)
        v_ptb = s.sACT.n
        s.op("vector", lambda e: e.memset(s.PBD[:], 0.0), inc=s.sDVE)
        s.op("vector", lambda e: e.memset(s.PBDT[:], 0.0), inc=s.sDVE)
        vz = s.sDVE.n
        s.copy("vector", s.PBD[0:64, 0:256], s.PTB[:], waits=[(s.sACT, v_ptb), (s.sDVE, vz)])
        s.copy("vector", s.PBD[64:128, 256:512], s.PTB[:])
        for hh in range(2):
            for mc in range(2):
                j = hh * 2 + mc
                s.copy("vector", s.PBDT[hh * 64:hh * 64 + 64, j * 128:(j + 1) * 128],
                       s.PTB[:, mc * 128:(mc + 1) * 128])
        s.op("vector", lambda e: e.nop(), inc=s.sDVE)
        s.v_pbd = s.sDVE.n

        s.copy("vector", s.BO2[:], BOF[:], waits=[(s.sXL[0], v_ph0)])
        s.op("vector", lambda e: e.nop(), inc=s.sDVE)
        s.v_bo2 = s.sDVE.n

        nzc = 2 * c.E * c.SLOTS // 128
        ZREC = s.sb("ZREC", [128, nzc])
        s.op("vector", lambda e: e.memset(ZREC[:], 0.0), inc=s.sDVE)
        vzr = s.sDVE.n
        s.op("sync", lambda e: e.dma_start(
            out=s.rec_in.rearrange("(p a) w -> p (a w)", p=128), in_=ZREC[:]),
            waits=[(s.sDVE, vzr)], inc=s.sXL[0], incn=16)
        s.v_zrec = s.sXL[0].n
        s.ph0_gate = [(s.sDVE, s.sDVE.n), (s.sPE, s.sPE.n), (s.sACT, s.sACT.n),
                      (s.sXL[0], s.sXL[0].n)]
        self.pop_scope()

    def load_wkv(self):
        s = self
        s.WKB = s.sb("WKB", [128, 8192], BF16)
        s.WVB = s.sb("WVB", [128, 8192], BF16)
        s.WRB = s.sb("WRB", [128, 64], BF16)
        for kc in range(8):
            s.op("gpsimd", lambda e, kc=kc: e.dma_start(
                out=s.WKB[:, kc * 1024:(kc + 1) * 1024], in_=s.wk[kc * 128:(kc + 1) * 128, :]),
                waits=s.ph0_gate if kc == 0 else (), inc=s.sWA, incn=16)
            s.op("gpsimd", lambda e, kc=kc: e.dma_start(
                out=s.WVB[:, kc * 1024:(kc + 1) * 1024], in_=s.wv[kc * 128:(kc + 1) * 128, :]),
                inc=s.sWA, incn=16)
            s.op("gpsimd", lambda e, kc=kc: e.dma_start(
                out=s.WRB[:, kc * 8:(kc + 1) * 8], in_=s.wri[kc * 128:(kc + 1) * 128, :]),
                inc=s.sWA, incn=16)
        s.v_wa = s.sWA.n

    def load_w_quarter(self, q, wait_pe=0):
        s, c = self, self.c
        FQ = s.FQ
        buf = q % 2
        KQ = FQ // 128
        sm = s.sWE[q % 2]
        for kc in range(8):
            s.op("gpsimd", lambda e, q=q, kc=kc, buf=buf, FQ=FQ: e.dma_start(
                out=s.W1Q[buf][:, kc * FQ:(kc + 1) * FQ],
                in_=s.w1i[kc * 128:(kc + 1) * 128, q * FQ:(q + 1) * FQ]),
                waits=[(s.sPE, wait_pe)] if kc == 0 else (), inc=sm, incn=16)
        for kq in range(KQ):
            s.op("gpsimd", lambda e, q=q, kq=kq, buf=buf, FQ=FQ: e.dma_start(
                out=s.W2Q[buf][:, kq * 1024:(kq + 1) * 1024],
                in_=s.w2i[q * FQ + kq * 128: q * FQ + (kq + 1) * 128, :]),
                inc=sm, incn=16)
        s.v_wq[q] = sm.n

    def load_wq_wo(self):
        s = self
        for kc in range(8):
            s.op("gpsimd", lambda e, kc=kc: e.dma_start(
                out=s.WQB[:, kc * 1024:(kc + 1) * 1024], in_=s.wq[kc * 128:(kc + 1) * 128, :]),
                inc=s.sWA, incn=16)
            s.op("gpsimd", lambda e, kc=kc: e.dma_start(
                out=s.WOB[:, kc * 1024:(kc + 1) * 1024], in_=s.wo[kc * 128:(kc + 1) * 128, :]),
                inc=s.sWA, incn=16)
        s.v_wqo = s.sWA.n

    def phase_a1(self):
        s, c = self, self.c
        D = c.D
        s.XNFM = s.sb("XNFM", [128, c.TPC * 8], BF16)
        XL = [s.sb("XL0", [128, 1024]), s.sb("XL1", [128, 1024])]
        XNF = s.sb("XNF", [128, 1024])
        XNH = [s.sb("XNH0", [128, 1024], BF16), s.sb("XNH1", [128, 1024], BF16)]
        LNS2 = [s.sb("LNS0", [128, 8]), s.sb("LNS1", [128, 8])]
        TTRS = XNF
        pTR = [s.PB[0], s.PB[1]]
        pRT = s.PF[0][0:8, :]
        RTF = s.sb("RTF", [8, 512])
        pRTT = s.PF[1][:, 0:32]
        LGT = s.sb("LGT", [128, 32])
        TOP8 = s.sb("TOP8", [128, 8])
        TI8 = s.sb("TI8", [128, 8], U32)
        EXPS = s.sb("EXPS", [128, 8])
        pLR = s.PF[2][:, 0:8]
        pCN = s.PF[2][0:1, 8:16]
        TMP8 = s.sb("TMP8", [128, 8])
        s.SEID = s.sb("SEID", [128, c.NT])
        s.SP = s.sb("SP", [128, c.NT])
        s.SLR = s.sb("SLR", [128, c.NT])
        s.SOH = s.sb("SOH", [128, 8 * c.NT])
        s.BASEC = [s.sb("BASEC0", [1, 8]), s.sb("BASEC1", [1, 8])]

        s.op("vector", lambda e: e.memset(s.BASEC[0][:], 0.0), inc=s.sDVE)
        base_ms = s.sDVE.n
        act_chain = 0
        xnf_free = 0
        lns1_free = [0, 0]

        xl_free = [0, 0]
        xnh_free = [0, 0]
        tr_free = [(s.sDVE, 0), (s.sDVE, 0)]
        rt_free = 0
        s.fm_done = {}
        for ts in range(c.NT):
            par = ts % 2
            g, sub = ts // 4, ts % 4
            s.op("sync", lambda e, ts=ts, par=par: e.dma_start(
                out=XL[par][:], in_=s.xmy[ts * 128:(ts + 1) * 128, :]),
                waits=[(s.sDVE, xl_free[par])] + (s.ph0_gate if ts < 2 else []),
                inc=s.sXL[par], incn=16)
            vload = s.sXL[par].n
            LNS = LNS2[par]
            s.op("vector", lambda e, par=par, LNS=LNS: e.tensor_reduce(
                out=LNS[:, 0:1], in_=XL[par][:], axis=AX.X, op=ALU.add),
                waits=[(s.sXL[par], vload), (s.sDVE, s.sDVE.n)], inc=s.sDVE)
            vst = s.sDVE.n
            s.op("scalar", lambda e, par=par, LNS=LNS: e.activation(
                out=TTRS[:], in_=XL[par][:], func=AF.Square, bias=0.0, scale=1.0,
                accum_out=LNS[:, 1:2]),
                waits=[(s.sXL[par], vload), (s.sDVE, xnf_free),
                       (s.sDVE, lns1_free[par]), (s.sACT, act_chain)], inc=s.sACT)
            vsq = s.sACT.n
            s.op("vector", lambda e, LNS=LNS: e.tensor_scalar(
                out=LNS[:, 2:3], in0=LNS[:, 0:1], scalar1=1.0 / D, scalar2=None,
                op0=ALU.mult), waits=[(s.sDVE, vst)], inc=s.sDVE)
            vmean = s.sDVE.n
            s.op("vector", lambda e, LNS=LNS: e.tensor_tensor(
                out=LNS[:, 3:4], in0=LNS[:, 2:3], in1=LNS[:, 2:3], op=ALU.mult),
                waits=[(s.sDVE, vmean)], inc=s.sDVE)
            vm2 = s.sDVE.n
            s.op("vector", lambda e, LNS=LNS: e.scalar_tensor_tensor(
                out=LNS[:, 4:5], in0=LNS[:, 1:2], scalar=1.0 / D, in1=LNS[:, 3:4],
                op0=ALU.mult, op1=ALU.subtract),
                waits=[(s.sDVE, vm2), (s.sACT, vsq)], inc=s.sDVE)
            vvar = s.sDVE.n
            lns1_free[par] = vvar
            s.op("scalar", lambda e, LNS=LNS: e.activation(
                out=LNS[:, 5:6], in_=LNS[:, 4:5], func=AF.Ln, bias=1e-5, scale=1.0),
                waits=[(s.sDVE, vvar), (s.sACT, act_chain)], inc=s.sACT)
            va1 = s.sACT.n
            s.op("scalar", lambda e, LNS=LNS: e.activation(
                out=LNS[:, 6:7], in_=LNS[:, 5:6], func=AF.Exp, bias=0.0, scale=-0.5),
                waits=[(s.sACT, va1)], inc=s.sACT)
            vrstd = s.sACT.n
            act_chain = vrstd
            s.op("vector", lambda e, par=par, LNS=LNS: e.tensor_scalar(
                out=XNF[:], in0=XL[par][:], scalar1=LNS[:, 2:3], scalar2=LNS[:, 6:7],
                op0=ALU.subtract, op1=ALU.mult),
                waits=[(s.sACT, vrstd), (s.sDVE, s.sDVE.n)], inc=s.sDVE)
            vxn = s.sDVE.n
            xl_free[par] = vxn
            s.op("vector", lambda e, par=par: e.tensor_copy(out=XNH[par][:], in_=XNF[:]),
                 waits=[(s.sPE, xnh_free[par]), (s.sDVE, vxn)], inc=s.sDVE)
            vhi = s.sDVE.n
            xnf_free = vhi
            for cp in range(4):
                tp = cp % 2
                trs_, trv_ = tr_free[tp]
                s.op("tensor", lambda e, par=par, cp=cp, tp=tp: e.transpose(
                    out=pTR[tp][:, 0:128], in_=XNH[par][:, (2 * cp) * 128:(2 * cp + 1) * 128],
                    identity=s.IDEB[:]),
                    waits=[(s.sDVE, vhi), (trs_, trv_), (s.sXL[0], s.v_consts)])
                s.op("tensor", lambda e, par=par, cp=cp, tp=tp: e.transpose(
                    out=pTR[tp][:, 128:256],
                    in_=XNH[par][:, (2 * cp + 1) * 128:(2 * cp + 2) * 128],
                    identity=s.IDEB[:]), inc=s.sPE)
                vtp = s.sPE.n
                eng = "vector" if cp % 2 == 0 else "scalar"
                b0 = g * 4096 + (2 * cp) * 512 + sub * 128
                b1 = g * 4096 + (2 * cp + 1) * 512 + sub * 128
                s.copy(eng, s.XNFM[:, b0:b0 + 128], pTR[tp][:, 0:128], waits=[(s.sPE, vtp)])
                lastc = s.copy(eng, s.XNFM[:, b1:b1 + 128], pTR[tp][:, 128:256])
                tr_free[tp] = (s.sDVE, lastc) if eng == "vector" else (s.sACT, lastc)
            xnh_free[par] = s.sPE.n
            vfa, vfd = s.sACT.n, s.sDVE.n
            s.op("vector", lambda e: e.nop(), waits=[(s.sACT, vfa), (s.sDVE, vfd)],
                 inc=s.sDVE)
            s.fm_done[ts] = s.sDVE.n

            if sub == 3:
                for ch in range(8):
                    s.op("tensor", lambda e, g=g, ch=ch: e.matmul(
                        out=pRT, lhsT=s.WRB[:, ch * 8:(ch + 1) * 8],
                        rhs=s.XNFM[:, g * 4096 + ch * 512: g * 4096 + (ch + 1) * 512],
                        start=(ch == 0), stop=(ch == 7)),
                        waits=[(s.sDVE, s.fm_done[ts]), (s.sWA, s.v_wa),
                               (s.sDVE, rt_free)] if ch == 0 else (),
                        inc=s.sPE if ch == 7 else None)
                vrt = s.sPE.n
                s.copy("vector", RTF[:], pRT, waits=[(s.sPE, vrt)])
                s.op("vector", lambda e: e.nop(), inc=s.sDVE)
                vrtf = s.sDVE.n
                for s2 in range(4):
                    s.op("tensor", lambda e, s2=s2: e.transpose(
                        out=pRTT[:, s2 * 8:(s2 + 1) * 8], in_=RTF[:, s2 * 128:(s2 + 1) * 128],
                        identity=s.IDE[0:8, 0:8]),
                        waits=[(s.sDVE, vrtf)] if s2 == 0 else (),
                        inc=s.sPE if s2 == 3 else None)
                vrtt = s.sPE.n
                s.copy("vector", LGT[:], pRTT, waits=[(s.sPE, vrtt)])
                s.op("vector", lambda e: e.nop(), inc=s.sDVE)
                vlgt = s.sDVE.n
                rt_free = vlgt
                for s2 in range(4):
                    ts2 = g * 4 + s2
                    LNR = LNS2[s2 % 2]
                    L = LGT[:, s2 * 8:(s2 + 1) * 8]
                    s.op("vector", lambda e, L=L, LNR=LNR: e.tensor_reduce(
                        out=LNR[:, 6:7], in_=L, axis=AX.X, op=ALU.max),
                        waits=[(s.sDVE, vlgt)], inc=s.sDVE)
                    vmx = s.sDVE.n
                    s.op("vector", lambda e, LNR=LNR: e.tensor_scalar(
                        out=LNR[:, 7:8], in0=LNR[:, 6:7], scalar1=-1.0, scalar2=None,
                        op0=ALU.mult), waits=[(s.sDVE, vmx)], inc=s.sDVE)
                    vnm = s.sDVE.n
                    s.op("scalar", lambda e, L=L, LNR=LNR: e.activation(
                        out=EXPS[:], in_=L, func=AF.Exp, bias=LNR[:, 7:8], scale=1.0,
                        accum_out=LNR[:, 0:1]),
                        waits=[(s.sDVE, vnm), (s.sACT, act_chain)], inc=s.sACT)
                    vse = s.sACT.n
                    act_chain = vse
                    s.op("vector", lambda e, ts2=ts2, LNR=LNR: e.reciprocal(
                        out=s.SP[:, ts2:ts2 + 1], in_=LNR[:, 0:1]),
                        waits=[(s.sACT, vse)], inc=s.sDVE)
                    s.op("vector", lambda e, L=L: e.max(out=TOP8[:], in_=L), inc=s.sDVE)
                    vt8 = s.sDVE.n
                    s.op("vector", lambda e, L=L: e.max_index(
                        out=TI8[:], in_max=TOP8[:], in_values=L),
                        waits=[(s.sDVE, vt8)], inc=s.sDVE)
                    vti = s.sDVE.n
                    s.op("vector", lambda e, ts2=ts2: e.tensor_copy(
                        out=s.SEID[:, ts2:ts2 + 1], in_=TI8[:, 0:1]),
                        waits=[(s.sDVE, vti)], inc=s.sDVE)
                    veid = s.sDVE.n
                    s.op("vector", lambda e, ts2=ts2: e.tensor_scalar(
                        out=s.SOH[:, ts2 * 8:(ts2 + 1) * 8], in0=s.IOT8[:],
                        scalar1=s.SEID[:, ts2:ts2 + 1], scalar2=None, op0=ALU.is_equal),
                        waits=[(s.sDVE, veid)], inc=s.sDVE)
                    voh = s.sDVE.n
                    bsrc = s.BASEC[ts2 % 2]
                    bdst = s.BASEC[(ts2 + 1) % 2]
                    s.op("tensor", lambda e, ts2=ts2: e.matmul(
                        out=pLR, lhsT=s.USTR[:], rhs=s.SOH[:, ts2 * 8:(ts2 + 1) * 8],
                        start=True, stop=False),
                        waits=[(s.sDVE, voh), (s.sDVE, base_ms)])
                    s.op("tensor", lambda e, bsrc=bsrc: e.matmul(
                        out=pLR, lhsT=s.O1x128f[:], rhs=bsrc[:], start=False, stop=True))
                    s.op("tensor", lambda e, ts2=ts2: e.matmul(
                        out=pCN, lhsT=s.O128x1f[:], rhs=s.SOH[:, ts2 * 8:(ts2 + 1) * 8],
                        start=True, stop=True), inc=s.sPE)
                    vlr = s.sPE.n
                    s.op("vector", lambda e, ts2=ts2: e.tensor_tensor(
                        out=TMP8[:], in0=s.SOH[:, ts2 * 8:(ts2 + 1) * 8], in1=pLR,
                        op=ALU.mult), waits=[(s.sPE, vlr)], inc=s.sDVE)
                    vt8b = s.sDVE.n
                    s.op("vector", lambda e, ts2=ts2: e.tensor_reduce(
                        out=s.SLR[:, ts2:ts2 + 1], in_=TMP8[:], axis=AX.X, op=ALU.add),
                        waits=[(s.sDVE, vt8b)], inc=s.sDVE)
                    s.op("vector", lambda e, bsrc=bsrc, bdst=bdst: e.tensor_tensor(
                        out=bdst[:], in0=bsrc[:], in1=pCN, op=ALU.add), inc=s.sDVE)
                    base_ms = s.sDVE.n
        s.a1_dve = s.sDVE.n
        s.fincnt = s.BASEC[c.NT % 2]

    def phase_coll1(self):
        s, c = self, self.c
        s.CNT8 = s.sb("CNT8", [8, 8])
        s.op("gpsimd", lambda e: e.dma_start(out=s.cc1_in, in_=s.fincnt[:]),
             waits=[(s.sDVE, s.a1_dve)], inc=s.sCD, incn=16)
        v1 = s.sCD.n
        s.op("gpsimd", lambda e: e.collective_compute(
            "AllGather", ALU.bypass, replica_groups=[list(range(8))],
            ins=[s.cc1_in], outs=[s.cc1_out]), waits=[(s.sCD, v1)], inc=s.sC1)
        vc1 = s.sC1.n
        s.op("gpsimd", lambda e: e.dma_start(out=s.CNT8[:], in_=s.cc1_out),
             waits=[(s.sC1, vc1)], inc=s.sCD, incn=16)
        s.v_cnt8 = s.sCD.n

        WCOL = s.sb("WCOL", [8, 2])
        pOFF = s.PF[3][0:1, 0:16]
        s.OFFG = s.sb("OFFG", [1, 8]); s.OFFEB = s.sb("OFFEB", [1, 8])
        s.op("gpsimd", lambda e: e.tensor_tensor(
            out=WCOL[:, 0:1], in0=s.IOTP[0:8, :], in1=s.META[0:8, 0:1],
            op=ALU.subtract), waits=[(s.sCD, s.v_cnt8), (s.sXL[0], s.v_consts)],
            inc=s.sGP)
        vws = s.sGP.n
        s.op("gpsimd", lambda e: e.tensor_scalar(
            out=WCOL[:, 0:1], in0=WCOL[:, 0:1], scalar1=0.0, scalar2=None,
            op0=ALU.is_lt), waits=[(s.sGP, vws)], inc=s.sGP)
        s.op("gpsimd", lambda e: e.tensor_tensor(
            out=WCOL[:, 1:2], in0=s.IOTP[0:8, :], in1=s.META[0:8, 3:4],
            op=ALU.subtract), inc=s.sGP)
        vws2 = s.sGP.n
        s.op("gpsimd", lambda e: e.tensor_scalar(
            out=WCOL[:, 1:2], in0=WCOL[:, 1:2], scalar1=0.0, scalar2=None,
            op0=ALU.is_equal), waits=[(s.sGP, vws2)], inc=s.sGP)
        vw = s.sGP.n
        s.op("tensor", lambda e: e.matmul(out=pOFF[:, 0:8], lhsT=WCOL[:, 0:1], rhs=s.CNT8[:],
                                          start=True, stop=True), waits=[(s.sGP, vw)])
        s.op("tensor", lambda e: e.matmul(out=pOFF[:, 8:16], lhsT=WCOL[:, 1:2], rhs=s.CNT8[:],
                                          start=True, stop=True), inc=s.sPE)
        voff = s.sPE.n
        s.copy("vector", s.OFFG[:], pOFF[:, 0:8], waits=[(s.sPE, voff)])
        s.copy("vector", s.OFFEB[:], pOFF[:, 8:16])
        s.op("vector", lambda e: e.nop(), inc=s.sDVE)
        voc = s.sDVE.n
        s.OFFGB = s.sb("OFFGB", [128, 8]); s.OFFEBB = s.sb("OFFEBB", [128, 8])
        pOFB = s.PF[4][:, 0:16]
        s.op("tensor", lambda e: e.matmul(out=pOFB[:, 0:8], lhsT=s.O1x128f[:], rhs=s.OFFG[:],
                                          start=True, stop=True), waits=[(s.sDVE, voc)])
        s.op("tensor", lambda e: e.matmul(out=pOFB[:, 8:16], lhsT=s.O1x128f[:], rhs=s.OFFEB[:],
                                          start=True, stop=True), inc=s.sPE)
        vofb = s.sPE.n
        s.copy("vector", s.OFFGB[:], pOFB[:, 0:8], waits=[(s.sPE, vofb)])
        s.copy("vector", s.OFFEBB[:], pOFB[:, 8:16])
        s.op("vector", lambda e: e.nop(), inc=s.sDVE)
        s.v_offb = s.sDVE.n

    def phase_scatter(self):
        s, c = self, self.c
        TMP8 = s.sb("STMP8", [128, 8])
        DC = s.sb("DC", [128, 16])
        SIDX = s.sb("SIDX", [128, c.NT], U32)
        RECT = s.sb("RECT", [128, 2 * c.NT])

        def gop(fn, waits=(), inc=None):
            return s.op("gpsimd", fn, waits=waits, inc=inc if inc is not None else s.sGP)

        for t in range(c.NT):
            oh = s.SOH[:, t * 8:(t + 1) * 8]
            def tree8(dst, src_ms):
                va = gop(lambda e: e.tensor_tensor(out=TMP8[:, 0:4], in0=TMP8[:, 0:4],
                                                   in1=TMP8[:, 4:8], op=ALU.add),
                         waits=[(s.sGP, src_ms)])
                vb = gop(lambda e: e.tensor_tensor(out=TMP8[:, 0:2], in0=TMP8[:, 0:2],
                                                   in1=TMP8[:, 2:4], op=ALU.add),
                         waits=[(s.sGP, va)])
                return gop(lambda e, dst=dst: e.tensor_tensor(out=dst, in0=TMP8[:, 0:1],
                                                              in1=TMP8[:, 1:2], op=ALU.add),
                           waits=[(s.sGP, vb)])
            v0 = gop(lambda e, oh=oh: e.tensor_tensor(out=TMP8[:], in0=oh, in1=s.OFFGB[:],
                                                      op=ALU.mult),
                     waits=[(s.sDVE, s.v_offb), (s.sGP, s.sGP.n)])
            v1 = tree8(DC[:, 0:1], v0)
            v2 = gop(lambda e, oh=oh: e.tensor_tensor(out=TMP8[:], in0=oh, in1=s.OFFEBB[:],
                                                      op=ALU.mult), waits=[(s.sGP, v1)])
            v3 = tree8(DC[:, 1:2], v2)
            v4 = gop(lambda e, t=t: e.tensor_tensor(out=DC[:, 2:3], in0=s.SLR[:, t:t + 1],
                                                    in1=DC[:, 1:2], op=ALU.add),
                     waits=[(s.sGP, v3)])
            v5 = gop(lambda e, t=t: e.tensor_tensor(out=DC[:, 3:4], in0=s.SLR[:, t:t + 1],
                                                    in1=DC[:, 0:1], op=ALU.add))
            v6 = gop(lambda e: e.tensor_scalar(out=DC[:, 4:5], in0=DC[:, 2:3],
                                               scalar1=float(c.CELL), scalar2=None,
                                               op0=ALU.is_lt), waits=[(s.sGP, v5)])
            v7 = gop(lambda e: e.tensor_scalar(out=DC[:, 5:6], in0=DC[:, 3:4],
                                               scalar1=float(c.CAP), scalar2=None,
                                               op0=ALU.is_lt))
            v8 = gop(lambda e, t=t: e.tensor_tensor(out=DC[:, 6:7], in0=s.SP[:, t:t + 1],
                                                    in1=DC[:, 5:6], op=ALU.mult),
                     waits=[(s.sGP, v7)])
            v9a = gop(lambda e, t=t: e.tensor_scalar(out=DC[:, 7:8], in0=s.IOTP[:],
                                                     scalar1=float(t * 128 + 1),
                                                     scalar2=None, op0=ALU.add))
            v9 = gop(lambda e: e.tensor_tensor(out=DC[:, 7:8], in0=DC[:, 7:8],
                                               in1=s.META[:, 1:2], op=ALU.add),
                     waits=[(s.sGP, v9a)])
            v10 = gop(lambda e: e.tensor_tensor(out=DC[:, 8:9], in0=DC[:, 2:3],
                                                in1=s.META[:, 4:5], op=ALU.add),
                      waits=[(s.sGP, v6)])
            M23 = float(1 << 23)
            n1 = gop(lambda e, M23=M23: e.tensor_scalar(
                out=DC[:, 9:10], in0=DC[:, 8:9], scalar1=1.0 / 128.0, scalar2=M23,
                op0=ALU.mult, op1=ALU.add), waits=[(s.sGP, v10)])
            n2 = gop(lambda e, M23=M23: e.tensor_scalar(
                out=DC[:, 9:10], in0=DC[:, 9:10], scalar1=-M23, scalar2=None,
                op0=ALU.add), waits=[(s.sGP, n1)])
            n3 = gop(lambda e: e.tensor_scalar(
                out=DC[:, 10:11], in0=DC[:, 9:10], scalar1=128.0, scalar2=None,
                op0=ALU.mult), waits=[(s.sGP, n2)])
            n4 = gop(lambda e: e.tensor_tensor(
                out=DC[:, 10:11], in0=DC[:, 8:9], in1=DC[:, 10:11], op=ALU.subtract),
                waits=[(s.sGP, n3)])
            n5 = gop(lambda e: e.tensor_scalar(
                out=DC[:, 11:12], in0=DC[:, 10:11], scalar1=0.0, scalar2=None,
                op0=ALU.is_lt), waits=[(s.sGP, n4)])
            n6 = gop(lambda e: e.tensor_tensor(
                out=DC[:, 9:10], in0=DC[:, 9:10], in1=DC[:, 11:12], op=ALU.subtract),
                waits=[(s.sGP, n5)])
            n7 = gop(lambda e: e.tensor_scalar(
                out=DC[:, 11:12], in0=DC[:, 11:12], scalar1=128.0, scalar2=None,
                op0=ALU.mult), waits=[(s.sGP, n6)])
            n8 = gop(lambda e: e.tensor_tensor(
                out=DC[:, 10:11], in0=DC[:, 10:11], in1=DC[:, 11:12], op=ALU.add),
                waits=[(s.sGP, n7)])
            n9 = gop(lambda e: e.tensor_scalar(
                out=DC[:, 10:11], in0=DC[:, 10:11], scalar1=float(c.TILES),
                scalar2=None, op0=ALU.mult), waits=[(s.sGP, n8)])
            n10 = gop(lambda e: e.tensor_tensor(
                out=DC[:, 9:10], in0=DC[:, 9:10], in1=DC[:, 10:11], op=ALU.add),
                waits=[(s.sGP, n9)])
            n11 = gop(lambda e, t=t: e.tensor_scalar(
                out=DC[:, 11:12], in0=s.SEID[:, t:t + 1], scalar1=float(c.SLOTS),
                scalar2=None, op0=ALU.mult), waits=[(s.sGP, n10)])
            n12 = gop(lambda e: e.tensor_tensor(
                out=DC[:, 9:10], in0=DC[:, 9:10], in1=DC[:, 11:12], op=ALU.add),
                waits=[(s.sGP, n11)])
            n13 = gop(lambda e: e.tensor_tensor(
                out=DC[:, 9:10], in0=DC[:, 9:10], in1=DC[:, 4:5], op=ALU.mult),
                waits=[(s.sGP, n12), (s.sGP, v6)])
            n14 = gop(lambda e: e.tensor_scalar(
                out=DC[:, 11:12], in0=DC[:, 4:5], scalar1=-1.0, scalar2=1.0,
                op0=ALU.mult, op1=ALU.add), waits=[(s.sGP, n13)])
            n15 = gop(lambda e: e.tensor_scalar(
                out=DC[:, 10:11], in0=DC[:, 7:8], scalar1=float(c.BIG), scalar2=None,
                op0=ALU.add), waits=[(s.sGP, v9), (s.sGP, n9)])
            n16 = gop(lambda e: e.tensor_tensor(
                out=DC[:, 10:11], in0=DC[:, 10:11], in1=DC[:, 11:12], op=ALU.mult),
                waits=[(s.sGP, n15), (s.sGP, n14)])
            n17 = gop(lambda e: e.tensor_tensor(
                out=DC[:, 9:10], in0=DC[:, 9:10], in1=DC[:, 10:11], op=ALU.add),
                waits=[(s.sGP, n16)])
            v22 = gop(lambda e, t=t: e.tensor_copy(out=SIDX[:, t:t + 1], in_=DC[:, 9:10]),
                      waits=[(s.sGP, n17)])
            v23 = gop(lambda e, t=t: e.tensor_copy(out=RECT[:, 2 * t:2 * t + 1],
                                                   in_=DC[:, 7:8]), waits=[(s.sGP, v8)])
            v24 = gop(lambda e, t=t: e.tensor_copy(out=RECT[:, 2 * t + 1:2 * t + 2],
                                                   in_=DC[:, 6:7]))
            s.op("gpsimd", lambda e, t=t: e.indirect_dma_start(
                out=s.rec_in,
                out_offset=bass.IndirectOffsetOnAxis(ap=SIDX[:, t:t + 1], axis=0),
                in_=RECT[:, 2 * t:2 * t + 2], in_offset=None,
                bounds_check=s.breg(e, c.E * c.SLOTS - 1), oob_is_err=False),
                waits=[(s.sGP, v24), (s.sGP, v22), (s.sXL[0], s.v_zrec)],
                inc=s.sSC, incn=16)
        s.v_scat = s.sSC.n

    def phase_a2(self):
        s, c = self, self.c
        KFM = s.sb("KFM", [128, 4096], BF16)
        VTM = [s.sb(f"VTM{i}", [128, 1024], BF16) for i in range(4)]
        SQB = [s.sb("SQB0", [128, 512], BF16), s.sb("SQB1", [128, 512], BF16)]
        ATL = [s.sb(f"ATL{i}", [128, 512], BF16) for i in range(2)]
        s.DKTM = s.sb("DKTM", [128, 64])
        MXKG = s.sb("MXKG", [128, 64])
        LNMX = s.sb("LNMX", [128, 64])
        MXR = s.sb("MXR", [128, 2])
        s.UCOMP = s.sb("UCOMP", [128, 2048])
        s.USB = s.sb("USB", [8, 512])
        pKV = [s.PF[0], s.PF[1]]
        pDD = [s.PF[2], s.PF[3]]
        pDT = s.PF[2][:, 0:64]
        pU = s.PF[4]
        pu8 = s.PF[5][0:8, :]
        s.pu8 = pu8

        s.op("vector", lambda e: e.memset(s.UCOMP[:], 0.0), inc=s.sDVE)
        s.op("vector", lambda e: e.memset(MXR[:], -1e30), inc=s.sDVE)
        s.op("vector", lambda e: e.memset(s.USB[:], 0.0), inc=s.sDVE)
        v_z = s.sDVE.n

        kv_free = [s.v_pbd, s.v_pbd]
        dd_free_act = [0, 0]
        sq_free = [0, 0]
        at_free = [0, 0]
        at_free_dve = [0, 0]
        u_guard = v_z
        usb_ms = v_z
        dkt_ms = 0
        for g in range(c.GA):
            gb = g * 4096
            for oc in range(8):
                par = oc % 2
                w0 = [(s.sDVE, kv_free[par]), (s.sWA, s.v_wa),
                      (s.sDVE, s.fm_done[g * 4 + 3]), (s.sDVE, dkt_ms)]
                for kc in range(8):
                    s.op("tensor", lambda e, oc=oc, kc=kc, par=par, gb=gb: e.matmul(
                        out=pKV[par][:],
                        lhsT=s.WKB[:, kc * 1024 + oc * 128: kc * 1024 + (oc + 1) * 128],
                        rhs=s.XNFM[:, gb + kc * 512: gb + (kc + 1) * 512],
                        start=(kc == 0), stop=(kc == 7)),
                        waits=w0 if kc == 0 else (), inc=s.sPE if kc == 7 else None)
                vk = s.sPE.n
                s.op("vector", lambda e, oc=oc, par=par: e.tensor_copy(
                    out=KFM[:, oc * 512:(oc + 1) * 512], in_=pKV[par][:]),
                    waits=[(s.sPE, vk), (s.sDVE, s.sDVE.n)], inc=s.sDVE)
                kv_free[par] = s.sDVE.n
            kfm_ms = s.sDVE.n
            for oc in range(8):
                par = oc % 2
                s.op("vector", lambda e, oc=oc, par=par: e.tensor_tensor(
                    out=SQB[par][:], in0=KFM[:, oc * 512:(oc + 1) * 512],
                    in1=KFM[:, oc * 512:(oc + 1) * 512], op=ALU.mult),
                    waits=[(s.sDVE, kfm_ms), (s.sPE, sq_free[par])], inc=s.sDVE)
                vsq = s.sDVE.n
                for s2 in range(4):
                    s.op("tensor", lambda e, oc=oc, s2=s2, par=par: e.matmul(
                        out=pDT[:, s2 * 16:(s2 + 1) * 16],
                        lhsT=SQB[par][:, s2 * 128:(s2 + 1) * 128],
                        rhs=s.OD16N[:, oc * 16:(oc + 1) * 16],
                        start=(oc == 0 and s2 == 0), stop=(oc == 7 and s2 == 3)),
                        waits=[(s.sDVE, vsq)] if s2 == 0 else (),
                        inc=s.sPE if s2 == 3 else None)
                sq_free[par] = s.sPE.n
            vdt = s.sPE.n
            s.op("vector", lambda e: e.tensor_copy(
                out=s.DKTM[:].rearrange("p (h q) -> p h q", h=16, q=4),
                in_=pDT.rearrange("p (q h) -> p h q", q=4, h=16)),
                waits=[(s.sPE, vdt), (s.sDVE, s.sDVE.n)], inc=s.sDVE)
            dkt_ms = s.sDVE.n
            vdk = dkt_ms
            for s2 in range(4):
                for half in range(2):
                    par = (s2 * 2 + half) % 2
                    w0 = [(s.sDVE, kv_free[par])]
                    for kc in range(8):
                        s.op("tensor", lambda e, s2=s2, half=half, kc=kc, par=par, gb=gb: e.matmul(
                            out=pKV[par][:],
                            lhsT=s.XNFM[:, gb + kc * 512 + s2 * 128: gb + kc * 512 + (s2 + 1) * 128],
                            rhs=s.WVB[:, kc * 1024 + half * 512: kc * 1024 + (half + 1) * 512],
                            start=(kc == 0), stop=(kc == 7)),
                            waits=w0 if kc == 0 else (), inc=s.sPE if kc == 7 else None)
                    vv = s.sPE.n
                    s.op("vector", lambda e, s2=s2, half=half, par=par: e.tensor_copy(
                        out=VTM[s2][:, half * 512:(half + 1) * 512], in_=pKV[par][:]),
                        waits=[(s.sPE, vv), (s.sDVE, s.sDVE.n)], inc=s.sDVE)
                    kv_free[par] = s.sDVE.n
            vtm_ms = s.sDVE.n
            for hp in range(8):
                for s2 in range(4):
                    par = s2 % 2
                    s.op("tensor", lambda e, hp=hp, s2=s2, par=par: e.matmul(
                        out=pDD[par][:],
                        lhsT=KFM[:, hp * 512 + s2 * 128: hp * 512 + (s2 + 1) * 128],
                        rhs=s.PBD[:], start=True, stop=True),
                        waits=[(s.sACT, dd_free_act[par]), (s.sDVE, vtm_ms),
                               (s.sDVE, vdk), (s.sDVE, s.v_pbd)],
                        inc=s.sPE)
                    vdd = s.sPE.n
                    apar = s2 % 2
                    for hh in range(2):
                        col = (2 * hp + hh) * 4 + s2
                        s.op("scalar", lambda e, col=col, hh=hh, par=par, apar=apar: e.activation(
                            out=ATL[apar][:, hh * 256:(hh + 1) * 256],
                            in_=pDD[par][:, hh * 256:(hh + 1) * 256], func=AF.Exp,
                            bias=s.DKTM[:, col:col + 1], scale=1.0),
                            waits=[(s.sPE, vdd), (s.sPE, at_free[apar]),
                                   (s.sDVE, at_free_dve[apar]),
                                   (s.sACT, s.sACT.n)] if hh == 0 else (),
                            inc=s.sACT)
                    vexp = s.sACT.n
                    dd_free_act[par] = vexp
                    for hh in range(2):
                        col = (2 * hp + hh) * 4 + s2
                        s.op("vector", lambda e, col=col, hh=hh, apar=apar: e.tensor_reduce(
                            out=MXKG[:, col:col + 1], in_=ATL[apar][:, hh * 256:(hh + 1) * 256],
                            axis=AX.X, op=ALU.max),
                            waits=[(s.sACT, vexp)], inc=s.sDVE)
                    at_free_dve[apar] = s.sDVE.n
                    s.op("tensor", lambda e, hp=hp, s2=s2, apar=apar: e.matmul(
                        out=pU[:], lhsT=VTM[s2][:, hp * 128:(hp + 1) * 128], rhs=ATL[apar][:],
                        start=(s2 == 0), stop=(s2 == 3)),
                        waits=[(s.sACT, vexp), (s.sDVE, u_guard)])
                    s.op("tensor", lambda e, hp=hp, apar=apar, s2=s2: e.matmul(
                        out=pu8, lhsT=s.OC8[:, hp * 8:(hp + 1) * 8], rhs=ATL[apar][:],
                        start=(s2 == 0), stop=(s2 == 3)),
                        waits=[(s.sDVE, usb_ms)] if s2 == 0 else (), inc=s.sPE)
                    at_free[apar] = s.sPE.n
                vU = s.sPE.n
                s.op("vector", lambda e, hp=hp: e.tensor_tensor(
                    out=s.UCOMP[0:64, hp * 256:(hp + 1) * 256],
                    in0=s.UCOMP[0:64, hp * 256:(hp + 1) * 256],
                    in1=pU[0:64, 0:256], op=ALU.add),
                    waits=[(s.sPE, vU), (s.sDVE, s.sDVE.n)], inc=s.sDVE)
                s.op("vector", lambda e, hp=hp: e.tensor_tensor(
                    out=s.UCOMP[64:128, hp * 256:(hp + 1) * 256],
                    in0=s.UCOMP[64:128, hp * 256:(hp + 1) * 256],
                    in1=pU[64:128, 256:512], op=ALU.add), inc=s.sDVE)
                u_guard = s.sDVE.n
                s.op("vector", lambda e: e.tensor_tensor(
                    out=s.USB[0:8, :], in0=s.USB[0:8, :], in1=pu8, op=ALU.add),
                    waits=[(s.sPE, vU), (s.sDVE, usb_ms)], inc=s.sDVE)
                usb_ms = s.sDVE.n
            vgd = s.sDVE.n
            s.op("scalar", lambda e: e.activation(
                out=LNMX[:], in_=MXKG[:], func=AF.Ln, bias=0.0, scale=1.0),
                waits=[(s.sDVE, vgd), (s.sACT, s.sACT.n)], inc=s.sACT)
            vln = s.sACT.n
            s.op("vector", lambda e: e.tensor_tensor(
                out=LNMX[:], in0=LNMX[:], in1=s.DKTM[:], op=ALU.subtract),
                waits=[(s.sACT, vln), (s.sDVE, s.sDVE.n)], inc=s.sDVE)
            vsub = s.sDVE.n
            s.op("vector", lambda e: e.tensor_reduce(
                out=MXR[:, 0:1], in_=LNMX[:], axis=AX.X, op=ALU.max),
                waits=[(s.sDVE, vsub)], inc=s.sDVE)
            vm0 = s.sDVE.n
            s.op("vector", lambda e: e.tensor_tensor(
                out=MXR[:, 1:2], in0=MXR[:, 1:2], in1=MXR[:, 0:1], op=ALU.max),
                waits=[(s.sDVE, vm0)], inc=s.sDVE)
        s.a2_pe = s.sPE.n
        s.a2_dve = s.sDVE.n
        s.MXR = MXR

    def phase_coll2(self):
        s, c = self, self.c
        pMX = s.PF[0][0:1, 0:128]
        MXROW = s.sb("MXROW", [1, 128])
        s.MXCOL = s.sb("MXCOL", [128, 1])
        s.op("tensor", lambda e: e.transpose(out=pMX, in_=s.MXR[:, 1:2], identity=s.IDE[:]),
             waits=[(s.sDVE, s.a2_dve), (s.sPE, s.a2_pe)], inc=s.sPE)
        v = s.sPE.n
        s.copy("vector", MXROW[:], pMX, waits=[(s.sPE, v)])
        s.op("vector", lambda e: e.nop(), inc=s.sDVE)
        v2 = s.sDVE.n
        s.op("vector", lambda e: e.tensor_reduce(
            out=MXROW[:, 0:1], in_=MXROW[:], axis=AX.X, op=ALU.max),
            waits=[(s.sDVE, v2)], inc=s.sDVE)
        v3 = s.sDVE.n
        pMB = s.PF[1][:, 0:1]
        s.op("tensor", lambda e: e.matmul(out=pMB, lhsT=s.O1x128f[:], rhs=MXROW[:, 0:1],
                                          start=True, stop=True),
             waits=[(s.sDVE, v3)], inc=s.sPE)
        v4 = s.sPE.n
        s.copy("vector", s.MXCOL[:], pMB, waits=[(s.sPE, v4)])
        s.op("vector", lambda e: e.nop(), inc=s.sDVE)
        v_mx = s.sDVE.n

        UTM = s.sb("UTM", [128, 32])
        USUM = s.sb("USUM", [1, 16])
        pUT = s.PF[2][:, 0:32]
        for ch in range(4):
            s.op("tensor", lambda e, ch=ch: e.transpose(
                out=pUT[:, ch * 8:(ch + 1) * 8], in_=s.USB[:, ch * 128:(ch + 1) * 128],
                identity=s.IDE[0:8, 0:8]),
                waits=[(s.sDVE, s.a2_dve)] if ch == 0 else (),
                inc=s.sPE if ch == 3 else None)
        vut = s.sPE.n
        s.copy("vector", UTM[:], pUT, waits=[(s.sPE, vut)])
        s.op("vector", lambda e: e.nop(), inc=s.sDVE)
        v_utm = s.sDVE.n
        pUS = s.PF[3][0:32, 0:1]
        USR = s.sb("USR", [32, 1])
        pUSR = s.PF[0][0:1, 200:232]
        USROW = s.sb("USROW", [1, 32])
        s.op("tensor", lambda e: e.matmul(out=pUS, lhsT=UTM[:], rhs=s.O128x1f[:],
                                          start=True, stop=True),
             waits=[(s.sDVE, v_utm)], inc=s.sPE)
        v5 = s.sPE.n
        s.copy("vector", USR[:], pUS, waits=[(s.sPE, v5)])
        s.op("vector", lambda e: e.nop(), inc=s.sDVE)
        v6 = s.sDVE.n
        s.op("tensor", lambda e: e.transpose(out=pUSR, in_=USR[:], identity=s.IDE[0:32, 0:32]),
             waits=[(s.sDVE, v6)], inc=s.sPE)
        v7 = s.sPE.n
        s.copy("vector", USROW[:], pUSR, waits=[(s.sPE, v7)])
        s.op("vector", lambda e: e.nop(), inc=s.sDVE)
        v8 = s.sDVE.n
        for pr in range(2):
            s.op("vector", lambda e, pr=pr: e.tensor_tensor(
                out=USUM[:, pr:16:2], in0=USROW[:, pr * 16:pr * 16 + 8],
                in1=USROW[:, pr * 16 + 8:pr * 16 + 16], op=ALU.add),
                waits=[(s.sDVE, v8)], inc=s.sDVE)
        v_usum = s.sDVE.n

        SX = s.sb("SX", [128, 8])
        SXP = s.sb("SXP", [128, 8 * c.GA])
        for ch in range(8):
            for g in range(c.GA):
                s.op("vector", lambda e, ch=ch, g=g: e.tensor_reduce(
                    out=SXP[:, ch * c.GA + g:ch * c.GA + g + 1],
                    in_=s.XNFM[:, g * 4096 + ch * 512:g * 4096 + (ch + 1) * 512],
                    axis=AX.X, op=ALU.add),
                    waits=[(s.sDVE, s.a2_dve)], inc=s.sDVE)
        v9 = s.sDVE.n
        for ch in range(8):
            s.op("vector", lambda e, ch=ch: e.tensor_reduce(
                out=SX[:, ch:ch + 1], in_=SXP[:, ch * c.GA:(ch + 1) * c.GA],
                axis=AX.X, op=ALU.add), waits=[(s.sDVE, v9)], inc=s.sDVE)
        v10 = s.sDVE.n
        SXB = s.sb("SXB", [128, 8], BF16)
        s.copy("vector", SXB[:], SX[:], waits=[(s.sDVE, v10)])
        s.op("vector", lambda e: e.nop(), inc=s.sDVE)
        v11 = s.sDVE.n
        pS = [s.PF[4][0:1, :], s.PF[5][0:1, :]]
        SROW = s.sb("SROW", [1, 1024])
        for half in range(2):
            for ch in range(8):
                s.op("tensor", lambda e, half=half, ch=ch: e.matmul(
                    out=pS[half], lhsT=SXB[:, ch:ch + 1],
                    rhs=s.WVB[:, ch * 1024 + half * 512: ch * 1024 + (half + 1) * 512],
                    start=(ch == 0), stop=(ch == 7)),
                    waits=[(s.sDVE, v11)] if (half == 0 and ch == 0) else (),
                    inc=s.sPE if ch == 7 else None)
            vs = s.sPE.n
            s.copy("vector", SROW[:, half * 512:(half + 1) * 512], pS[half],
                   waits=[(s.sPE, vs)])
        s.op("vector", lambda e: e.nop(), inc=s.sDVE)
        v_srow = s.sDVE.n
        s.S8 = s.sb("S8", [128, 8])
        s.op("sync", lambda e: e.dma_start(out=s.srow_dram, in_=SROW[:]),
             waits=[(s.sDVE, v_srow)], inc=s.sSR, incn=16)
        vsr1 = s.sSR.n
        s.op("sync", lambda e: e.dma_start(
            out=s.S8[:], in_=s.srow_dram.rearrange("o (p f) -> (o p) f", p=128)),
            waits=[(s.sSR, vsr1)], inc=s.sSR, incn=16)
        v_s8 = s.sSR.n

        s.op("gpsimd", lambda e: e.dma_start(out=s.pk_in[:, 0:2048], in_=s.UCOMP[:]),
             waits=[(s.sDVE, s.a2_dve)], inc=s.sCD, incn=16)
        s.op("gpsimd", lambda e: e.dma_start(out=s.pk_in[:, 2048:2080], in_=UTM[:]),
             waits=[(s.sDVE, v_utm)], inc=s.sCD, incn=16)
        PKF = s.sb("PKF", [128, 16])
        s.op("vector", lambda e: e.memset(PKF[:], 0.0), waits=[(s.sDVE, s.sDVE.n)],
             inc=s.sDVE)
        vpkz = s.sDVE.n
        s.op("vector", lambda e: e.tensor_copy(out=PKF[0:1, :], in_=USUM[:]),
             waits=[(s.sDVE, vpkz), (s.sDVE, v_usum)], inc=s.sDVE)
        vpkf = s.sDVE.n
        s.op("gpsimd", lambda e: e.dma_start(out=s.pk_in[:, 2080:2096], in_=PKF[:]),
             waits=[(s.sDVE, vpkf)], inc=s.sCD, incn=16)
        s.op("gpsimd", lambda e: e.dma_start(out=s.pk_in[:, 2096:2097], in_=s.MXCOL[:], allow_slow_non_contiguous=True),
             waits=[(s.sDVE, v_mx)], inc=s.sCD, incn=16)
        s.op("gpsimd", lambda e: e.dma_start(out=s.pk_in[:, 2097:2105], in_=s.S8[:]),
             waits=[(s.sSR, v_s8)], inc=s.sCD, incn=16)
        v_pk = s.sCD.n
        s.op("gpsimd", lambda e: e.collective_compute(
            "AllGather", ALU.bypass, replica_groups=[list(range(8))],
            ins=[s.rec_in], outs=[s.rec_out]),
            waits=[(s.sSC, s.v_scat)], inc=s.sC2a)
        s.v_c2a = s.sC2a.n
        s.op("gpsimd", lambda e: e.collective_compute(
            "AllGather", ALU.bypass, replica_groups=[list(range(8))],
            ins=[s.pk_in], outs=[s.pk_out]),
            waits=[(s.sCD, v_pk)], inc=s.sC2b)
        s.v_c2b = s.sC2b.n

    def phase_merge(self):
        s, c = self, self.c
        W = 2 * c.TILES
        MRG = s.sb("MRG", [128, W])
        s.RTID = s.sb("RTID", [128, c.TILES])
        s.IDXO = s.sb("IDXO", [128, c.TILES], I32)
        IDXF = s.sb("IDXF", [128, c.TILES])
        GIF = s.sb("GIF", [128, c.TILES])
        MSK = s.sb("MSK", [128, c.TILES])
        NMSK = s.sb("NMSK", [128, c.TILES])
        RIDX = s.sb("RIDX", [128, 8], U32)
        RIF = s.sb("RIF", [128, 1])
        MSL = [s.sb(f"MSL{i}", [128, W]) for i in range(2)]

        for r in range(8):
            v0a = s.op("gpsimd", lambda e, r=r: e.tensor_scalar(
                out=RIF[:], in0=s.IOTP[:], scalar1=float(r * c.E * 128),
                scalar2=None, op0=ALU.add),
                waits=[(s.sXL[0], s.v_consts), (s.sGP, s.sGP.n)], inc=s.sGP)
            v0 = s.op("gpsimd", lambda e: e.tensor_tensor(
                out=RIF[:], in0=RIF[:], in1=s.META[:, 5:6], op=ALU.add),
                waits=[(s.sGP, v0a)], inc=s.sGP)
            s.op("gpsimd", lambda e, r=r: e.tensor_copy(out=RIDX[:, r:r + 1], in_=RIF[:]),
                 waits=[(s.sGP, v0)], inc=s.sGP)
        v_ridx = s.sGP.n
        recv = s.rec_out.rearrange("(a j) w -> a (j w)", j=c.TILES)
        vml = [0, 0]
        for r in range(8):
            par = r % 2
            s.op("gpsimd", lambda e, r=r, par=par: e.indirect_dma_start(
                out=MSL[par][:], out_offset=None, in_=recv,
                in_offset=bass.IndirectOffsetOnAxis(ap=RIDX[:, r:r + 1], axis=0)),
                waits=[(s.sGP, v_ridx), (s.sC2a, s.v_c2a), (s.sDVE, vml[par])],
                inc=s.sEG[par], incn=16)
            vg = s.sEG[par].n
            if r == 0:
                s.copy("vector", MRG[:], MSL[par][:], waits=[(s.sEG[par], vg)])
            else:
                s.op("vector", lambda e, par=par: e.tensor_tensor(
                    out=MRG[:], in0=MRG[:], in1=MSL[par][:], op=ALU.add),
                    waits=[(s.sEG[par], vg), (s.sDVE, s.sDVE.n)], inc=s.sDVE)
            vml[par] = s.op("vector", lambda e: e.nop(), inc=s.sDVE)
        v_mrg = s.sDVE.n
        s.op("vector", lambda e: e.tensor_copy(
            out=s.RTID[:], in_=MRG[:].rearrange("p (t w) -> p w t", w=2)[:, 0, :]),
            waits=[(s.sDVE, v_mrg)], inc=s.sDVE)
        s.op("vector", lambda e: e.tensor_copy(
            out=s.RSCL[:], in_=MRG[:].rearrange("p (t w) -> p w t", w=2)[:, 1, :]),
            inc=s.sDVE)
        v_sp = s.sDVE.n
        s.op("vector", lambda e: e.tensor_scalar(
            out=IDXF[:], in0=s.RTID[:], scalar1=-1.0, scalar2=None, op0=ALU.add),
            waits=[(s.sDVE, v_sp)], inc=s.sDVE)
        v_idxf = s.sDVE.n
        s.op("vector", lambda e: e.tensor_copy(out=s.IDXO[:], in_=IDXF[:]),
             waits=[(s.sDVE, v_idxf)], inc=s.sDVE)
        v_idxo = s.sDVE.n
        s.op("vector", lambda e: e.tensor_scalar(
            out=MSK[:], in0=s.RTID[:], scalar1=0.5, scalar2=None, op0=ALU.is_gt),
            waits=[(s.sDVE, v_sp)], inc=s.sDVE)
        vmsk = s.sDVE.n
        s.op("vector", lambda e: e.tensor_scalar(
            out=NMSK[:], in0=s.RTID[:], scalar1=0.5, scalar2=None, op0=ALU.is_le),
            inc=s.sDVE)
        vnmsk = s.sDVE.n
        s.op("vector", lambda e: e.tensor_tensor(
            out=GIF[:], in0=MSK[:], in1=IDXF[:], op=ALU.mult),
            waits=[(s.sDVE, vmsk), (s.sDVE, v_idxf)], inc=s.sDVE)
        vg1 = s.sDVE.n
        s.op("vector", lambda e: e.scalar_tensor_tensor(
            out=GIF[:], in0=NMSK[:], scalar=float(c.BIG), in1=GIF[:],
            op0=ALU.mult, op1=ALU.add),
            waits=[(s.sDVE, vnmsk), (s.sDVE, vg1)], inc=s.sDVE)
        vgp = s.sDVE.n
        s.op("vector", lambda e: e.tensor_copy(out=s.GIDX[:], in_=GIF[:]),
             waits=[(s.sDVE, vgp)], inc=s.sDVE)
        s.v_gidx = s.sDVE.n
        s.op("sync", lambda e: e.dma_start(out=s.o_idx, in_=s.IDXO[:]),
             waits=[(s.sDVE, v_idxo)], inc=s.sOUT, incn=16)

    def phase_expert(self):
        s, c = self, self.c
        FQ = s.FQ
        NFC = FQ // 128
        KQ = FQ // 128
        EGT = c.EGT
        TW = EGT * 128
        XGT = [s.sb(f"XGT{i}", [128, 1024]) for i in range(EGT)]
        XNN = s.sb("XNN", [128, 1024])
        XNB = [s.sb("XNB0", [128, 1024], BF16), s.sb("XNB1", [128, 1024], BF16)]
        ELN = [s.sb("ELN0", [128, 8]), s.sb("ELN1", [128, 8])]
        ETT = XNN
        XEG2 = [s.sb("XEGa", [128, 8 * TW], BF16), s.sb("XEGb", [128, 8 * TW], BF16)]
        HFMG2 = [s.sb("HFMGa", [128, NFC * TW], BF16),
                 s.sb("HFMGb", [128, NFC * TW], BF16)]
        YE2 = [s.sb("YEa", [128, 1024]), s.sb("YEb", [128, 1024])]
        pE = [s.PF[0], s.PF[1]]
        pY = [s.PF[2], s.PF[3]]
        pT2 = [s.PB[0], s.PB[1]]
        scr = s.xefm_scr.rearrange("(gg p) f -> gg p f", p=128)

        xg_free = [0] * EGT
        xg_dma = [None] * EGT
        ye_rel2 = [[(self.sYW, 0)], [(self.sYW, 0)]]
        xnn_free = 0
        eln1_free = [0, 0]
        vrelus = [0] * NFC
        xnb_free = [0, 0]
        t2_free = [0, 0]
        pe_free = [0, 0]
        py_free = [0, 0]
        xeg_free2 = [0, 0]
        xeg_w2 = [0, 0]
        hfm_free2 = [0, 0]
        eact = s.sACT.n
        v_h0w = []

        for q in range(4):
            wbuf = q % 2
            ggi_base = q * c.EG
            if q == 1:
                v_h0w = [(s.sYW, s.sYW.n), (s.sYWb, s.sYWb.n)]
                v_xf = [(s.sXF2[0], s.sXF2[0].n), (s.sXF2[1], s.sXF2[1].n)]
            if 2 <= q + 1 <= 3:
                pass
            if q >= 1:
                if q + 1 <= 3:
                    self.load_w_quarter(q + 1, wait_pe=s.sPE.n)
            elif q == 0:
                if False:
                    pass
            for gi in range(c.EG):
                tiles = list(range(gi * EGT, (gi + 1) * EGT))
                ggi = ggi_base + gi
                xb = ggi % 2
                XEG = XEG2[xb]
                HFMG = HFMG2[xb]
                if q == 0:
                    for ti, t in enumerate(tiles):
                        vz = s.op("vector", lambda e, ti=ti: e.memset(XGT[ti][:], 0.0),
                                  waits=[(s.sDVE, xg_free[ti])] +
                                        ([xg_dma[ti]] if xg_dma[ti] else []),
                                  inc=s.sDVE)
                        par = ti % 2
                        s.op("gpsimd", lambda e, t=t, ti=ti: e.indirect_dma_start(
                            out=XGT[ti][:], out_offset=None, in_=s.xfull,
                            in_offset=bass.IndirectOffsetOnAxis(ap=s.GIDX[:, t:t + 1], axis=0),
                            bounds_check=s.breg(e, c.T - 1), oob_is_err=False),
                            waits=[(s.sDVE, vz), (s.sDVE, s.v_gidx)],
                            inc=s.sEG[par], incn=16)
                        vg = s.sEG[par].n
                        LN = ELN[ti % 2]
                        s.op("vector", lambda e, ti=ti, LN=LN: e.tensor_reduce(
                            out=LN[:, 0:1], in_=XGT[ti][:], axis=AX.X, op=ALU.add),
                            waits=[(s.sEG[par], vg), (s.sDVE, s.sDVE.n)], inc=s.sDVE)
                        vst = s.sDVE.n
                        s.op("scalar", lambda e, ti=ti, LN=LN: e.activation(
                            out=ETT[:], in_=XGT[ti][:], func=AF.Square, bias=0.0,
                            scale=1.0, accum_out=LN[:, 1:2]),
                            waits=[(s.sEG[par], vg), (s.sDVE, xnn_free),
                                   (s.sDVE, eln1_free[ti % 2]), (s.sACT, eact)],
                            inc=s.sACT)
                        vsq = s.sACT.n
                        s.op("vector", lambda e, LN=LN: e.tensor_scalar(
                            out=LN[:, 2:3], in0=LN[:, 0:1], scalar1=1.0 / c.D, scalar2=None,
                            op0=ALU.mult), waits=[(s.sDVE, vst)], inc=s.sDVE)
                        vmean = s.sDVE.n
                        s.op("vector", lambda e, LN=LN: e.tensor_tensor(
                            out=LN[:, 3:4], in0=LN[:, 2:3], in1=LN[:, 2:3], op=ALU.mult),
                            waits=[(s.sDVE, vmean)], inc=s.sDVE)
                        vm2 = s.sDVE.n
                        s.op("vector", lambda e, LN=LN: e.scalar_tensor_tensor(
                            out=LN[:, 4:5], in0=LN[:, 1:2], scalar=1.0 / c.D, in1=LN[:, 3:4],
                            op0=ALU.mult, op1=ALU.subtract),
                            waits=[(s.sDVE, vm2), (s.sACT, vsq)], inc=s.sDVE)
                        vvar = s.sDVE.n
                        eln1_free[ti % 2] = vvar
                        s.op("scalar", lambda e, LN=LN: e.activation(
                            out=LN[:, 5:6], in_=LN[:, 4:5], func=AF.Ln, bias=1e-5, scale=1.0),
                            waits=[(s.sDVE, vvar), (s.sACT, eact)], inc=s.sACT)
                        va = s.sACT.n
                        s.op("scalar", lambda e, LN=LN: e.activation(
                            out=LN[:, 6:7], in_=LN[:, 5:6], func=AF.Exp, bias=0.0, scale=-0.5),
                            waits=[(s.sACT, va)], inc=s.sACT)
                        vrstd = s.sACT.n
                        eact = vrstd
                        s.op("vector", lambda e, ti=ti, LN=LN: e.tensor_scalar(
                            out=XNN[:], in0=XGT[ti][:], scalar1=LN[:, 2:3], scalar2=LN[:, 6:7],
                            op0=ALU.subtract, op1=ALU.mult),
                            waits=[(s.sACT, vrstd), (s.sDVE, s.sDVE.n)], inc=s.sDVE)
                        vxn = s.sDVE.n
                        bpar = ti % 2
                        s.op("vector", lambda e, bpar=bpar: e.tensor_copy(
                            out=XNB[bpar][:], in_=XNN[:]),
                            waits=[(s.sPE, xnb_free[bpar]), (s.sDVE, vxn)], inc=s.sDVE)
                        vhi = s.sDVE.n
                        xnn_free = vhi
                        for cp in range(4):
                            tp = cp % 2
                            s.op("tensor", lambda e, bpar=bpar, cp=cp, tp=tp: e.transpose(
                                out=pT2[tp][:, 0:128],
                                in_=XNB[bpar][:, (2 * cp) * 128:(2 * cp + 1) * 128],
                                identity=s.IDEB[:]),
                                waits=[(s.sDVE, vhi), (s.sDVE, t2_free[tp]),
                                       (s.sPE, xeg_free2[xb]),
                                       (s.sXF2[xb], xeg_w2[xb])])
                            s.op("tensor", lambda e, bpar=bpar, cp=cp, tp=tp: e.transpose(
                                out=pT2[tp][:, 128:256],
                                in_=XNB[bpar][:, (2 * cp + 1) * 128:(2 * cp + 2) * 128],
                                identity=s.IDEB[:]), inc=s.sPE)
                            vtp = s.sPE.n
                            b0 = (2 * cp) * TW + ti * 128
                            b1 = (2 * cp + 1) * TW + ti * 128
                            s.copy("vector", XEG[:, b0:b0 + 128], pT2[tp][:, 0:128],
                                   waits=[(s.sPE, vtp)])
                            t2_free[tp] = s.copy("vector", XEG[:, b1:b1 + 128],
                                                 pT2[tp][:, 128:256])
                        xnb_free[bpar] = s.sPE.n
                    vxeg = s.op("vector", lambda e: e.nop(), inc=s.sDVE)
                    s.op("sync", lambda e, gi=gi, XEG=XEG: e.dma_start(
                        out=scr[gi], in_=XEG[:]), waits=[(s.sDVE, vxeg)],
                        inc=s.sXF2[xb], incn=16)
                    xeg_w2[xb] = s.sXF2[xb].n
                    w_xeg = [(s.sDVE, vxeg)]
                else:
                    s.op("sync", lambda e, gi=gi, XEG=XEG: e.dma_start(
                        out=XEG[:], in_=scr[gi]),
                        waits=[(s.sPE, xeg_free2[xb]),
                               (s.sXE[xb], s.sXE[xb].n)] + v_xf,
                        inc=s.sXE[xb], incn=16)
                    w_xeg = [(s.sXE[xb], s.sXE[xb].n)]
                for fc in range(NFC):
                    par = fc % 2
                    w0 = list(w_xeg) + [(s.sWE[q % 2], s.v_wq[q]), (s.sACT, pe_free[par]),
                                        (s.sPE, hfm_free2[xb])]
                    for kc in range(8):
                        s.op("tensor", lambda e, fc=fc, kc=kc, par=par, wbuf=wbuf, TW=TW, FQ=FQ, XEG=XEG: e.matmul(
                            out=pE[par][:, 0:TW],
                            lhsT=s.W1Q[wbuf][:, kc * FQ + fc * 128: kc * FQ + (fc + 1) * 128],
                            rhs=XEG[:, kc * TW:(kc + 1) * TW],
                            start=(kc == 0), stop=(kc == 7)),
                            waits=w0 if kc == 0 else (), inc=s.sPE if kc == 7 else None)
                    vh = s.sPE.n
                    s.op("scalar", lambda e, fc=fc, par=par, TW=TW, HFMG=HFMG: e.activation(
                        out=HFMG[:, fc * TW:(fc + 1) * TW], in_=pE[par][:, 0:TW],
                        func=AF.Relu, bias=0.0, scale=1.0),
                        waits=[(s.sPE, vh), (s.sACT, eact)], inc=s.sACT)
                    pe_free[par] = s.sACT.n
                    eact = s.sACT.n
                    vrelus[fc] = s.sACT.n
                xeg_free2[xb] = s.sPE.n
                for ti, t in enumerate(tiles):
                    for dh in range(2):
                        par = dh
                        for kq in range(KQ):
                            s.op("tensor", lambda e, ti=ti, dh=dh, kq=kq, par=par, wbuf=wbuf, TW=TW, HFMG=HFMG: e.matmul(
                                out=pY[par][:],
                                lhsT=HFMG[:, kq * TW + ti * 128: kq * TW + (ti + 1) * 128],
                                rhs=s.W2Q[wbuf][:, kq * 1024 + dh * 512: kq * 1024 + (dh + 1) * 512],
                                start=(kq == 0), stop=(kq == KQ - 1)),
                                waits=([(s.sACT, vrelus[kq]), (s.sDVE, py_free[par])]
                                       if kq == 0 else [(s.sACT, vrelus[kq])]),
                                inc=s.sPE if kq == KQ - 1 else None)
                        vy = s.sPE.n
                        tp2 = ti % 2
                        YE = YE2[tp2]
                        if q == 0:
                            s.op("vector", lambda e, t=t, ti=ti, dh=dh, par=par, YE=YE: e.scalar_tensor_tensor(
                                out=YE[:, dh * 512:(dh + 1) * 512], in0=pY[par][:],
                                scalar=s.RSCL[:, t:t + 1],
                                in1=XGT[ti][:, dh * 512:(dh + 1) * 512],
                                op0=ALU.mult, op1=ALU.add),
                                waits=[(s.sPE, vy), (s.sDVE, s.sDVE.n)] + ye_rel2[tp2],
                                inc=s.sDVE)
                        else:
                            s.op("vector", lambda e, t=t, dh=dh, par=par, YE=YE: e.tensor_scalar(
                                out=YE[:, dh * 512:(dh + 1) * 512], in0=pY[par][:],
                                scalar1=s.RSCL[:, t:t + 1], scalar2=None, op0=ALU.mult),
                                waits=[(s.sPE, vy), (s.sDVE, s.sDVE.n)] + ye_rel2[tp2],
                                inc=s.sDVE)
                        py_free[par] = s.sDVE.n
                    vye = s.sDVE.n
                    tp2 = ti % 2
                    YE = YE2[tp2]
                    if q == 0:
                        xg_free[ti] = vye
                        ysem = s.sYW if tp2 == 0 else s.sYWb
                        s.op("sync", lambda e, t=t, YE=YE: e.dma_start(
                            out=s.xye_scr[t * 128:(t + 1) * 128, :], in_=YE[:]),
                            waits=[(s.sDVE, vye)], inc=ysem, incn=16)
                        xg_dma[ti] = (ysem, ysem.n)
                        ye_rel2[tp2] = [(ysem, ysem.n)]
                    elif tp2 == 0:
                        s.op("gpsimd", lambda e, t=t, YE=YE: e.dma_start(
                            out=s.xye_scr[t * 128:(t + 1) * 128, :], in_=YE[:],
                            accum_op=ALU.add),
                            waits=[(s.sDVE, vye)] + v_h0w, inc=s.sYB, incn=16)
                        ye_rel2[tp2] = [(s.sYB, s.sYB.n)]
                    else:
                        s.op("gpsimd", lambda e, t=t, YE=YE: e.dma_start(
                            out=s.xye_scr[t * 128:(t + 1) * 128, :], in_=YE[:],
                            accum_op=ALU.add),
                            waits=[(s.sDVE, vye)] + v_h0w, inc=s.sYA, incn=16)
                        ye_rel2[tp2] = [(s.sYA, s.sYA.n)]
                hfm_free2[xb] = s.sPE.n
        s.v_ya = s.sYA.n
        s.v_yb = s.sYB.n
        s.v_yw = s.sYW.n
        s.v_ywb = s.sYWb.n
        s.e_pe = s.sPE.n
        s.e_dve = s.sDVE.n
        s.e_act = s.sACT.n

    def phase_ctx(self):
        s, c = self, self.c
        H, NB = c.H, c.NB
        UCG = s.sb("UCG", [128, 2112])
        UCG2 = s.sb("UCG2", [128, 2112])
        EMX = s.sb("EMX", [128, 1])
        MX8 = s.sb("MX8", [128, 8])
        SBC = s.sb("SBC", [128, 1024])
        S8C = s.sb("S8C", [128, 8])
        pCT = [s.PF[0][:, 0:128], s.PF[1][:, 0:128]]
        pB0 = s.PF[2][0:65, 0:1]
        B0C = s.sb("B0C", [65, 1])
        pB0T = s.PF[3][0:1, 0:65]
        pSB = [s.PF[4][:, 0:512], s.PF[5][:, 0:512]]
        SRWS = s.sb("SRWS", [1, 1024])

        def ctxa(b, h, mc):
            off = ((b * H + h) * 2 + mc) * 65
            return s.CTXA[:, off:off + 65]

        def b0t(b, h):
            off = (b * H + h) * 65
            return s.B0T[:, off:off + 65]
        s.ctxa_view = ctxa
        s.b0t_view = b0t

        bar = [(s.sPE, s.e_pe), (s.sDVE, s.e_dve), (s.sACT, s.e_act)]
        pko = s.pk_out.rearrange("(r p) w -> r p w", p=128)
        for r in range(8):
            s.op("sync", lambda e, r=r: e.dma_start(
                out=MX8[:, r:r + 1], in_=pko[r, :, 2096:2097],
                allow_slow_non_contiguous=True),
                 waits=[(s.sC2b, s.v_c2b)], inc=s.sMG, incn=16)
        vmx8 = s.sMG.n
        s.op("vector", lambda e: e.tensor_reduce(out=EMX[:], in_=MX8[:], axis=AX.X,
                                                 op=ALU.max),
             waits=[(s.sMG, vmx8), (s.sDVE, s.e_dve)], inc=s.sDVE)
        vmx = s.sDVE.n
        s.op("scalar", lambda e: e.activation(out=EMX[:], in_=EMX[:], func=AF.Exp,
                                              bias=math.log(c.ratio), scale=-1.0),
             waits=[(s.sDVE, vmx), (s.sACT, s.e_act)], inc=s.sACT)
        s.v_emx = s.sACT.n

        ct_free = [0, 0]
        ucg_free = 0
        sbc_ms = 0
        for b in range(NB):
            s.op("sync", lambda e, b=b: e.dma_start(out=UCG[:, 0:2105], in_=pko[2 * b, :, 0:2105]),
                 waits=[(s.sC2b, s.v_c2b), (s.sDVE, ucg_free), (s.sMG, s.sMG.n)],
                 inc=s.sMG, incn=16)
            s.op("sync", lambda e, b=b: e.dma_start(out=UCG2[:, 0:2105], in_=pko[2 * b + 1, :, 0:2105]),
                 inc=s.sMG, incn=16)
            v1 = s.sMG.n
            s.op("vector", lambda e: e.tensor_tensor(out=UCG[:, 0:2105], in0=UCG[:, 0:2105],
                                                     in1=UCG2[:, 0:2105], op=ALU.add),
                 waits=[(s.sMG, v1), (s.sDVE, s.sDVE.n)], inc=s.sDVE)
            v_cmb = s.sDVE.n
            s.op("vector", lambda e: e.tensor_scalar(
                out=S8C[:], in0=UCG[:, 2097:2105], scalar1=c.ratio * c.EPS, scalar2=None,
                op0=ALU.mult), waits=[(s.sDVE, v_cmb)], inc=s.sDVE)
            v_s8c = s.sDVE.n
            s.op("sync", lambda e: e.dma_start(
                out=s.srow_dram.rearrange("o (p f) -> (o p) f", p=128), in_=S8C[:]),
                waits=[(s.sDVE, v_s8c), (s.sSR, s.sSR.n)], inc=s.sSR, incn=16)
            vsr = s.sSR.n
            SRW = SRWS
            s.op("sync", lambda e, SRW=SRW: e.dma_start(out=SRW[:], in_=s.srow_dram),
                 waits=[(s.sSR, vsr), (s.sPE, s.sPE.n)], inc=s.sSR, incn=16)
            vsr2 = s.sSR.n
            for hf in range(2):
                s.op("tensor", lambda e, hf=hf, SRW=SRW: e.matmul(
                    out=pSB[hf], lhsT=s.O1x128f[:], rhs=SRW[:, hf * 512:(hf + 1) * 512],
                    start=True, stop=True),
                    waits=[(s.sSR, vsr2), (s.sDVE, sbc_ms)] + (bar if b == 0 and hf == 0 else []),
                    inc=s.sPE)
                vps = s.sPE.n
                s.copy("vector", SBC[:, hf * 512:(hf + 1) * 512], pSB[hf],
                       waits=[(s.sPE, vps)])
            s.op("vector", lambda e: e.nop(), inc=s.sDVE)
            v_sbc = s.sDVE.n
            for hp in range(H // 2):
                for mc in range(2):
                    par = mc
                    s.op("tensor", lambda e, hp=hp, mc=mc, par=par: e.transpose(
                        out=pCT[par],
                        in_=UCG[0:128, hp * 256 + mc * 128: hp * 256 + (mc + 1) * 128],
                        identity=s.IDE[:]),
                        waits=[(s.sDVE, v_cmb), (s.sDVE, ct_free[par])], inc=s.sPE)
                    vt = s.sPE.n
                    for hh in range(2):
                        h = hp * 2 + hh
                        s.op("vector", lambda e, b=b, h=h, hh=hh, mc=mc, par=par: e.scalar_tensor_tensor(
                            out=ctxa(b, h, mc)[:, 0:64],
                            in0=pCT[par][:, hh * 64:(hh + 1) * 64],
                            scalar=EMX[:, 0:1], in1=SBC[:, h * 64:(h + 1) * 64],
                            op0=ALU.mult, op1=ALU.add),
                            waits=[(s.sPE, vt), (s.sACT, s.v_emx), (s.sDVE, v_sbc)],
                            inc=s.sDVE)
                        ucol = 2048 + (hh * 2 + mc) * 8 + hp
                        s.op("vector", lambda e, b=b, h=h, mc=mc, ucol=ucol: e.tensor_scalar(
                            out=ctxa(b, h, mc)[:, 64:65], in0=UCG[:, ucol:ucol + 1],
                            scalar1=EMX[:, 0:1], scalar2=c.ratio * c.EPS * c.N,
                            op0=ALU.mult, op1=ALU.add), inc=s.sDVE)
                    ct_free[par] = s.sDVE.n
                vcx = s.sDVE.n
                for hh in range(2):
                    h = hp * 2 + hh
                    for mc in range(2):
                        s.op("tensor", lambda e, b=b, h=h, mc=mc: e.matmul(
                            out=pB0, lhsT=ctxa(b, h, mc)[:], rhs=s.O128x1b[:],
                            start=(mc == 0), stop=(mc == 1)),
                            waits=[(s.sDVE, vcx)] if mc == 0 else (),
                            inc=s.sPE if mc == 1 else None)
                    vb0 = s.sPE.n
                    s.copy("vector", B0C[:], pB0, waits=[(s.sPE, vb0)])
                    vb0c = s.op("vector", lambda e: e.nop(), inc=s.sDVE)
                    s.op("tensor", lambda e: e.transpose(out=pB0T, in_=B0C[:], identity=s.IDE[0:65, 0:65]),
                         waits=[(s.sDVE, vb0c)], inc=s.sPE)
                    vb0t = s.sPE.n
                    s.copy("vector", b0t(b, h), pB0T, waits=[(s.sPE, vb0t)])
                    s.op("vector", lambda e: e.nop(), inc=s.sDVE)
            ucg_free = s.sDVE.n
            sbc_ms = s.sDVE.n
        s.v_ctx = s.sDVE.n
        s.ctx_pe = s.sPE.n

    def phase_b(self):
        s, c = self, self.c
        H = c.H
        TW = c.BGT * 128
        XEGB = [s.sb("XEGB0", [128, 8 * TW], BF16), s.sb("XEGB1", [128, 8 * TW], BF16)]
        QFM = s.sb("QFM", [128, 8 * TW], BF16)
        SQ2 = [s.sb("SQ20", [128, TW], BF16), s.sb("SQ21", [128, TW], BF16)]
        DIAGQ = s.sb("DIAGQ", [128, H * c.BGT])
        MXQG = s.sb("MXQG", [128, H * c.BGT])
        SUMW = s.sb("SUMW", [128, H * c.BGT])
        WTM = s.sb("WTM", [128, H * c.BGT], BF16)
        WTS = s.sb("WTS", [H * c.BGT, 128], BF16)
        WROWS = [s.sb("WROWS0", [1, H * TW], BF16), s.sb("WROWS1", [1, H * TW], BF16)]
        E0B = s.sb("E0B", [128, 2 * TW], BF16)
        RCP = s.sb("RCP", [1, 512])
        RCPB = s.sb("RCPB", [1, 512], BF16)
        RBS = s.sb("RBS", [64, 512])
        ATTN = s.sb("ATTN", [128, 8 * TW], BF16)
        XYET = [s.sb("XYET0", [128, 1024]), s.sb("XYET1", [128, 1024])]
        OUTT = [s.sb("OUTT0", [128, 1024]), s.sb("OUTT1", [128, 1024])]
        pQ = [s.PF[0], s.PF[1]]
        pDGQ = s.PF[2][:, 0:64]
        pD2 = [s.PF[2], s.PF[3]]
        pO = [s.PF[4][0:65, :], s.PF[5][0:65, :]]
        pWT = s.PB[0][0:H * c.BGT, 0:128]
        pRB = s.PF[0][0:64, :]
        wr = s.wrow_dram.rearrange("(g r) f -> g r f", r=H * c.BGT)
        scr = s.xefm_scr.rearrange("(gg p) f -> gg p f", p=128)

        bar = [(s.sPE, s.ctx_pe), (s.sDVE, s.v_ctx)]
        q_free = [0, 0]
        sq_free = [0, 0]
        d2_free_act = [0, 0]
        d2_free_dve = [0, 0]
        o_free = [0, 0]
        xegb_ms = [0, 0]
        xyet_ms = [0, 0]
        out_free = [0, 0]
        outt_dma = [0, 0]
        rbs_free = 0
        qfm_free = 0
        e0_free = 0
        attn_free = 0
        dgq_ms = 0
        bact = s.sACT.n

        for gi in range(c.BG):
            b = (gi * c.BGT) // c.CELLT
            tiles = list(range(gi * c.BGT, (gi + 1) * c.BGT))
            par = gi % 2
            for ti, t in enumerate(tiles):
                ge, tt = t // c.EGT, t % c.EGT
                ETW = c.EGT * 128
                s.op("sync", lambda e, ge=ge, tt=tt, ti=ti, par=par, ETW=ETW, TW=TW: e.dma_start(
                    out=XEGB[par][:].rearrange("p (ch w) -> p ch w", ch=8)[:, :, ti * 128:(ti + 1) * 128],
                    in_=scr[ge].rearrange("p (ch w) -> p ch w", ch=8)[:, :, tt * 128:(tt + 1) * 128]),
                    waits=[(s.sXF2[0], s.sXF2[0].n), (s.sXF2[1], s.sXF2[1].n),
                           (s.sPE, xegb_ms[par]), (s.sYA, s.v_ya)],
                    inc=s.sBR[1], incn=16)
            vxb = s.sBR[1].n
            for oc in range(8):
                qp_ = oc % 2
                w0 = [(s.sBR[1], vxb), (s.sWA, s.v_wqo), (s.sDVE, q_free[qp_]),
                      (s.sPE, qfm_free), (s.sDVE, dgq_ms)] + (bar if gi == 0 and oc == 0 else [])
                for kc in range(8):
                    s.op("tensor", lambda e, oc=oc, kc=kc, qp_=qp_, par=par, TW=TW: e.matmul(
                        out=pQ[qp_][:, 0:TW],
                        lhsT=s.WQB[:, kc * 1024 + oc * 128: kc * 1024 + (oc + 1) * 128],
                        rhs=XEGB[par][:, kc * TW:(kc + 1) * TW],
                        start=(kc == 0), stop=(kc == 7)),
                        waits=w0 if kc == 0 else (), inc=s.sPE if kc == 7 else None)
                vq = s.sPE.n
                s.op("vector", lambda e, oc=oc, qp_=qp_, TW=TW: e.tensor_copy(
                    out=QFM[:, oc * TW:(oc + 1) * TW], in_=pQ[qp_][:, 0:TW]),
                    waits=[(s.sPE, vq), (s.sDVE, s.sDVE.n)], inc=s.sDVE)
                q_free[qp_] = s.sDVE.n
                vqe = s.sDVE.n
                s.op("vector", lambda e, oc=oc, qp_=qp_, TW=TW: e.tensor_tensor(
                    out=SQ2[qp_][:], in0=QFM[:, oc * TW:(oc + 1) * TW],
                    in1=QFM[:, oc * TW:(oc + 1) * TW], op=ALU.mult),
                    waits=[(s.sDVE, vqe), (s.sPE, sq_free[qp_])], inc=s.sDVE)
                vsq = s.sDVE.n
                for s2 in range(c.BGT):
                    s.op("tensor", lambda e, oc=oc, s2=s2, qp_=qp_: e.matmul(
                        out=pDGQ[:, s2 * 16:(s2 + 1) * 16],
                        lhsT=SQ2[qp_][:, s2 * 128:(s2 + 1) * 128],
                        rhs=s.OD16P[:, oc * 16:(oc + 1) * 16],
                        start=(oc == 0 and s2 == 0), stop=(oc == 7 and s2 == c.BGT - 1)),
                        waits=[(s.sDVE, vsq)] if s2 == 0 else (),
                        inc=s.sPE if s2 == c.BGT - 1 else None)
                sq_free[qp_] = s.sPE.n
            vqf = s.sDVE.n
            vdq = s.sPE.n
            s.op("vector", lambda e: e.tensor_copy(
                out=DIAGQ[:, 0:16 * c.BGT].rearrange("p (h q) -> p h q", h=16, q=c.BGT),
                in_=pDGQ[:, 0:16 * c.BGT].rearrange("p (q h) -> p h q", q=c.BGT, h=16)),
                waits=[(s.sPE, vdq), (s.sDVE, s.sDVE.n)], inc=s.sDVE)
            vdqr = s.sDVE.n
            for hp in range(8):
                for s2 in range(c.BGT):
                    dpar = (hp * c.BGT + s2) % 2
                    s.op("tensor", lambda e, hp=hp, s2=s2, dpar=dpar, TW=TW: e.matmul(
                        out=pD2[dpar][:], lhsT=QFM[:, hp * TW + s2 * 128: hp * TW + (s2 + 1) * 128],
                        rhs=s.PBD[:], start=True, stop=True),
                        waits=[(s.sDVE, vqf), (s.sDVE, d2_free_dve[dpar]),
                               (s.sACT, d2_free_act[dpar]), (s.sDVE, vdqr)],
                        inc=s.sPE)
                    vdd = s.sPE.n
                    for hh in range(2):
                        col = (2 * hp + hh) * c.BGT + s2
                        s.op("vector", lambda e, col=col, hh=hh, dpar=dpar: e.tensor_reduce(
                            out=MXQG[:, col:col + 1], in_=pD2[dpar][:, hh * 256:(hh + 1) * 256],
                            axis=AX.X, op=ALU.max),
                            waits=[(s.sPE, vdd)], inc=s.sDVE)
                    d2_free_dve[dpar] = s.sDVE.n
            vmxq = s.sDVE.n
            s.op("vector", lambda e: e.tensor_tensor(out=SUMW[:], in0=DIAGQ[:], in1=MXQG[:],
                                                     op=ALU.add),
                 waits=[(s.sDVE, vmxq)], inc=s.sDVE)
            vsw = s.sDVE.n
            s.op("scalar", lambda e: e.activation(out=WTM[:], in_=SUMW[:], func=AF.Exp,
                                                  bias=c.LNEPS, scale=1.0),
                 waits=[(s.sDVE, vsw), (s.sACT, bact)], inc=s.sACT)
            vwtm = s.sACT.n
            bact = vwtm
            s.op("tensor", lambda e: e.transpose(out=pWT, in_=WTM[:], identity=s.IDEB[:]),
                 waits=[(s.sACT, vwtm)], inc=s.sPE)
            vwt = s.sPE.n
            s.copy("vector", WTS[:], pWT, waits=[(s.sPE, vwt)])
            vwts = s.op("vector", lambda e: e.nop(), inc=s.sDVE)
            s.op("sync", lambda e, gi=gi: e.dma_start(out=wr[gi], in_=WTS[:]),
                 waits=[(s.sDVE, vwts), (s.sWR[par], s.sWR[par].n)], inc=s.sWR[par], incn=16)
            vw1 = s.sWR[par].n
            s.op("sync", lambda e, gi=gi, par=par: e.dma_start(
                out=WROWS[par][:], in_=wr[gi].rearrange("r f -> (r f)").rearrange("(o q) -> o q", o=1)),
                waits=[(s.sWR[par], vw1)], inc=s.sWR[par], incn=16)
            vw2 = s.sWR[par].n
            for h in range(H):
                hh, hp = h % 2, h // 2
                opar = h % 2
                for mc in range(2):
                    dpar = mc
                    s.op("tensor", lambda e, hh=hh, hp=hp, mc=mc, dpar=dpar, TW=TW: e.matmul(
                        out=pD2[dpar][:, 0:TW],
                        lhsT=s.PBDT[:, (hh * 2 + mc) * 128:(hh * 2 + mc + 1) * 128],
                        rhs=QFM[:, hp * TW:(hp + 1) * TW], start=True, stop=True),
                        waits=[(s.sDVE, d2_free_dve[dpar]), (s.sACT, d2_free_act[dpar]),
                               (s.sPE, e0_free)],
                        inc=s.sPE)
                    vdd = s.sPE.n
                    s.op("scalar", lambda e, mc=mc, dpar=dpar, TW=TW: e.activation(
                        out=E0B[:, mc * TW:(mc + 1) * TW], in_=pD2[dpar][:, 0:TW],
                        func=AF.Exp, bias=0.0, scale=1.0),
                        waits=[(s.sPE, vdd), (s.sACT, bact)], inc=s.sACT)
                    d2_free_act[dpar] = s.sACT.n
                    bact = s.sACT.n
                    if mc == 0:
                        ve0a = s.sACT.n
                ve0 = s.sACT.n
                s.op("tensor", lambda e, b=b, h=h, opar=opar, TW=TW: e.matmul(
                    out=pO[opar][:, 0:TW], lhsT=s.ctxa_view(b, h, 0)[:], rhs=E0B[:, 0:TW],
                    start=True, stop=False),
                    waits=[(s.sACT, ve0a), (s.sDVE, o_free[opar]), (s.sDVE, s.v_ctx)])
                s.op("tensor", lambda e, b=b, h=h, opar=opar, TW=TW: e.matmul(
                    out=pO[opar][:, 0:TW], lhsT=s.ctxa_view(b, h, 1)[:],
                    rhs=E0B[:, TW:2 * TW], start=False, stop=False),
                    waits=[(s.sACT, ve0)])
                s.op("tensor", lambda e, b=b, h=h, opar=opar, par=par, TW=TW: e.matmul(
                    out=pO[opar][:, 0:TW], lhsT=s.b0t_view(b, h)[:],
                    rhs=WROWS[par][:, h * TW:(h + 1) * TW],
                    start=False, stop=True),
                    waits=[(s.sWR[par], vw2)], inc=s.sPE)
                e0_free = s.sPE.n
                vO = s.sPE.n
                s.op("vector", lambda e, opar=opar, TW=TW: e.tensor_copy(
                    out=RCPB[:, 0:TW], in_=pO[opar][64:65, 0:TW]),
                    waits=[(s.sPE, vO), (s.sDVE, s.sDVE.n)], inc=s.sDVE)
                vrcb = s.sDVE.n
                s.op("tensor", lambda e, TW=TW: e.matmul(
                    out=pRB[:, 0:TW], lhsT=s.O1x64b[:], rhs=RCPB[:, 0:TW],
                    start=True, stop=True),
                    waits=[(s.sDVE, vrcb), (s.sDVE, rbs_free)], inc=s.sPE)
                vrb = s.sPE.n
                s.op("scalar", lambda e, TW=TW: e.activation(
                    out=RBS[0:64, 0:TW], in_=pRB[0:64, 0:TW], func=AF.Ln,
                    bias=0.0, scale=1.0),
                    waits=[(s.sPE, vrb), (s.sDVE, rbs_free), (s.sACT, s.sACT.n)],
                    inc=s.sACT)
                vln = s.sACT.n
                s.op("scalar", lambda e, TW=TW: e.activation(
                    out=RBS[0:64, 0:TW], in_=RBS[0:64, 0:TW], func=AF.Exp,
                    bias=0.0, scale=-1.0),
                    waits=[(s.sACT, vln)], inc=s.sACT)
                vrbs = s.sACT.n
                bact = vrbs
                s.op("vector", lambda e, hh=hh, hp=hp, opar=opar, TW=TW: e.scalar_tensor_tensor(
                    out=ATTN[hh * 64:hh * 64 + 64, hp * TW:(hp + 1) * TW],
                    in0=pO[opar][0:64, 0:TW], scalar=1.0, in1=RBS[0:64, 0:TW],
                    op0=ALU.mult, op1=ALU.mult),
                    waits=[(s.sACT, vrbs), (s.sPE, attn_free), (s.sDVE, s.sDVE.n)],
                    inc=s.sDVE)
                o_free[opar] = s.sDVE.n
                rbs_free = s.sDVE.n
            vattn = s.sDVE.n
            for ti, t in enumerate(tiles):
                tp = ti % 2
                s.op("sync", lambda e, t=t, tp=tp: e.dma_start(
                    out=XYET[tp][:], in_=s.xye_scr[t * 128:(t + 1) * 128, :]),
                    waits=[(s.sYA, s.v_ya), (s.sYB, s.v_yb), (s.sYW, s.v_yw),
                           (s.sYWb, s.v_ywb),
                           (s.sDVE, xyet_ms[tp]), (s.sBR[1], s.sBR[1].n)],
                    inc=s.sBR[1], incn=16)
                vxy = s.sBR[1].n
                for dh in range(2):
                    wpar = dh
                    for kc in range(8):
                        s.op("tensor", lambda e, ti=ti, dh=dh, kc=kc, wpar=wpar, TW=TW: e.matmul(
                            out=pQ[wpar][:],
                            lhsT=ATTN[:, kc * TW + ti * 128: kc * TW + (ti + 1) * 128],
                            rhs=s.WOB[:, kc * 1024 + dh * 512: kc * 1024 + (dh + 1) * 512],
                            start=(kc == 0), stop=False),
                            waits=[(s.sDVE, vattn), (s.sDVE, q_free[wpar]),
                                   (s.sDVE, out_free[wpar])] if kc == 0 else ())
                    s.op("tensor", lambda e, dh=dh, wpar=wpar: e.matmul(
                        out=pQ[wpar][:], lhsT=s.OB1x128[:], rhs=s.BO2[:, dh * 512:(dh + 1) * 512],
                        start=False, stop=True), waits=[(s.sDVE, s.v_bo2)], inc=s.sPE)
                    vwo = s.sPE.n
                    s.op("vector", lambda e, dh=dh, tp=tp, wpar=wpar: e.tensor_tensor(
                        out=OUTT[tp][:, dh * 512:(dh + 1) * 512], in0=pQ[wpar][:],
                        in1=XYET[tp][:, dh * 512:(dh + 1) * 512], op=ALU.add),
                        waits=[(s.sPE, vwo), (s.sBR[1], vxy), (s.sDVE, s.sDVE.n),
                               (s.sOT[tp], outt_dma[tp])],
                        inc=s.sDVE)
                    q_free[wpar] = s.sDVE.n
                    out_free[wpar] = s.sDVE.n
                vout = s.sDVE.n
                xyet_ms[tp] = vout
                s.op("sync", lambda e, t=t, tp=tp: e.dma_start(
                    out=s.o_rows[t * 128:(t + 1) * 128, :], in_=OUTT[tp][:]),
                    waits=[(s.sDVE, vout)], inc=s.sOT[tp], incn=16)
                outt_dma[tp] = s.sOT[tp].n
            attn_free = s.sPE.n
            qfm_free = s.sPE.n
            xegb_ms[par] = s.sPE.n
            dgq_ms = s.sDVE.n
        s.op("sync", lambda e: e.nop(),
             waits=[(s.sOUT, s.sOUT.n), (s.sOT[0], s.sOT[0].n),
                    (s.sOT[1], s.sOT[1].n)])


def _consts(cfg):
    c = cfg
    dn2 = 0.5 * c.dn * c.dn
    e16n = np.zeros((128, 128), np.float32)
    e16p = np.zeros((128, 128), np.float32)
    for ch in range(8):
        for r in range(128):
            h = 2 * ch + (1 if r >= 64 else 0)
            e16n[r, ch * 16 + h] = -dn2
            e16p[r, ch * 16 + h] = dn2
    oc8 = np.zeros((128, 64), np.float32)
    for hp in range(8):
        oc8[:, hp * 8 + hp] = 1.0
    return {
        "cident": np.eye(128, dtype=np.float32),
        "cidentb": np.eye(128, dtype=np.float32).astype(ml_dtypes.bfloat16),
        "custrict": np.triu(np.ones((128, 128), np.float32), 1),
        "conesd16n": e16n.astype(ml_dtypes.bfloat16),
        "conesd16p": e16p.astype(ml_dtypes.bfloat16),
        "conescol8": oc8.astype(ml_dtypes.bfloat16),
        "cones1x128f": np.ones((1, 128), np.float32),
        "cones1x128b": np.ones((1, 128), ml_dtypes.bfloat16),
        "cones1x64b": np.ones((1, 64), ml_dtypes.bfloat16),
        "cones128x1f": np.ones((128, 1), np.float32),
        "cones128x1b": np.ones((128, 1), ml_dtypes.bfloat16),
        "ciota8": np.tile(np.arange(8, dtype=np.float32)[None, :], (128, 1)),
        "ciotap": np.arange(128, dtype=np.float32)[:, None].copy(),
        "cconstf": np.tile(np.array([0.0, 1e-5, math.log(cfg.EPS),
                                     math.log(cfg.ratio)], np.float32)[None, :], (128, 1)),
    }


_BUILD_CACHE = {}


def _get_nc(T):
    if T not in _BUILD_CACHE:
        _BUILD_CACHE[T] = Builder(Cfg(T)).build()
    return _BUILD_CACHE[T]


def make_in_maps(c, xf, Wq, Wk, Wv, Wo, bo, P, Wr, W1, W2):
    consts = _consts(c)
    in_maps = []
    for core in range(8):
        meta = np.zeros((128, 8), np.float32)
        meta[:, 0] = core
        meta[:, 1] = core * c.TPC
        meta[:, 3] = core - 1 if core % 2 == 1 else 999.0
        meta[:, 4] = (core // 2) * c.CELL
        meta[:, 5] = core * 128
        m = {
            "xmy": np.ascontiguousarray(xf[core * c.TPC:(core + 1) * c.TPC]),
            "xfull": xf,
            "wq": Wq, "wk": Wk, "wv": Wv, "wo": Wo,
            "boi": bo.reshape(1, -1),
            "ppi": P, "wri": Wr,
            "w1i": np.ascontiguousarray(W1[core]),
            "w2i": np.ascontiguousarray(W2[core]),
            "meta": meta,
        }
        m.update(consts)
        in_maps.append(m)
    return in_maps


def kernel(L0_attn_x, L0_attn_Wq, L0_attn_Wk, L0_attn_Wv, L0_attn_Wo, L0_attn_bo,
           L0_attn_P, L0_ffn_Wr, L0_ffn_W1, L0_ffn_W2):
    x = np.ascontiguousarray(np.asarray(L0_attn_x, np.float32))
    NBv, Nv, Dv = x.shape
    T = NBv * Nv
    c = Cfg(T)
    nc = _get_nc(T)
    xf = np.ascontiguousarray(x.reshape(T, Dv))
    in_maps = make_in_maps(
        c, xf,
        np.ascontiguousarray(np.asarray(L0_attn_Wq, np.float32)),
        np.ascontiguousarray(np.asarray(L0_attn_Wk, np.float32)),
        np.ascontiguousarray(np.asarray(L0_attn_Wv, np.float32)),
        np.ascontiguousarray(np.asarray(L0_attn_Wo, np.float32)),
        np.asarray(L0_attn_bo, np.float32),
        np.ascontiguousarray(np.asarray(L0_attn_P, np.float32)),
        np.ascontiguousarray(np.asarray(L0_ffn_Wr, np.float32)),
        np.asarray(L0_ffn_W1, np.float32),
        np.asarray(L0_ffn_W2, np.float32))
    res = run_bass_kernel_spmd(nc, in_maps, core_ids=list(range(8)))
    y = np.zeros((T, Dv), np.float32)
    for core in range(8):
        idx = np.asarray(res.results[core]["o_idx"]).T.reshape(-1)
        rows = np.asarray(res.results[core]["o_rows"])
        v = idx >= 0
        y[idx[v]] = rows[v]
    return y.reshape(NBv, Nv, Dv)
